# version 05
# baseline (speedup 1.0000x reference)
from dataclasses import dataclass

import numpy as np
import concourse.bass as bass
import concourse.mybir as mybir
from concourse.bass_utils import run_bass_kernel_spmd

F32 = mybir.dt.float32
BF16 = mybir.dt.bfloat16
AF = mybir.ActivationFunctionType
ALU = mybir.AluOpType
AXX = mybir.AxisListType.X

NCORES = 8
DM = 1024
DCH = DM // 128
H = 16
D = 64
C = 128
FH = 4096
FCH = FH // 128
ENGS = ("tensor", "vector", "scalar", "gpsimd", "sync")


@dataclass
class CFG:
    tpc: int = 2048
    n_cores: int = NCORES
    stage: int = 9
    debug: bool = False
    p2c: int = 0
    p2s: int = 99


class Prog:
    def __init__(self):
        self.steps = {e: [] for e in ENGS}
        self.cnt = {}
        self.sem_names = []
        self._dmafloor = {}
        self.phases = []
        self.seg = {}
        self._seg_start = {e: 0 for e in ENGS}
        self.allocs = {}
        self.frees = []

    def phase(self, name):
        if self.phases:
            prev = self.phases[-1]
            self.seg[prev] = {e: (self._seg_start[e], len(self.steps[e])) for e in ENGS}
        self._seg_start = {e: len(self.steps[e]) for e in ENGS}
        self.phases.append(name)
        self.allocs[name] = []

    def close(self):
        prev = self.phases[-1]
        self.seg[prev] = {e: (self._seg_start[e], len(self.steps[e])) for e in ENGS}

    def alloc(self, name, kind, shape, dt, side=None):
        self.allocs[self.phases[-1]].append((name, kind, list(shape), dt, side))

    def free(self, *names):
        for n in names:
            self.frees.append((len(self.phases) - 1, n))

    def sem(self, name):
        self.sem_names.append(name)
        self.cnt[name] = 0

    def raw(self, eng, fn):
        self.steps[eng].append(fn)

    def wait(self, eng, sname, val=None):
        v = self.cnt[sname] if val is None else val
        assert v <= self.cnt[sname], (sname, v, self.cnt[sname])
        if v > 0:
            self.raw(eng, lambda E, ctx, s=sname, v=v: E.wait_ge(ctx[s], v))

    def inst(self, eng, fn, sname=None, n=1):
        if sname is None:
            self.raw(eng, fn)
        else:
            self.cnt[sname] += n
            self.raw(eng, lambda E, ctx, f=fn, s=sname, n=n: f(E, ctx).then_inc(ctx[s], n))

    def dma(self, eng, ch, fn):
        self.inst(eng, fn, ch, 16)

    def group(self, eng, ch):
        key = (eng, ch)
        if self.cnt[ch] > self._dmafloor.get(key, 0):
            self.wait(eng, ch, self.cnt[ch])
            self._dmafloor[key] = self.cnt[ch]


def _retile_w(w):
    I, O = w.shape
    return np.ascontiguousarray(
        w.reshape(I // 128, 128, O).transpose(1, 0, 2)).reshape(128, -1)


def _panels(wT, po=128):
    I, O = wT.shape
    npan = O // po
    a = wT.reshape(I // 128, 128, npan, po).transpose(2, 1, 0, 3)
    return np.ascontiguousarray(a).reshape(npan, 128, (I // 128) * po)


def _cvec(v):
    return np.ascontiguousarray(v.reshape(-1, 128).T)


def build_consts():
    ident = np.eye(128, dtype=np.float32)
    e64 = np.zeros((128, 64), np.float32); e64[:64, :64] = np.eye(64)
    maskl = np.tril(np.ones((128, 128), np.float32), -1)
    masku = np.triu(np.ones((128, 128), np.float32), 1)
    maskui = np.triu(np.ones((128, 128), np.float32), 0)
    ones1 = np.ones((1, 128), np.float32)
    onesc = np.ones((128, 1), np.float32)
    oneh16 = np.zeros((16, 16 * 128), np.float32)
    for h in range(16):
        oneh16[h, h * 128:(h + 1) * 128] = 1.0
    oneh4 = np.zeros((4, 4 * 128), np.float32)
    for i in range(4):
        oneh4[i, i * 128:(i + 1) * 128] = 1.0
    return dict(ident=ident, e64=e64, maskl=maskl, masku=masku, maskui=maskui,
                ones1=ones1, onesc=onesc, oneh16=oneh16, oneh4=oneh4)


def build_nc(cfg: CFG):
    T = cfg.tpc
    NCH = T // C
    NT = T // 512
    nc = bass.Bass("TRN2", target_bir_lowering=False, debug=False,
                   detect_race_conditions=False)

    def din(name, shape, dt=F32):
        return nc.dram_tensor(name, list(shape), dt, kind="ExternalInput").ap()

    x_in = din("x", (DM, T))
    wq = din("wq", (128, DCH * DM)); bq = din("bq", (128, DCH))
    wk = din("wk", (128, DCH * DM)); bk = din("bk", (1, DM))
    wv = din("wv", (128, DCH * DM)); bv = din("bv", (1, DM))
    wb = din("wb", (128, DCH * 16)); bb = din("bb", (1, 16))
    wo = din("wo", (128, DCH * DM)); bo = din("bo", (128, DCH))
    w1p = din("w1p", (FCH, 128, DCH * 128)); w1s = din("w1s", (128, FCH))
    w2p = din("w2p", (FCH, 128, DCH * 128)); w2s = din("w2s", (128, FCH))
    h1p = din("h1p", (FCH, 128, FCH * 128)); h1s = din("h1s", (128, FCH))
    h2p = din("h2p", (FCH, 128, FCH * 128)); h2s = din("h2s", (128, FCH))
    w3p = din("w3p", (DCH, 128, FCH * 128)); w3s = din("w3s", (128, DCH))
    ident = din("ident", (128, 128)); e64 = din("e64", (128, 64))
    maskl = din("maskl", (128, 128)); masku = din("masku", (128, 128))
    maskui = din("maskui", (128, 128))
    ones1 = din("ones1", (1, 128)); onesc = din("onesc", (128, 1))
    oneh16 = din("oneh16", (16, 16 * 128)); oneh4 = din("oneh4", (4, 4 * 128))
    pmask = din("pmask", (64, 8)); imask = din("imask", (64, 8 * 64))

    out = nc.dram_tensor("out", [DM, T], F32, kind="ExternalOutput").ap()
    out_pct = out.rearrange("(c p) t -> p c t", p=128)

    cUB = nc.dram_tensor("cUB", [NCH, 128, H * 64], BF16).ap()
    cUPT = nc.dram_tensor("cUPT", [NCH, 64, H * 128], BF16).ap()
    cZPT = nc.dram_tensor("cZPT", [NCH, 64, H * 64], BF16).ap()
    cB = nc.dram_tensor("cB", [NCH, 128, (H // 2) * 64], BF16).ap()
    agi = nc.dram_tensor("agi", [192, 1024], F32).ap()
    ago = nc.dram_tensor("ago", [cfg.n_cores * 192, 1024], F32).ap()

    dbg = {}
    if cfg.debug:
        def dout(name, shape, dt=F32):
            dbg[name] = nc.dram_tensor(name, list(shape), dt, kind="ExternalOutput").ap()
        dout("d_y1", (128, DCH * T))
        dout("d_ktm", (128, NCH * DM))
        dout("d_vtm", (128, NCH * DM))
        dout("d_qfm", (128, DCH * T))
        dout("d_kfm", (128, DCH * T))
        dout("d_beta", (128, NCH * 16))
        if cfg.stage >= 2:
            dout("d_UB", (NCH, 128, H * 64), BF16)
            dout("d_UPT", (NCH, 64, H * 128), BF16)
            dout("d_ZPT", (NCH, 64, H * 64), BF16)
            dout("d_B", (NCH, 128, (H // 2) * 64), BF16)
            dout("d_zfin", (128, (H // 2) * 128))
        if cfg.stage >= 3:
            dout("d_zcore", (64, H * 64), BF16)
        if cfg.stage >= 4:
            dout("d_ofm", (128, DCH * T))
        if cfg.stage >= 5:
            dout("d_y2", (128, DCH * T))
        if cfg.stage >= 9:
            dout("d_hffn", (128, FCH * (T // 2)), BF16)

    p = Prog()
    for s in ["gd", "sd", "tm", "ve", "sc", "cc"]:
        p.sem(s)

    def sb(name, shape, dt=BF16, side=None):
        p.alloc(name, "sbuf", list(shape), dt, side)

    def ps(name, shape, dt=F32):
        p.alloc(name, "psum", list(shape), dt, None)

    NPOW = 6
    TFh = T // 2

    p.phase("P0")
    ps("PA", (128, 512)); ps("PB", (128, 512)); ps("PC", (128, 512))
    ps("PZ1", (128, 1024)); ps("PZ2", (128, 1024))
    ps("PT", (128, 256), BF16)
    sb("c_ident", (128, 128)); sb("c_e64", (128, 64))
    sb("c_maskl", (128, 128)); sb("c_masku", (128, 128)); sb("c_maskui", (128, 128))
    sb("c_ones1", (1, 128), F32); sb("c_onesc", (128, 1), BF16)
    sb("czero", (128, 128))
    sb("c_ones1b", (1, 128), BF16)
    sb("y1", (128, DCH * T))
    sb("lntmp", (1, 2048), F32)
    sb("lnr", (1, 2 * T), BF16)
    sb("bc1", (128, T)); sb("bc2", (128, T))
    sb("sq", (128, 4096))

    def ln_stats_math(nt, save_rows=None):
        p.inst("scalar", lambda E, ctx: E.mul(ctx["lntmp"][0:1, 1024:1536], ctx["lntmp"][0:1, 0:512], np.float32(1.0 / DM)), "sc")
        p.inst("scalar", lambda E, ctx: E.mul(ctx["lntmp"][0:1, 1536:2048], ctx["lntmp"][0:1, 512:1024], np.float32(1.0 / DM)), "sc")
        p.wait("vector", "sc")
        p.inst("vector", lambda E, ctx: E.tensor_mul(
            ctx["lntmp"][0:1, 0:512], ctx["lntmp"][0:1, 1024:1536], ctx["lntmp"][0:1, 1024:1536]), "ve")
        p.inst("vector", lambda E, ctx: E.tensor_sub(
            ctx["lntmp"][0:1, 512:1024], ctx["lntmp"][0:1, 1536:2048], ctx["lntmp"][0:1, 0:512]), "ve")
        p.wait("scalar", "ve")
        p.inst("scalar", lambda E, ctx: E.activation(
            ctx["lntmp"][0:1, 0:512], ctx["lntmp"][0:1, 512:1024], AF.Ln, bias=np.float32(1e-5)), "sc")
        p.inst("scalar", lambda E, ctx, nt=nt: E.activation(
            ctx["lnr"][0:1, nt * 512:(nt + 1) * 512], ctx["lntmp"][0:1, 0:512],
            AF.Exp, scale=np.float32(-0.5)), "sc")
        p.inst("scalar", lambda E, ctx: E.activation(
            ctx["lntmp"][0:1, 512:1024], ctx["lntmp"][0:1, 0:512],
            AF.Exp, scale=np.float32(-0.5)), "sc")
        if save_rows is not None:
            p.inst("scalar", lambda E, ctx, sr=save_rows, nt=nt: E.activation(
                ctx[sr][0:1, T + nt * 512: T + (nt + 1) * 512], ctx["lntmp"][0:1, 0:512],
                AF.Exp, scale=np.float32(0.5)), "sc")
            p.inst("scalar", lambda E, ctx, sr=save_rows, nt=nt: E.copy(
                ctx[sr][0:1, nt * 512:(nt + 1) * 512], ctx["lntmp"][0:1, 1024:1536]), "sc")
        p.wait("vector", "sc")
        p.inst("vector", lambda E, ctx: E.tensor_mul(
            ctx["lntmp"][0:1, 0:512], ctx["lntmp"][0:1, 1024:1536], ctx["lntmp"][0:1, 512:1024]), "ve")
        p.inst("vector", lambda E, ctx, nt=nt: E.tensor_scalar_mul(
            ctx["lnr"][0:1, T + nt * 512: T + (nt + 1) * 512], ctx["lntmp"][0:1, 0:512], -1.0), "ve")

    def ln_bcast(nt):
        p.wait("tensor", "ve")
        p.wait("tensor", "sc")
        p.inst("tensor", lambda E, ctx, nt=nt: E.matmul(
            ctx["PA"][:, :], ctx["c_ones1b"][:, :],
            ctx["lnr"][0:1, nt * 512:(nt + 1) * 512], start=True, stop=True,
            skip_group_check=True))
        p.inst("tensor", lambda E, ctx, nt=nt: E.matmul(
            ctx["PB"][:, :], ctx["c_ones1b"][:, :],
            ctx["lnr"][0:1, T + nt * 512: T + (nt + 1) * 512], start=True, stop=True,
            skip_group_check=True), "tm")
        p.wait("vector", "tm")
        p.inst("vector", lambda E, ctx, nt=nt: E.tensor_copy(
            ctx["bc1"][:, nt * 512:(nt + 1) * 512], ctx["PA"][:, :]), "ve")
        p.inst("vector", lambda E, ctx, nt=nt: E.tensor_copy(
            ctx["bc2"][:, nt * 512:(nt + 1) * 512], ctx["PB"][:, :]), "ve")
        p.wait("tensor", "ve")

    def ln_feature_major(src, dst, save_rows=None):
        for nt in range(NT):
            p.wait("scalar", "tm")
            for c in range(DCH):
                reg = c * 512
                p.inst("scalar", lambda E, ctx, c=c, nt=nt, reg=reg, src=src: E.activation(
                    ctx["sq"][:, reg:reg + 512],
                    ctx[src][:, c * T + nt * 512: c * T + (nt + 1) * 512],
                    AF.Square), "sc")
                p.wait("tensor", "sc")
                p.inst("tensor", lambda E, ctx, c=c, nt=nt, src=src: E.matmul(
                    ctx["PA"][0:1, :], ctx["c_onesc"][:, :],
                    ctx[src][:, c * T + nt * 512: c * T + (nt + 1) * 512],
                    start=(c == 0), stop=(c == DCH - 1), skip_group_check=True))
                p.inst("tensor", lambda E, ctx, c=c, nt=nt, reg=reg: E.matmul(
                    ctx["PB"][0:1, :], ctx["c_onesc"][:, :],
                    ctx["sq"][:, reg:reg + 512],
                    start=(c == 0), stop=(c == DCH - 1), skip_group_check=True),
                    "tm" if c == DCH - 1 else None)
            p.wait("vector", "tm")
            p.inst("vector", lambda E, ctx, nt=nt: E.tensor_copy(
                ctx["lntmp"][0:1, 0:512], ctx["PA"][0:1, :]), "ve")
            p.inst("vector", lambda E, ctx, nt=nt: E.tensor_copy(
                ctx["lntmp"][0:1, 512:1024], ctx["PB"][0:1, :]), "ve")
            p.wait("tensor", "ve")
            p.wait("scalar", "ve")
            ln_stats_math(nt, save_rows)
            ln_bcast(nt)
        for c in range(DCH):
            p.inst("vector", lambda E, ctx, c=c, src=src, dst=dst: E.tensor_mul(
                ctx[dst][:, c * T:(c + 1) * T], ctx[src][:, c * T:(c + 1) * T],
                ctx["bc1"][:, :]), "ve")
            p.inst("vector", lambda E, ctx, c=c, dst=dst: E.tensor_add(
                ctx[dst][:, c * T:(c + 1) * T], ctx[dst][:, c * T:(c + 1) * T],
                ctx["bc2"][:, :]), "ve")

    consts = [(ident, "c_ident"), (e64, "c_e64"), (maskl, "c_maskl"),
              (masku, "c_masku"), (maskui, "c_maskui"), (ones1, "c_ones1"),
              (onesc, "c_onesc"), (ones1, "c_ones1b")]
    for iap, sname in consts:
        p.inst("gpsimd", lambda E, ctx, iap=iap, sname=sname: E.dma_start(
            out=ctx[sname].ap(), in_=iap), "gd", 16)
    p.inst("gpsimd", lambda E, ctx: E.dma_start(out=ctx["y1"].ap().rearrange("p (c t) -> p c t", c=DCH), in_=x_in.rearrange("(c p) t -> p c t", p=128)), "gd", 16)
    p.inst("vector", lambda E, ctx: E.memset(ctx["czero"][:, :], 0.0), "ve")
    ng0 = p.cnt["gd"]
    for e in ("scalar", "vector", "tensor", "sync"):
        p.wait(e, "gd", ng0)

    ln_feature_major("y1", "y1")

    p.free("bc1", "bc2", "sq")
    p.phase("P1a")
    sb("ktm", (128, NCH * DM))
    sb("vtm", (128, NCH * DM))
    sb("kfm", (128, DCH * T))
    sb("beta", (128, NCH * 16), F32)
    sb("nbeta", (128, NCH * 16), F32)
    sb("betab", (128, NCH * 16), BF16)
    sb("wt", (128, DCH * DM))
    sb("bconst", (128, DM), F32)
    sb("bcol", (128, DCH), F32)
    sb("knr", (128, NCH * 16), F32)
    sb("knrb", (128, NCH * 16), BF16)
    sb("sqk", (128, DM))

    def bias_broadcast(bias_ap, width):
        p.wait("sync", "ve")
        p.wait("sync", "tm")
        p.group("sync", "sd")
        p.inst("sync", lambda E, ctx, b=bias_ap, w=width: E.dma_start(
            out=ctx["lntmp"][0:1, 0:w], in_=b), "sd", 16)
        p.wait("tensor", "sd")
        ntw = (width + 511) // 512
        for nt in range(ntw):
            w0, w1 = nt * 512, min(width, (nt + 1) * 512)
            p.inst("tensor", lambda E, ctx, w0=w0, w1=w1: E.matmul(
                ctx["PA"][:, 0:w1 - w0], ctx["c_ones1"][:, :],
                ctx["lntmp"][0:1, w0:w1], start=True, stop=True, skip_group_check=True), "tm")
            p.wait("vector", "tm")
            p.inst("vector", lambda E, ctx, w0=w0, w1=w1: E.tensor_copy(
                ctx["bconst"][:, w0:w1], ctx["PA"][:, 0:w1 - w0]), "ve")
            p.wait("tensor", "ve")

    def load_w(src_ap, width=DCH * DM):
        p.wait("gpsimd", "tm")
        p.group("gpsimd", "gd")
        p.inst("gpsimd", lambda E, ctx, s=src_ap, w=width: E.dma_start(
            out=ctx["wt"][:, 0:w], in_=s), "gd", 16)

    def proj_tm(dst):
        for tc in range(NCH):
            for nt in range(2):
                pp = "PA" if (tc * 2 + nt) % 2 == 0 else "PB"
                for c in range(DCH):
                    p.inst("tensor", lambda E, ctx, tc=tc, nt=nt, c=c, pp=pp: E.matmul(
                        ctx[pp][:, :],
                        ctx["y1"][:, c * T + tc * 128: c * T + (tc + 1) * 128],
                        ctx["wt"][:, c * DM + nt * 512: c * DM + (nt + 1) * 512],
                        start=(c == 0), stop=(c == DCH - 1), skip_group_check=True),
                        "tm" if c == DCH - 1 else None)
                p.wait("vector", "tm")
                p.inst("vector", lambda E, ctx, dst=dst, tc=tc, nt=nt, pp=pp:
                       E.scalar_tensor_tensor(
                           ctx[dst][:, tc * DM + nt * 512: tc * DM + (nt + 1) * 512],
                           ctx[pp][:, :], 1.0,
                           ctx["bconst"][:, nt * 512:(nt + 1) * 512],
                           ALU.mult, ALU.add), "ve")
                if (tc * 2 + nt) % 2 == 1:
                    p.wait("tensor", "ve")

    def proj_fm(dst, src="y1", bias="bcol"):
        for oc in range(DCH):
            for nt in range(NT):
                pp = "PA" if (oc * NT + nt) % 2 == 0 else "PB"
                for c in range(DCH):
                    p.inst("tensor", lambda E, ctx, oc=oc, nt=nt, c=c, pp=pp, src=src: E.matmul(
                        ctx[pp][:, :],
                        ctx["wt"][:, c * DM + oc * 128: c * DM + (oc + 1) * 128],
                        ctx[src][:, c * T + nt * 512: c * T + (nt + 1) * 512],
                        start=(c == 0), stop=(c == DCH - 1), skip_group_check=True),
                        "tm" if c == DCH - 1 else None)
                p.wait("scalar", "tm")
                p.inst("scalar", lambda E, ctx, dst=dst, oc=oc, nt=nt, pp=pp, bias=bias:
                       E.activation(
                           ctx[dst][:, oc * T + nt * 512: oc * T + (nt + 1) * 512],
                           ctx[pp][:, :], AF.Identity,
                           bias=ctx[bias][:, oc:oc + 1]), "sc")
                if (oc * NT + nt) % 2 == 1:
                    p.wait("tensor", "sc")

    load_w(wk)
    bias_broadcast(bk, DM)
    p.wait("tensor", "gd")
    proj_tm("ktm")
    load_w(wv)
    bias_broadcast(bv, DM)
    p.wait("tensor", "gd")
    p.wait("vector", "gd")
    proj_tm("vtm")
    p.wait("gpsimd", "tm")
    p.group("gpsimd", "gd")
    p.inst("gpsimd", lambda E, ctx: E.dma_start(out=ctx["wt"][:, 0:DCH * 16], in_=wb), "gd", 16)
    bias_broadcast(bb, 16)
    p.wait("tensor", "gd")
    for tc in range(NCH):
        pp = "PA" if tc % 2 == 0 else "PB"
        for c in range(DCH):
            p.inst("tensor", lambda E, ctx, tc=tc, c=c, pp=pp: E.matmul(
                ctx[pp][:, 0:16],
                ctx["y1"][:, c * T + tc * 128: c * T + (tc + 1) * 128],
                ctx["wt"][:, c * 16:(c + 1) * 16],
                start=(c == 0), stop=(c == DCH - 1), skip_group_check=True),
                "tm" if c == DCH - 1 else None)
        p.wait("vector", "tm")
        p.inst("vector", lambda E, ctx, tc=tc, pp=pp: E.tensor_add(
            ctx["beta"][:, tc * 16:(tc + 1) * 16], ctx[pp][:, 0:16],
            ctx["bconst"][:, 0:16]), "ve")
        if tc % 2 == 1:
            p.wait("tensor", "ve")
    p.wait("scalar", "ve")
    p.inst("scalar", lambda E, ctx: E.activation(
        ctx["beta"][:, :], ctx["beta"][:, :], AF.Sigmoid), "sc")
    p.inst("scalar", lambda E, ctx: E.activation(
        ctx["nbeta"][:, :], ctx["beta"][:, :], AF.Identity, scale=np.float32(-1.0)), "sc")
    p.inst("scalar", lambda E, ctx: E.copy(ctx["betab"][:, :], ctx["beta"][:, :]), "sc")
    for tc in range(NCH):
        p.wait("scalar", "ve")
        p.wait("scalar", "sc")
        p.inst("scalar", lambda E, ctx, tc=tc: E.activation(
            ctx["sqk"][:, 0:DM], ctx["ktm"][:, tc * DM:(tc + 1) * DM], AF.Square), "sc")
        p.wait("vector", "sc")
        p.inst("vector", lambda E, ctx, tc=tc: E.reduce_sum(
            ctx["knr"][:, tc * 16:(tc + 1) * 16].rearrange("p (h o) -> p h o", o=1),
            ctx["sqk"][:, 0:DM].rearrange("p (h d) -> p h d", d=64), AXX), "ve")
    p.wait("scalar", "ve")
    p.inst("scalar", lambda E, ctx: E.activation(ctx["knr"][:, :], ctx["knr"][:, :], AF.Ln), "sc")
    p.inst("scalar", lambda E, ctx: E.activation(ctx["knr"][:, :], ctx["knr"][:, :], AF.Exp, scale=np.float32(0.5)), "sc")
    p.inst("scalar", lambda E, ctx: E.add(ctx["knr"][:, :], ctx["knr"][:, :], np.float32(1e-6)), "sc")
    p.wait("vector", "sc")
    p.inst("vector", lambda E, ctx: E.reciprocal(ctx["knr"][:, :], ctx["knr"][:, :]), "ve")
    p.inst("vector", lambda E, ctx: E.tensor_copy(ctx["knrb"][:, :], ctx["knr"][:, :]), "ve")
    for tc in range(NCH):
        p.inst("vector", lambda E, ctx, tc=tc: E.tensor_mul(
            ctx["ktm"][:, tc * DM:(tc + 1) * DM].rearrange("p (h d) -> p h d", d=64),
            ctx["ktm"][:, tc * DM:(tc + 1) * DM].rearrange("p (h d) -> p h d", d=64),
            ctx["knrb"][:, tc * 16:(tc + 1) * 16].rearrange("p (h o) -> p h o", o=1)
            .broadcast_to([128, 16, 64])), "ve")
    p.wait("tensor", "ve")
    for tc in range(NCH):
        for oc in range(DCH):
            half = ((tc * DCH + oc) % 2) * 128
            p.inst("tensor", lambda E, ctx, tc=tc, oc=oc, half=half: E.transpose(
                ctx["PT"][:, half:half + 128],
                ctx["ktm"][:, tc * DM + oc * 128: tc * DM + (oc + 1) * 128],
                ctx["c_ident"][:, :]), "tm")
            p.wait("vector", "tm")
            p.inst("vector", lambda E, ctx, tc=tc, oc=oc, half=half: E.tensor_copy(
                ctx["kfm"][:, oc * T + tc * 128: oc * T + (tc + 1) * 128],
                ctx["PT"][:, half:half + 128]), "ve")
            if (tc * DCH + oc) % 2 == 1:
                p.wait("tensor", "ve")

    p.free("bconst", "knr", "knrb", "sqk")
    p.phase("P1b")
    sb("qfm", (128, DCH * T))
    load_w(wq)
    p.wait("tensor", "gd")
    p.wait("scalar", "gd")
    p.inst("gpsimd", lambda E, ctx: E.dma_start(out=ctx["bcol"][:, 0:DCH], in_=bq), "gd", 16)
    p.wait("scalar", "gd")
    proj_fm("qfm")

    if cfg.debug:
        def dump(src, dname, reg=None):
            for e in ("ve", "sc", "tm"):
                p.wait("gpsimd", e)
            p.group("gpsimd", "gd")
            if reg is None:
                p.inst("gpsimd", lambda E, ctx, src=src, dname=dname: E.dma_start(
                    out=dbg[dname], in_=ctx[src].ap()), "gd", 16)
            else:
                p.inst("gpsimd", lambda E, ctx, src=src, dname=dname, reg=reg: E.dma_start(
                    out=dbg[dname], in_=ctx[src][reg]), "gd", 16)
        dump("y1", "d_y1"); dump("ktm", "d_ktm"); dump("vtm", "d_vtm")
        dump("qfm", "d_qfm"); dump("kfm", "d_kfm"); dump("beta", "d_beta")

    if cfg.stage >= 2:
        p.free("y1", "wt")
        p.phase("P2")
        sb("c_oneh16", (16, 16 * 128))
        sb("Tt", (128, 256))
        sb("powN", (128, NPOW * 128))
        sb("powNT", (128, 5 * 128))
        sb("Nlow", (128, 128)); sb("Nup", (128, 128))
        sb("chR", (128, 2 * 128))
        sb("mbB", (128, 128))
        sb("zbd", (128, (H // 2) * 256))
        sb("vg", (128, 128))
        sb("rh", (128, 256))
        sb("Ush", (128, 256))
        sb("stUB", (128, H * 64))
        sb("stUPT", (64, H * 128))
        sb("stZPT", (64, H * 64))
        sb("brow", (16, 128))
        sb("zfin", (128, 1024), F32)
        sb("ptfin", (64, 1024), F32)
        p.group("gpsimd", "gd")
        p.inst("gpsimd", lambda E, ctx: E.dma_start(
            out=ctx["c_oneh16"].ap(), in_=oneh16), "gd", 16)
        for e in ("tensor", "vector", "scalar"):
            p.wait(e, "gd")
        p.wait("tensor", "ve")
        for pr in range(8):
            p.inst("tensor", lambda E, ctx, pr=pr: E.matmul(
                ctx["PZ1"][0:64, pr * 128: pr * 128 + 64], ctx["c_e64"][:, :],
                ctx["c_e64"][:, :], start=(pr % 4 == 0), stop=False,
                tile_position=(0, 0), skip_group_check=True))
            p.inst("tensor", lambda E, ctx, pr=pr: E.matmul(
                ctx["PZ1"][64:128, pr * 128: pr * 128 + 64], ctx["c_e64"][:, :],
                ctx["c_e64"][:, :], start=(pr % 4 == 0), stop=False,
                tile_position=(0, 64), skip_group_check=True))
            for j in range(2):
                p.inst("tensor", lambda E, ctx, pr=pr, j=j: E.matmul(
                    ctx["PZ2"][0:64, pr * 128 + j * 64: pr * 128 + (j + 1) * 64],
                    ctx["c_e64"][:, :], ctx["c_e64"][:, :],
                    start=(pr % 4 == 0 and j == 0), stop=False,
                    tile_position=(0, 0), skip_group_check=True))
            p.inst("tensor", lambda E, ctx, pr=pr: E.matmul(
                ctx["PZ2"][64:128, pr * 128:(pr + 1) * 128], ctx["c_e64"][:, :],
                ctx["czero"][:, :], start=(pr % 4 == 0), stop=False,
                tile_position=(0, 64), skip_group_check=True))
            for j in range(2):
                hb = j * 64
                p.inst("tensor", lambda E, ctx, pr=pr, hb=hb: E.matmul(
                    ctx["PZ1"][hb:hb + 64, pr * 128 + 64:(pr + 1) * 128],
                    ctx["c_e64"][:, :], ctx["czero"][:, 0:64], start=False,
                    stop=False, tile_position=(0, hb), skip_group_check=True),
                    "tm" if (pr == 7 and hb == 64) else None)
        p.wait("vector", "tm")
        p.inst("vector", lambda E, ctx: E.memset(ctx["zbd"][:, :], 0.0), "ve")
        for pr in range(8):
            p.inst("vector", lambda E, ctx, pr=pr: E.tensor_copy(
                ctx["zbd"][0:64, pr * 256: pr * 256 + 128],
                ctx["PZ1"][0:64, pr * 128: (pr + 1) * 128]), "ve")
            p.inst("vector", lambda E, ctx, pr=pr: E.tensor_copy(
                ctx["zbd"][64:128, pr * 256 + 128: (pr + 1) * 256],
                ctx["PZ1"][64:128, pr * 128: (pr + 1) * 128]), "ve")

        def inv_head(c, pr, j, lim=99):
            h = 2 * pr + j
            jj = j * 128
            bcol = lambda ctx, t: ctx[t][:, c * 16 + h: c * 16 + h + 1]
            p.wait("vector", "tm")
            if lim == 43:
                if j == 0:
                    p.inst("vector", lambda E, ctx: E.tensor_copy(
                        ctx["mbB"][:, :], ctx["PC"][:, 0:128]), "ve")
                    p.inst("vector", lambda E, ctx: E.tensor_copy(
                        ctx["Nup"][:, :], ctx["PA"][:, 0:128]), "ve")
                else:
                    p.inst("vector", lambda E, ctx: E.tensor_copy(
                        ctx["Nup"][:, :], ctx["PB"][:, 0:128]), "ve")
                return
            if lim in (41, 42):
                if j == 0:
                    p.inst("vector", lambda E, ctx: E.scalar_tensor_tensor(
                        ctx["mbB"][:, :], ctx["PC"][:, 0:128], 1.0,
                        ctx["c_masku"][:, :], ALU.mult, ALU.mult), "ve")
                    p.inst("vector", lambda E, ctx: E.scalar_tensor_tensor(
                        ctx["Nup"][:, :], ctx["PA"][:, 0:128], 1.0,
                        ctx["mbB"][:, :], ALU.mult, ALU.mult), "ve")
                else:
                    p.inst("vector", lambda E, ctx: E.scalar_tensor_tensor(
                        ctx["Nup"][:, :], ctx["PB"][:, 0:128], 1.0,
                        ctx["mbB"][:, :], ALU.mult, ALU.mult), "ve")
                return
            if lim == 25:
                p.inst("vector", lambda E, ctx, jj=jj: E.tensor_copy(
                    ctx["mbB"][:, :], ctx["PB"][:, jj:jj + 128]), "ve")
                p.inst("vector", lambda E, ctx, jj=jj: E.tensor_copy(
                    ctx["Nup"][:, :], ctx["PA"][:, jj:jj + 128]), "ve")
                p.inst("vector", lambda E, ctx, jj=jj: E.tensor_copy(
                    ctx["Nlow"][:, :], ctx["PA"][:, jj:jj + 128]), "ve")
                return
            if lim == 26:
                p.inst("vector", lambda E, ctx, jj=jj: E.scalar_tensor_tensor(
                    ctx["mbB"][:, :], ctx["PB"][:, jj:jj + 128], 1.0,
                    ctx["c_masku"][:, :], ALU.mult, ALU.mult), "ve")
                return
            p.inst("vector", lambda E, ctx, jj=jj: E.scalar_tensor_tensor(
                ctx["mbB"][:, :], ctx["PB"][:, jj:jj + 128], 1.0,
                ctx["c_masku"][:, :], ALU.mult, ALU.mult), "ve")
            p.inst("vector", lambda E, ctx, jj=jj: E.scalar_tensor_tensor(
                ctx["Nup"][:, :], ctx["PA"][:, jj:jj + 128], 1.0,
                ctx["mbB"][:, :], ALU.mult, ALU.mult), "ve")
            p.inst("vector", lambda E, ctx, jj=jj, c=c, h=h: E.scalar_tensor_tensor(
                ctx["Nlow"][:, :], ctx["PA"][:, jj:jj + 128],
                ctx["beta"][:, c * 16 + h: c * 16 + h + 1],
                ctx["c_maskl"][:, :], ALU.mult, ALU.mult), "ve")
            if lim < 21:
                return
            prevN, prevNT = "Nlow", "Nup"
            prevNs = prevNTs = slice(0, 128)
            for lvl in range(1, 7):
                p.wait("tensor", "ve")
                p.wait("tensor", "sc")
                p.inst("tensor", lambda E, ctx, pN=prevN, pNs=prevNs, pT=prevNT, pTs=prevNTs: E.matmul(
                    ctx["PC"][:, 0:128], ctx[pT][:, pTs], ctx[pN][:, pNs],
                    start=True, stop=True, skip_group_check=True),
                    None if lvl < 6 else "tm")
                if lvl < 6:
                    p.inst("tensor", lambda E, ctx, pN=prevN, pNs=prevNs, pT=prevNT, pTs=prevNTs: E.matmul(
                        ctx["PC"][:, 128:256], ctx[pN][:, pNs], ctx[pT][:, pTs],
                        start=True, stop=True, skip_group_check=True), "tm")
                eng = "vector"
                sem = "ve"
                p.wait(eng, "tm")
                if eng == "vector":
                    p.inst(eng, lambda E, ctx, lvl=lvl: E.tensor_copy(
                        ctx["powN"][:, (lvl - 1) * 128: lvl * 128], ctx["PC"][:, 0:128]), sem)
                    if lvl < 6:
                        p.inst(eng, lambda E, ctx, lvl=lvl: E.tensor_copy(
                            ctx["powNT"][:, (lvl - 1) * 128: lvl * 128], ctx["PC"][:, 128:256]), sem)
                else:
                    p.inst(eng, lambda E, ctx, lvl=lvl: E.copy(
                        ctx["powN"][:, (lvl - 1) * 128: lvl * 128], ctx["PC"][:, 0:128]), sem)
                    if lvl < 6:
                        p.inst(eng, lambda E, ctx, lvl=lvl: E.copy(
                            ctx["powNT"][:, (lvl - 1) * 128: lvl * 128], ctx["PC"][:, 128:256]), sem)
                prevN, prevNs = "powN", slice((lvl - 1) * 128, lvl * 128)
                prevNT, prevNTs = "powNT", slice((lvl - 1) * 128, lvl * 128)
            if lim < 22:
                return
            p.inst("vector", lambda E, ctx: E.tensor_sub(
                ctx["chR"][:, 0:128], ctx["c_ident"][:, :], ctx["Nup"][:, :]), "ve")
            for lvl in range(1, 7):
                src = ((lvl - 1) % 2) * 128
                dst = (lvl % 2) * 128
                p.wait("tensor", "ve")
                p.inst("tensor", lambda E, ctx, lvl=lvl, src=src: E.matmul(
                    ctx["PC"][:, 256:384], ctx["powN"][:, (lvl - 1) * 128: lvl * 128],
                    ctx["chR"][:, src:src + 128], start=True, stop=True,
                    skip_group_check=True), "tm")
                p.wait("vector", "tm")
                if lvl < 6:
                    p.inst("vector", lambda E, ctx, src=src, dst=dst: E.scalar_tensor_tensor(
                        ctx["chR"][:, dst:dst + 128], ctx["PC"][:, 256:384], 1.0,
                        ctx["chR"][:, src:src + 128], ALU.mult, ALU.add), "ve")
                else:
                    p.inst("vector", lambda E, ctx, src=src, jj=jj: E.scalar_tensor_tensor(
                        ctx["Tt"][:, jj:jj + 128], ctx["PC"][:, 256:384], 1.0,
                        ctx["chR"][:, src:src + 128], ALU.mult, ALU.add), "ve")

        for c in range(NCH if cfg.p2c == 0 else min(NCH, cfg.p2c)):
            if cfg.p2s < 1:
                continue
            p.wait("tensor", "ve")
            p.inst("tensor", lambda E, ctx, c=c: E.transpose(
                ctx["PT"][0:16, 0:128],
                ctx["betab"][:, c * 16:(c + 1) * 16], ctx["c_ident"][:, :]), "tm")
            p.wait("vector", "tm")
            p.inst("vector", lambda E, ctx: E.tensor_copy(
                ctx["brow"][:, :], ctx["PT"][0:16, 0:128]), "ve")
            p.wait("vector", "sd")
            p.wait("scalar", "sd")
            p.wait("vector", "tm")
            p.inst("vector", lambda E, ctx: E.tensor_copy(
                ctx["stZPT"][0:64, :], ctx["PZ2"][0:64, :]), "ve")
            p.wait("sync", "sc")
            p.wait("sync", "ve")
            p.group("sync", "sd")
            p.inst("sync", lambda E, ctx, c=c: E.dma_start(
                out=cZPT[c], in_=ctx["stZPT"].ap()), "sd", 16)
            p.inst("sync", lambda E, ctx, c=c: E.dma_start(
                out=cB[c][0:64, :],
                in_=ctx["zbd"].ap().rearrange("p (r x) -> p r x", r=8)[0:64, :, 64:128]), "sd", 16)
            p.inst("sync", lambda E, ctx, c=c: E.dma_start(
                out=cB[c][64:128, :],
                in_=ctx["zbd"].ap().rearrange("p (r x) -> p r x", r=8)[64:128, :, 192:256]), "sd", 16)
            if cfg.debug:
                p.group("sync", "sd")
                p.inst("sync", lambda E, ctx, c=c: E.dma_start(
                    out=dbg["d_ZPT"][c], in_=ctx["stZPT"].ap()), "sd", 16)
                p.inst("sync", lambda E, ctx, c=c: E.dma_start(
                    out=dbg["d_B"][c][0:64, :],
                    in_=ctx["zbd"].ap().rearrange("p (r x) -> p r x", r=8)[0:64, :, 64:128]), "sd", 16)
                p.inst("sync", lambda E, ctx, c=c: E.dma_start(
                    out=dbg["d_B"][c][64:128, :],
                    in_=ctx["zbd"].ap().rearrange("p (r x) -> p r x", r=8)[64:128, :, 192:256]), "sd", 16)
            zbd_cached = p.cnt["sd"]
            for pr in range(8):
                if pr == 0:
                    p.wait("vector", "sd", zbd_cached)
                kc = lambda ctx, pr=pr, c=c: ctx["kfm"][:, pr * T + c * 128: pr * T + (c + 1) * 128]
                p.wait("tensor", "ve")
                p.wait("tensor", "sc")
                for j in range(2):
                    hb = j * 64
                    Adst = ("PA" if j == 0 else "PB") if cfg.p2s in (41, 42, 43) else "PA"
                    Aoff = 0 if cfg.p2s in (41, 42, 43) else j * 128
                    p.inst("tensor", lambda E, ctx, pr=pr, c=c, hb=hb, j=j, Adst=Adst, Aoff=Aoff: E.matmul(
                        ctx[Adst][:, Aoff:Aoff + 128],
                        ctx["kfm"][:, pr * T + c * 128: pr * T + (c + 1) * 128][hb:hb + 64, :],
                        ctx["kfm"][:, pr * T + c * 128: pr * T + (c + 1) * 128][hb:hb + 64, :],
                        start=True, stop=True, tile_position=(hb, 0),
                        skip_group_check=True))
                    if cfg.p2s in (41, 43):
                        if j == 0:
                            p.inst("tensor", lambda E, ctx, pr=pr, j=j: E.matmul(
                                ctx["PC"][:, 0:128],
                                ctx["c_oneh16"][:, (2 * pr + j) * 128:(2 * pr + j + 1) * 128],
                                ctx["brow"][:, :], start=True, stop=True,
                                skip_group_check=True))
                    elif cfg.p2s == 42:
                        if j == 0:
                            p.inst("tensor", lambda E, ctx: E.matmul(
                                ctx["PC"][:, 0:128], ctx["c_ident"][:, :],
                                ctx["c_masku"][:, :], start=True, stop=True,
                                skip_group_check=True))
                    elif cfg.p2s == 29:
                        p.inst("tensor", lambda E, ctx, pr=pr, j=j: E.matmul(
                            ctx["PB"][:, j * 128:(j + 1) * 128],
                            ctx["c_ident"][:, :],
                            ctx["c_masku"][:, :], start=True, stop=True,
                            skip_group_check=True))
                    else:
                        p.inst("tensor", lambda E, ctx, pr=pr, j=j: E.matmul(
                            ctx["PB"][:, j * 128:(j + 1) * 128],
                            ctx["c_oneh16"][:, (2 * pr + j) * 128:(2 * pr + j + 1) * 128],
                            ctx["brow"][:, :], start=True, stop=True,
                            skip_group_check=True), "tm" if j == 1 else None)
                if cfg.p2s < 2:
                    continue
                for j in range(2):
                    inv_head(c, pr, j, lim=cfg.p2s)
                if cfg.p2s < 3:
                    continue
                p.wait("tensor", "ve")
                p.inst("tensor", lambda E, ctx, pr=pr, c=c: E.matmul(
                    ctx["PB"][:, 256:512],
                    ctx["kfm"][:, pr * T + c * 128: pr * T + (c + 1) * 128],
                    ctx["zbd"][:, pr * 256:(pr + 1) * 256],
                    start=True, stop=True, skip_group_check=True), "tm")
                if cfg.p2s == 45:
                    continue
                if cfg.p2s < 4:
                    continue
                p.wait("vector", "tm")
                for j in range(2):
                    h = 2 * pr + j
                    p.inst("vector", lambda E, ctx, j=j, c=c, h=h: E.tensor_scalar(
                        ctx["rh"][:, j * 128: j * 128 + 64],
                        ctx["PB"][:, 256 + j * 128: 256 + j * 128 + 64],
                        ctx["beta"][:, c * 16 + h: c * 16 + h + 1], -1.0,
                        ALU.mult, ALU.mult), "ve")
                    p.inst("vector", lambda E, ctx, j=j, c=c, h=h: E.tensor_sub(
                        ctx["vg"][:, j * 64:(j + 1) * 64],
                        ctx["vtm"][:, c * DM + h * 64: c * DM + (h + 1) * 64],
                        ctx["PB"][:, 256 + j * 128 + 64: 256 + (j + 1) * 128]), "ve")
                    p.inst("vector", lambda E, ctx, j=j, c=c, h=h: E.tensor_scalar(
                        ctx["rh"][:, j * 128 + 64:(j + 1) * 128],
                        ctx["vg"][:, j * 64:(j + 1) * 64],
                        ctx["beta"][:, c * 16 + h: c * 16 + h + 1], None,
                        ALU.mult), "ve")
                if cfg.p2s < 5:
                    continue
                p.wait("tensor", "sc")
                p.wait("tensor", "ve")
                for j in range(2):
                    p.inst("tensor", lambda E, ctx, j=j: E.matmul(
                        ctx["PC"][:, j * 128:(j + 1) * 128],
                        ctx["Tt"][:, j * 128:(j + 1) * 128],
                        ctx["rh"][:, j * 128:(j + 1) * 128],
                        start=True, stop=True, skip_group_check=True))
                    p.inst("tensor", lambda E, ctx, j=j: E.matmul(
                        ctx["PC"][0:64, 256 + j * 128: 256 + (j + 1) * 128],
                        ctx["rh"][:, j * 128: j * 128 + 64],
                        ctx["Tt"][:, j * 128:(j + 1) * 128],
                        start=True, stop=True, skip_group_check=True),
                        "tm" if j == 1 else None)
                p.wait("vector", "tm")
                p.wait("scalar", "tm")
                for j in range(2):
                    h = 2 * pr + j
                    p.inst("vector", lambda E, ctx, j=j: E.tensor_copy(
                        ctx["Ush"][:, j * 128:(j + 1) * 128],
                        ctx["PC"][:, j * 128:(j + 1) * 128]), "ve")
                    p.inst("vector", lambda E, ctx, j=j, h=h: E.tensor_copy(
                        ctx["stUB"][:, h * 64:(h + 1) * 64],
                        ctx["PC"][:, j * 128 + 64:(j + 1) * 128]), "ve")
                    p.inst("vector", lambda E, ctx, j=j, h=h: E.tensor_copy(
                        ctx["stUPT"][0:64, h * 128:(h + 1) * 128],
                        ctx["PC"][0:64, 256 + j * 128: 256 + (j + 1) * 128]), "ve")
                if cfg.p2s < 6:
                    continue
                p.wait("tensor", "ve")
                for j in range(2):
                    h = 2 * pr + j
                    hb = j * 64
                    p.inst("tensor", lambda E, ctx, j=j, h=h, hb=hb, pr=pr, c=c: E.matmul(
                        ctx["PZ1"][hb:hb + 64, pr * 128:(pr + 1) * 128],
                        ctx["ktm"][:, c * DM + h * 64: c * DM + (h + 1) * 64],
                        ctx["Ush"][:, j * 128:(j + 1) * 128],
                        start=False, stop=False, tile_position=(0, hb),
                        skip_group_check=True))
                    p.inst("tensor", lambda E, ctx, j=j, h=h, pr=pr, c=c: E.matmul(
                        ctx["PZ2"][:, pr * 128 + j * 64: pr * 128 + (j + 1) * 64],
                        ctx["Ush"][:, j * 128:(j + 1) * 128],
                        ctx["ktm"][:, c * DM + h * 64: c * DM + (h + 1) * 64],
                        start=False, stop=False, tile_position=(0, 0),
                        skip_group_check=True), "tm" if j == 1 else None)
                p.wait("vector", "tm")
                p.inst("vector", lambda E, ctx, pr=pr: E.tensor_copy(
                    ctx["zbd"][0:64, pr * 256: pr * 256 + 128],
                    ctx["PZ1"][0:64, pr * 128:(pr + 1) * 128]), "ve")
                p.inst("vector", lambda E, ctx, pr=pr: E.tensor_copy(
                    ctx["zbd"][64:128, pr * 256 + 128:(pr + 1) * 256],
                    ctx["PZ1"][64:128, pr * 128:(pr + 1) * 128]), "ve")
            p.wait("sync", "ve")
            p.wait("sync", "sc")
            p.group("sync", "sd")
            p.inst("sync", lambda E, ctx, c=c: E.dma_start(
                out=cUB[c], in_=ctx["stUB"].ap()), "sd", 16)
            p.inst("sync", lambda E, ctx, c=c: E.dma_start(
                out=cUPT[c], in_=ctx["stUPT"].ap()), "sd", 16)
            if cfg.debug:
                p.group("sync", "sd")
                p.inst("sync", lambda E, ctx, c=c: E.dma_start(
                    out=dbg["d_UB"][c], in_=ctx["stUB"].ap()), "sd", 16)
                p.inst("sync", lambda E, ctx, c=c: E.dma_start(
                    out=dbg["d_UPT"][c], in_=ctx["stUPT"].ap()), "sd", 16)
        if cfg.debug:
            p.wait("vector", "tm")
            p.inst("vector", lambda E, ctx: E.tensor_copy(
                ctx["zfin"][:, :], ctx["PZ1"][:, :]), "ve")
            p.wait("gpsimd", "ve")
            p.group("gpsimd", "gd")
            p.inst("gpsimd", lambda E, ctx: E.dma_start(
                out=dbg["d_zfin"], in_=ctx["zfin"].ap()), "gd", 16)

    if cfg.stage >= 3:
        p.free("ktm", "vtm", "beta", "nbeta", "betab", "Tt", "powN", "powNT",
               "Nlow", "Nup", "chR", "mbB", "zbd", "vg", "rh", "Ush",
               "stUB", "stUPT", "stZPT", "brow", "c_oneh16")
        p.phase("P3")
        sb("gA", (64, 8 * DM), F32)
        sb("gB", (64, 8 * DM), F32)
        sb("zcore", (64, H * 64))
        sb("pj", (64, DM)); sb("bj", (64, DM), F32)
        sb("c_pmask", (64, 8), F32)
        sb("c_imask", (64, 8 * 64), F32)
        NC_ = cfg.n_cores
        p.group("gpsimd", "gd")
        p.inst("gpsimd", lambda E, ctx: E.dma_start(out=ctx["c_pmask"].ap(), in_=pmask), "gd", 16)
        p.inst("gpsimd", lambda E, ctx: E.dma_start(out=ctx["c_imask"].ap(), in_=imask), "gd", 16)
        p.wait("vector", "gd")
        p.wait("vector", "tm")
        p.inst("vector", lambda E, ctx: E.tensor_copy(
            ctx["zfin"][:, :], ctx["PZ1"][:, :]), "ve")
        p.wait("scalar", "tm")
        p.inst("scalar", lambda E, ctx: E.copy(
            ctx["ptfin"][0:64, :], ctx["PZ2"][0:64, :]), "sc")
        p.wait("sync", "ve")
        p.wait("sync", "sc")
        p.group("sync", "sd")
        p.inst("sync", lambda E, ctx: E.dma_start(
            out=agi[0:128, :], in_=ctx["zfin"].ap()), "sd", 16)
        p.inst("sync", lambda E, ctx: E.dma_start(
            out=agi[128:192, :], in_=ctx["ptfin"].ap()), "sd", 16)
        p.wait("gpsimd", "sd")
        p.inst("gpsimd", lambda E, ctx: E.collective_compute(
            "AllGather", ALU.bypass, replica_groups=[list(range(NC_))],
            ins=[agi], outs=[ago]), "cc")
        p.wait("gpsimd", "cc")
        p.group("gpsimd", "gd")
        for j in range(NC_):
            p.inst("gpsimd", lambda E, ctx, j=j: E.dma_start(
                out=ctx["gB"][0:64, j * 1024:(j + 1) * 1024],
                in_=ago[j * 192 + 128: j * 192 + 192, :]), "gd", 16)
            p.inst("gpsimd", lambda E, ctx, j=j: E.dma_start(
                out=ctx["gA"][0:64, j * 1024:(j + 1) * 1024]
                .rearrange("p (r x) -> p r x", x=128)[:, :, 0:64],
                in_=ago[j * 192: j * 192 + 64, :]
                .rearrange("p (r x) -> p r x", r=8)[:, :, 64:128]), "gd", 16)
            p.inst("gpsimd", lambda E, ctx, j=j: E.dma_start(
                out=ctx["gA"][0:64, j * 1024:(j + 1) * 1024]
                .rearrange("p (r x) -> p r x", x=128)[:, :, 64:128],
                in_=ago[j * 192 + 64: j * 192 + 128, :]
                .rearrange("p (r x) -> p r x", r=8)[:, :, 64:128]), "gd", 16)
        p.wait("vector", "gd")
        p.inst("vector", lambda E, ctx: E.memset(ctx["zcore"][:, :], 0.0), "ve")
        for j in range(NC_):
            p.inst("vector", lambda E, ctx, j=j: E.scalar_tensor_tensor(
                ctx["pj"][0:64, :].rearrange("p (h d) -> p h d", d=64),
                ctx["gB"][0:64, j * 1024:(j + 1) * 1024].rearrange("p (h d) -> p h d", d=64),
                ctx["c_pmask"][0:64, j:j + 1],
                ctx["c_imask"][0:64, j * 64:(j + 1) * 64]
                .rearrange("p (o d) -> p o d", o=1).broadcast_to([64, 16, 64]),
                ALU.mult, ALU.add), "ve")
            p.inst("vector", lambda E, ctx, j=j: E.tensor_scalar_mul(
                ctx["bj"][0:64, :], ctx["gA"][0:64, j * 1024:(j + 1) * 1024],
                ctx["c_pmask"][0:64, j:j + 1]), "ve")
            p.wait("tensor", "ve")
            for h in range(H):
                p.inst("tensor", lambda E, ctx, h=h: E.matmul(
                    ctx["PZ1"][0:64, h * 64:(h + 1) * 64],
                    ctx["pj"][0:64, h * 64:(h + 1) * 64],
                    ctx["zcore"][0:64, h * 64:(h + 1) * 64],
                    start=True, stop=True, skip_group_check=True),
                    "tm" if h == H - 1 else None)
            p.wait("vector", "tm")
            p.inst("vector", lambda E, ctx: E.scalar_tensor_tensor(
                ctx["zcore"][0:64, :], ctx["PZ1"][0:64, 0:1024], 1.0,
                ctx["bj"][0:64, :], ALU.mult, ALU.add), "ve")
        if cfg.debug:
            p.wait("gpsimd", "ve")
            p.group("gpsimd", "gd")
            p.inst("gpsimd", lambda E, ctx: E.dma_start(
                out=dbg["d_zcore"], in_=ctx["zcore"].ap()), "gd", 16)

    if cfg.stage >= 4:
        p.free("zfin", "ptfin", "gA", "gB", "pj", "bj", "c_pmask", "c_imask")
        p.phase("P4")
        sb("ofm", (128, DCH * T))
        sb("ldUB", (128, 2 * H * 64))
        sb("ldUPT", (64, 2 * H * 128))
        sb("ldZPT", (64, 2 * H * 64))
        sb("ldB", (128, 2 * (H // 2) * 64))
        sb("z0pad", (128, H * 64))
        sb("urel", (128, 128))
        sb("ltp", (128, 256))
        p.wait("vector", "cc")
        p.inst("vector", lambda E, ctx: E.memset(ctx["z0pad"][:, :], 0.0), "ve")
        for c in range(NCH):
            buf = c % 2
            p.wait("sync", "ve")
            p.group("sync", "sd")
            p.inst("sync", lambda E, ctx, c=c, buf=buf: E.dma_start(
                out=ctx["ldUB"][:, buf * H * 64:(buf + 1) * H * 64], in_=cUB[c]), "sd", 16)
            p.inst("sync", lambda E, ctx, c=c, buf=buf: E.dma_start(
                out=ctx["ldUPT"][0:64, buf * H * 128:(buf + 1) * H * 128], in_=cUPT[c]), "sd", 16)
            p.inst("sync", lambda E, ctx, c=c, buf=buf: E.dma_start(
                out=ctx["ldZPT"][0:64, buf * H * 64:(buf + 1) * H * 64], in_=cZPT[c]), "sd", 16)
            p.inst("sync", lambda E, ctx, c=c, buf=buf: E.dma_start(
                out=ctx["ldB"][:, buf * 512:(buf + 1) * 512], in_=cB[c]), "sd", 16)
            p.wait("tensor", "sd")
            oUB = buf * H * 64
            oUPT = buf * H * 128
            oZPT = buf * H * 64
            oB = buf * 512
            for pr in range(8):
                ccols = slice(pr * T + c * 128, pr * T + (c + 1) * 128)
                p.wait("tensor", "ve")
                for j in range(2):
                    h = 2 * pr + j
                    hb = j * 64
                    p.inst("tensor", lambda E, ctx, j=j, h=h, hb=hb, pr=pr, c=c: E.matmul(
                        ctx["PA"][:, j * 128:(j + 1) * 128],
                        ctx["kfm"][:, pr * T + c * 128: pr * T + (c + 1) * 128][hb:hb + 64, :],
                        ctx["qfm"][:, pr * T + c * 128: pr * T + (c + 1) * 128][hb:hb + 64, :],
                        start=True, stop=True, tile_position=(hb, 0),
                        skip_group_check=True))
                    p.inst("tensor", lambda E, ctx, j=j, h=h, oUPT=oUPT: E.matmul(
                        ctx["PB"][:, j * 64:(j + 1) * 64],
                        ctx["ldUPT"][0:64, oUPT + h * 128: oUPT + (h + 1) * 128],
                        ctx["zcore"][0:64, h * 64:(h + 1) * 64],
                        start=True, stop=True, skip_group_check=True))
                    p.inst("tensor", lambda E, ctx, j=j, h=h, hb=hb, oZPT=oZPT: E.matmul(
                        ctx["PB"][hb:hb + 64, 128 + j * 64: 128 + (j + 1) * 64],
                        ctx["ldZPT"][0:64, oZPT + h * 64: oZPT + (h + 1) * 64],
                        ctx["zcore"][0:64, h * 64:(h + 1) * 64],
                        start=True, stop=True, tile_position=(0, hb),
                        skip_group_check=True), "tm" if j == 1 else None)
                p.wait("vector", "tm")
                for j in range(2):
                    h = 2 * pr + j
                    hb = j * 64
                    p.inst("vector", lambda E, ctx, j=j: E.scalar_tensor_tensor(
                        ctx["ltp"][:, j * 128:(j + 1) * 128],
                        ctx["PA"][:, j * 128:(j + 1) * 128], 1.0,
                        ctx["c_maskui"][:, :], ALU.mult, ALU.mult), "ve")
                    p.inst("vector", lambda E, ctx, j=j, h=h, oUB=oUB: E.scalar_tensor_tensor(
                        ctx["urel"][:, j * 64:(j + 1) * 64],
                        ctx["PB"][:, j * 64:(j + 1) * 64], 1.0,
                        ctx["ldUB"][:, oUB + h * 64: oUB + (h + 1) * 64],
                        ALU.mult, ALU.add), "ve")
                    p.inst("vector", lambda E, ctx, j=j, h=h, hb=hb, pr=pr, oB=oB: E.scalar_tensor_tensor(
                        ctx["z0pad"][hb:hb + 64, h * 64:(h + 1) * 64],
                        ctx["PB"][hb:hb + 64, 128 + j * 64: 128 + (j + 1) * 64], 1.0,
                        ctx["ldB"][hb:hb + 64, oB + pr * 64: oB + (pr + 1) * 64],
                        ALU.mult, ALU.add), "ve")
                p.wait("tensor", "ve")
                for j in range(2):
                    h = 2 * pr + j
                    hb = j * 64
                    p.inst("tensor", lambda E, ctx, j=j, h=h, hb=hb, pr=pr, c=c: E.matmul(
                        ctx["PC"][hb:hb + 64, 0:128],
                        ctx["z0pad"][:, h * 64:(h + 1) * 64],
                        ctx["qfm"][:, pr * T + c * 128: pr * T + (c + 1) * 128],
                        start=True, stop=False, tile_position=(0, hb),
                        skip_group_check=True))
                    p.inst("tensor", lambda E, ctx, j=j, hb=hb: E.matmul(
                        ctx["PC"][hb:hb + 64, 0:128],
                        ctx["urel"][:, j * 64:(j + 1) * 64],
                        ctx["ltp"][:, j * 128:(j + 1) * 128],
                        start=False, stop=(j == 1), tile_position=(0, hb),
                        skip_group_check=True), "tm" if j == 1 else None)
                p.wait("vector", "tm")
                p.inst("vector", lambda E, ctx, pr=pr, c=c: E.tensor_copy(
                    ctx["ofm"][:, pr * T + c * 128: pr * T + (c + 1) * 128],
                    ctx["PC"][:, 0:128]), "ve")
        if cfg.debug:
            p.wait("gpsimd", "ve")
            p.group("gpsimd", "gd")
            p.inst("gpsimd", lambda E, ctx: E.dma_start(
                out=dbg["d_ofm"], in_=ctx["ofm"].ap()), "gd", 16)

    if cfg.stage >= 5:
        p.free("kfm", "qfm", "zcore", "ldUB", "ldUPT", "ldZPT", "ldB",
               "z0pad", "urel", "ltp")
        p.phase("P5")
        sb("wt", (128, DCH * DM))
        sb("anorm", (128, DCH * T))
        sb("bcol", (128, DCH), F32)
        p.wait("gpsimd", "tm")
        p.group("gpsimd", "gd")
        p.inst("gpsimd", lambda E, ctx: E.dma_start(out=ctx["wt"][:, :], in_=wo), "gd", 16)
        p.inst("gpsimd", lambda E, ctx: E.dma_start(out=ctx["bcol"][:, 0:DCH], in_=bo), "gd", 16)
        p.wait("tensor", "gd")
        p.wait("scalar", "gd")
        proj_fm("anorm", src="ofm", bias="bcol")

    if cfg.stage >= 5:
        p.free("ofm", "wt")
        p.phase("P6")
        sb("xf", (128, DCH * T), F32)
        sb("y2", (128, DCH * T))
        sb("bc1", (128, T)); sb("bc2", (128, T))
        sb("sq", (128, 4096))
        sb("x2c", (128, 4096))
        sb("st2", (1, 2 * T), BF16)
        p.group("gpsimd", "gd")
        p.inst("gpsimd", lambda E, ctx: E.dma_start(
            out=ctx["xf"].ap().rearrange("p (c t) -> p c t", c=DCH),
            in_=x_in.rearrange("(c p) t -> p c t", p=128)), "gd", 16)
        for e in ("tensor", "vector", "scalar"):
            p.wait(e, "gd")
        ln_feature_major("anorm", "anorm")
        for nt in range(NT):
            p.wait("scalar", "tm")
            p.wait("vector", "tm")
            for c in range(DCH):
                reg = c * 512
                p.inst("vector", lambda E, ctx, c=c, nt=nt, reg=reg: E.tensor_add(
                    ctx["x2c"][:, reg:reg + 512],
                    ctx["xf"][:, c * T + nt * 512: c * T + (nt + 1) * 512],
                    ctx["anorm"][:, c * T + nt * 512: c * T + (nt + 1) * 512]), "ve")
                p.wait("scalar", "ve")
                p.inst("scalar", lambda E, ctx, c=c, reg=reg: E.activation(
                    ctx["sq"][:, reg:reg + 512], ctx["x2c"][:, reg:reg + 512],
                    AF.Square), "sc")
                p.wait("tensor", "sc")
                p.inst("tensor", lambda E, ctx, c=c, reg=reg: E.matmul(
                    ctx["PA"][0:1, :], ctx["c_onesc"][:, :],
                    ctx["x2c"][:, reg:reg + 512],
                    start=(c == 0), stop=(c == DCH - 1), skip_group_check=True))
                p.inst("tensor", lambda E, ctx, c=c, reg=reg: E.matmul(
                    ctx["PB"][0:1, :], ctx["c_onesc"][:, :],
                    ctx["sq"][:, reg:reg + 512],
                    start=(c == 0), stop=(c == DCH - 1), skip_group_check=True),
                    "tm" if c == DCH - 1 else None)
            p.wait("vector", "tm")
            p.inst("vector", lambda E, ctx: E.tensor_copy(
                ctx["lntmp"][0:1, 0:512], ctx["PA"][0:1, :]), "ve")
            p.inst("vector", lambda E, ctx: E.tensor_copy(
                ctx["lntmp"][0:1, 512:1024], ctx["PB"][0:1, :]), "ve")
            p.wait("tensor", "ve")
            p.wait("scalar", "ve")
            ln_stats_math(nt, save_rows="st2")
            ln_bcast(nt)
        for c in range(DCH):
            p.inst("vector", lambda E, ctx, c=c: E.tensor_add(
                ctx["x2c"][:, 0:T], ctx["xf"][:, c * T:(c + 1) * T],
                ctx["anorm"][:, c * T:(c + 1) * T]), "ve")
            p.inst("vector", lambda E, ctx, c=c: E.tensor_mul(
                ctx["y2"][:, c * T:(c + 1) * T], ctx["x2c"][:, 0:T],
                ctx["bc1"][:, :]), "ve")
            p.inst("vector", lambda E, ctx, c=c: E.tensor_add(
                ctx["y2"][:, c * T:(c + 1) * T], ctx["y2"][:, c * T:(c + 1) * T],
                ctx["bc2"][:, :]), "ve")
        if cfg.debug and cfg.stage >= 5:
            p.wait("gpsimd", "ve")
            p.group("gpsimd", "gd")
            p.inst("gpsimd", lambda E, ctx: E.dma_start(
                out=dbg["d_y2"], in_=ctx["y2"].ap()), "gd", 16)

    if cfg.stage >= 9:
        p.free("anorm", "xf", "bc1", "bc2", "sq", "x2c", "bcol", "lnr", "lntmp",
               "c_maskl", "c_masku", "c_maskui", "c_ident", "c_e64", "czero")
        p.phase("P7")
        NT3 = TFh // 512
        sb("hA", (128, FCH * TFh))
        sb("hp", (128, 2 * FCH * 128))
        sb("fsc", (128, 4 * FCH + DCH), F32)
        p.group("gpsimd", "gd")
        for i_, src_ in enumerate([w1s, w2s, h1s, h2s]):
            p.inst("gpsimd", lambda E, ctx, i_=i_, src_=src_: E.dma_start(
                out=ctx["fsc"][:, i_ * FCH:(i_ + 1) * FCH], in_=src_), "gd", 16)
        p.inst("gpsimd", lambda E, ctx: E.dma_start(
            out=ctx["fsc"][:, 4 * FCH:4 * FCH + DCH], in_=w3s), "gd", 16)
        for e in ("tensor", "vector", "scalar"):
            p.wait(e, "gd")

        for hf in range(2):
            toff = hf * TFh
            pw = f"P7w{hf}"
            p.phase(pw)
            sb("hB", (128, FCH * TFh))
            sb("silu", (128, 1024))
            sb("wp", (128, 4 * DCH * 128))
            gates = {0: (p.cnt["ve"], p.cnt["sc"]), 1: (p.cnt["ve"], p.cnt["sc"])}
            wgate = {0: p.cnt["tm"], 1: p.cnt["tm"]}
            gd_tot = {}

            def w12_dma(po):
                slot = po % 2
                p.wait("gpsimd", "tm", wgate[slot])
                p.group("gpsimd", "gd")
                p.inst("gpsimd", lambda E, ctx, po=po, slot=slot: E.dma_start(
                    out=ctx["wp"][:, (2 * slot) * DCH * 128:(2 * slot + 1) * DCH * 128],
                    in_=w1p[po]), "gd", 16)
                p.inst("gpsimd", lambda E, ctx, po=po, slot=slot: E.dma_start(
                    out=ctx["wp"][:, (2 * slot + 1) * DCH * 128:(2 * slot + 2) * DCH * 128],
                    in_=w2p[po]), "gd", 16)
                gd_tot[po] = p.cnt["gd"]

            w12_dma(0)
            for po in range(FCH):
                slot = po % 2
                if po + 1 < FCH:
                    w12_dma(po + 1)
                p.wait("tensor", "gd", gd_tot[po])
                for nt in range(NT3):
                    par = (po * NT3 + nt) % 2
                    pzA = "PA" if par == 0 else "PC"
                    pzB = "PB" if par == 0 else "PZ1"
                    p.wait("tensor", "ve", gates[par][0])
                    p.wait("tensor", "sc", gates[par][1])
                    for w_i, pz in ((0, pzA), (1, pzB)):
                        for kc in range(DCH):
                            p.inst("tensor", lambda E, ctx, po=po, nt=nt, kc=kc, slot=slot,
                                   w_i=w_i, pz=pz, toff=toff: E.matmul(
                                ctx[pz][:, 0:512],
                                ctx["wp"][:, (2 * slot + w_i) * DCH * 128 + kc * 128:
                                          (2 * slot + w_i) * DCH * 128 + (kc + 1) * 128],
                                ctx["y2"][:, kc * T + toff + nt * 512: kc * T + toff + (nt + 1) * 512],
                                start=(kc == 0), stop=(kc == DCH - 1),
                                skip_group_check=True), "tm" if kc == DCH - 1 else None)
                    p.wait("scalar", "tm")
                    p.wait("scalar", "ve", gates[par][0])
                    p.inst("scalar", lambda E, ctx, po=po, nt=nt, par=par, pzA=pzA: E.activation(
                        ctx["silu"][:, par * 512:(par + 1) * 512], ctx[pzA][:, 0:512],
                        AF.Silu, scale=ctx["fsc"][:, po:po + 1]), "sc")
                    p.wait("vector", "sc")
                    p.wait("vector", "tm")
                    p.inst("vector", lambda E, ctx, po=po, nt=nt, par=par, pzB=pzB: E.scalar_tensor_tensor(
                        ctx["hA"][:, po * TFh + nt * 512: po * TFh + (nt + 1) * 512],
                        ctx[pzB][:, 0:512], ctx["fsc"][:, FCH + po: FCH + po + 1],
                        ctx["silu"][:, par * 512:(par + 1) * 512],
                        ALU.mult, ALU.mult), "ve")
                    gates[par] = (p.cnt["ve"], p.cnt["sc"])
                wgate[slot] = p.cnt["tm"]
            if cfg.debug and hf == 1:
                p.wait("gpsimd", "ve")
                p.group("gpsimd", "gd")
                p.inst("gpsimd", lambda E, ctx: E.dma_start(
                    out=dbg["d_hffn"], in_=ctx["hA"].ap()), "gd", 16)
            for (wsrc, scoff, srcb, dstb) in ((h1p, 2 * FCH, "hA", "hB"),
                                              (h2p, 3 * FCH, "hB", "hA")):
                p.wait("tensor", "ve")
                p.wait("tensor", "sc")
                gates2 = {0: p.cnt["sc"], 1: p.cnt["sc"]}
                hgate = {0: p.cnt["tm"], 1: p.cnt["tm"]}
                hgd_tot = {}

                def h_dma(po, wsrc=wsrc):
                    slot = po % 2
                    p.wait("gpsimd", "tm", hgate[slot])
                    p.group("gpsimd", "gd")
                    p.inst("gpsimd", lambda E, ctx, po=po, slot=slot, wsrc=wsrc: E.dma_start(
                        out=ctx["hp"][:, slot * FCH * 128:(slot + 1) * FCH * 128],
                        in_=wsrc[po]), "gd", 16)
                    hgd_tot[po] = p.cnt["gd"]

                h_dma(0)
                for po in range(FCH):
                    slot = po % 2
                    if po + 1 < FCH:
                        h_dma(po + 1)
                    p.wait("tensor", "gd", hgd_tot[po])
                    for nt in range(NT3):
                        par = (po * NT3 + nt) % 2
                        pz = "PZ1" if par == 0 else "PZ2"
                        preg = 512 if par == 0 else 0
                        p.wait("tensor", "sc", gates2[par])
                        for kc in range(FCH):
                            p.inst("tensor", lambda E, ctx, po=po, nt=nt, kc=kc, slot=slot,
                                   pz=pz, preg=preg, srcb=srcb: E.matmul(
                                ctx[pz][:, preg:preg + 512],
                                ctx["hp"][:, slot * FCH * 128 + kc * 128: slot * FCH * 128 + (kc + 1) * 128],
                                ctx[srcb][:, kc * TFh + nt * 512: kc * TFh + (nt + 1) * 512],
                                start=(kc == 0), stop=(kc == FCH - 1),
                                skip_group_check=True), "tm" if kc == FCH - 1 else None)
                        p.wait("scalar", "tm")
                        p.inst("scalar", lambda E, ctx, po=po, nt=nt, pz=pz, preg=preg,
                               scoff=scoff, dstb=dstb: E.activation(
                            ctx[dstb][:, po * TFh + nt * 512: po * TFh + (nt + 1) * 512],
                            ctx[pz][:, preg:preg + 512], AF.Gelu,
                            scale=ctx["fsc"][:, scoff + po: scoff + po + 1]), "sc")
                        gates2[par] = p.cnt["sc"]
                    hgate[slot] = p.cnt["tm"]
                if srcb == "hA":
                    pass
            po7 = f"P7o{hf}"
            p.free("hB", "silu", "wp")
            p.phase(po7)
            sb("fout", (128, DCH * TFh))
            sb("lnr", (1, 2 * T), BF16)
            sb("lntmp", (1, 2048), F32)
            sb("sq3", (128, 4096))
            sb("b3r", (128, TFh)); sb("b3n", (128, TFh))
            sb("b2r", (128, TFh)); sb("b2n", (128, TFh))
            sb("t7", (128, 2048))
            sb("ostg", (128, 1024), F32)
            p.wait("tensor", "sc")
            p.wait("tensor", "ve")
            gates3 = {0: p.cnt["ve"], 1: p.cnt["ve"]}
            w3gate = {0: p.cnt["tm"], 1: p.cnt["tm"]}
            wgd_tot = {}

            def w3_dma(po):
                slot = po % 2
                p.wait("gpsimd", "tm", w3gate[slot])
                p.group("gpsimd", "gd")
                p.inst("gpsimd", lambda E, ctx, po=po, slot=slot: E.dma_start(
                    out=ctx["hp"][:, slot * FCH * 128:(slot + 1) * FCH * 128],
                    in_=w3p[po]), "gd", 16)
                wgd_tot[po] = p.cnt["gd"]

            w3_dma(0)
            for po in range(DCH):
                slot = po % 2
                if po + 1 < DCH:
                    w3_dma(po + 1)
                p.wait("tensor", "gd", wgd_tot[po])
                for nt in range(NT3):
                    par = (po * NT3 + nt) % 2
                    pz = "PZ1" if par == 0 else "PZ2"
                    preg = 512 if par == 0 else 0
                    p.wait("tensor", "ve", gates3[par])
                    for kc in range(FCH):
                        p.inst("tensor", lambda E, ctx, po=po, nt=nt, kc=kc, slot=slot,
                               pz=pz, preg=preg: E.matmul(
                            ctx[pz][:, preg:preg + 512],
                            ctx["hp"][:, slot * FCH * 128 + kc * 128: slot * FCH * 128 + (kc + 1) * 128],
                            ctx["hA"][:, kc * TFh + nt * 512: kc * TFh + (nt + 1) * 512],
                            start=(kc == 0), stop=(kc == FCH - 1),
                            skip_group_check=True), "tm" if kc == FCH - 1 else None)
                    p.wait("vector", "tm")
                    p.inst("vector", lambda E, ctx, po=po, nt=nt, pz=pz, preg=preg: E.tensor_scalar_mul(
                        ctx["fout"][:, po * TFh + nt * 512: po * TFh + (nt + 1) * 512],
                        ctx[pz][:, preg:preg + 512],
                        ctx["fsc"][:, 4 * FCH + po: 4 * FCH + po + 1]), "ve")
                    gates3[par] = p.cnt["ve"]
                w3gate[slot] = p.cnt["tm"]
            for nt in range(NT3):
                p.wait("scalar", "tm")
                p.wait("scalar", "ve")
                for c in range(DCH):
                    reg = c * 512
                    p.inst("scalar", lambda E, ctx, c=c, nt=nt, reg=reg: E.activation(
                        ctx["sq3"][:, reg:reg + 512],
                        ctx["fout"][:, c * TFh + nt * 512: c * TFh + (nt + 1) * 512],
                        AF.Square), "sc")
                    p.wait("tensor", "sc")
                    p.wait("tensor", "ve")
                    p.inst("tensor", lambda E, ctx, c=c, nt=nt: E.matmul(
                        ctx["PA"][0:1, :], ctx["c_onesc"][:, :],
                        ctx["fout"][:, c * TFh + nt * 512: c * TFh + (nt + 1) * 512],
                        start=(c == 0), stop=(c == DCH - 1), skip_group_check=True))
                    p.inst("tensor", lambda E, ctx, c=c, nt=nt, reg=reg: E.matmul(
                        ctx["PB"][0:1, :], ctx["c_onesc"][:, :],
                        ctx["sq3"][:, reg:reg + 512],
                        start=(c == 0), stop=(c == DCH - 1), skip_group_check=True),
                        "tm" if c == DCH - 1 else None)
                p.wait("vector", "tm")
                p.inst("vector", lambda E, ctx: E.tensor_copy(
                    ctx["lntmp"][0:1, 0:512], ctx["PA"][0:1, :]), "ve")
                p.inst("vector", lambda E, ctx: E.tensor_copy(
                    ctx["lntmp"][0:1, 512:1024], ctx["PB"][0:1, :]), "ve")
                p.wait("tensor", "ve")
                p.wait("scalar", "ve")
                ln_stats_math(nt)
                gnt = hf * NT3 + nt
                p.wait("tensor", "ve")
                p.wait("tensor", "sc")
                p.inst("tensor", lambda E, ctx, nt=nt: E.matmul(
                    ctx["PA"][:, :], ctx["c_ones1b"][:, :],
                    ctx["lnr"][0:1, nt * 512:(nt + 1) * 512], start=True, stop=True,
                    skip_group_check=True))
                p.inst("tensor", lambda E, ctx, nt=nt: E.matmul(
                    ctx["PB"][:, :], ctx["c_ones1b"][:, :],
                    ctx["lnr"][0:1, T + nt * 512: T + (nt + 1) * 512], start=True,
                    stop=True, skip_group_check=True))
                p.inst("tensor", lambda E, ctx, gnt=gnt: E.matmul(
                    ctx["PC"][:, :], ctx["c_ones1b"][:, :],
                    ctx["st2"][0:1, T + gnt * 512: T + (gnt + 1) * 512], start=True,
                    stop=True, skip_group_check=True))
                p.inst("tensor", lambda E, ctx, gnt=gnt: E.matmul(
                    ctx["PZ1"][:, 0:512], ctx["c_ones1b"][:, :],
                    ctx["st2"][0:1, gnt * 512:(gnt + 1) * 512], start=True,
                    stop=True, skip_group_check=True), "tm")
                p.wait("vector", "tm")
                p.inst("vector", lambda E, ctx, nt=nt: E.tensor_copy(
                    ctx["b3r"][:, nt * 512:(nt + 1) * 512], ctx["PA"][:, :]), "ve")
                p.inst("vector", lambda E, ctx, nt=nt: E.tensor_copy(
                    ctx["b3n"][:, nt * 512:(nt + 1) * 512], ctx["PB"][:, :]), "ve")
                p.inst("vector", lambda E, ctx, nt=nt: E.tensor_copy(
                    ctx["b2r"][:, nt * 512:(nt + 1) * 512], ctx["PC"][:, :]), "ve")
                p.inst("vector", lambda E, ctx, nt=nt: E.tensor_copy(
                    ctx["b2n"][:, nt * 512:(nt + 1) * 512], ctx["PZ1"][:, 0:512]), "ve")
                p.wait("tensor", "ve")
            sgate = {0: 0, 1: 0}
            for c in range(DCH):
                for nt in range(NT3):
                    par = (c * NT3 + nt) % 2
                    treg = par * 512
                    p.wait("vector", "sd", sgate[par])
                    ta, tb = par * 1024, par * 1024 + 512
                    p.inst("vector", lambda E, ctx, c=c, nt=nt, toff=toff, ta=ta: E.tensor_mul(
                        ctx["t7"][:, ta:ta + 512],
                        ctx["y2"][:, c * T + toff + nt * 512: c * T + toff + (nt + 1) * 512],
                        ctx["b2r"][:, nt * 512:(nt + 1) * 512]), "ve")
                    p.inst("vector", lambda E, ctx, nt=nt, ta=ta: E.tensor_add(
                        ctx["t7"][:, ta:ta + 512], ctx["t7"][:, ta:ta + 512],
                        ctx["b2n"][:, nt * 512:(nt + 1) * 512]), "ve")
                    p.inst("vector", lambda E, ctx, c=c, nt=nt, tb=tb: E.tensor_mul(
                        ctx["t7"][:, tb:tb + 512],
                        ctx["fout"][:, c * TFh + nt * 512: c * TFh + (nt + 1) * 512],
                        ctx["b3r"][:, nt * 512:(nt + 1) * 512]), "ve")
                    p.inst("vector", lambda E, ctx, nt=nt, tb=tb: E.tensor_add(
                        ctx["t7"][:, tb:tb + 512], ctx["t7"][:, tb:tb + 512],
                        ctx["b3n"][:, nt * 512:(nt + 1) * 512]), "ve")
                    p.inst("vector", lambda E, ctx, ta=ta, tb=tb, treg=treg: E.tensor_add(
                        ctx["ostg"][:, treg:treg + 512], ctx["t7"][:, ta:ta + 512],
                        ctx["t7"][:, tb:tb + 512]), "ve")
                    p.wait("sync", "ve")
                    p.group("sync", "sd")
                    p.inst("sync", lambda E, ctx, c=c, nt=nt, toff=toff, treg=treg: E.dma_start(
                        out=out.rearrange("(c p) t -> p c t", p=128)[:, c,
                            toff + nt * 512: toff + (nt + 1) * 512],
                        in_=ctx["ostg"][:, treg:treg + 512]), "sd", 16)
                    sgate[par] = p.cnt["sd"]
            p.free("fout", "sq3", "b3r", "b3n", "b2r", "b2n", "t7", "ostg",
                   "lnr", "lntmp")

    if cfg.stage < 5:
        phsrc = "y1" if cfg.stage < 2 else "qfm"
        for e in ("ve", "sc", "tm"):
            p.wait("gpsimd", e)
        p.group("gpsimd", "gd")
        p.inst("gpsimd", lambda E, ctx, phsrc=phsrc: E.dma_start(out=out_pct, in_=ctx[phsrc].ap().rearrange("p (c t) -> p c t", c=DCH)), "gd", 16)
    p.wait("sync", "sd", p.cnt["sd"])
    p.wait("sync", "gd", p.cnt["gd"])
    p.wait("gpsimd", "gd", p.cnt["gd"])

    p.close()
    import contextlib
    nph = len(p.phases)

    reqs = []
    for pi, ph in enumerate(p.phases):
        for (name, kind, shape, dt, side) in p.allocs[ph]:
            if kind != "sbuf":
                continue
            fidx = nph - 1
            for (fpi, fname) in p.frees:
                if fname == name and fpi >= pi:
                    fidx = fpi
                    break
            size = shape[1] * (4 if dt == F32 else 2)
            reqs.append([pi, ph, name, shape, dt, size, fidx])
    base, top = (nc.sbuf_base + 127) // 128 * 128, nc.sbuf_top
    placed = []
    offs = {}
    for r in sorted(reqs, key=lambda r: -r[5]):
        pi, ph, name, shape, dt, size, fidx = r
        cands = [base]
        for (o2, s2, p2, f2) in placed:
            if not (f2 < pi or p2 > fidx):
                cands.append((o2 + s2 + 127) // 128 * 128)
        ok_off = None
        for cand in sorted(cands):
            if cand + size > top:
                continue
            conflict = False
            for (o2, s2, p2, f2) in placed:
                if f2 < pi or p2 > fidx:
                    continue
                if not (cand + size <= o2 or o2 + s2 <= cand):
                    conflict = True
                    break
            if not conflict:
                ok_off = cand
                break
        if ok_off is None:
            raise MemoryError(f"SBUF plan OOM: {name} size {size} phase {ph}")
        placed.append((ok_off, size, pi, fidx))
        offs[(pi, name)] = ok_off

    with contextlib.ExitStack() as outer:
        ctx = {}
        for sname in p.sem_names:
            ctx[sname] = outer.enter_context(nc.semaphore(sname))
        for pi, ph in enumerate(p.phases):
            for (name, kind, shape, dt, side) in p.allocs[ph]:
                uname = f"{name}__{ph}"
                if kind == "sbuf":
                    ctx[name] = nc.alloc_sbuf_tensor_at(
                        uname, shape, dt, offset=offs[(pi, name)])
                else:
                    ctx[name] = outer.enter_context(nc.psum_tensor(uname, shape, dt))
            with nc.Block() as block:
                for eng in ENGS:
                    a, b = p.seg[ph][eng]
                    steps = p.steps[eng][a:b]
                    if steps:
                        getattr(block, eng)(lambda E, steps=steps: [f(E, ctx) for f in steps])
    return nc, dbg

def prep_in_maps(cfg: CFG, inp):
    T = cfg.tpc
    x = np.asarray(inp["L0_attn_x"], np.float32)[0]
    xT = np.ascontiguousarray(x.T)
    con = build_consts()
    shared = {
        "wq": _retile_w(np.asarray(inp["L0_attn_Wq"], np.float32)),
        "bq": _cvec(np.asarray(inp["L0_attn_bq"], np.float32)),
        "wk": _retile_w(np.asarray(inp["L0_attn_Wk"], np.float32)),
        "bk": np.asarray(inp["L0_attn_bk"], np.float32).reshape(1, -1),
        "wv": _retile_w(np.asarray(inp["L0_attn_Wv"], np.float32)),
        "bv": np.asarray(inp["L0_attn_bv"], np.float32).reshape(1, -1),
        "wb": _retile_w(np.asarray(inp["L0_attn_Wbeta"], np.float32)),
        "bb": np.asarray(inp["L0_attn_bbeta"], np.float32).reshape(1, -1),
        "wo": _retile_w(np.asarray(inp["L0_attn_Wo"], np.float32)),
        "bo": _cvec(np.asarray(inp["L0_attn_bo"], np.float32)),
        "w1p": _panels(np.ascontiguousarray(np.asarray(inp["L0_ffn_w1_w"], np.float32).T)),
        "w1s": _cvec(np.asarray(inp["L0_ffn_w1_s"], np.float32)),
        "w2p": _panels(np.ascontiguousarray(np.asarray(inp["L0_ffn_w2_w"], np.float32).T)),
        "w2s": _cvec(np.asarray(inp["L0_ffn_w2_s"], np.float32)),
        "h1p": _panels(np.ascontiguousarray(np.asarray(inp["L0_ffn_h1_w"], np.float32).T)),
        "h1s": _cvec(np.asarray(inp["L0_ffn_h1_s"], np.float32)),
        "h2p": _panels(np.ascontiguousarray(np.asarray(inp["L0_ffn_h2_w"], np.float32).T)),
        "h2s": _cvec(np.asarray(inp["L0_ffn_h2_s"], np.float32)),
        "w3p": _panels(np.ascontiguousarray(np.asarray(inp["L0_ffn_w3_w"], np.float32).T)),
        "w3s": _cvec(np.asarray(inp["L0_ffn_w3_s"], np.float32)),
    }
    shared.update({k: np.ascontiguousarray(v) for k, v in con.items()})
    in_maps = []
    for ci in range(cfg.n_cores):
        m = dict(shared)
        m["x"] = np.ascontiguousarray(xT[:, ci * T:(ci + 1) * T])
        pm = np.zeros((64, 8), np.float32)
        pm[:, :ci] = 1.0
        m["pmask"] = pm
        im = np.zeros((64, 8 * 64), np.float32)
        for j in range(8):
            if j >= ci:
                im[:, j * 64:(j + 1) * 64] = np.eye(64, dtype=np.float32)
        m["imask"] = im
        in_maps.append(m)
    return in_maps


def kernel(**inputs):
    cfg = CFG()
    in_maps = prep_in_maps(cfg, inputs)
    nc, _ = build_nc(cfg)
    res = run_bass_kernel_spmd(nc, in_maps, core_ids=list(range(cfg.n_cores)))
    outs = [r["out"] for r in res.results]
    full = np.concatenate(outs, axis=1)
    return np.ascontiguousarray(full.T)[None].astype(np.float32)


# version 07
# speedup vs baseline: 1.2460x; 1.2460x over previous
from dataclasses import dataclass

import numpy as np
import concourse.bass as bass
import concourse.mybir as mybir
from concourse.bass_utils import run_bass_kernel_spmd

F32 = mybir.dt.float32
BF16 = mybir.dt.bfloat16
AF = mybir.ActivationFunctionType
ALU = mybir.AluOpType
AXX = mybir.AxisListType.X

NCORES = 8
DM = 1024
DCH = DM // 128
H = 16
D = 64
C = 128
FH = 4096
FCH = FH // 128
ENGS = ("tensor", "vector", "scalar", "gpsimd", "sync")


@dataclass
class CFG:
    tpc: int = 2048
    n_cores: int = NCORES
    stage: int = 9
    debug: bool = False
    p2c: int = 0
    p2s: int = 99


class Prog:
    def __init__(self):
        self.steps = {e: [] for e in ENGS}
        self.cnt = {}
        self.sem_names = []
        self._dmafloor = {}
        self.phases = []
        self.seg = {}
        self._seg_start = {e: 0 for e in ENGS}
        self.allocs = {}
        self.frees = []

    def phase(self, name):
        if self.phases:
            prev = self.phases[-1]
            self.seg[prev] = {e: (self._seg_start[e], len(self.steps[e])) for e in ENGS}
        self._seg_start = {e: len(self.steps[e]) for e in ENGS}
        self.phases.append(name)
        self.allocs[name] = []

    def close(self):
        prev = self.phases[-1]
        self.seg[prev] = {e: (self._seg_start[e], len(self.steps[e])) for e in ENGS}

    def alloc(self, name, kind, shape, dt, side=None):
        self.allocs[self.phases[-1]].append((name, kind, list(shape), dt, side))

    def free(self, *names):
        for n in names:
            self.frees.append((len(self.phases) - 1, n))

    def sem(self, name):
        self.sem_names.append(name)
        self.cnt[name] = 0

    def raw(self, eng, fn):
        self.steps[eng].append(fn)

    def wait(self, eng, sname, val=None):
        v = self.cnt[sname] if val is None else val
        assert v <= self.cnt[sname], (sname, v, self.cnt[sname])
        if v > 0:
            self.raw(eng, lambda E, ctx, s=sname, v=v: E.wait_ge(ctx[s], v))

    def inst(self, eng, fn, sname=None, n=1):
        if sname is None:
            self.raw(eng, fn)
        else:
            self.cnt[sname] += n
            self.raw(eng, lambda E, ctx, f=fn, s=sname, n=n: f(E, ctx).then_inc(ctx[s], n))

    def dma(self, eng, ch, fn):
        self.inst(eng, fn, ch, 16)

    def group(self, eng, ch):
        key = (eng, ch)
        if self.cnt[ch] > self._dmafloor.get(key, 0):
            self.wait(eng, ch, self.cnt[ch])
            self._dmafloor[key] = self.cnt[ch]


def _retile_w(w):
    I, O = w.shape
    return np.ascontiguousarray(
        w.reshape(I // 128, 128, O).transpose(1, 0, 2)).reshape(128, -1)


def _panels(wT, po=128):
    I, O = wT.shape
    npan = O // po
    a = wT.reshape(I // 128, 128, npan, po).transpose(2, 1, 0, 3)
    return np.ascontiguousarray(a).reshape(npan, 128, (I // 128) * po)


def _cvec(v):
    return np.ascontiguousarray(v.reshape(-1, 128).T)


def build_consts():
    ident = np.eye(128, dtype=np.float32)
    e64 = np.zeros((128, 64), np.float32); e64[:64, :64] = np.eye(64)
    maskl = np.tril(np.ones((128, 128), np.float32), -1)
    masku = np.triu(np.ones((128, 128), np.float32), 1)
    maskui = np.triu(np.ones((128, 128), np.float32), 0)
    ones1 = np.ones((1, 128), np.float32)
    onesc = np.ones((128, 1), np.float32)
    oneh16 = np.zeros((16, 16 * 128), np.float32)
    for h in range(16):
        oneh16[h, h * 128:(h + 1) * 128] = 1.0
    oneh4 = np.zeros((4, 4 * 128), np.float32)
    for i in range(4):
        oneh4[i, i * 128:(i + 1) * 128] = 1.0
    return dict(ident=ident, e64=e64, maskl=maskl, masku=masku, maskui=maskui,
                ones1=ones1, onesc=onesc, oneh16=oneh16, oneh4=oneh4)


def build_nc(cfg: CFG):
    T = cfg.tpc
    NCH = T // C
    NT = T // 512
    nc = bass.Bass("TRN2", target_bir_lowering=False, debug=False,
                   detect_race_conditions=False)

    def din(name, shape, dt=F32):
        return nc.dram_tensor(name, list(shape), dt, kind="ExternalInput").ap()

    x_in = din("x", (DM, T))
    wq = din("wq", (128, DCH * DM)); bq = din("bq", (128, DCH))
    wk = din("wk", (128, DCH * DM)); bk = din("bk", (1, DM))
    wv = din("wv", (128, DCH * DM)); bv = din("bv", (1, DM))
    wb = din("wb", (128, DCH * 16)); bb = din("bb", (1, 16))
    wo = din("wo", (128, DCH * DM)); bo = din("bo", (128, DCH))
    w1p = din("w1p", (FCH, 128, DCH * 128)); w1s = din("w1s", (128, FCH))
    w2p = din("w2p", (FCH, 128, DCH * 128)); w2s = din("w2s", (128, FCH))
    h1p = din("h1p", (FCH, 128, FCH * 128)); h1s = din("h1s", (128, FCH))
    h2p = din("h2p", (FCH, 128, FCH * 128)); h2s = din("h2s", (128, FCH))
    w3p = din("w3p", (DCH, 128, FCH * 128)); w3s = din("w3s", (128, DCH))
    ident = din("ident", (128, 128)); e64 = din("e64", (128, 64))
    maskl = din("maskl", (128, 128)); masku = din("masku", (128, 128))
    maskui = din("maskui", (128, 128))
    ones1 = din("ones1", (1, 128)); onesc = din("onesc", (128, 1))
    oneh16 = din("oneh16", (16, 16 * 128)); oneh4 = din("oneh4", (4, 4 * 128))
    pmask = din("pmask", (64, 8)); imask = din("imask", (64, 8 * 64))

    out = nc.dram_tensor("out", [DM, T], F32, kind="ExternalOutput").ap()
    out_pct = out.rearrange("(c p) t -> p c t", p=128)

    cUB = nc.dram_tensor("cUB", [NCH, 128, H * 64], BF16).ap()
    cUPT = nc.dram_tensor("cUPT", [NCH, 64, H * 128], BF16).ap()
    cZPT = nc.dram_tensor("cZPT", [NCH, 64, H * 64], BF16).ap()
    cB = nc.dram_tensor("cB", [NCH, 128, (H // 2) * 64], BF16).ap()
    agi = nc.dram_tensor("agi", [192, 1024], F32).ap()
    ago = nc.dram_tensor("ago", [cfg.n_cores * 192, 1024], F32).ap()

    dbg = {}
    if cfg.debug:
        def dout(name, shape, dt=F32):
            dbg[name] = nc.dram_tensor(name, list(shape), dt, kind="ExternalOutput").ap()
        dout("d_y1", (128, DCH * T))
        dout("d_ktm", (128, NCH * DM))
        dout("d_vtm", (128, NCH * DM))
        dout("d_qfm", (128, DCH * T))
        dout("d_kfm", (128, DCH * T))
        dout("d_beta", (128, NCH * 16))
        if cfg.stage >= 2:
            dout("d_UB", (NCH, 128, H * 64), BF16)
            dout("d_UPT", (NCH, 64, H * 128), BF16)
            dout("d_ZPT", (NCH, 64, H * 64), BF16)
            dout("d_B", (NCH, 128, (H // 2) * 64), BF16)
            dout("d_zfin", (128, (H // 2) * 128))
        if cfg.stage >= 3:
            dout("d_zcore", (64, H * 64), BF16)
        if cfg.stage >= 4:
            dout("d_ofm", (128, DCH * T))
        if cfg.stage >= 5:
            dout("d_y2", (128, DCH * T))
        if cfg.stage >= 9:
            dout("d_hffn", (128, FCH * (T // 2)), BF16)

    p = Prog()
    for s in ["gd", "sd", "tm", "ve", "sc", "cc"]:
        p.sem(s)

    def sb(name, shape, dt=BF16, side=None):
        p.alloc(name, "sbuf", list(shape), dt, side)

    def ps(name, shape, dt=F32):
        p.alloc(name, "psum", list(shape), dt, None)

    NPOW = 6
    TFh = T // 2

    p.phase("P0")
    ps("PA", (128, 512)); ps("PB", (128, 512)); ps("PC", (128, 512))
    ps("PZ1", (128, 1024)); ps("PZ2", (128, 1024))
    ps("PT", (128, 256), BF16)
    sb("c_ident", (128, 128)); sb("c_e64", (128, 64))
    sb("c_maskl", (128, 128)); sb("c_masku", (128, 128)); sb("c_maskui", (128, 128))
    sb("c_ones1", (1, 128), F32); sb("c_onesc", (128, 1), BF16)
    sb("czero", (128, 128))
    sb("c_ones1b", (1, 128), BF16)
    sb("y1", (128, DCH * T))
    sb("lntmp", (1, 2048), F32)
    sb("lnr", (1, 2 * T), BF16)
    sb("bc1", (128, T)); sb("bc2", (128, T))
    sb("sq", (128, 4096))

    def ln_stats_math(nt, save_rows=None):
        p.inst("scalar", lambda E, ctx: E.mul(ctx["lntmp"][0:1, 1024:1536], ctx["lntmp"][0:1, 0:512], np.float32(1.0 / DM)), "sc")
        p.inst("scalar", lambda E, ctx: E.mul(ctx["lntmp"][0:1, 1536:2048], ctx["lntmp"][0:1, 512:1024], np.float32(1.0 / DM)), "sc")
        p.wait("vector", "sc")
        p.inst("vector", lambda E, ctx: E.tensor_mul(
            ctx["lntmp"][0:1, 0:512], ctx["lntmp"][0:1, 1024:1536], ctx["lntmp"][0:1, 1024:1536]), "ve")
        p.inst("vector", lambda E, ctx: E.tensor_sub(
            ctx["lntmp"][0:1, 512:1024], ctx["lntmp"][0:1, 1536:2048], ctx["lntmp"][0:1, 0:512]), "ve")
        p.wait("scalar", "ve")
        p.inst("scalar", lambda E, ctx: E.activation(
            ctx["lntmp"][0:1, 0:512], ctx["lntmp"][0:1, 512:1024], AF.Ln, bias=np.float32(1e-5)), "sc")
        p.inst("scalar", lambda E, ctx, nt=nt: E.activation(
            ctx["lnr"][0:1, nt * 512:(nt + 1) * 512], ctx["lntmp"][0:1, 0:512],
            AF.Exp, scale=np.float32(-0.5)), "sc")
        p.inst("scalar", lambda E, ctx: E.activation(
            ctx["lntmp"][0:1, 512:1024], ctx["lntmp"][0:1, 0:512],
            AF.Exp, scale=np.float32(-0.5)), "sc")
        if save_rows is not None:
            p.inst("scalar", lambda E, ctx, sr=save_rows, nt=nt: E.activation(
                ctx[sr][0:1, T + nt * 512: T + (nt + 1) * 512], ctx["lntmp"][0:1, 0:512],
                AF.Exp, scale=np.float32(0.5)), "sc")
            p.inst("scalar", lambda E, ctx, sr=save_rows, nt=nt: E.copy(
                ctx[sr][0:1, nt * 512:(nt + 1) * 512], ctx["lntmp"][0:1, 1024:1536]), "sc")
        p.wait("vector", "sc")
        p.inst("vector", lambda E, ctx: E.tensor_mul(
            ctx["lntmp"][0:1, 0:512], ctx["lntmp"][0:1, 1024:1536], ctx["lntmp"][0:1, 512:1024]), "ve")
        p.inst("vector", lambda E, ctx, nt=nt: E.tensor_scalar_mul(
            ctx["lnr"][0:1, T + nt * 512: T + (nt + 1) * 512], ctx["lntmp"][0:1, 0:512], -1.0), "ve")

    def ln_bcast(nt):
        p.wait("tensor", "ve")
        p.wait("tensor", "sc")
        p.inst("tensor", lambda E, ctx, nt=nt: E.matmul(
            ctx["PA"][:, :], ctx["c_ones1b"][:, :],
            ctx["lnr"][0:1, nt * 512:(nt + 1) * 512], start=True, stop=True,
            skip_group_check=True))
        p.inst("tensor", lambda E, ctx, nt=nt: E.matmul(
            ctx["PB"][:, :], ctx["c_ones1b"][:, :],
            ctx["lnr"][0:1, T + nt * 512: T + (nt + 1) * 512], start=True, stop=True,
            skip_group_check=True), "tm")
        p.wait("vector", "tm")
        p.inst("vector", lambda E, ctx, nt=nt: E.tensor_copy(
            ctx["bc1"][:, nt * 512:(nt + 1) * 512], ctx["PA"][:, :]), "ve")
        p.inst("vector", lambda E, ctx, nt=nt: E.tensor_copy(
            ctx["bc2"][:, nt * 512:(nt + 1) * 512], ctx["PB"][:, :]), "ve")
        p.wait("tensor", "ve")

    def ln_feature_major(src, dst, save_rows=None):
        for nt in range(NT):
            p.wait("scalar", "tm")
            for c in range(DCH):
                reg = c * 512
                p.inst("scalar", lambda E, ctx, c=c, nt=nt, reg=reg, src=src: E.activation(
                    ctx["sq"][:, reg:reg + 512],
                    ctx[src][:, c * T + nt * 512: c * T + (nt + 1) * 512],
                    AF.Square), "sc")
                p.wait("tensor", "sc")
                p.inst("tensor", lambda E, ctx, c=c, nt=nt, src=src: E.matmul(
                    ctx["PA"][0:1, :], ctx["c_onesc"][:, :],
                    ctx[src][:, c * T + nt * 512: c * T + (nt + 1) * 512],
                    start=(c == 0), stop=(c == DCH - 1), skip_group_check=True))
                p.inst("tensor", lambda E, ctx, c=c, nt=nt, reg=reg: E.matmul(
                    ctx["PB"][0:1, :], ctx["c_onesc"][:, :],
                    ctx["sq"][:, reg:reg + 512],
                    start=(c == 0), stop=(c == DCH - 1), skip_group_check=True),
                    "tm" if c == DCH - 1 else None)
            p.wait("vector", "tm")
            p.inst("vector", lambda E, ctx, nt=nt: E.tensor_copy(
                ctx["lntmp"][0:1, 0:512], ctx["PA"][0:1, :]), "ve")
            p.inst("vector", lambda E, ctx, nt=nt: E.tensor_copy(
                ctx["lntmp"][0:1, 512:1024], ctx["PB"][0:1, :]), "ve")
            p.wait("tensor", "ve")
            p.wait("scalar", "ve")
            ln_stats_math(nt, save_rows)
            ln_bcast(nt)
        for c in range(DCH):
            p.inst("vector", lambda E, ctx, c=c, src=src, dst=dst: E.tensor_mul(
                ctx[dst][:, c * T:(c + 1) * T], ctx[src][:, c * T:(c + 1) * T],
                ctx["bc1"][:, :]), "ve")
            p.inst("vector", lambda E, ctx, c=c, dst=dst: E.tensor_add(
                ctx[dst][:, c * T:(c + 1) * T], ctx[dst][:, c * T:(c + 1) * T],
                ctx["bc2"][:, :]), "ve")

    consts = [(ident, "c_ident"), (e64, "c_e64"), (maskl, "c_maskl"),
              (masku, "c_masku"), (maskui, "c_maskui"), (ones1, "c_ones1"),
              (onesc, "c_onesc"), (ones1, "c_ones1b")]
    for iap, sname in consts:
        p.inst("gpsimd", lambda E, ctx, iap=iap, sname=sname: E.dma_start(
            out=ctx[sname].ap(), in_=iap), "gd", 16)
    p.inst("gpsimd", lambda E, ctx: E.dma_start(out=ctx["y1"].ap().rearrange("p (c t) -> p c t", c=DCH), in_=x_in.rearrange("(c p) t -> p c t", p=128)), "gd", 16)
    p.inst("vector", lambda E, ctx: E.memset(ctx["czero"][:, :], 0.0), "ve")
    ng0 = p.cnt["gd"]
    for e in ("scalar", "vector", "tensor", "sync"):
        p.wait(e, "gd", ng0)

    ln_feature_major("y1", "y1")

    p.free("bc1", "bc2", "sq")
    p.phase("P1a")
    sb("ktm", (128, NCH * DM))
    sb("vtm", (128, NCH * DM))
    sb("kfm", (128, DCH * T))
    sb("beta", (128, NCH * 16), F32)
    sb("nbeta", (128, NCH * 16), F32)
    sb("betab", (128, NCH * 16), BF16)
    sb("wt", (128, DCH * DM))
    sb("bconst", (128, DM), F32)
    sb("bcol", (128, DCH), F32)
    sb("knr", (128, NCH * 16), F32)
    sb("knrb", (128, NCH * 16), BF16)
    sb("sqk", (128, DM))

    def bias_broadcast(bias_ap, width):
        p.wait("sync", "ve")
        p.wait("sync", "tm")
        p.group("sync", "sd")
        p.inst("sync", lambda E, ctx, b=bias_ap, w=width: E.dma_start(
            out=ctx["lntmp"][0:1, 0:w], in_=b), "sd", 16)
        p.wait("tensor", "sd")
        ntw = (width + 511) // 512
        for nt in range(ntw):
            w0, w1 = nt * 512, min(width, (nt + 1) * 512)
            p.inst("tensor", lambda E, ctx, w0=w0, w1=w1: E.matmul(
                ctx["PA"][:, 0:w1 - w0], ctx["c_ones1"][:, :],
                ctx["lntmp"][0:1, w0:w1], start=True, stop=True, skip_group_check=True), "tm")
            p.wait("vector", "tm")
            p.inst("vector", lambda E, ctx, w0=w0, w1=w1: E.tensor_copy(
                ctx["bconst"][:, w0:w1], ctx["PA"][:, 0:w1 - w0]), "ve")
            p.wait("tensor", "ve")

    def load_w(src_ap, width=DCH * DM):
        p.wait("gpsimd", "tm")
        p.group("gpsimd", "gd")
        p.inst("gpsimd", lambda E, ctx, s=src_ap, w=width: E.dma_start(
            out=ctx["wt"][:, 0:w], in_=s), "gd", 16)

    def proj_tm(dst):
        for tc in range(NCH):
            for nt in range(2):
                pp = "PA" if (tc * 2 + nt) % 2 == 0 else "PB"
                for c in range(DCH):
                    p.inst("tensor", lambda E, ctx, tc=tc, nt=nt, c=c, pp=pp: E.matmul(
                        ctx[pp][:, :],
                        ctx["y1"][:, c * T + tc * 128: c * T + (tc + 1) * 128],
                        ctx["wt"][:, c * DM + nt * 512: c * DM + (nt + 1) * 512],
                        start=(c == 0), stop=(c == DCH - 1), skip_group_check=True),
                        "tm" if c == DCH - 1 else None)
                p.wait("vector", "tm")
                p.inst("vector", lambda E, ctx, dst=dst, tc=tc, nt=nt, pp=pp:
                       E.scalar_tensor_tensor(
                           ctx[dst][:, tc * DM + nt * 512: tc * DM + (nt + 1) * 512],
                           ctx[pp][:, :], 1.0,
                           ctx["bconst"][:, nt * 512:(nt + 1) * 512],
                           ALU.mult, ALU.add), "ve")
                if (tc * 2 + nt) % 2 == 1:
                    p.wait("tensor", "ve")

    def proj_fm(dst, src="y1", bias="bcol"):
        for oc in range(DCH):
            for nt in range(NT):
                pp = "PA" if (oc * NT + nt) % 2 == 0 else "PB"
                for c in range(DCH):
                    p.inst("tensor", lambda E, ctx, oc=oc, nt=nt, c=c, pp=pp, src=src: E.matmul(
                        ctx[pp][:, :],
                        ctx["wt"][:, c * DM + oc * 128: c * DM + (oc + 1) * 128],
                        ctx[src][:, c * T + nt * 512: c * T + (nt + 1) * 512],
                        start=(c == 0), stop=(c == DCH - 1), skip_group_check=True),
                        "tm" if c == DCH - 1 else None)
                p.wait("scalar", "tm")
                p.inst("scalar", lambda E, ctx, dst=dst, oc=oc, nt=nt, pp=pp, bias=bias:
                       E.activation(
                           ctx[dst][:, oc * T + nt * 512: oc * T + (nt + 1) * 512],
                           ctx[pp][:, :], AF.Identity,
                           bias=ctx[bias][:, oc:oc + 1]), "sc")
                if (oc * NT + nt) % 2 == 1:
                    p.wait("tensor", "sc")

    load_w(wk)
    bias_broadcast(bk, DM)
    p.wait("tensor", "gd")
    proj_tm("ktm")
    load_w(wv)
    bias_broadcast(bv, DM)
    p.wait("tensor", "gd")
    p.wait("vector", "gd")
    proj_tm("vtm")
    p.wait("gpsimd", "tm")
    p.group("gpsimd", "gd")
    p.inst("gpsimd", lambda E, ctx: E.dma_start(out=ctx["wt"][:, 0:DCH * 16], in_=wb), "gd", 16)
    bias_broadcast(bb, 16)
    p.wait("tensor", "gd")
    for tc in range(NCH):
        pp = "PA" if tc % 2 == 0 else "PB"
        for c in range(DCH):
            p.inst("tensor", lambda E, ctx, tc=tc, c=c, pp=pp: E.matmul(
                ctx[pp][:, 0:16],
                ctx["y1"][:, c * T + tc * 128: c * T + (tc + 1) * 128],
                ctx["wt"][:, c * 16:(c + 1) * 16],
                start=(c == 0), stop=(c == DCH - 1), skip_group_check=True),
                "tm" if c == DCH - 1 else None)
        p.wait("vector", "tm")
        p.inst("vector", lambda E, ctx, tc=tc, pp=pp: E.tensor_add(
            ctx["beta"][:, tc * 16:(tc + 1) * 16], ctx[pp][:, 0:16],
            ctx["bconst"][:, 0:16]), "ve")
        if tc % 2 == 1:
            p.wait("tensor", "ve")
    p.wait("scalar", "ve")
    p.inst("scalar", lambda E, ctx: E.activation(
        ctx["beta"][:, :], ctx["beta"][:, :], AF.Sigmoid), "sc")
    p.inst("scalar", lambda E, ctx: E.activation(
        ctx["nbeta"][:, :], ctx["beta"][:, :], AF.Identity, scale=np.float32(-1.0)), "sc")
    p.inst("scalar", lambda E, ctx: E.copy(ctx["betab"][:, :], ctx["beta"][:, :]), "sc")
    for tc in range(NCH):
        p.wait("scalar", "ve")
        p.wait("scalar", "sc")
        p.inst("scalar", lambda E, ctx, tc=tc: E.activation(
            ctx["sqk"][:, 0:DM], ctx["ktm"][:, tc * DM:(tc + 1) * DM], AF.Square), "sc")
        p.wait("vector", "sc")
        p.inst("vector", lambda E, ctx, tc=tc: E.reduce_sum(
            ctx["knr"][:, tc * 16:(tc + 1) * 16].rearrange("p (h o) -> p h o", o=1),
            ctx["sqk"][:, 0:DM].rearrange("p (h d) -> p h d", d=64), AXX), "ve")
    p.wait("scalar", "ve")
    p.inst("scalar", lambda E, ctx: E.activation(ctx["knr"][:, :], ctx["knr"][:, :], AF.Ln), "sc")
    p.inst("scalar", lambda E, ctx: E.activation(ctx["knr"][:, :], ctx["knr"][:, :], AF.Exp, scale=np.float32(0.5)), "sc")
    p.inst("scalar", lambda E, ctx: E.add(ctx["knr"][:, :], ctx["knr"][:, :], np.float32(1e-6)), "sc")
    p.wait("vector", "sc")
    p.inst("vector", lambda E, ctx: E.reciprocal(ctx["knr"][:, :], ctx["knr"][:, :]), "ve")
    p.inst("vector", lambda E, ctx: E.tensor_copy(ctx["knrb"][:, :], ctx["knr"][:, :]), "ve")
    for tc in range(NCH):
        p.inst("vector", lambda E, ctx, tc=tc: E.tensor_mul(
            ctx["ktm"][:, tc * DM:(tc + 1) * DM].rearrange("p (h d) -> p h d", d=64),
            ctx["ktm"][:, tc * DM:(tc + 1) * DM].rearrange("p (h d) -> p h d", d=64),
            ctx["knrb"][:, tc * 16:(tc + 1) * 16].rearrange("p (h o) -> p h o", o=1)
            .broadcast_to([128, 16, 64])), "ve")
    p.wait("tensor", "ve")
    for tc in range(NCH):
        for oc in range(DCH):
            half = ((tc * DCH + oc) % 2) * 128
            p.inst("tensor", lambda E, ctx, tc=tc, oc=oc, half=half: E.transpose(
                ctx["PT"][:, half:half + 128],
                ctx["ktm"][:, tc * DM + oc * 128: tc * DM + (oc + 1) * 128],
                ctx["c_ident"][:, :]), "tm")
            p.wait("vector", "tm")
            p.inst("vector", lambda E, ctx, tc=tc, oc=oc, half=half: E.tensor_copy(
                ctx["kfm"][:, oc * T + tc * 128: oc * T + (tc + 1) * 128],
                ctx["PT"][:, half:half + 128]), "ve")
            if (tc * DCH + oc) % 2 == 1:
                p.wait("tensor", "ve")

    p.free("bconst", "knr", "knrb", "sqk")
    p.phase("P1b")
    sb("qfm", (128, DCH * T))
    load_w(wq)
    p.wait("tensor", "gd")
    p.wait("scalar", "gd")
    p.inst("gpsimd", lambda E, ctx: E.dma_start(out=ctx["bcol"][:, 0:DCH], in_=bq), "gd", 16)
    p.wait("scalar", "gd")
    proj_fm("qfm")

    if cfg.debug:
        def dump(src, dname, reg=None):
            for e in ("ve", "sc", "tm"):
                p.wait("gpsimd", e)
            p.group("gpsimd", "gd")
            if reg is None:
                p.inst("gpsimd", lambda E, ctx, src=src, dname=dname: E.dma_start(
                    out=dbg[dname], in_=ctx[src].ap()), "gd", 16)
            else:
                p.inst("gpsimd", lambda E, ctx, src=src, dname=dname, reg=reg: E.dma_start(
                    out=dbg[dname], in_=ctx[src][reg]), "gd", 16)
        dump("y1", "d_y1"); dump("ktm", "d_ktm"); dump("vtm", "d_vtm")
        dump("qfm", "d_qfm"); dump("kfm", "d_kfm"); dump("beta", "d_beta")

    if cfg.stage >= 2:
        p.free("y1", "wt")
        p.phase("P2")
        sb("c_oneh16", (16, 16 * 128))
        sb("Tt", (128, 256))
        sb("pw", (128, 2 * NPOW * 256))
        sb("Nlow", (128, 256)); sb("Nup", (128, 256))
        sb("chR", (128, 512))
        sb("mbB", (128, 256))
        sb("zbd", (128, (H // 2) * 256))
        sb("vg", (128, 128))
        sb("rh", (128, 256))
        sb("Ush", (128, 256))
        sb("stUB", (128, H * 64))
        sb("stUPT", (64, H * 128))
        sb("stZPT", (64, H * 64))
        sb("brow", (16, 128))
        sb("zfin", (128, 1024), F32)
        sb("ptfin", (64, 1024), F32)
        p.group("gpsimd", "gd")
        p.inst("gpsimd", lambda E, ctx: E.dma_start(
            out=ctx["c_oneh16"].ap(), in_=oneh16), "gd", 16)
        for e in ("tensor", "vector", "scalar"):
            p.wait(e, "gd")
        p.wait("tensor", "ve")
        for pr in range(8):
            p.inst("tensor", lambda E, ctx, pr=pr: E.matmul(
                ctx["PZ1"][0:64, pr * 128: pr * 128 + 64], ctx["c_e64"][:, :],
                ctx["c_e64"][:, :], start=(pr % 4 == 0), stop=False,
                tile_position=(0, 0), skip_group_check=True))
            p.inst("tensor", lambda E, ctx, pr=pr: E.matmul(
                ctx["PZ1"][64:128, pr * 128: pr * 128 + 64], ctx["c_e64"][:, :],
                ctx["c_e64"][:, :], start=(pr % 4 == 0), stop=False,
                tile_position=(0, 64), skip_group_check=True))
            for j in range(2):
                p.inst("tensor", lambda E, ctx, pr=pr, j=j: E.matmul(
                    ctx["PZ2"][0:64, pr * 128 + j * 64: pr * 128 + (j + 1) * 64],
                    ctx["c_e64"][:, :], ctx["c_e64"][:, :],
                    start=(pr % 4 == 0 and j == 0), stop=False,
                    tile_position=(0, 0), skip_group_check=True))
            p.inst("tensor", lambda E, ctx, pr=pr: E.matmul(
                ctx["PZ2"][64:128, pr * 128:(pr + 1) * 128], ctx["c_e64"][:, :],
                ctx["czero"][:, :], start=(pr % 4 == 0), stop=False,
                tile_position=(0, 64), skip_group_check=True))
            for j in range(2):
                hb = j * 64
                p.inst("tensor", lambda E, ctx, pr=pr, hb=hb: E.matmul(
                    ctx["PZ1"][hb:hb + 64, pr * 128 + 64:(pr + 1) * 128],
                    ctx["c_e64"][:, :], ctx["czero"][:, 0:64], start=False,
                    stop=False, tile_position=(0, hb), skip_group_check=True),
                    "tm" if (pr == 7 and hb == 64) else None)
        p.wait("vector", "tm")
        p.inst("vector", lambda E, ctx: E.memset(ctx["zbd"][:, :], 0.0), "ve")
        for pr in range(8):
            p.inst("vector", lambda E, ctx, pr=pr: E.tensor_copy(
                ctx["zbd"][0:64, pr * 256: pr * 256 + 128],
                ctx["PZ1"][0:64, pr * 128: (pr + 1) * 128]), "ve")
            p.inst("vector", lambda E, ctx, pr=pr: E.tensor_copy(
                ctx["zbd"][64:128, pr * 256 + 128: (pr + 1) * 256],
                ctx["PZ1"][64:128, pr * 128: (pr + 1) * 128]), "ve")

        def inv_pair(c, pr, lim=99):
            p.wait("vector", "tm")
            for j in range(2):
                jj = j * 128
                h = 2 * pr + j
                p.inst("vector", lambda E, ctx, jj=jj: E.scalar_tensor_tensor(
                    ctx["mbB"][:, jj:jj + 128], ctx["PB"][:, jj:jj + 128], 1.0,
                    ctx["c_masku"][:, :], ALU.mult, ALU.mult), "ve")
                p.inst("vector", lambda E, ctx, jj=jj: E.scalar_tensor_tensor(
                    ctx["Nup"][:, jj:jj + 128], ctx["PA"][:, jj:jj + 128], 1.0,
                    ctx["mbB"][:, jj:jj + 128], ALU.mult, ALU.mult), "ve")
                p.inst("vector", lambda E, ctx, jj=jj, c=c, h=h: E.scalar_tensor_tensor(
                    ctx["Nlow"][:, jj:jj + 128], ctx["PA"][:, jj:jj + 128],
                    ctx["beta"][:, c * 16 + h: c * 16 + h + 1],
                    ctx["c_maskl"][:, :], ALU.mult, ALU.mult), "ve")
            if lim < 21:
                return
            evg = [p.cnt["ve"], p.cnt["ve"]]
            for lvl in range(1, 7):
                for j in range(2):
                    pz = "PC" if j == 0 else "PB"
                    if lvl == 1:
                        pN = ("Nlow", slice(j * 128, j * 128 + 128))
                        pT = ("Nup", slice(j * 128, j * 128 + 128))
                    else:
                        off = j * NPOW * 256 + (lvl - 2) * 256
                        pN = ("pw", slice(off, off + 128))
                        pT = ("pw", slice(off + 128, off + 256))
                    p.wait("tensor", "ve", evg[j])
                    p.inst("tensor", lambda E, ctx, pN=pN, pT=pT, pz=pz: E.matmul(
                        ctx[pz][:, 0:128], ctx[pT[0]][:, pT[1]],
                        ctx[pN[0]][:, pN[1]], start=True, stop=True,
                        skip_group_check=True), None if lvl < 6 else "tm")
                    if lvl < 6:
                        p.inst("tensor", lambda E, ctx, pN=pN, pT=pT, pz=pz: E.matmul(
                            ctx[pz][:, 128:256], ctx[pN[0]][:, pN[1]],
                            ctx[pT[0]][:, pT[1]], start=True, stop=True,
                            skip_group_check=True), "tm")
                    p.wait("vector", "tm")
                    wid = 256 if lvl < 6 else 128
                    off2 = j * NPOW * 256 + (lvl - 1) * 256
                    p.inst("vector", lambda E, ctx, pz=pz, off2=off2, wid=wid: E.tensor_copy(
                        ctx["pw"][:, off2:off2 + wid], ctx[pz][:, 0:wid]), "ve")
                    evg[j] = p.cnt["ve"]
            if lim < 22:
                return
            for j in range(2):
                p.inst("vector", lambda E, ctx, j=j: E.tensor_sub(
                    ctx["chR"][:, j * 256: j * 256 + 128], ctx["c_ident"][:, :],
                    ctx["Nup"][:, j * 128:(j + 1) * 128]), "ve")
            evg = [p.cnt["ve"], p.cnt["ve"]]
            for lvl in range(1, 7):
                for j in range(2):
                    src = j * 256 + ((lvl - 1) % 2) * 128
                    dst = j * 256 + (lvl % 2) * 128
                    poff = j * NPOW * 256 + (lvl - 1) * 256
                    pz = "PA" if j == 0 else "PC"
                    p.wait("tensor", "ve", evg[j])
                    p.inst("tensor", lambda E, ctx, poff=poff, src=src, pz=pz: E.matmul(
                        ctx[pz][:, 0:128], ctx["pw"][:, poff:poff + 128],
                        ctx["chR"][:, src:src + 128], start=True, stop=True,
                        skip_group_check=True), "tm")
                    p.wait("vector", "tm")
                    if lvl < 6:
                        p.inst("vector", lambda E, ctx, src=src, dst=dst, pz=pz: E.scalar_tensor_tensor(
                            ctx["chR"][:, dst:dst + 128], ctx[pz][:, 0:128],
                            1.0, ctx["chR"][:, src:src + 128], ALU.mult, ALU.add), "ve")
                    else:
                        p.inst("vector", lambda E, ctx, src=src, j=j, pz=pz: E.scalar_tensor_tensor(
                            ctx["Tt"][:, j * 128:(j + 1) * 128],
                            ctx[pz][:, 0:128],
                            1.0, ctx["chR"][:, src:src + 128], ALU.mult, ALU.add), "ve")
                    evg[j] = p.cnt["ve"]

        for c in range(NCH if cfg.p2c == 0 else min(NCH, cfg.p2c)):
            if cfg.p2s < 1:
                continue
            p.wait("tensor", "ve")
            p.inst("tensor", lambda E, ctx, c=c: E.transpose(
                ctx["PT"][0:16, 0:128],
                ctx["betab"][:, c * 16:(c + 1) * 16], ctx["c_ident"][:, :]), "tm")
            p.wait("vector", "tm")
            p.inst("vector", lambda E, ctx: E.tensor_copy(
                ctx["brow"][:, :], ctx["PT"][0:16, 0:128]), "ve")
            p.wait("vector", "sd")
            p.wait("scalar", "sd")
            p.wait("vector", "tm")
            p.inst("vector", lambda E, ctx: E.tensor_copy(
                ctx["stZPT"][0:64, :], ctx["PZ2"][0:64, :]), "ve")
            p.wait("sync", "sc")
            p.wait("sync", "ve")
            p.group("sync", "sd")
            p.inst("sync", lambda E, ctx, c=c: E.dma_start(
                out=cZPT[c], in_=ctx["stZPT"].ap()), "sd", 16)
            p.inst("sync", lambda E, ctx, c=c: E.dma_start(
                out=cB[c][0:64, :],
                in_=ctx["zbd"].ap().rearrange("p (r x) -> p r x", r=8)[0:64, :, 64:128]), "sd", 16)
            p.inst("sync", lambda E, ctx, c=c: E.dma_start(
                out=cB[c][64:128, :],
                in_=ctx["zbd"].ap().rearrange("p (r x) -> p r x", r=8)[64:128, :, 192:256]), "sd", 16)
            if cfg.debug:
                p.group("sync", "sd")
                p.inst("sync", lambda E, ctx, c=c: E.dma_start(
                    out=dbg["d_ZPT"][c], in_=ctx["stZPT"].ap()), "sd", 16)
                p.inst("sync", lambda E, ctx, c=c: E.dma_start(
                    out=dbg["d_B"][c][0:64, :],
                    in_=ctx["zbd"].ap().rearrange("p (r x) -> p r x", r=8)[0:64, :, 64:128]), "sd", 16)
                p.inst("sync", lambda E, ctx, c=c: E.dma_start(
                    out=dbg["d_B"][c][64:128, :],
                    in_=ctx["zbd"].ap().rearrange("p (r x) -> p r x", r=8)[64:128, :, 192:256]), "sd", 16)
            zbd_cached = p.cnt["sd"]
            for pr in range(8):
                if pr == 0:
                    p.wait("vector", "sd", zbd_cached)
                kc = lambda ctx, pr=pr, c=c: ctx["kfm"][:, pr * T + c * 128: pr * T + (c + 1) * 128]
                p.wait("tensor", "ve")
                p.wait("tensor", "sc")
                for j in range(2):
                    hb = j * 64
                    Adst = ("PA" if j == 0 else "PB") if cfg.p2s in (41, 42, 43) else "PA"
                    Aoff = 0 if cfg.p2s in (41, 42, 43) else j * 128
                    p.inst("tensor", lambda E, ctx, pr=pr, c=c, hb=hb, j=j, Adst=Adst, Aoff=Aoff: E.matmul(
                        ctx[Adst][:, Aoff:Aoff + 128],
                        ctx["kfm"][:, pr * T + c * 128: pr * T + (c + 1) * 128][hb:hb + 64, :],
                        ctx["kfm"][:, pr * T + c * 128: pr * T + (c + 1) * 128][hb:hb + 64, :],
                        start=True, stop=True, tile_position=(hb, 0),
                        skip_group_check=True))
                    if cfg.p2s in (41, 43):
                        if j == 0:
                            p.inst("tensor", lambda E, ctx, pr=pr, j=j: E.matmul(
                                ctx["PC"][:, 0:128],
                                ctx["c_oneh16"][:, (2 * pr + j) * 128:(2 * pr + j + 1) * 128],
                                ctx["brow"][:, :], start=True, stop=True,
                                skip_group_check=True))
                    elif cfg.p2s == 42:
                        if j == 0:
                            p.inst("tensor", lambda E, ctx: E.matmul(
                                ctx["PC"][:, 0:128], ctx["c_ident"][:, :],
                                ctx["c_masku"][:, :], start=True, stop=True,
                                skip_group_check=True))
                    elif cfg.p2s == 29:
                        p.inst("tensor", lambda E, ctx, pr=pr, j=j: E.matmul(
                            ctx["PB"][:, j * 128:(j + 1) * 128],
                            ctx["c_ident"][:, :],
                            ctx["c_masku"][:, :], start=True, stop=True,
                            skip_group_check=True))
                    else:
                        p.inst("tensor", lambda E, ctx, pr=pr, j=j: E.matmul(
                            ctx["PB"][:, j * 128:(j + 1) * 128],
                            ctx["c_oneh16"][:, (2 * pr + j) * 128:(2 * pr + j + 1) * 128],
                            ctx["brow"][:, :], start=True, stop=True,
                            skip_group_check=True), "tm" if j == 1 else None)
                if cfg.p2s < 2:
                    continue
                inv_pair(c, pr, lim=cfg.p2s)
                if cfg.p2s < 3:
                    continue
                p.wait("tensor", "ve")
                p.inst("tensor", lambda E, ctx, pr=pr, c=c: E.matmul(
                    ctx["PB"][:, 256:512],
                    ctx["kfm"][:, pr * T + c * 128: pr * T + (c + 1) * 128],
                    ctx["zbd"][:, pr * 256:(pr + 1) * 256],
                    start=True, stop=True, skip_group_check=True), "tm")
                if cfg.p2s == 45:
                    continue
                if cfg.p2s < 4:
                    continue
                p.wait("vector", "tm")
                for j in range(2):
                    h = 2 * pr + j
                    p.inst("vector", lambda E, ctx, j=j, c=c, h=h: E.tensor_scalar(
                        ctx["rh"][:, j * 128: j * 128 + 64],
                        ctx["PB"][:, 256 + j * 128: 256 + j * 128 + 64],
                        ctx["beta"][:, c * 16 + h: c * 16 + h + 1], -1.0,
                        ALU.mult, ALU.mult), "ve")
                    p.inst("vector", lambda E, ctx, j=j, c=c, h=h: E.tensor_sub(
                        ctx["vg"][:, j * 64:(j + 1) * 64],
                        ctx["vtm"][:, c * DM + h * 64: c * DM + (h + 1) * 64],
                        ctx["PB"][:, 256 + j * 128 + 64: 256 + (j + 1) * 128]), "ve")
                    p.inst("vector", lambda E, ctx, j=j, c=c, h=h: E.tensor_scalar(
                        ctx["rh"][:, j * 128 + 64:(j + 1) * 128],
                        ctx["vg"][:, j * 64:(j + 1) * 64],
                        ctx["beta"][:, c * 16 + h: c * 16 + h + 1], None,
                        ALU.mult), "ve")
                if cfg.p2s < 5:
                    continue
                p.wait("tensor", "sc")
                p.wait("tensor", "ve")
                for j in range(2):
                    p.inst("tensor", lambda E, ctx, j=j: E.matmul(
                        ctx["PC"][:, j * 128:(j + 1) * 128],
                        ctx["Tt"][:, j * 128:(j + 1) * 128],
                        ctx["rh"][:, j * 128:(j + 1) * 128],
                        start=True, stop=True, skip_group_check=True))
                    p.inst("tensor", lambda E, ctx, j=j: E.matmul(
                        ctx["PC"][0:64, 256 + j * 128: 256 + (j + 1) * 128],
                        ctx["rh"][:, j * 128: j * 128 + 64],
                        ctx["Tt"][:, j * 128:(j + 1) * 128],
                        start=True, stop=True, skip_group_check=True),
                        "tm" if j == 1 else None)
                p.wait("vector", "tm")
                p.wait("scalar", "tm")
                for j in range(2):
                    h = 2 * pr + j
                    p.inst("vector", lambda E, ctx, j=j: E.tensor_copy(
                        ctx["Ush"][:, j * 128:(j + 1) * 128],
                        ctx["PC"][:, j * 128:(j + 1) * 128]), "ve")
                    p.inst("vector", lambda E, ctx, j=j, h=h: E.tensor_copy(
                        ctx["stUB"][:, h * 64:(h + 1) * 64],
                        ctx["PC"][:, j * 128 + 64:(j + 1) * 128]), "ve")
                    p.inst("vector", lambda E, ctx, j=j, h=h: E.tensor_copy(
                        ctx["stUPT"][0:64, h * 128:(h + 1) * 128],
                        ctx["PC"][0:64, 256 + j * 128: 256 + (j + 1) * 128]), "ve")
                if cfg.p2s < 6:
                    continue
                p.wait("tensor", "ve")
                for j in range(2):
                    h = 2 * pr + j
                    hb = j * 64
                    p.inst("tensor", lambda E, ctx, j=j, h=h, hb=hb, pr=pr, c=c: E.matmul(
                        ctx["PZ1"][hb:hb + 64, pr * 128:(pr + 1) * 128],
                        ctx["ktm"][:, c * DM + h * 64: c * DM + (h + 1) * 64],
                        ctx["Ush"][:, j * 128:(j + 1) * 128],
                        start=False, stop=False, tile_position=(0, hb),
                        skip_group_check=True))
                    p.inst("tensor", lambda E, ctx, j=j, h=h, pr=pr, c=c: E.matmul(
                        ctx["PZ2"][:, pr * 128 + j * 64: pr * 128 + (j + 1) * 64],
                        ctx["Ush"][:, j * 128:(j + 1) * 128],
                        ctx["ktm"][:, c * DM + h * 64: c * DM + (h + 1) * 64],
                        start=False, stop=False, tile_position=(0, 0),
                        skip_group_check=True), "tm" if j == 1 else None)
                p.wait("vector", "tm")
                p.inst("vector", lambda E, ctx, pr=pr: E.tensor_copy(
                    ctx["zbd"][0:64, pr * 256: pr * 256 + 128],
                    ctx["PZ1"][0:64, pr * 128:(pr + 1) * 128]), "ve")
                p.inst("vector", lambda E, ctx, pr=pr: E.tensor_copy(
                    ctx["zbd"][64:128, pr * 256 + 128:(pr + 1) * 256],
                    ctx["PZ1"][64:128, pr * 128:(pr + 1) * 128]), "ve")
            p.wait("sync", "ve")
            p.wait("sync", "sc")
            p.group("sync", "sd")
            p.inst("sync", lambda E, ctx, c=c: E.dma_start(
                out=cUB[c], in_=ctx["stUB"].ap()), "sd", 16)
            p.inst("sync", lambda E, ctx, c=c: E.dma_start(
                out=cUPT[c], in_=ctx["stUPT"].ap()), "sd", 16)
            if cfg.debug:
                p.group("sync", "sd")
                p.inst("sync", lambda E, ctx, c=c: E.dma_start(
                    out=dbg["d_UB"][c], in_=ctx["stUB"].ap()), "sd", 16)
                p.inst("sync", lambda E, ctx, c=c: E.dma_start(
                    out=dbg["d_UPT"][c], in_=ctx["stUPT"].ap()), "sd", 16)
        if cfg.debug:
            p.wait("vector", "tm")
            p.inst("vector", lambda E, ctx: E.tensor_copy(
                ctx["zfin"][:, :], ctx["PZ1"][:, :]), "ve")
            p.wait("gpsimd", "ve")
            p.group("gpsimd", "gd")
            p.inst("gpsimd", lambda E, ctx: E.dma_start(
                out=dbg["d_zfin"], in_=ctx["zfin"].ap()), "gd", 16)

    if cfg.stage >= 3:
        p.free("ktm", "vtm", "beta", "nbeta", "betab", "Tt", "pw",
               "Nlow", "Nup", "chR", "mbB", "zbd", "vg", "rh", "Ush",
               "stUB", "stUPT", "stZPT", "brow", "c_oneh16")
        p.phase("P3")
        sb("gA", (64, 8 * DM), F32)
        sb("gB", (64, 8 * DM), F32)
        sb("zcore", (64, H * 64))
        sb("pj", (64, DM)); sb("bj", (64, DM), F32)
        sb("c_pmask", (64, 8), F32)
        sb("c_imask", (64, 8 * 64), F32)
        NC_ = cfg.n_cores
        p.group("gpsimd", "gd")
        p.inst("gpsimd", lambda E, ctx: E.dma_start(out=ctx["c_pmask"].ap(), in_=pmask), "gd", 16)
        p.inst("gpsimd", lambda E, ctx: E.dma_start(out=ctx["c_imask"].ap(), in_=imask), "gd", 16)
        p.wait("vector", "gd")
        p.wait("vector", "tm")
        p.inst("vector", lambda E, ctx: E.tensor_copy(
            ctx["zfin"][:, :], ctx["PZ1"][:, :]), "ve")
        p.wait("scalar", "tm")
        p.inst("scalar", lambda E, ctx: E.copy(
            ctx["ptfin"][0:64, :], ctx["PZ2"][0:64, :]), "sc")
        p.wait("sync", "ve")
        p.wait("sync", "sc")
        p.group("sync", "sd")
        p.inst("sync", lambda E, ctx: E.dma_start(
            out=agi[0:128, :], in_=ctx["zfin"].ap()), "sd", 16)
        p.inst("sync", lambda E, ctx: E.dma_start(
            out=agi[128:192, :], in_=ctx["ptfin"].ap()), "sd", 16)
        p.wait("gpsimd", "sd")
        p.inst("gpsimd", lambda E, ctx: E.collective_compute(
            "AllGather", ALU.bypass, replica_groups=[list(range(NC_))],
            ins=[agi], outs=[ago]), "cc")
        p.wait("gpsimd", "cc")
        p.group("gpsimd", "gd")
        for j in range(NC_):
            p.inst("gpsimd", lambda E, ctx, j=j: E.dma_start(
                out=ctx["gB"][0:64, j * 1024:(j + 1) * 1024],
                in_=ago[j * 192 + 128: j * 192 + 192, :]), "gd", 16)
            p.inst("gpsimd", lambda E, ctx, j=j: E.dma_start(
                out=ctx["gA"][0:64, j * 1024:(j + 1) * 1024]
                .rearrange("p (r x) -> p r x", x=128)[:, :, 0:64],
                in_=ago[j * 192: j * 192 + 64, :]
                .rearrange("p (r x) -> p r x", r=8)[:, :, 64:128]), "gd", 16)
            p.inst("gpsimd", lambda E, ctx, j=j: E.dma_start(
                out=ctx["gA"][0:64, j * 1024:(j + 1) * 1024]
                .rearrange("p (r x) -> p r x", x=128)[:, :, 64:128],
                in_=ago[j * 192 + 64: j * 192 + 128, :]
                .rearrange("p (r x) -> p r x", r=8)[:, :, 64:128]), "gd", 16)
        p.wait("vector", "gd")
        p.inst("vector", lambda E, ctx: E.memset(ctx["zcore"][:, :], 0.0), "ve")
        for j in range(NC_):
            p.inst("vector", lambda E, ctx, j=j: E.scalar_tensor_tensor(
                ctx["pj"][0:64, :].rearrange("p (h d) -> p h d", d=64),
                ctx["gB"][0:64, j * 1024:(j + 1) * 1024].rearrange("p (h d) -> p h d", d=64),
                ctx["c_pmask"][0:64, j:j + 1],
                ctx["c_imask"][0:64, j * 64:(j + 1) * 64]
                .rearrange("p (o d) -> p o d", o=1).broadcast_to([64, 16, 64]),
                ALU.mult, ALU.add), "ve")
            p.inst("vector", lambda E, ctx, j=j: E.tensor_scalar_mul(
                ctx["bj"][0:64, :], ctx["gA"][0:64, j * 1024:(j + 1) * 1024],
                ctx["c_pmask"][0:64, j:j + 1]), "ve")
            p.wait("tensor", "ve")
            for h in range(H):
                p.inst("tensor", lambda E, ctx, h=h: E.matmul(
                    ctx["PZ1"][0:64, h * 64:(h + 1) * 64],
                    ctx["pj"][0:64, h * 64:(h + 1) * 64],
                    ctx["zcore"][0:64, h * 64:(h + 1) * 64],
                    start=True, stop=True, skip_group_check=True),
                    "tm" if h == H - 1 else None)
            p.wait("vector", "tm")
            p.inst("vector", lambda E, ctx: E.scalar_tensor_tensor(
                ctx["zcore"][0:64, :], ctx["PZ1"][0:64, 0:1024], 1.0,
                ctx["bj"][0:64, :], ALU.mult, ALU.add), "ve")
        if cfg.debug:
            p.wait("gpsimd", "ve")
            p.group("gpsimd", "gd")
            p.inst("gpsimd", lambda E, ctx: E.dma_start(
                out=dbg["d_zcore"], in_=ctx["zcore"].ap()), "gd", 16)

    if cfg.stage >= 4:
        p.free("zfin", "ptfin", "gA", "gB", "pj", "bj", "c_pmask", "c_imask")
        p.phase("P4")
        sb("ofm", (128, DCH * T))
        sb("ldUB", (128, 2 * H * 64))
        sb("ldUPT", (64, 2 * H * 128))
        sb("ldZPT", (64, 2 * H * 64))
        sb("ldB", (128, 2 * (H // 2) * 64))
        sb("z0pad", (128, H * 64))
        sb("urel", (128, 128))
        sb("ltp", (128, 256))
        p.wait("vector", "cc")
        p.inst("vector", lambda E, ctx: E.memset(ctx["z0pad"][:, :], 0.0), "ve")
        for c in range(NCH):
            buf = c % 2
            p.wait("sync", "ve")
            p.group("sync", "sd")
            p.inst("sync", lambda E, ctx, c=c, buf=buf: E.dma_start(
                out=ctx["ldUB"][:, buf * H * 64:(buf + 1) * H * 64], in_=cUB[c]), "sd", 16)
            p.inst("sync", lambda E, ctx, c=c, buf=buf: E.dma_start(
                out=ctx["ldUPT"][0:64, buf * H * 128:(buf + 1) * H * 128], in_=cUPT[c]), "sd", 16)
            p.inst("sync", lambda E, ctx, c=c, buf=buf: E.dma_start(
                out=ctx["ldZPT"][0:64, buf * H * 64:(buf + 1) * H * 64], in_=cZPT[c]), "sd", 16)
            p.inst("sync", lambda E, ctx, c=c, buf=buf: E.dma_start(
                out=ctx["ldB"][:, buf * 512:(buf + 1) * 512], in_=cB[c]), "sd", 16)
            p.wait("tensor", "sd")
            oUB = buf * H * 64
            oUPT = buf * H * 128
            oZPT = buf * H * 64
            oB = buf * 512
            for pr in range(8):
                ccols = slice(pr * T + c * 128, pr * T + (c + 1) * 128)
                p.wait("tensor", "ve")
                for j in range(2):
                    h = 2 * pr + j
                    hb = j * 64
                    p.inst("tensor", lambda E, ctx, j=j, h=h, hb=hb, pr=pr, c=c: E.matmul(
                        ctx["PA"][:, j * 128:(j + 1) * 128],
                        ctx["kfm"][:, pr * T + c * 128: pr * T + (c + 1) * 128][hb:hb + 64, :],
                        ctx["qfm"][:, pr * T + c * 128: pr * T + (c + 1) * 128][hb:hb + 64, :],
                        start=True, stop=True, tile_position=(hb, 0),
                        skip_group_check=True))
                    p.inst("tensor", lambda E, ctx, j=j, h=h, oUPT=oUPT: E.matmul(
                        ctx["PB"][:, j * 64:(j + 1) * 64],
                        ctx["ldUPT"][0:64, oUPT + h * 128: oUPT + (h + 1) * 128],
                        ctx["zcore"][0:64, h * 64:(h + 1) * 64],
                        start=True, stop=True, skip_group_check=True))
                    p.inst("tensor", lambda E, ctx, j=j, h=h, hb=hb, oZPT=oZPT: E.matmul(
                        ctx["PB"][hb:hb + 64, 128 + j * 64: 128 + (j + 1) * 64],
                        ctx["ldZPT"][0:64, oZPT + h * 64: oZPT + (h + 1) * 64],
                        ctx["zcore"][0:64, h * 64:(h + 1) * 64],
                        start=True, stop=True, tile_position=(0, hb),
                        skip_group_check=True), "tm" if j == 1 else None)
                p.wait("vector", "tm")
                for j in range(2):
                    h = 2 * pr + j
                    hb = j * 64
                    p.inst("vector", lambda E, ctx, j=j: E.scalar_tensor_tensor(
                        ctx["ltp"][:, j * 128:(j + 1) * 128],
                        ctx["PA"][:, j * 128:(j + 1) * 128], 1.0,
                        ctx["c_maskui"][:, :], ALU.mult, ALU.mult), "ve")
                    p.inst("vector", lambda E, ctx, j=j, h=h, oUB=oUB: E.scalar_tensor_tensor(
                        ctx["urel"][:, j * 64:(j + 1) * 64],
                        ctx["PB"][:, j * 64:(j + 1) * 64], 1.0,
                        ctx["ldUB"][:, oUB + h * 64: oUB + (h + 1) * 64],
                        ALU.mult, ALU.add), "ve")
                    p.inst("vector", lambda E, ctx, j=j, h=h, hb=hb, pr=pr, oB=oB: E.scalar_tensor_tensor(
                        ctx["z0pad"][hb:hb + 64, h * 64:(h + 1) * 64],
                        ctx["PB"][hb:hb + 64, 128 + j * 64: 128 + (j + 1) * 64], 1.0,
                        ctx["ldB"][hb:hb + 64, oB + pr * 64: oB + (pr + 1) * 64],
                        ALU.mult, ALU.add), "ve")
                p.wait("tensor", "ve")
                for j in range(2):
                    h = 2 * pr + j
                    hb = j * 64
                    p.inst("tensor", lambda E, ctx, j=j, h=h, hb=hb, pr=pr, c=c: E.matmul(
                        ctx["PC"][hb:hb + 64, 0:128],
                        ctx["z0pad"][:, h * 64:(h + 1) * 64],
                        ctx["qfm"][:, pr * T + c * 128: pr * T + (c + 1) * 128],
                        start=True, stop=False, tile_position=(0, hb),
                        skip_group_check=True))
                    p.inst("tensor", lambda E, ctx, j=j, hb=hb: E.matmul(
                        ctx["PC"][hb:hb + 64, 0:128],
                        ctx["urel"][:, j * 64:(j + 1) * 64],
                        ctx["ltp"][:, j * 128:(j + 1) * 128],
                        start=False, stop=(j == 1), tile_position=(0, hb),
                        skip_group_check=True), "tm" if j == 1 else None)
                p.wait("vector", "tm")
                p.inst("vector", lambda E, ctx, pr=pr, c=c: E.tensor_copy(
                    ctx["ofm"][:, pr * T + c * 128: pr * T + (c + 1) * 128],
                    ctx["PC"][:, 0:128]), "ve")
        if cfg.debug:
            p.wait("gpsimd", "ve")
            p.group("gpsimd", "gd")
            p.inst("gpsimd", lambda E, ctx: E.dma_start(
                out=dbg["d_ofm"], in_=ctx["ofm"].ap()), "gd", 16)

    if cfg.stage >= 5:
        p.free("kfm", "qfm", "zcore", "ldUB", "ldUPT", "ldZPT", "ldB",
               "z0pad", "urel", "ltp")
        p.phase("P5")
        sb("wt", (128, DCH * DM))
        sb("anorm", (128, DCH * T))
        sb("bcol", (128, DCH), F32)
        p.wait("gpsimd", "tm")
        p.group("gpsimd", "gd")
        p.inst("gpsimd", lambda E, ctx: E.dma_start(out=ctx["wt"][:, :], in_=wo), "gd", 16)
        p.inst("gpsimd", lambda E, ctx: E.dma_start(out=ctx["bcol"][:, 0:DCH], in_=bo), "gd", 16)
        p.wait("tensor", "gd")
        p.wait("scalar", "gd")
        proj_fm("anorm", src="ofm", bias="bcol")

    if cfg.stage >= 5:
        p.free("ofm", "wt")
        p.phase("P6")
        sb("xf", (128, DCH * T), F32)
        sb("y2", (128, DCH * T))
        sb("bc1", (128, T)); sb("bc2", (128, T))
        sb("sq", (128, 4096))
        sb("x2c", (128, 4096))
        sb("st2", (1, 2 * T), BF16)
        p.group("gpsimd", "gd")
        p.inst("gpsimd", lambda E, ctx: E.dma_start(
            out=ctx["xf"].ap().rearrange("p (c t) -> p c t", c=DCH),
            in_=x_in.rearrange("(c p) t -> p c t", p=128)), "gd", 16)
        for e in ("tensor", "vector", "scalar"):
            p.wait(e, "gd")
        ln_feature_major("anorm", "anorm")
        for nt in range(NT):
            p.wait("scalar", "tm")
            p.wait("vector", "tm")
            for c in range(DCH):
                reg = c * 512
                p.inst("vector", lambda E, ctx, c=c, nt=nt, reg=reg: E.tensor_add(
                    ctx["x2c"][:, reg:reg + 512],
                    ctx["xf"][:, c * T + nt * 512: c * T + (nt + 1) * 512],
                    ctx["anorm"][:, c * T + nt * 512: c * T + (nt + 1) * 512]), "ve")
                p.wait("scalar", "ve")
                p.inst("scalar", lambda E, ctx, c=c, reg=reg: E.activation(
                    ctx["sq"][:, reg:reg + 512], ctx["x2c"][:, reg:reg + 512],
                    AF.Square), "sc")
                p.wait("tensor", "sc")
                p.inst("tensor", lambda E, ctx, c=c, reg=reg: E.matmul(
                    ctx["PA"][0:1, :], ctx["c_onesc"][:, :],
                    ctx["x2c"][:, reg:reg + 512],
                    start=(c == 0), stop=(c == DCH - 1), skip_group_check=True))
                p.inst("tensor", lambda E, ctx, c=c, reg=reg: E.matmul(
                    ctx["PB"][0:1, :], ctx["c_onesc"][:, :],
                    ctx["sq"][:, reg:reg + 512],
                    start=(c == 0), stop=(c == DCH - 1), skip_group_check=True),
                    "tm" if c == DCH - 1 else None)
            p.wait("vector", "tm")
            p.inst("vector", lambda E, ctx: E.tensor_copy(
                ctx["lntmp"][0:1, 0:512], ctx["PA"][0:1, :]), "ve")
            p.inst("vector", lambda E, ctx: E.tensor_copy(
                ctx["lntmp"][0:1, 512:1024], ctx["PB"][0:1, :]), "ve")
            p.wait("tensor", "ve")
            p.wait("scalar", "ve")
            ln_stats_math(nt, save_rows="st2")
            ln_bcast(nt)
        for c in range(DCH):
            p.inst("vector", lambda E, ctx, c=c: E.tensor_add(
                ctx["x2c"][:, 0:T], ctx["xf"][:, c * T:(c + 1) * T],
                ctx["anorm"][:, c * T:(c + 1) * T]), "ve")
            p.inst("vector", lambda E, ctx, c=c: E.tensor_mul(
                ctx["y2"][:, c * T:(c + 1) * T], ctx["x2c"][:, 0:T],
                ctx["bc1"][:, :]), "ve")
            p.inst("vector", lambda E, ctx, c=c: E.tensor_add(
                ctx["y2"][:, c * T:(c + 1) * T], ctx["y2"][:, c * T:(c + 1) * T],
                ctx["bc2"][:, :]), "ve")
        if cfg.debug and cfg.stage >= 5:
            p.wait("gpsimd", "ve")
            p.group("gpsimd", "gd")
            p.inst("gpsimd", lambda E, ctx: E.dma_start(
                out=dbg["d_y2"], in_=ctx["y2"].ap()), "gd", 16)

    if cfg.stage >= 9:
        p.free("anorm", "xf", "bc1", "bc2", "sq", "x2c", "bcol", "lnr", "lntmp",
               "c_maskl", "c_masku", "c_maskui", "c_ident", "c_e64", "czero")
        p.phase("P7")
        NT3 = TFh // 512
        sb("hA", (128, FCH * TFh))
        sb("hp", (128, 2 * FCH * 128))
        sb("fsc", (128, 4 * FCH + DCH), F32)
        p.group("gpsimd", "gd")
        for i_, src_ in enumerate([w1s, w2s, h1s, h2s]):
            p.inst("gpsimd", lambda E, ctx, i_=i_, src_=src_: E.dma_start(
                out=ctx["fsc"][:, i_ * FCH:(i_ + 1) * FCH], in_=src_), "gd", 16)
        p.inst("gpsimd", lambda E, ctx: E.dma_start(
            out=ctx["fsc"][:, 4 * FCH:4 * FCH + DCH], in_=w3s), "gd", 16)
        for e in ("tensor", "vector", "scalar"):
            p.wait(e, "gd")

        for hf in range(2):
            toff = hf * TFh
            pw = f"P7w{hf}"
            p.phase(pw)
            sb("hB", (128, FCH * TFh))
            sb("silu", (128, 1024))
            sb("wp", (128, 4 * DCH * 128))
            gates = {0: (p.cnt["ve"], p.cnt["sc"]), 1: (p.cnt["ve"], p.cnt["sc"])}
            wgate = {0: p.cnt["tm"], 1: p.cnt["tm"]}
            gd_tot = {}

            def w12_dma(po):
                slot = po % 2
                p.wait("gpsimd", "tm", wgate[slot])
                p.group("gpsimd", "gd")
                p.inst("gpsimd", lambda E, ctx, po=po, slot=slot: E.dma_start(
                    out=ctx["wp"][:, (2 * slot) * DCH * 128:(2 * slot + 1) * DCH * 128],
                    in_=w1p[po]), "gd", 16)
                p.inst("gpsimd", lambda E, ctx, po=po, slot=slot: E.dma_start(
                    out=ctx["wp"][:, (2 * slot + 1) * DCH * 128:(2 * slot + 2) * DCH * 128],
                    in_=w2p[po]), "gd", 16)
                gd_tot[po] = p.cnt["gd"]

            w12_dma(0)
            for po in range(FCH):
                slot = po % 2
                if po + 1 < FCH:
                    w12_dma(po + 1)
                p.wait("tensor", "gd", gd_tot[po])
                for nt in range(NT3):
                    par = (po * NT3 + nt) % 2
                    pzA = "PA" if par == 0 else "PC"
                    pzB = "PB" if par == 0 else "PZ1"
                    p.wait("tensor", "ve", gates[par][0])
                    p.wait("tensor", "sc", gates[par][1])
                    for w_i, pz in ((0, pzA), (1, pzB)):
                        for kc in range(DCH):
                            p.inst("tensor", lambda E, ctx, po=po, nt=nt, kc=kc, slot=slot,
                                   w_i=w_i, pz=pz, toff=toff: E.matmul(
                                ctx[pz][:, 0:512],
                                ctx["wp"][:, (2 * slot + w_i) * DCH * 128 + kc * 128:
                                          (2 * slot + w_i) * DCH * 128 + (kc + 1) * 128],
                                ctx["y2"][:, kc * T + toff + nt * 512: kc * T + toff + (nt + 1) * 512],
                                start=(kc == 0), stop=(kc == DCH - 1),
                                skip_group_check=True), "tm" if kc == DCH - 1 else None)
                    p.wait("scalar", "tm")
                    p.wait("scalar", "ve", gates[par][0])
                    p.inst("scalar", lambda E, ctx, po=po, nt=nt, par=par, pzA=pzA: E.activation(
                        ctx["silu"][:, par * 512:(par + 1) * 512], ctx[pzA][:, 0:512],
                        AF.Silu, scale=ctx["fsc"][:, po:po + 1]), "sc")
                    p.wait("vector", "sc")
                    p.wait("vector", "tm")
                    p.inst("vector", lambda E, ctx, po=po, nt=nt, par=par, pzB=pzB: E.scalar_tensor_tensor(
                        ctx["hA"][:, po * TFh + nt * 512: po * TFh + (nt + 1) * 512],
                        ctx[pzB][:, 0:512], ctx["fsc"][:, FCH + po: FCH + po + 1],
                        ctx["silu"][:, par * 512:(par + 1) * 512],
                        ALU.mult, ALU.mult), "ve")
                    gates[par] = (p.cnt["ve"], p.cnt["sc"])
                wgate[slot] = p.cnt["tm"]
            if cfg.debug and hf == 1:
                p.wait("gpsimd", "ve")
                p.group("gpsimd", "gd")
                p.inst("gpsimd", lambda E, ctx: E.dma_start(
                    out=dbg["d_hffn"], in_=ctx["hA"].ap()), "gd", 16)
            for (wsrc, scoff, srcb, dstb) in ((h1p, 2 * FCH, "hA", "hB"),
                                              (h2p, 3 * FCH, "hB", "hA")):
                p.wait("tensor", "ve")
                p.wait("tensor", "sc")
                gates2 = {0: p.cnt["sc"], 1: p.cnt["sc"]}
                hgate = {0: p.cnt["tm"], 1: p.cnt["tm"]}
                hgd_tot = {}

                def h_dma(po, wsrc=wsrc):
                    slot = po % 2
                    p.wait("gpsimd", "tm", hgate[slot])
                    p.group("gpsimd", "gd")
                    p.inst("gpsimd", lambda E, ctx, po=po, slot=slot, wsrc=wsrc: E.dma_start(
                        out=ctx["hp"][:, slot * FCH * 128:(slot + 1) * FCH * 128],
                        in_=wsrc[po]), "gd", 16)
                    hgd_tot[po] = p.cnt["gd"]

                h_dma(0)
                for po in range(FCH):
                    slot = po % 2
                    if po + 1 < FCH:
                        h_dma(po + 1)
                    p.wait("tensor", "gd", hgd_tot[po])
                    for nt in range(NT3):
                        par = (po * NT3 + nt) % 2
                        pz = "PZ1" if par == 0 else "PZ2"
                        preg = 512 if par == 0 else 0
                        p.wait("tensor", "sc", gates2[par])
                        for kc in range(FCH):
                            p.inst("tensor", lambda E, ctx, po=po, nt=nt, kc=kc, slot=slot,
                                   pz=pz, preg=preg, srcb=srcb: E.matmul(
                                ctx[pz][:, preg:preg + 512],
                                ctx["hp"][:, slot * FCH * 128 + kc * 128: slot * FCH * 128 + (kc + 1) * 128],
                                ctx[srcb][:, kc * TFh + nt * 512: kc * TFh + (nt + 1) * 512],
                                start=(kc == 0), stop=(kc == FCH - 1),
                                skip_group_check=True), "tm" if kc == FCH - 1 else None)
                        p.wait("scalar", "tm")
                        p.inst("scalar", lambda E, ctx, po=po, nt=nt, pz=pz, preg=preg,
                               scoff=scoff, dstb=dstb: E.activation(
                            ctx[dstb][:, po * TFh + nt * 512: po * TFh + (nt + 1) * 512],
                            ctx[pz][:, preg:preg + 512], AF.Gelu,
                            scale=ctx["fsc"][:, scoff + po: scoff + po + 1]), "sc")
                        gates2[par] = p.cnt["sc"]
                    hgate[slot] = p.cnt["tm"]
                if srcb == "hA":
                    pass
            po7 = f"P7o{hf}"
            p.free("hB", "silu", "wp")
            p.phase(po7)
            sb("fout", (128, DCH * TFh))
            sb("lnr", (1, 2 * T), BF16)
            sb("lntmp", (1, 2048), F32)
            sb("sq3", (128, 4096))
            sb("b3r", (128, TFh)); sb("b3n", (128, TFh))
            sb("b2r", (128, TFh)); sb("b2n", (128, TFh))
            sb("t7", (128, 2048))
            sb("ostg", (128, 1024), F32)
            p.wait("tensor", "sc")
            p.wait("tensor", "ve")
            gates3 = {0: p.cnt["ve"], 1: p.cnt["ve"]}
            w3gate = {0: p.cnt["tm"], 1: p.cnt["tm"]}
            wgd_tot = {}

            def w3_dma(po):
                slot = po % 2
                p.wait("gpsimd", "tm", w3gate[slot])
                p.group("gpsimd", "gd")
                p.inst("gpsimd", lambda E, ctx, po=po, slot=slot: E.dma_start(
                    out=ctx["hp"][:, slot * FCH * 128:(slot + 1) * FCH * 128],
                    in_=w3p[po]), "gd", 16)
                wgd_tot[po] = p.cnt["gd"]

            w3_dma(0)
            for po in range(DCH):
                slot = po % 2
                if po + 1 < DCH:
                    w3_dma(po + 1)
                p.wait("tensor", "gd", wgd_tot[po])
                for nt in range(NT3):
                    par = (po * NT3 + nt) % 2
                    pz = "PZ1" if par == 0 else "PZ2"
                    preg = 512 if par == 0 else 0
                    p.wait("tensor", "ve", gates3[par])
                    for kc in range(FCH):
                        p.inst("tensor", lambda E, ctx, po=po, nt=nt, kc=kc, slot=slot,
                               pz=pz, preg=preg: E.matmul(
                            ctx[pz][:, preg:preg + 512],
                            ctx["hp"][:, slot * FCH * 128 + kc * 128: slot * FCH * 128 + (kc + 1) * 128],
                            ctx["hA"][:, kc * TFh + nt * 512: kc * TFh + (nt + 1) * 512],
                            start=(kc == 0), stop=(kc == FCH - 1),
                            skip_group_check=True), "tm" if kc == FCH - 1 else None)
                    p.wait("vector", "tm")
                    p.inst("vector", lambda E, ctx, po=po, nt=nt, pz=pz, preg=preg: E.tensor_scalar_mul(
                        ctx["fout"][:, po * TFh + nt * 512: po * TFh + (nt + 1) * 512],
                        ctx[pz][:, preg:preg + 512],
                        ctx["fsc"][:, 4 * FCH + po: 4 * FCH + po + 1]), "ve")
                    gates3[par] = p.cnt["ve"]
                w3gate[slot] = p.cnt["tm"]
            for nt in range(NT3):
                p.wait("scalar", "tm")
                p.wait("scalar", "ve")
                for c in range(DCH):
                    reg = c * 512
                    p.inst("scalar", lambda E, ctx, c=c, nt=nt, reg=reg: E.activation(
                        ctx["sq3"][:, reg:reg + 512],
                        ctx["fout"][:, c * TFh + nt * 512: c * TFh + (nt + 1) * 512],
                        AF.Square), "sc")
                    p.wait("tensor", "sc")
                    p.wait("tensor", "ve")
                    p.inst("tensor", lambda E, ctx, c=c, nt=nt: E.matmul(
                        ctx["PA"][0:1, :], ctx["c_onesc"][:, :],
                        ctx["fout"][:, c * TFh + nt * 512: c * TFh + (nt + 1) * 512],
                        start=(c == 0), stop=(c == DCH - 1), skip_group_check=True))
                    p.inst("tensor", lambda E, ctx, c=c, nt=nt, reg=reg: E.matmul(
                        ctx["PB"][0:1, :], ctx["c_onesc"][:, :],
                        ctx["sq3"][:, reg:reg + 512],
                        start=(c == 0), stop=(c == DCH - 1), skip_group_check=True),
                        "tm" if c == DCH - 1 else None)
                p.wait("vector", "tm")
                p.inst("vector", lambda E, ctx: E.tensor_copy(
                    ctx["lntmp"][0:1, 0:512], ctx["PA"][0:1, :]), "ve")
                p.inst("vector", lambda E, ctx: E.tensor_copy(
                    ctx["lntmp"][0:1, 512:1024], ctx["PB"][0:1, :]), "ve")
                p.wait("tensor", "ve")
                p.wait("scalar", "ve")
                ln_stats_math(nt)
                gnt = hf * NT3 + nt
                p.wait("tensor", "ve")
                p.wait("tensor", "sc")
                p.inst("tensor", lambda E, ctx, nt=nt: E.matmul(
                    ctx["PA"][:, :], ctx["c_ones1b"][:, :],
                    ctx["lnr"][0:1, nt * 512:(nt + 1) * 512], start=True, stop=True,
                    skip_group_check=True))
                p.inst("tensor", lambda E, ctx, nt=nt: E.matmul(
                    ctx["PB"][:, :], ctx["c_ones1b"][:, :],
                    ctx["lnr"][0:1, T + nt * 512: T + (nt + 1) * 512], start=True,
                    stop=True, skip_group_check=True))
                p.inst("tensor", lambda E, ctx, gnt=gnt: E.matmul(
                    ctx["PC"][:, :], ctx["c_ones1b"][:, :],
                    ctx["st2"][0:1, T + gnt * 512: T + (gnt + 1) * 512], start=True,
                    stop=True, skip_group_check=True))
                p.inst("tensor", lambda E, ctx, gnt=gnt: E.matmul(
                    ctx["PZ1"][:, 0:512], ctx["c_ones1b"][:, :],
                    ctx["st2"][0:1, gnt * 512:(gnt + 1) * 512], start=True,
                    stop=True, skip_group_check=True), "tm")
                p.wait("vector", "tm")
                p.inst("vector", lambda E, ctx, nt=nt: E.tensor_copy(
                    ctx["b3r"][:, nt * 512:(nt + 1) * 512], ctx["PA"][:, :]), "ve")
                p.inst("vector", lambda E, ctx, nt=nt: E.tensor_copy(
                    ctx["b3n"][:, nt * 512:(nt + 1) * 512], ctx["PB"][:, :]), "ve")
                p.inst("vector", lambda E, ctx, nt=nt: E.tensor_copy(
                    ctx["b2r"][:, nt * 512:(nt + 1) * 512], ctx["PC"][:, :]), "ve")
                p.inst("vector", lambda E, ctx, nt=nt: E.tensor_copy(
                    ctx["b2n"][:, nt * 512:(nt + 1) * 512], ctx["PZ1"][:, 0:512]), "ve")
                p.wait("tensor", "ve")
            sgate = {0: 0, 1: 0}
            for c in range(DCH):
                for nt in range(NT3):
                    par = (c * NT3 + nt) % 2
                    treg = par * 512
                    p.wait("vector", "sd", sgate[par])
                    ta, tb = par * 1024, par * 1024 + 512
                    p.inst("vector", lambda E, ctx, c=c, nt=nt, toff=toff, ta=ta: E.tensor_mul(
                        ctx["t7"][:, ta:ta + 512],
                        ctx["y2"][:, c * T + toff + nt * 512: c * T + toff + (nt + 1) * 512],
                        ctx["b2r"][:, nt * 512:(nt + 1) * 512]), "ve")
                    p.inst("vector", lambda E, ctx, nt=nt, ta=ta: E.tensor_add(
                        ctx["t7"][:, ta:ta + 512], ctx["t7"][:, ta:ta + 512],
                        ctx["b2n"][:, nt * 512:(nt + 1) * 512]), "ve")
                    p.inst("vector", lambda E, ctx, c=c, nt=nt, tb=tb: E.tensor_mul(
                        ctx["t7"][:, tb:tb + 512],
                        ctx["fout"][:, c * TFh + nt * 512: c * TFh + (nt + 1) * 512],
                        ctx["b3r"][:, nt * 512:(nt + 1) * 512]), "ve")
                    p.inst("vector", lambda E, ctx, nt=nt, tb=tb: E.tensor_add(
                        ctx["t7"][:, tb:tb + 512], ctx["t7"][:, tb:tb + 512],
                        ctx["b3n"][:, nt * 512:(nt + 1) * 512]), "ve")
                    p.inst("vector", lambda E, ctx, ta=ta, tb=tb, treg=treg: E.tensor_add(
                        ctx["ostg"][:, treg:treg + 512], ctx["t7"][:, ta:ta + 512],
                        ctx["t7"][:, tb:tb + 512]), "ve")
                    p.wait("sync", "ve")
                    p.group("sync", "sd")
                    p.inst("sync", lambda E, ctx, c=c, nt=nt, toff=toff, treg=treg: E.dma_start(
                        out=out.rearrange("(c p) t -> p c t", p=128)[:, c,
                            toff + nt * 512: toff + (nt + 1) * 512],
                        in_=ctx["ostg"][:, treg:treg + 512]), "sd", 16)
                    sgate[par] = p.cnt["sd"]
            p.free("fout", "sq3", "b3r", "b3n", "b2r", "b2n", "t7", "ostg",
                   "lnr", "lntmp")

    if cfg.stage < 5:
        phsrc = "y1" if cfg.stage < 2 else "qfm"
        for e in ("ve", "sc", "tm"):
            p.wait("gpsimd", e)
        p.group("gpsimd", "gd")
        p.inst("gpsimd", lambda E, ctx, phsrc=phsrc: E.dma_start(out=out_pct, in_=ctx[phsrc].ap().rearrange("p (c t) -> p c t", c=DCH)), "gd", 16)
    p.wait("sync", "sd", p.cnt["sd"])
    p.wait("sync", "gd", p.cnt["gd"])
    p.wait("gpsimd", "gd", p.cnt["gd"])

    p.close()
    import contextlib
    nph = len(p.phases)

    reqs = []
    for pi, ph in enumerate(p.phases):
        for (name, kind, shape, dt, side) in p.allocs[ph]:
            if kind != "sbuf":
                continue
            fidx = nph - 1
            for (fpi, fname) in p.frees:
                if fname == name and fpi >= pi:
                    fidx = fpi
                    break
            size = shape[1] * (4 if dt == F32 else 2)
            reqs.append([pi, ph, name, shape, dt, size, fidx])
    base, top = (nc.sbuf_base + 127) // 128 * 128, nc.sbuf_top
    placed = []
    offs = {}
    for r in sorted(reqs, key=lambda r: (0 if (r[0] == 0 and r[6] == nph - 1) else 1, -r[5])):
        pi, ph, name, shape, dt, size, fidx = r
        cands = [base]
        for (o2, s2, p2, f2) in placed:
            if not (f2 < pi or p2 > fidx):
                cands.append((o2 + s2 + 127) // 128 * 128)
        ok_off = None
        for cand in sorted(cands):
            if cand + size > top:
                continue
            conflict = False
            for (o2, s2, p2, f2) in placed:
                if f2 < pi or p2 > fidx:
                    continue
                if not (cand + size <= o2 or o2 + s2 <= cand):
                    conflict = True
                    break
            if not conflict:
                ok_off = cand
                break
        if ok_off is None:
            occ = [sum(r2[5] + 127 for r2 in reqs if r2[0] <= qi <= r2[6]) for qi in range(nph)]
            raise MemoryError(f"SBUF plan OOM: {name} size {size} phase {ph}; cap {top-base}; per-phase {occ}")
        placed.append((ok_off, size, pi, fidx))
        offs[(pi, name)] = ok_off

    with contextlib.ExitStack() as outer:
        ctx = {}
        for sname in p.sem_names:
            ctx[sname] = outer.enter_context(nc.semaphore(sname))
        for pi, ph in enumerate(p.phases):
            for (name, kind, shape, dt, side) in p.allocs[ph]:
                uname = f"{name}__{ph}"
                if kind == "sbuf":
                    ctx[name] = nc.alloc_sbuf_tensor_at(
                        uname, shape, dt, offset=offs[(pi, name)])
                else:
                    ctx[name] = outer.enter_context(nc.psum_tensor(uname, shape, dt))
            with nc.Block() as block:
                for eng in ENGS:
                    a, b = p.seg[ph][eng]
                    steps = p.steps[eng][a:b]
                    if steps:
                        getattr(block, eng)(lambda E, steps=steps: [f(E, ctx) for f in steps])
    return nc, dbg

def prep_in_maps(cfg: CFG, inp):
    T = cfg.tpc
    x = np.asarray(inp["L0_attn_x"], np.float32)[0]
    xT = np.ascontiguousarray(x.T)
    con = build_consts()
    shared = {
        "wq": _retile_w(np.asarray(inp["L0_attn_Wq"], np.float32)),
        "bq": _cvec(np.asarray(inp["L0_attn_bq"], np.float32)),
        "wk": _retile_w(np.asarray(inp["L0_attn_Wk"], np.float32)),
        "bk": np.asarray(inp["L0_attn_bk"], np.float32).reshape(1, -1),
        "wv": _retile_w(np.asarray(inp["L0_attn_Wv"], np.float32)),
        "bv": np.asarray(inp["L0_attn_bv"], np.float32).reshape(1, -1),
        "wb": _retile_w(np.asarray(inp["L0_attn_Wbeta"], np.float32)),
        "bb": np.asarray(inp["L0_attn_bbeta"], np.float32).reshape(1, -1),
        "wo": _retile_w(np.asarray(inp["L0_attn_Wo"], np.float32)),
        "bo": _cvec(np.asarray(inp["L0_attn_bo"], np.float32)),
        "w1p": _panels(np.ascontiguousarray(np.asarray(inp["L0_ffn_w1_w"], np.float32).T)),
        "w1s": _cvec(np.asarray(inp["L0_ffn_w1_s"], np.float32)),
        "w2p": _panels(np.ascontiguousarray(np.asarray(inp["L0_ffn_w2_w"], np.float32).T)),
        "w2s": _cvec(np.asarray(inp["L0_ffn_w2_s"], np.float32)),
        "h1p": _panels(np.ascontiguousarray(np.asarray(inp["L0_ffn_h1_w"], np.float32).T)),
        "h1s": _cvec(np.asarray(inp["L0_ffn_h1_s"], np.float32)),
        "h2p": _panels(np.ascontiguousarray(np.asarray(inp["L0_ffn_h2_w"], np.float32).T)),
        "h2s": _cvec(np.asarray(inp["L0_ffn_h2_s"], np.float32)),
        "w3p": _panels(np.ascontiguousarray(np.asarray(inp["L0_ffn_w3_w"], np.float32).T)),
        "w3s": _cvec(np.asarray(inp["L0_ffn_w3_s"], np.float32)),
    }
    shared.update({k: np.ascontiguousarray(v) for k, v in con.items()})
    in_maps = []
    for ci in range(cfg.n_cores):
        m = dict(shared)
        m["x"] = np.ascontiguousarray(xT[:, ci * T:(ci + 1) * T])
        pm = np.zeros((64, 8), np.float32)
        pm[:, :ci] = 1.0
        m["pmask"] = pm
        im = np.zeros((64, 8 * 64), np.float32)
        for j in range(8):
            if j >= ci:
                im[:, j * 64:(j + 1) * 64] = np.eye(64, dtype=np.float32)
        m["imask"] = im
        in_maps.append(m)
    return in_maps


def kernel(**inputs):
    cfg = CFG()
    in_maps = prep_in_maps(cfg, inputs)
    nc, _ = build_nc(cfg)
    res = run_bass_kernel_spmd(nc, in_maps, core_ids=list(range(cfg.n_cores)))
    outs = [r["out"] for r in res.results]
    full = np.concatenate(outs, axis=1)
    return np.ascontiguousarray(full.T)[None].astype(np.float32)


# version 08
# speedup vs baseline: 1.2740x; 1.0225x over previous
from dataclasses import dataclass

import numpy as np
import concourse.bass as bass
import concourse.mybir as mybir
from concourse.bass_utils import run_bass_kernel_spmd

F32 = mybir.dt.float32
BF16 = mybir.dt.bfloat16
AF = mybir.ActivationFunctionType
ALU = mybir.AluOpType
AXX = mybir.AxisListType.X

NCORES = 8
DM = 1024
DCH = DM // 128
H = 16
D = 64
C = 128
FH = 4096
FCH = FH // 128
ENGS = ("tensor", "vector", "scalar", "gpsimd", "sync")


@dataclass
class CFG:
    tpc: int = 2048
    n_cores: int = NCORES
    stage: int = 9
    debug: bool = False
    p2c: int = 0
    p2s: int = 99


class Prog:
    def __init__(self):
        self.steps = {e: [] for e in ENGS}
        self.cnt = {}
        self.sem_names = []
        self._dmafloor = {}
        self.phases = []
        self.seg = {}
        self._seg_start = {e: 0 for e in ENGS}
        self.allocs = {}
        self.frees = []

    def phase(self, name):
        if self.phases:
            prev = self.phases[-1]
            self.seg[prev] = {e: (self._seg_start[e], len(self.steps[e])) for e in ENGS}
        self._seg_start = {e: len(self.steps[e]) for e in ENGS}
        self.phases.append(name)
        self.allocs[name] = []

    def close(self):
        prev = self.phases[-1]
        self.seg[prev] = {e: (self._seg_start[e], len(self.steps[e])) for e in ENGS}

    def alloc(self, name, kind, shape, dt, side=None):
        self.allocs[self.phases[-1]].append((name, kind, list(shape), dt, side))

    def free(self, *names):
        for n in names:
            self.frees.append((len(self.phases) - 1, n))

    def sem(self, name):
        self.sem_names.append(name)
        self.cnt[name] = 0

    def raw(self, eng, fn):
        self.steps[eng].append(fn)

    def wait(self, eng, sname, val=None):
        v = self.cnt[sname] if val is None else val
        assert v <= self.cnt[sname], (sname, v, self.cnt[sname])
        if v > 0:
            self.raw(eng, lambda E, ctx, s=sname, v=v: E.wait_ge(ctx[s], v))

    def inst(self, eng, fn, sname=None, n=1):
        if sname is None:
            self.raw(eng, fn)
        else:
            self.cnt[sname] += n
            self.raw(eng, lambda E, ctx, f=fn, s=sname, n=n: f(E, ctx).then_inc(ctx[s], n))

    def dma(self, eng, ch, fn):
        self.inst(eng, fn, ch, 16)

    def group(self, eng, ch):
        key = (eng, ch)
        if self.cnt[ch] > self._dmafloor.get(key, 0):
            self.wait(eng, ch, self.cnt[ch])
            self._dmafloor[key] = self.cnt[ch]


def _retile_w(w):
    I, O = w.shape
    return np.ascontiguousarray(
        w.reshape(I // 128, 128, O).transpose(1, 0, 2)).reshape(128, -1)


def _panels(wT, po=128):
    I, O = wT.shape
    npan = O // po
    a = wT.reshape(I // 128, 128, npan, po).transpose(2, 1, 0, 3)
    return np.ascontiguousarray(a).reshape(npan, 128, (I // 128) * po)


def _cvec(v):
    return np.ascontiguousarray(v.reshape(-1, 128).T)


def build_consts():
    ident = np.eye(128, dtype=np.float32)
    e64 = np.zeros((128, 64), np.float32); e64[:64, :64] = np.eye(64)
    maskl = np.tril(np.ones((128, 128), np.float32), -1)
    masku = np.triu(np.ones((128, 128), np.float32), 1)
    maskui = np.triu(np.ones((128, 128), np.float32), 0)
    ones1 = np.ones((1, 128), np.float32)
    onesc = np.ones((128, 1), np.float32)
    oneh16 = np.zeros((16, 16 * 128), np.float32)
    for h in range(16):
        oneh16[h, h * 128:(h + 1) * 128] = 1.0
    oneh4 = np.zeros((4, 4 * 128), np.float32)
    for i in range(4):
        oneh4[i, i * 128:(i + 1) * 128] = 1.0
    return dict(ident=ident, e64=e64, maskl=maskl, masku=masku, maskui=maskui,
                ones1=ones1, onesc=onesc, oneh16=oneh16, oneh4=oneh4)


def build_nc(cfg: CFG):
    T = cfg.tpc
    NCH = T // C
    NT = T // 512
    nc = bass.Bass("TRN2", target_bir_lowering=False, debug=False,
                   detect_race_conditions=False)

    def din(name, shape, dt=F32):
        return nc.dram_tensor(name, list(shape), dt, kind="ExternalInput").ap()

    x_in = din("x", (DM, T))
    wq = din("wq", (128, DCH * DM)); bq = din("bq", (128, DCH))
    wk = din("wk", (128, DCH * DM)); bk = din("bk", (1, DM))
    wv = din("wv", (128, DCH * DM)); bv = din("bv", (1, DM))
    wb = din("wb", (128, DCH * 16)); bb = din("bb", (1, 16))
    wo = din("wo", (128, DCH * DM)); bo = din("bo", (128, DCH))
    w1p = din("w1p", (FCH, 128, DCH * 128)); w1s = din("w1s", (128, FCH))
    w2p = din("w2p", (FCH, 128, DCH * 128)); w2s = din("w2s", (128, FCH))
    h1p = din("h1p", (FCH, 128, FCH * 128)); h1s = din("h1s", (128, FCH))
    h2p = din("h2p", (FCH, 128, FCH * 128)); h2s = din("h2s", (128, FCH))
    w3p = din("w3p", (DCH, 128, FCH * 128)); w3s = din("w3s", (128, DCH))
    ident = din("ident", (128, 128)); e64 = din("e64", (128, 64))
    maskl = din("maskl", (128, 128)); masku = din("masku", (128, 128))
    maskui = din("maskui", (128, 128))
    ones1 = din("ones1", (1, 128)); onesc = din("onesc", (128, 1))
    oneh16 = din("oneh16", (16, 16 * 128)); oneh4 = din("oneh4", (4, 4 * 128))
    pmask = din("pmask", (64, 8)); imask = din("imask", (64, 8 * 64))

    out = nc.dram_tensor("out", [DM, T], F32, kind="ExternalOutput").ap()
    out_pct = out.rearrange("(c p) t -> p c t", p=128)

    cUB = nc.dram_tensor("cUB", [NCH, 128, H * 64], BF16).ap()
    cUPT = nc.dram_tensor("cUPT", [NCH, 64, H * 128], BF16).ap()
    cZPT = nc.dram_tensor("cZPT", [NCH, 64, H * 64], BF16).ap()
    cB = nc.dram_tensor("cB", [NCH, 128, (H // 2) * 64], BF16).ap()
    agi = nc.dram_tensor("agi", [192, 1024], F32).ap()
    ago = nc.dram_tensor("ago", [cfg.n_cores * 192, 1024], F32).ap()

    dbg = {}
    if cfg.debug:
        def dout(name, shape, dt=F32):
            dbg[name] = nc.dram_tensor(name, list(shape), dt, kind="ExternalOutput").ap()
        dout("d_y1", (128, DCH * T))
        dout("d_ktm", (128, NCH * DM))
        dout("d_vtm", (128, NCH * DM))
        dout("d_qfm", (128, DCH * T))
        dout("d_kfm", (128, DCH * T))
        dout("d_beta", (128, NCH * 16))
        if cfg.stage >= 2:
            dout("d_UB", (NCH, 128, H * 64), BF16)
            dout("d_UPT", (NCH, 64, H * 128), BF16)
            dout("d_ZPT", (NCH, 64, H * 64), BF16)
            dout("d_B", (NCH, 128, (H // 2) * 64), BF16)
            dout("d_zfin", (128, (H // 2) * 128))
        if cfg.stage >= 3:
            dout("d_zcore", (64, H * 64), BF16)
        if cfg.stage >= 4:
            dout("d_ofm", (128, DCH * T))
        if cfg.stage >= 5:
            dout("d_y2", (128, DCH * T))
        if cfg.stage >= 9:
            dout("d_hffn", (128, FCH * (T // 2)), BF16)

    p = Prog()
    for s in ["gd", "sd", "tm", "ve", "sc", "cc"]:
        p.sem(s)

    def sb(name, shape, dt=BF16, side=None):
        p.alloc(name, "sbuf", list(shape), dt, side)

    def ps(name, shape, dt=F32):
        p.alloc(name, "psum", list(shape), dt, None)

    NPOW = 6
    TFh = T // 2

    p.phase("P0")
    ps("PA", (128, 512)); ps("PB", (128, 512)); ps("PC", (128, 512))
    ps("PZ1", (128, 1024)); ps("PZ2", (128, 1024))
    ps("PT", (128, 256), BF16)
    sb("c_ident", (128, 128)); sb("c_e64", (128, 64))
    sb("c_maskl", (128, 128)); sb("c_masku", (128, 128)); sb("c_maskui", (128, 128))
    sb("c_ones1", (1, 128), F32); sb("c_onesc", (128, 1), BF16)
    sb("czero", (128, 128))
    sb("c_ones1b", (1, 128), BF16)
    sb("y1", (128, DCH * T))
    sb("lntmp", (1, 2048), F32)
    sb("lnr", (1, 2 * T), BF16)
    sb("bc1", (128, T)); sb("bc2", (128, T))
    sb("sq", (128, 4096))

    def ln_stats_math(nt, save_rows=None):
        p.inst("scalar", lambda E, ctx: E.mul(ctx["lntmp"][0:1, 1024:1536], ctx["lntmp"][0:1, 0:512], np.float32(1.0 / DM)), "sc")
        p.inst("scalar", lambda E, ctx: E.mul(ctx["lntmp"][0:1, 1536:2048], ctx["lntmp"][0:1, 512:1024], np.float32(1.0 / DM)), "sc")
        p.wait("vector", "sc")
        p.inst("vector", lambda E, ctx: E.tensor_mul(
            ctx["lntmp"][0:1, 0:512], ctx["lntmp"][0:1, 1024:1536], ctx["lntmp"][0:1, 1024:1536]), "ve")
        p.inst("vector", lambda E, ctx: E.tensor_sub(
            ctx["lntmp"][0:1, 512:1024], ctx["lntmp"][0:1, 1536:2048], ctx["lntmp"][0:1, 0:512]), "ve")
        p.wait("scalar", "ve")
        p.inst("scalar", lambda E, ctx: E.activation(
            ctx["lntmp"][0:1, 0:512], ctx["lntmp"][0:1, 512:1024], AF.Ln, bias=np.float32(1e-5)), "sc")
        p.inst("scalar", lambda E, ctx, nt=nt: E.activation(
            ctx["lnr"][0:1, nt * 512:(nt + 1) * 512], ctx["lntmp"][0:1, 0:512],
            AF.Exp, scale=np.float32(-0.5)), "sc")
        p.inst("scalar", lambda E, ctx: E.activation(
            ctx["lntmp"][0:1, 512:1024], ctx["lntmp"][0:1, 0:512],
            AF.Exp, scale=np.float32(-0.5)), "sc")
        if save_rows is not None:
            p.inst("scalar", lambda E, ctx, sr=save_rows, nt=nt: E.activation(
                ctx[sr][0:1, T + nt * 512: T + (nt + 1) * 512], ctx["lntmp"][0:1, 0:512],
                AF.Exp, scale=np.float32(0.5)), "sc")
            p.inst("scalar", lambda E, ctx, sr=save_rows, nt=nt: E.copy(
                ctx[sr][0:1, nt * 512:(nt + 1) * 512], ctx["lntmp"][0:1, 1024:1536]), "sc")
        p.wait("vector", "sc")
        p.inst("vector", lambda E, ctx: E.tensor_mul(
            ctx["lntmp"][0:1, 0:512], ctx["lntmp"][0:1, 1024:1536], ctx["lntmp"][0:1, 512:1024]), "ve")
        p.inst("vector", lambda E, ctx, nt=nt: E.tensor_scalar_mul(
            ctx["lnr"][0:1, T + nt * 512: T + (nt + 1) * 512], ctx["lntmp"][0:1, 0:512], -1.0), "ve")

    def ln_bcast(nt):
        p.wait("tensor", "ve")
        p.wait("tensor", "sc")
        p.inst("tensor", lambda E, ctx, nt=nt: E.matmul(
            ctx["PA"][:, :], ctx["c_ones1b"][:, :],
            ctx["lnr"][0:1, nt * 512:(nt + 1) * 512], start=True, stop=True,
            skip_group_check=True))
        p.inst("tensor", lambda E, ctx, nt=nt: E.matmul(
            ctx["PB"][:, :], ctx["c_ones1b"][:, :],
            ctx["lnr"][0:1, T + nt * 512: T + (nt + 1) * 512], start=True, stop=True,
            skip_group_check=True), "tm")
        p.wait("vector", "tm")
        p.inst("vector", lambda E, ctx, nt=nt: E.tensor_copy(
            ctx["bc1"][:, nt * 512:(nt + 1) * 512], ctx["PA"][:, :]), "ve")
        p.inst("vector", lambda E, ctx, nt=nt: E.tensor_copy(
            ctx["bc2"][:, nt * 512:(nt + 1) * 512], ctx["PB"][:, :]), "ve")
        p.wait("tensor", "ve")

    def ln_feature_major(src, dst, save_rows=None):
        for nt in range(NT):
            p.wait("scalar", "tm")
            for c in range(DCH):
                reg = c * 512
                p.inst("scalar", lambda E, ctx, c=c, nt=nt, reg=reg, src=src: E.activation(
                    ctx["sq"][:, reg:reg + 512],
                    ctx[src][:, c * T + nt * 512: c * T + (nt + 1) * 512],
                    AF.Square), "sc")
                p.wait("tensor", "sc")
                p.inst("tensor", lambda E, ctx, c=c, nt=nt, src=src: E.matmul(
                    ctx["PA"][0:1, :], ctx["c_onesc"][:, :],
                    ctx[src][:, c * T + nt * 512: c * T + (nt + 1) * 512],
                    start=(c == 0), stop=(c == DCH - 1), skip_group_check=True))
                p.inst("tensor", lambda E, ctx, c=c, nt=nt, reg=reg: E.matmul(
                    ctx["PB"][0:1, :], ctx["c_onesc"][:, :],
                    ctx["sq"][:, reg:reg + 512],
                    start=(c == 0), stop=(c == DCH - 1), skip_group_check=True),
                    "tm" if c == DCH - 1 else None)
            p.wait("vector", "tm")
            p.inst("vector", lambda E, ctx, nt=nt: E.tensor_copy(
                ctx["lntmp"][0:1, 0:512], ctx["PA"][0:1, :]), "ve")
            p.inst("vector", lambda E, ctx, nt=nt: E.tensor_copy(
                ctx["lntmp"][0:1, 512:1024], ctx["PB"][0:1, :]), "ve")
            p.wait("tensor", "ve")
            p.wait("scalar", "ve")
            ln_stats_math(nt, save_rows)
            ln_bcast(nt)
        for c in range(DCH):
            p.inst("vector", lambda E, ctx, c=c, src=src, dst=dst: E.tensor_mul(
                ctx[dst][:, c * T:(c + 1) * T], ctx[src][:, c * T:(c + 1) * T],
                ctx["bc1"][:, :]), "ve")
            p.inst("vector", lambda E, ctx, c=c, dst=dst: E.tensor_add(
                ctx[dst][:, c * T:(c + 1) * T], ctx[dst][:, c * T:(c + 1) * T],
                ctx["bc2"][:, :]), "ve")

    consts = [(ident, "c_ident"), (e64, "c_e64"), (maskl, "c_maskl"),
              (masku, "c_masku"), (maskui, "c_maskui"), (ones1, "c_ones1"),
              (onesc, "c_onesc"), (ones1, "c_ones1b")]
    for iap, sname in consts:
        p.inst("gpsimd", lambda E, ctx, iap=iap, sname=sname: E.dma_start(
            out=ctx[sname].ap(), in_=iap), "gd", 16)
    p.inst("gpsimd", lambda E, ctx: E.dma_start(out=ctx["y1"].ap().rearrange("p (c t) -> p c t", c=DCH), in_=x_in.rearrange("(c p) t -> p c t", p=128)), "gd", 16)
    p.inst("vector", lambda E, ctx: E.memset(ctx["czero"][:, :], 0.0), "ve")
    ng0 = p.cnt["gd"]
    for e in ("scalar", "vector", "tensor", "sync"):
        p.wait(e, "gd", ng0)

    ln_feature_major("y1", "y1")

    p.free("bc1", "bc2", "sq")
    p.phase("P1a")
    sb("ktm", (128, NCH * DM))
    sb("vtm", (128, NCH * DM))
    sb("kfm", (128, DCH * T))
    sb("beta", (128, NCH * 16), F32)
    sb("nbeta", (128, NCH * 16), F32)
    sb("betab", (128, NCH * 16), BF16)
    sb("wt", (128, DCH * DM))
    sb("bconst", (128, DM), F32)
    sb("bcol", (128, DCH), F32)
    sb("knr", (128, NCH * 16), F32)
    sb("knrb", (128, NCH * 16), BF16)
    sb("sqk", (128, DM))

    def bias_broadcast(bias_ap, width):
        p.wait("sync", "ve")
        p.wait("sync", "tm")
        p.group("sync", "sd")
        p.inst("sync", lambda E, ctx, b=bias_ap, w=width: E.dma_start(
            out=ctx["lntmp"][0:1, 0:w], in_=b), "sd", 16)
        p.wait("tensor", "sd")
        ntw = (width + 511) // 512
        for nt in range(ntw):
            w0, w1 = nt * 512, min(width, (nt + 1) * 512)
            p.inst("tensor", lambda E, ctx, w0=w0, w1=w1: E.matmul(
                ctx["PA"][:, 0:w1 - w0], ctx["c_ones1"][:, :],
                ctx["lntmp"][0:1, w0:w1], start=True, stop=True, skip_group_check=True), "tm")
            p.wait("vector", "tm")
            p.inst("vector", lambda E, ctx, w0=w0, w1=w1: E.tensor_copy(
                ctx["bconst"][:, w0:w1], ctx["PA"][:, 0:w1 - w0]), "ve")
            p.wait("tensor", "ve")

    def load_w(src_ap, width=DCH * DM):
        p.wait("gpsimd", "tm")
        p.group("gpsimd", "gd")
        p.inst("gpsimd", lambda E, ctx, s=src_ap, w=width: E.dma_start(
            out=ctx["wt"][:, 0:w], in_=s), "gd", 16)

    def proj_tm(dst):
        for tc in range(NCH):
            for nt in range(2):
                pp = "PA" if (tc * 2 + nt) % 2 == 0 else "PB"
                for c in range(DCH):
                    p.inst("tensor", lambda E, ctx, tc=tc, nt=nt, c=c, pp=pp: E.matmul(
                        ctx[pp][:, :],
                        ctx["y1"][:, c * T + tc * 128: c * T + (tc + 1) * 128],
                        ctx["wt"][:, c * DM + nt * 512: c * DM + (nt + 1) * 512],
                        start=(c == 0), stop=(c == DCH - 1), skip_group_check=True),
                        "tm" if c == DCH - 1 else None)
                p.wait("vector", "tm")
                p.inst("vector", lambda E, ctx, dst=dst, tc=tc, nt=nt, pp=pp:
                       E.scalar_tensor_tensor(
                           ctx[dst][:, tc * DM + nt * 512: tc * DM + (nt + 1) * 512],
                           ctx[pp][:, :], 1.0,
                           ctx["bconst"][:, nt * 512:(nt + 1) * 512],
                           ALU.mult, ALU.add), "ve")
                if (tc * 2 + nt) % 2 == 1:
                    p.wait("tensor", "ve")

    def proj_fm(dst, src="y1", bias="bcol"):
        for oc in range(DCH):
            for nt in range(NT):
                pp = "PA" if (oc * NT + nt) % 2 == 0 else "PB"
                for c in range(DCH):
                    p.inst("tensor", lambda E, ctx, oc=oc, nt=nt, c=c, pp=pp, src=src: E.matmul(
                        ctx[pp][:, :],
                        ctx["wt"][:, c * DM + oc * 128: c * DM + (oc + 1) * 128],
                        ctx[src][:, c * T + nt * 512: c * T + (nt + 1) * 512],
                        start=(c == 0), stop=(c == DCH - 1), skip_group_check=True),
                        "tm" if c == DCH - 1 else None)
                p.wait("scalar", "tm")
                p.inst("scalar", lambda E, ctx, dst=dst, oc=oc, nt=nt, pp=pp, bias=bias:
                       E.activation(
                           ctx[dst][:, oc * T + nt * 512: oc * T + (nt + 1) * 512],
                           ctx[pp][:, :], AF.Identity,
                           bias=ctx[bias][:, oc:oc + 1]), "sc")
                if (oc * NT + nt) % 2 == 1:
                    p.wait("tensor", "sc")

    load_w(wk)
    bias_broadcast(bk, DM)
    p.wait("tensor", "gd")
    proj_tm("ktm")
    load_w(wv)
    bias_broadcast(bv, DM)
    p.wait("tensor", "gd")
    p.wait("vector", "gd")
    proj_tm("vtm")
    p.wait("gpsimd", "tm")
    p.group("gpsimd", "gd")
    p.inst("gpsimd", lambda E, ctx: E.dma_start(out=ctx["wt"][:, 0:DCH * 16], in_=wb), "gd", 16)
    bias_broadcast(bb, 16)
    p.wait("tensor", "gd")
    for tc in range(NCH):
        pp = "PA" if tc % 2 == 0 else "PB"
        for c in range(DCH):
            p.inst("tensor", lambda E, ctx, tc=tc, c=c, pp=pp: E.matmul(
                ctx[pp][:, 0:16],
                ctx["y1"][:, c * T + tc * 128: c * T + (tc + 1) * 128],
                ctx["wt"][:, c * 16:(c + 1) * 16],
                start=(c == 0), stop=(c == DCH - 1), skip_group_check=True),
                "tm" if c == DCH - 1 else None)
        p.wait("vector", "tm")
        p.inst("vector", lambda E, ctx, tc=tc, pp=pp: E.tensor_add(
            ctx["beta"][:, tc * 16:(tc + 1) * 16], ctx[pp][:, 0:16],
            ctx["bconst"][:, 0:16]), "ve")
        if tc % 2 == 1:
            p.wait("tensor", "ve")
    p.wait("scalar", "ve")
    p.inst("scalar", lambda E, ctx: E.activation(
        ctx["beta"][:, :], ctx["beta"][:, :], AF.Sigmoid), "sc")
    p.inst("scalar", lambda E, ctx: E.activation(
        ctx["nbeta"][:, :], ctx["beta"][:, :], AF.Identity, scale=np.float32(-1.0)), "sc")
    p.inst("scalar", lambda E, ctx: E.copy(ctx["betab"][:, :], ctx["beta"][:, :]), "sc")
    for tc in range(NCH):
        p.wait("scalar", "ve")
        p.wait("scalar", "sc")
        p.inst("scalar", lambda E, ctx, tc=tc: E.activation(
            ctx["sqk"][:, 0:DM], ctx["ktm"][:, tc * DM:(tc + 1) * DM], AF.Square), "sc")
        p.wait("vector", "sc")
        p.inst("vector", lambda E, ctx, tc=tc: E.reduce_sum(
            ctx["knr"][:, tc * 16:(tc + 1) * 16].rearrange("p (h o) -> p h o", o=1),
            ctx["sqk"][:, 0:DM].rearrange("p (h d) -> p h d", d=64), AXX), "ve")
    p.wait("scalar", "ve")
    p.inst("scalar", lambda E, ctx: E.activation(ctx["knr"][:, :], ctx["knr"][:, :], AF.Ln), "sc")
    p.inst("scalar", lambda E, ctx: E.activation(ctx["knr"][:, :], ctx["knr"][:, :], AF.Exp, scale=np.float32(0.5)), "sc")
    p.inst("scalar", lambda E, ctx: E.add(ctx["knr"][:, :], ctx["knr"][:, :], np.float32(1e-6)), "sc")
    p.wait("vector", "sc")
    p.inst("vector", lambda E, ctx: E.reciprocal(ctx["knr"][:, :], ctx["knr"][:, :]), "ve")
    p.inst("vector", lambda E, ctx: E.tensor_copy(ctx["knrb"][:, :], ctx["knr"][:, :]), "ve")
    for tc in range(NCH):
        p.inst("vector", lambda E, ctx, tc=tc: E.tensor_mul(
            ctx["ktm"][:, tc * DM:(tc + 1) * DM].rearrange("p (h d) -> p h d", d=64),
            ctx["ktm"][:, tc * DM:(tc + 1) * DM].rearrange("p (h d) -> p h d", d=64),
            ctx["knrb"][:, tc * 16:(tc + 1) * 16].rearrange("p (h o) -> p h o", o=1)
            .broadcast_to([128, 16, 64])), "ve")
    p.wait("tensor", "ve")
    for tc in range(NCH):
        for oc in range(DCH):
            half = ((tc * DCH + oc) % 2) * 128
            p.inst("tensor", lambda E, ctx, tc=tc, oc=oc, half=half: E.transpose(
                ctx["PT"][:, half:half + 128],
                ctx["ktm"][:, tc * DM + oc * 128: tc * DM + (oc + 1) * 128],
                ctx["c_ident"][:, :]), "tm")
            p.wait("vector", "tm")
            p.inst("vector", lambda E, ctx, tc=tc, oc=oc, half=half: E.tensor_copy(
                ctx["kfm"][:, oc * T + tc * 128: oc * T + (tc + 1) * 128],
                ctx["PT"][:, half:half + 128]), "ve")
            if (tc * DCH + oc) % 2 == 1:
                p.wait("tensor", "ve")

    p.free("bconst", "knr", "knrb", "sqk")
    p.phase("P1b")
    sb("qfm", (128, DCH * T))
    load_w(wq)
    p.wait("tensor", "gd")
    p.wait("scalar", "gd")
    p.inst("gpsimd", lambda E, ctx: E.dma_start(out=ctx["bcol"][:, 0:DCH], in_=bq), "gd", 16)
    p.wait("scalar", "gd")
    proj_fm("qfm")

    if cfg.debug:
        def dump(src, dname, reg=None):
            for e in ("ve", "sc", "tm"):
                p.wait("gpsimd", e)
            p.group("gpsimd", "gd")
            if reg is None:
                p.inst("gpsimd", lambda E, ctx, src=src, dname=dname: E.dma_start(
                    out=dbg[dname], in_=ctx[src].ap()), "gd", 16)
            else:
                p.inst("gpsimd", lambda E, ctx, src=src, dname=dname, reg=reg: E.dma_start(
                    out=dbg[dname], in_=ctx[src][reg]), "gd", 16)
        dump("y1", "d_y1"); dump("ktm", "d_ktm"); dump("vtm", "d_vtm")
        dump("qfm", "d_qfm"); dump("kfm", "d_kfm"); dump("beta", "d_beta")

    if cfg.stage >= 2:
        p.free("y1", "wt")
        p.phase("P2")
        sb("c_oneh16", (16, 16 * 128))
        sb("Tt", (128, 256))
        sb("pmw", (128, 1024))
        sb("Nlow", (128, 256)); sb("Nup", (128, 256))
        sb("chR", (128, 512))
        sb("mbB", (128, 256))
        sb("zbd", (128, (H // 2) * 256))
        sb("vg", (128, 128))
        sb("rh", (128, 256))
        sb("Ush", (128, 256))
        sb("stUB", (128, H * 64))
        sb("stUPT", (64, H * 128))
        sb("stZPT", (64, H * 64))
        sb("brow", (16, 128))
        sb("zfin", (128, 1024), F32)
        sb("ptfin", (64, 1024), F32)
        p.group("gpsimd", "gd")
        p.inst("gpsimd", lambda E, ctx: E.dma_start(
            out=ctx["c_oneh16"].ap(), in_=oneh16), "gd", 16)
        for e in ("tensor", "vector", "scalar"):
            p.wait(e, "gd")
        p.wait("tensor", "ve")
        for pr in range(8):
            p.inst("tensor", lambda E, ctx, pr=pr: E.matmul(
                ctx["PZ1"][0:64, pr * 128: pr * 128 + 64], ctx["c_e64"][:, :],
                ctx["c_e64"][:, :], start=(pr % 4 == 0), stop=False,
                tile_position=(0, 0), skip_group_check=True))
            p.inst("tensor", lambda E, ctx, pr=pr: E.matmul(
                ctx["PZ1"][64:128, pr * 128: pr * 128 + 64], ctx["c_e64"][:, :],
                ctx["c_e64"][:, :], start=(pr % 4 == 0), stop=False,
                tile_position=(0, 64), skip_group_check=True))
            for j in range(2):
                p.inst("tensor", lambda E, ctx, pr=pr, j=j: E.matmul(
                    ctx["PZ2"][0:64, pr * 128 + j * 64: pr * 128 + (j + 1) * 64],
                    ctx["c_e64"][:, :], ctx["c_e64"][:, :],
                    start=(pr % 4 == 0 and j == 0), stop=False,
                    tile_position=(0, 0), skip_group_check=True))
            p.inst("tensor", lambda E, ctx, pr=pr: E.matmul(
                ctx["PZ2"][64:128, pr * 128:(pr + 1) * 128], ctx["c_e64"][:, :],
                ctx["czero"][:, :], start=(pr % 4 == 0), stop=False,
                tile_position=(0, 64), skip_group_check=True))
            for j in range(2):
                hb = j * 64
                p.inst("tensor", lambda E, ctx, pr=pr, hb=hb: E.matmul(
                    ctx["PZ1"][hb:hb + 64, pr * 128 + 64:(pr + 1) * 128],
                    ctx["c_e64"][:, :], ctx["czero"][:, 0:64], start=False,
                    stop=False, tile_position=(0, hb), skip_group_check=True),
                    "tm" if (pr == 7 and hb == 64) else None)
        p.wait("vector", "tm")
        p.inst("vector", lambda E, ctx: E.memset(ctx["zbd"][:, :], 0.0), "ve")
        for pr in range(8):
            p.inst("vector", lambda E, ctx, pr=pr: E.tensor_copy(
                ctx["zbd"][0:64, pr * 256: pr * 256 + 128],
                ctx["PZ1"][0:64, pr * 128: (pr + 1) * 128]), "ve")
            p.inst("vector", lambda E, ctx, pr=pr: E.tensor_copy(
                ctx["zbd"][64:128, pr * 256 + 128: (pr + 1) * 256],
                ctx["PZ1"][64:128, pr * 128: (pr + 1) * 128]), "ve")

        def inv_pair(c, pr, lim=99):
            p.wait("vector", "tm")
            for j in range(2):
                jj = j * 128
                h = 2 * pr + j
                p.inst("vector", lambda E, ctx, jj=jj: E.scalar_tensor_tensor(
                    ctx["mbB"][:, jj:jj + 128], ctx["PB"][:, jj:jj + 128], 1.0,
                    ctx["c_masku"][:, :], ALU.mult, ALU.mult), "ve")
                p.inst("vector", lambda E, ctx, jj=jj: E.scalar_tensor_tensor(
                    ctx["Nup"][:, jj:jj + 128], ctx["PA"][:, jj:jj + 128], 1.0,
                    ctx["mbB"][:, jj:jj + 128], ALU.mult, ALU.mult), "ve")
                p.inst("vector", lambda E, ctx, jj=jj, c=c, h=h: E.scalar_tensor_tensor(
                    ctx["Nlow"][:, jj:jj + 128], ctx["PA"][:, jj:jj + 128],
                    ctx["beta"][:, c * 16 + h: c * 16 + h + 1],
                    ctx["c_maskl"][:, :], ALU.mult, ALU.mult), "ve")
                p.inst("vector", lambda E, ctx, j=j: E.tensor_sub(
                    ctx["chR"][:, j * 256: j * 256 + 128], ctx["c_ident"][:, :],
                    ctx["Nup"][:, j * 128:(j + 1) * 128]), "ve")
            if lim < 21:
                return
            evg = [p.cnt["ve"], p.cnt["ve"]]
            for it in range(1, 8):
                cur = (it - 1) % 2
                nxt = it % 2
                for j in range(2):
                    pz = "PC" if j == 0 else "PB"
                    if it == 1:
                        pN = ("Nlow", slice(j * 128, j * 128 + 128))
                        pT = ("Nup", slice(j * 128, j * 128 + 128))
                    else:
                        off = j * 512 + cur * 256
                        pN = ("pmw", slice(off, off + 128))
                        pT = ("pmw", slice(off + 128, off + 256))
                    p.wait("tensor", "ve", evg[j])
                    last_sq = (it >= 7)
                    if it <= 6:
                        p.inst("tensor", lambda E, ctx, pN=pN, pT=pT, pz=pz: E.matmul(
                            ctx[pz][:, 0:128], ctx[pT[0]][:, pT[1]],
                            ctx[pN[0]][:, pN[1]], start=True, stop=True,
                            skip_group_check=True))
                        if it < 6:
                            p.inst("tensor", lambda E, ctx, pN=pN, pT=pT, pz=pz: E.matmul(
                                ctx[pz][:, 128:256], ctx[pN[0]][:, pN[1]],
                                ctx[pT[0]][:, pT[1]], start=True, stop=True,
                                skip_group_check=True), None if it >= 2 else "tm")
                        elif it < 2:
                            pass
                    if it >= 2:
                        src = j * 256 + ((it - 2) % 2) * 128
                        p.inst("tensor", lambda E, ctx, pN=pN, src=src, pz=pz: E.matmul(
                            ctx[pz][:, 256:384], ctx[pN[0]][:, pN[1]],
                            ctx["chR"][:, src:src + 128], start=True, stop=True,
                            skip_group_check=True), "tm")
                    p.wait("vector", "tm")
                    if it <= 6:
                        wid = 256 if it < 6 else 128
                        off2 = j * 512 + nxt * 256
                        p.inst("vector", lambda E, ctx, pz=pz, off2=off2, wid=wid: E.tensor_copy(
                            ctx["pmw"][:, off2:off2 + wid], ctx[pz][:, 0:wid]), "ve")
                    if it >= 2:
                        src = j * 256 + ((it - 2) % 2) * 128
                        dst = j * 256 + ((it - 1) % 2) * 128
                        if it < 7:
                            p.inst("vector", lambda E, ctx, src=src, dst=dst, pz=pz: E.scalar_tensor_tensor(
                                ctx["chR"][:, dst:dst + 128], ctx[pz][:, 256:384],
                                1.0, ctx["chR"][:, src:src + 128], ALU.mult, ALU.add), "ve")
                        else:
                            p.inst("vector", lambda E, ctx, src=src, j=j, pz=pz: E.scalar_tensor_tensor(
                                ctx["Tt"][:, j * 128:(j + 1) * 128],
                                ctx[pz][:, 256:384],
                                1.0, ctx["chR"][:, src:src + 128], ALU.mult, ALU.add), "ve")
                    evg[j] = p.cnt["ve"]


        for c in range(NCH if cfg.p2c == 0 else min(NCH, cfg.p2c)):
            if cfg.p2s < 1:
                continue
            p.wait("tensor", "ve")
            p.inst("tensor", lambda E, ctx, c=c: E.transpose(
                ctx["PT"][0:16, 0:128],
                ctx["betab"][:, c * 16:(c + 1) * 16], ctx["c_ident"][:, :]), "tm")
            p.wait("vector", "tm")
            p.inst("vector", lambda E, ctx: E.tensor_copy(
                ctx["brow"][:, :], ctx["PT"][0:16, 0:128]), "ve")
            p.wait("vector", "sd")
            p.wait("scalar", "sd")
            p.wait("vector", "tm")
            p.inst("vector", lambda E, ctx: E.tensor_copy(
                ctx["stZPT"][0:64, :], ctx["PZ2"][0:64, :]), "ve")
            p.wait("sync", "sc")
            p.wait("sync", "ve")
            p.group("sync", "sd")
            p.inst("sync", lambda E, ctx, c=c: E.dma_start(
                out=cZPT[c], in_=ctx["stZPT"].ap()), "sd", 16)
            p.inst("sync", lambda E, ctx, c=c: E.dma_start(
                out=cB[c][0:64, :],
                in_=ctx["zbd"].ap().rearrange("p (r x) -> p r x", r=8)[0:64, :, 64:128]), "sd", 16)
            p.inst("sync", lambda E, ctx, c=c: E.dma_start(
                out=cB[c][64:128, :],
                in_=ctx["zbd"].ap().rearrange("p (r x) -> p r x", r=8)[64:128, :, 192:256]), "sd", 16)
            if cfg.debug:
                p.group("sync", "sd")
                p.inst("sync", lambda E, ctx, c=c: E.dma_start(
                    out=dbg["d_ZPT"][c], in_=ctx["stZPT"].ap()), "sd", 16)
                p.inst("sync", lambda E, ctx, c=c: E.dma_start(
                    out=dbg["d_B"][c][0:64, :],
                    in_=ctx["zbd"].ap().rearrange("p (r x) -> p r x", r=8)[0:64, :, 64:128]), "sd", 16)
                p.inst("sync", lambda E, ctx, c=c: E.dma_start(
                    out=dbg["d_B"][c][64:128, :],
                    in_=ctx["zbd"].ap().rearrange("p (r x) -> p r x", r=8)[64:128, :, 192:256]), "sd", 16)
            zbd_cached = p.cnt["sd"]
            for pr in range(8):
                if pr == 0:
                    p.wait("vector", "sd", zbd_cached)
                kc = lambda ctx, pr=pr, c=c: ctx["kfm"][:, pr * T + c * 128: pr * T + (c + 1) * 128]
                p.wait("tensor", "ve")
                p.wait("tensor", "sc")
                for j in range(2):
                    hb = j * 64
                    Adst = ("PA" if j == 0 else "PB") if cfg.p2s in (41, 42, 43) else "PA"
                    Aoff = 0 if cfg.p2s in (41, 42, 43) else j * 128
                    p.inst("tensor", lambda E, ctx, pr=pr, c=c, hb=hb, j=j, Adst=Adst, Aoff=Aoff: E.matmul(
                        ctx[Adst][:, Aoff:Aoff + 128],
                        ctx["kfm"][:, pr * T + c * 128: pr * T + (c + 1) * 128][hb:hb + 64, :],
                        ctx["kfm"][:, pr * T + c * 128: pr * T + (c + 1) * 128][hb:hb + 64, :],
                        start=True, stop=True, tile_position=(hb, 0),
                        skip_group_check=True))
                    if cfg.p2s in (41, 43):
                        if j == 0:
                            p.inst("tensor", lambda E, ctx, pr=pr, j=j: E.matmul(
                                ctx["PC"][:, 0:128],
                                ctx["c_oneh16"][:, (2 * pr + j) * 128:(2 * pr + j + 1) * 128],
                                ctx["brow"][:, :], start=True, stop=True,
                                skip_group_check=True))
                    elif cfg.p2s == 42:
                        if j == 0:
                            p.inst("tensor", lambda E, ctx: E.matmul(
                                ctx["PC"][:, 0:128], ctx["c_ident"][:, :],
                                ctx["c_masku"][:, :], start=True, stop=True,
                                skip_group_check=True))
                    elif cfg.p2s == 29:
                        p.inst("tensor", lambda E, ctx, pr=pr, j=j: E.matmul(
                            ctx["PB"][:, j * 128:(j + 1) * 128],
                            ctx["c_ident"][:, :],
                            ctx["c_masku"][:, :], start=True, stop=True,
                            skip_group_check=True))
                    else:
                        p.inst("tensor", lambda E, ctx, pr=pr, j=j: E.matmul(
                            ctx["PB"][:, j * 128:(j + 1) * 128],
                            ctx["c_oneh16"][:, (2 * pr + j) * 128:(2 * pr + j + 1) * 128],
                            ctx["brow"][:, :], start=True, stop=True,
                            skip_group_check=True), "tm" if j == 1 else None)
                if cfg.p2s < 2:
                    continue
                inv_pair(c, pr, lim=cfg.p2s)
                if cfg.p2s < 3:
                    continue
                p.wait("tensor", "ve")
                p.inst("tensor", lambda E, ctx, pr=pr, c=c: E.matmul(
                    ctx["PB"][:, 256:512],
                    ctx["kfm"][:, pr * T + c * 128: pr * T + (c + 1) * 128],
                    ctx["zbd"][:, pr * 256:(pr + 1) * 256],
                    start=True, stop=True, skip_group_check=True), "tm")
                if cfg.p2s == 45:
                    continue
                if cfg.p2s < 4:
                    continue
                p.wait("vector", "tm")
                for j in range(2):
                    h = 2 * pr + j
                    p.inst("vector", lambda E, ctx, j=j, c=c, h=h: E.tensor_scalar(
                        ctx["rh"][:, j * 128: j * 128 + 64],
                        ctx["PB"][:, 256 + j * 128: 256 + j * 128 + 64],
                        ctx["beta"][:, c * 16 + h: c * 16 + h + 1], -1.0,
                        ALU.mult, ALU.mult), "ve")
                    p.inst("vector", lambda E, ctx, j=j, c=c, h=h: E.tensor_sub(
                        ctx["vg"][:, j * 64:(j + 1) * 64],
                        ctx["vtm"][:, c * DM + h * 64: c * DM + (h + 1) * 64],
                        ctx["PB"][:, 256 + j * 128 + 64: 256 + (j + 1) * 128]), "ve")
                    p.inst("vector", lambda E, ctx, j=j, c=c, h=h: E.tensor_scalar(
                        ctx["rh"][:, j * 128 + 64:(j + 1) * 128],
                        ctx["vg"][:, j * 64:(j + 1) * 64],
                        ctx["beta"][:, c * 16 + h: c * 16 + h + 1], None,
                        ALU.mult), "ve")
                if cfg.p2s < 5:
                    continue
                p.wait("tensor", "sc")
                p.wait("tensor", "ve")
                for j in range(2):
                    p.inst("tensor", lambda E, ctx, j=j: E.matmul(
                        ctx["PC"][:, j * 128:(j + 1) * 128],
                        ctx["Tt"][:, j * 128:(j + 1) * 128],
                        ctx["rh"][:, j * 128:(j + 1) * 128],
                        start=True, stop=True, skip_group_check=True))
                    p.inst("tensor", lambda E, ctx, j=j: E.matmul(
                        ctx["PC"][0:64, 256 + j * 128: 256 + (j + 1) * 128],
                        ctx["rh"][:, j * 128: j * 128 + 64],
                        ctx["Tt"][:, j * 128:(j + 1) * 128],
                        start=True, stop=True, skip_group_check=True),
                        "tm" if j == 1 else None)
                p.wait("vector", "tm")
                p.wait("scalar", "tm")
                for j in range(2):
                    h = 2 * pr + j
                    p.inst("vector", lambda E, ctx, j=j: E.tensor_copy(
                        ctx["Ush"][:, j * 128:(j + 1) * 128],
                        ctx["PC"][:, j * 128:(j + 1) * 128]), "ve")
                    p.inst("vector", lambda E, ctx, j=j, h=h: E.tensor_copy(
                        ctx["stUB"][:, h * 64:(h + 1) * 64],
                        ctx["PC"][:, j * 128 + 64:(j + 1) * 128]), "ve")
                    p.inst("vector", lambda E, ctx, j=j, h=h: E.tensor_copy(
                        ctx["stUPT"][0:64, h * 128:(h + 1) * 128],
                        ctx["PC"][0:64, 256 + j * 128: 256 + (j + 1) * 128]), "ve")
                if cfg.p2s < 6:
                    continue
                p.wait("tensor", "ve")
                for j in range(2):
                    h = 2 * pr + j
                    hb = j * 64
                    p.inst("tensor", lambda E, ctx, j=j, h=h, hb=hb, pr=pr, c=c: E.matmul(
                        ctx["PZ1"][hb:hb + 64, pr * 128:(pr + 1) * 128],
                        ctx["ktm"][:, c * DM + h * 64: c * DM + (h + 1) * 64],
                        ctx["Ush"][:, j * 128:(j + 1) * 128],
                        start=False, stop=False, tile_position=(0, hb),
                        skip_group_check=True))
                    p.inst("tensor", lambda E, ctx, j=j, h=h, pr=pr, c=c: E.matmul(
                        ctx["PZ2"][:, pr * 128 + j * 64: pr * 128 + (j + 1) * 64],
                        ctx["Ush"][:, j * 128:(j + 1) * 128],
                        ctx["ktm"][:, c * DM + h * 64: c * DM + (h + 1) * 64],
                        start=False, stop=False, tile_position=(0, 0),
                        skip_group_check=True), "tm" if j == 1 else None)
                p.wait("vector", "tm")
                p.inst("vector", lambda E, ctx, pr=pr: E.tensor_copy(
                    ctx["zbd"][0:64, pr * 256: pr * 256 + 128],
                    ctx["PZ1"][0:64, pr * 128:(pr + 1) * 128]), "ve")
                p.inst("vector", lambda E, ctx, pr=pr: E.tensor_copy(
                    ctx["zbd"][64:128, pr * 256 + 128:(pr + 1) * 256],
                    ctx["PZ1"][64:128, pr * 128:(pr + 1) * 128]), "ve")
            p.wait("sync", "ve")
            p.wait("sync", "sc")
            p.group("sync", "sd")
            p.inst("sync", lambda E, ctx, c=c: E.dma_start(
                out=cUB[c], in_=ctx["stUB"].ap()), "sd", 16)
            p.inst("sync", lambda E, ctx, c=c: E.dma_start(
                out=cUPT[c], in_=ctx["stUPT"].ap()), "sd", 16)
            if cfg.debug:
                p.group("sync", "sd")
                p.inst("sync", lambda E, ctx, c=c: E.dma_start(
                    out=dbg["d_UB"][c], in_=ctx["stUB"].ap()), "sd", 16)
                p.inst("sync", lambda E, ctx, c=c: E.dma_start(
                    out=dbg["d_UPT"][c], in_=ctx["stUPT"].ap()), "sd", 16)
        if cfg.debug:
            p.wait("vector", "tm")
            p.inst("vector", lambda E, ctx: E.tensor_copy(
                ctx["zfin"][:, :], ctx["PZ1"][:, :]), "ve")
            p.wait("gpsimd", "ve")
            p.group("gpsimd", "gd")
            p.inst("gpsimd", lambda E, ctx: E.dma_start(
                out=dbg["d_zfin"], in_=ctx["zfin"].ap()), "gd", 16)

    if cfg.stage >= 3:
        p.free("ktm", "vtm", "beta", "nbeta", "betab", "Tt", "pmw",
               "Nlow", "Nup", "chR", "mbB", "zbd", "vg", "rh", "Ush",
               "stUB", "stUPT", "stZPT", "brow", "c_oneh16")
        p.phase("P3")
        sb("gA", (64, 8 * DM), F32)
        sb("gB", (64, 8 * DM), F32)
        sb("zcore", (64, H * 64))
        sb("pj", (64, DM)); sb("bj", (64, DM), F32)
        sb("c_pmask", (64, 8), F32)
        sb("c_imask", (64, 8 * 64), F32)
        NC_ = cfg.n_cores
        p.group("gpsimd", "gd")
        p.inst("gpsimd", lambda E, ctx: E.dma_start(out=ctx["c_pmask"].ap(), in_=pmask), "gd", 16)
        p.inst("gpsimd", lambda E, ctx: E.dma_start(out=ctx["c_imask"].ap(), in_=imask), "gd", 16)
        p.wait("vector", "gd")
        p.wait("vector", "tm")
        p.inst("vector", lambda E, ctx: E.tensor_copy(
            ctx["zfin"][:, :], ctx["PZ1"][:, :]), "ve")
        p.wait("scalar", "tm")
        p.inst("scalar", lambda E, ctx: E.copy(
            ctx["ptfin"][0:64, :], ctx["PZ2"][0:64, :]), "sc")
        p.wait("sync", "ve")
        p.wait("sync", "sc")
        p.group("sync", "sd")
        p.inst("sync", lambda E, ctx: E.dma_start(
            out=agi[0:128, :], in_=ctx["zfin"].ap()), "sd", 16)
        p.inst("sync", lambda E, ctx: E.dma_start(
            out=agi[128:192, :], in_=ctx["ptfin"].ap()), "sd", 16)
        p.wait("gpsimd", "sd")
        p.inst("gpsimd", lambda E, ctx: E.collective_compute(
            "AllGather", ALU.bypass, replica_groups=[list(range(NC_))],
            ins=[agi], outs=[ago]), "cc")
        p.wait("gpsimd", "cc")
        p.group("gpsimd", "gd")
        for j in range(NC_):
            p.inst("gpsimd", lambda E, ctx, j=j: E.dma_start(
                out=ctx["gB"][0:64, j * 1024:(j + 1) * 1024],
                in_=ago[j * 192 + 128: j * 192 + 192, :]), "gd", 16)
            p.inst("gpsimd", lambda E, ctx, j=j: E.dma_start(
                out=ctx["gA"][0:64, j * 1024:(j + 1) * 1024]
                .rearrange("p (r x) -> p r x", x=128)[:, :, 0:64],
                in_=ago[j * 192: j * 192 + 64, :]
                .rearrange("p (r x) -> p r x", r=8)[:, :, 64:128]), "gd", 16)
            p.inst("gpsimd", lambda E, ctx, j=j: E.dma_start(
                out=ctx["gA"][0:64, j * 1024:(j + 1) * 1024]
                .rearrange("p (r x) -> p r x", x=128)[:, :, 64:128],
                in_=ago[j * 192 + 64: j * 192 + 128, :]
                .rearrange("p (r x) -> p r x", r=8)[:, :, 64:128]), "gd", 16)
        p.wait("vector", "gd")
        p.inst("vector", lambda E, ctx: E.memset(ctx["zcore"][:, :], 0.0), "ve")
        for j in range(NC_):
            p.inst("vector", lambda E, ctx, j=j: E.scalar_tensor_tensor(
                ctx["pj"][0:64, :].rearrange("p (h d) -> p h d", d=64),
                ctx["gB"][0:64, j * 1024:(j + 1) * 1024].rearrange("p (h d) -> p h d", d=64),
                ctx["c_pmask"][0:64, j:j + 1],
                ctx["c_imask"][0:64, j * 64:(j + 1) * 64]
                .rearrange("p (o d) -> p o d", o=1).broadcast_to([64, 16, 64]),
                ALU.mult, ALU.add), "ve")
            p.inst("vector", lambda E, ctx, j=j: E.tensor_scalar_mul(
                ctx["bj"][0:64, :], ctx["gA"][0:64, j * 1024:(j + 1) * 1024],
                ctx["c_pmask"][0:64, j:j + 1]), "ve")
            p.wait("tensor", "ve")
            for h in range(H):
                p.inst("tensor", lambda E, ctx, h=h: E.matmul(
                    ctx["PZ1"][0:64, h * 64:(h + 1) * 64],
                    ctx["pj"][0:64, h * 64:(h + 1) * 64],
                    ctx["zcore"][0:64, h * 64:(h + 1) * 64],
                    start=True, stop=True, skip_group_check=True),
                    "tm" if h == H - 1 else None)
            p.wait("vector", "tm")
            p.inst("vector", lambda E, ctx: E.scalar_tensor_tensor(
                ctx["zcore"][0:64, :], ctx["PZ1"][0:64, 0:1024], 1.0,
                ctx["bj"][0:64, :], ALU.mult, ALU.add), "ve")
        if cfg.debug:
            p.wait("gpsimd", "ve")
            p.group("gpsimd", "gd")
            p.inst("gpsimd", lambda E, ctx: E.dma_start(
                out=dbg["d_zcore"], in_=ctx["zcore"].ap()), "gd", 16)

    if cfg.stage >= 4:
        p.free("zfin", "ptfin", "gA", "gB", "pj", "bj", "c_pmask", "c_imask")
        p.phase("P4")
        sb("ofm", (128, DCH * T))
        sb("ldUB", (128, 2 * H * 64))
        sb("ldUPT", (64, 2 * H * 128))
        sb("ldZPT", (64, 2 * H * 64))
        sb("ldB", (128, 2 * (H // 2) * 64))
        sb("z0pad", (128, H * 64))
        sb("urel", (128, 128))
        sb("ltp", (128, 256))
        p.wait("vector", "cc")
        p.inst("vector", lambda E, ctx: E.memset(ctx["z0pad"][:, :], 0.0), "ve")
        for c in range(NCH):
            buf = c % 2
            p.wait("sync", "ve")
            p.group("sync", "sd")
            p.inst("sync", lambda E, ctx, c=c, buf=buf: E.dma_start(
                out=ctx["ldUB"][:, buf * H * 64:(buf + 1) * H * 64], in_=cUB[c]), "sd", 16)
            p.inst("sync", lambda E, ctx, c=c, buf=buf: E.dma_start(
                out=ctx["ldUPT"][0:64, buf * H * 128:(buf + 1) * H * 128], in_=cUPT[c]), "sd", 16)
            p.inst("sync", lambda E, ctx, c=c, buf=buf: E.dma_start(
                out=ctx["ldZPT"][0:64, buf * H * 64:(buf + 1) * H * 64], in_=cZPT[c]), "sd", 16)
            p.inst("sync", lambda E, ctx, c=c, buf=buf: E.dma_start(
                out=ctx["ldB"][:, buf * 512:(buf + 1) * 512], in_=cB[c]), "sd", 16)
            p.wait("tensor", "sd")
            oUB = buf * H * 64
            oUPT = buf * H * 128
            oZPT = buf * H * 64
            oB = buf * 512
            for pr in range(8):
                ccols = slice(pr * T + c * 128, pr * T + (c + 1) * 128)
                p.wait("tensor", "ve")
                for j in range(2):
                    h = 2 * pr + j
                    hb = j * 64
                    p.inst("tensor", lambda E, ctx, j=j, h=h, hb=hb, pr=pr, c=c: E.matmul(
                        ctx["PA"][:, j * 128:(j + 1) * 128],
                        ctx["kfm"][:, pr * T + c * 128: pr * T + (c + 1) * 128][hb:hb + 64, :],
                        ctx["qfm"][:, pr * T + c * 128: pr * T + (c + 1) * 128][hb:hb + 64, :],
                        start=True, stop=True, tile_position=(hb, 0),
                        skip_group_check=True))
                    p.inst("tensor", lambda E, ctx, j=j, h=h, oUPT=oUPT: E.matmul(
                        ctx["PB"][:, j * 64:(j + 1) * 64],
                        ctx["ldUPT"][0:64, oUPT + h * 128: oUPT + (h + 1) * 128],
                        ctx["zcore"][0:64, h * 64:(h + 1) * 64],
                        start=True, stop=True, skip_group_check=True))
                    p.inst("tensor", lambda E, ctx, j=j, h=h, hb=hb, oZPT=oZPT: E.matmul(
                        ctx["PB"][hb:hb + 64, 128 + j * 64: 128 + (j + 1) * 64],
                        ctx["ldZPT"][0:64, oZPT + h * 64: oZPT + (h + 1) * 64],
                        ctx["zcore"][0:64, h * 64:(h + 1) * 64],
                        start=True, stop=True, tile_position=(0, hb),
                        skip_group_check=True), "tm" if j == 1 else None)
                p.wait("vector", "tm")
                for j in range(2):
                    h = 2 * pr + j
                    hb = j * 64
                    p.inst("vector", lambda E, ctx, j=j: E.scalar_tensor_tensor(
                        ctx["ltp"][:, j * 128:(j + 1) * 128],
                        ctx["PA"][:, j * 128:(j + 1) * 128], 1.0,
                        ctx["c_maskui"][:, :], ALU.mult, ALU.mult), "ve")
                    p.inst("vector", lambda E, ctx, j=j, h=h, oUB=oUB: E.scalar_tensor_tensor(
                        ctx["urel"][:, j * 64:(j + 1) * 64],
                        ctx["PB"][:, j * 64:(j + 1) * 64], 1.0,
                        ctx["ldUB"][:, oUB + h * 64: oUB + (h + 1) * 64],
                        ALU.mult, ALU.add), "ve")
                    p.inst("vector", lambda E, ctx, j=j, h=h, hb=hb, pr=pr, oB=oB: E.scalar_tensor_tensor(
                        ctx["z0pad"][hb:hb + 64, h * 64:(h + 1) * 64],
                        ctx["PB"][hb:hb + 64, 128 + j * 64: 128 + (j + 1) * 64], 1.0,
                        ctx["ldB"][hb:hb + 64, oB + pr * 64: oB + (pr + 1) * 64],
                        ALU.mult, ALU.add), "ve")
                p.wait("tensor", "ve")
                for j in range(2):
                    h = 2 * pr + j
                    hb = j * 64
                    p.inst("tensor", lambda E, ctx, j=j, h=h, hb=hb, pr=pr, c=c: E.matmul(
                        ctx["PC"][hb:hb + 64, 0:128],
                        ctx["z0pad"][:, h * 64:(h + 1) * 64],
                        ctx["qfm"][:, pr * T + c * 128: pr * T + (c + 1) * 128],
                        start=True, stop=False, tile_position=(0, hb),
                        skip_group_check=True))
                    p.inst("tensor", lambda E, ctx, j=j, hb=hb: E.matmul(
                        ctx["PC"][hb:hb + 64, 0:128],
                        ctx["urel"][:, j * 64:(j + 1) * 64],
                        ctx["ltp"][:, j * 128:(j + 1) * 128],
                        start=False, stop=(j == 1), tile_position=(0, hb),
                        skip_group_check=True), "tm" if j == 1 else None)
                p.wait("vector", "tm")
                p.inst("vector", lambda E, ctx, pr=pr, c=c: E.tensor_copy(
                    ctx["ofm"][:, pr * T + c * 128: pr * T + (c + 1) * 128],
                    ctx["PC"][:, 0:128]), "ve")
        if cfg.debug:
            p.wait("gpsimd", "ve")
            p.group("gpsimd", "gd")
            p.inst("gpsimd", lambda E, ctx: E.dma_start(
                out=dbg["d_ofm"], in_=ctx["ofm"].ap()), "gd", 16)

    if cfg.stage >= 5:
        p.free("kfm", "qfm", "zcore", "ldUB", "ldUPT", "ldZPT", "ldB",
               "z0pad", "urel", "ltp")
        p.phase("P5")
        sb("wt", (128, DCH * DM))
        sb("anorm", (128, DCH * T))
        sb("bcol", (128, DCH), F32)
        p.wait("gpsimd", "tm")
        p.group("gpsimd", "gd")
        p.inst("gpsimd", lambda E, ctx: E.dma_start(out=ctx["wt"][:, :], in_=wo), "gd", 16)
        p.inst("gpsimd", lambda E, ctx: E.dma_start(out=ctx["bcol"][:, 0:DCH], in_=bo), "gd", 16)
        p.wait("tensor", "gd")
        p.wait("scalar", "gd")
        proj_fm("anorm", src="ofm", bias="bcol")

    if cfg.stage >= 5:
        p.free("ofm", "wt")
        p.phase("P6")
        sb("xf", (128, DCH * T), F32)
        sb("y2", (128, DCH * T))
        sb("bc1", (128, T)); sb("bc2", (128, T))
        sb("sq", (128, 4096))
        sb("x2c", (128, 4096))
        sb("st2", (1, 2 * T), BF16)
        p.group("gpsimd", "gd")
        p.inst("gpsimd", lambda E, ctx: E.dma_start(
            out=ctx["xf"].ap().rearrange("p (c t) -> p c t", c=DCH),
            in_=x_in.rearrange("(c p) t -> p c t", p=128)), "gd", 16)
        for e in ("tensor", "vector", "scalar"):
            p.wait(e, "gd")
        ln_feature_major("anorm", "anorm")
        for nt in range(NT):
            p.wait("scalar", "tm")
            p.wait("vector", "tm")
            for c in range(DCH):
                reg = c * 512
                p.inst("vector", lambda E, ctx, c=c, nt=nt, reg=reg: E.tensor_add(
                    ctx["x2c"][:, reg:reg + 512],
                    ctx["xf"][:, c * T + nt * 512: c * T + (nt + 1) * 512],
                    ctx["anorm"][:, c * T + nt * 512: c * T + (nt + 1) * 512]), "ve")
                p.wait("scalar", "ve")
                p.inst("scalar", lambda E, ctx, c=c, reg=reg: E.activation(
                    ctx["sq"][:, reg:reg + 512], ctx["x2c"][:, reg:reg + 512],
                    AF.Square), "sc")
                p.wait("tensor", "sc")
                p.inst("tensor", lambda E, ctx, c=c, reg=reg: E.matmul(
                    ctx["PA"][0:1, :], ctx["c_onesc"][:, :],
                    ctx["x2c"][:, reg:reg + 512],
                    start=(c == 0), stop=(c == DCH - 1), skip_group_check=True))
                p.inst("tensor", lambda E, ctx, c=c, reg=reg: E.matmul(
                    ctx["PB"][0:1, :], ctx["c_onesc"][:, :],
                    ctx["sq"][:, reg:reg + 512],
                    start=(c == 0), stop=(c == DCH - 1), skip_group_check=True),
                    "tm" if c == DCH - 1 else None)
            p.wait("vector", "tm")
            p.inst("vector", lambda E, ctx: E.tensor_copy(
                ctx["lntmp"][0:1, 0:512], ctx["PA"][0:1, :]), "ve")
            p.inst("vector", lambda E, ctx: E.tensor_copy(
                ctx["lntmp"][0:1, 512:1024], ctx["PB"][0:1, :]), "ve")
            p.wait("tensor", "ve")
            p.wait("scalar", "ve")
            ln_stats_math(nt, save_rows="st2")
            ln_bcast(nt)
        for c in range(DCH):
            p.inst("vector", lambda E, ctx, c=c: E.tensor_add(
                ctx["x2c"][:, 0:T], ctx["xf"][:, c * T:(c + 1) * T],
                ctx["anorm"][:, c * T:(c + 1) * T]), "ve")
            p.inst("vector", lambda E, ctx, c=c: E.tensor_mul(
                ctx["y2"][:, c * T:(c + 1) * T], ctx["x2c"][:, 0:T],
                ctx["bc1"][:, :]), "ve")
            p.inst("vector", lambda E, ctx, c=c: E.tensor_add(
                ctx["y2"][:, c * T:(c + 1) * T], ctx["y2"][:, c * T:(c + 1) * T],
                ctx["bc2"][:, :]), "ve")
        if cfg.debug and cfg.stage >= 5:
            p.wait("gpsimd", "ve")
            p.group("gpsimd", "gd")
            p.inst("gpsimd", lambda E, ctx: E.dma_start(
                out=dbg["d_y2"], in_=ctx["y2"].ap()), "gd", 16)

    if cfg.stage >= 9:
        p.free("anorm", "xf", "bc1", "bc2", "sq", "x2c", "bcol", "lnr", "lntmp",
               "c_maskl", "c_masku", "c_maskui", "c_ident", "c_e64", "czero")
        p.phase("P7")
        NT3 = TFh // 512
        sb("hA", (128, FCH * TFh))
        sb("hp", (128, 2 * FCH * 128))
        sb("fsc", (128, 4 * FCH + DCH), F32)
        p.group("gpsimd", "gd")
        for i_, src_ in enumerate([w1s, w2s, h1s, h2s]):
            p.inst("gpsimd", lambda E, ctx, i_=i_, src_=src_: E.dma_start(
                out=ctx["fsc"][:, i_ * FCH:(i_ + 1) * FCH], in_=src_), "gd", 16)
        p.inst("gpsimd", lambda E, ctx: E.dma_start(
            out=ctx["fsc"][:, 4 * FCH:4 * FCH + DCH], in_=w3s), "gd", 16)
        for e in ("tensor", "vector", "scalar"):
            p.wait(e, "gd")

        for hf in range(2):
            toff = hf * TFh
            pw = f"P7w{hf}"
            p.phase(pw)
            sb("hB", (128, FCH * TFh))
            sb("silu", (128, 1024))
            sb("wp", (128, 4 * DCH * 128))
            gates = {0: (p.cnt["ve"], p.cnt["sc"]), 1: (p.cnt["ve"], p.cnt["sc"])}
            wgate = {0: p.cnt["tm"], 1: p.cnt["tm"]}
            gd_tot = {}

            def w12_dma(po):
                slot = po % 2
                p.wait("gpsimd", "tm", wgate[slot])
                p.group("gpsimd", "gd")
                p.inst("gpsimd", lambda E, ctx, po=po, slot=slot: E.dma_start(
                    out=ctx["wp"][:, (2 * slot) * DCH * 128:(2 * slot + 1) * DCH * 128],
                    in_=w1p[po]), "gd", 16)
                p.inst("gpsimd", lambda E, ctx, po=po, slot=slot: E.dma_start(
                    out=ctx["wp"][:, (2 * slot + 1) * DCH * 128:(2 * slot + 2) * DCH * 128],
                    in_=w2p[po]), "gd", 16)
                gd_tot[po] = p.cnt["gd"]

            w12_dma(0)
            for po in range(FCH):
                slot = po % 2
                if po + 1 < FCH:
                    w12_dma(po + 1)
                p.wait("tensor", "gd", gd_tot[po])
                for nt in range(NT3):
                    par = (po * NT3 + nt) % 2
                    pzA = "PA" if par == 0 else "PC"
                    pzB = "PB" if par == 0 else "PZ1"
                    p.wait("tensor", "ve", gates[par][0])
                    p.wait("tensor", "sc", gates[par][1])
                    for w_i, pz in ((0, pzA), (1, pzB)):
                        for kc in range(DCH):
                            p.inst("tensor", lambda E, ctx, po=po, nt=nt, kc=kc, slot=slot,
                                   w_i=w_i, pz=pz, toff=toff: E.matmul(
                                ctx[pz][:, 0:512],
                                ctx["wp"][:, (2 * slot + w_i) * DCH * 128 + kc * 128:
                                          (2 * slot + w_i) * DCH * 128 + (kc + 1) * 128],
                                ctx["y2"][:, kc * T + toff + nt * 512: kc * T + toff + (nt + 1) * 512],
                                start=(kc == 0), stop=(kc == DCH - 1),
                                skip_group_check=True), "tm" if kc == DCH - 1 else None)
                    p.wait("scalar", "tm")
                    p.wait("scalar", "ve", gates[par][0])
                    p.inst("scalar", lambda E, ctx, po=po, nt=nt, par=par, pzA=pzA: E.activation(
                        ctx["silu"][:, par * 512:(par + 1) * 512], ctx[pzA][:, 0:512],
                        AF.Silu, scale=ctx["fsc"][:, po:po + 1]), "sc")
                    p.wait("vector", "sc")
                    p.wait("vector", "tm")
                    p.inst("vector", lambda E, ctx, po=po, nt=nt, par=par, pzB=pzB: E.scalar_tensor_tensor(
                        ctx["hA"][:, po * TFh + nt * 512: po * TFh + (nt + 1) * 512],
                        ctx[pzB][:, 0:512], ctx["fsc"][:, FCH + po: FCH + po + 1],
                        ctx["silu"][:, par * 512:(par + 1) * 512],
                        ALU.mult, ALU.mult), "ve")
                    gates[par] = (p.cnt["ve"], p.cnt["sc"])
                wgate[slot] = p.cnt["tm"]
            if cfg.debug and hf == 1:
                p.wait("gpsimd", "ve")
                p.group("gpsimd", "gd")
                p.inst("gpsimd", lambda E, ctx: E.dma_start(
                    out=dbg["d_hffn"], in_=ctx["hA"].ap()), "gd", 16)
            for (wsrc, scoff, srcb, dstb) in ((h1p, 2 * FCH, "hA", "hB"),
                                              (h2p, 3 * FCH, "hB", "hA")):
                p.wait("tensor", "ve")
                p.wait("tensor", "sc")
                gates2 = {0: p.cnt["sc"], 1: p.cnt["sc"]}
                hgate = {0: p.cnt["tm"], 1: p.cnt["tm"]}
                hgd_tot = {}

                def h_dma(po, wsrc=wsrc):
                    slot = po % 2
                    p.wait("gpsimd", "tm", hgate[slot])
                    p.group("gpsimd", "gd")
                    p.inst("gpsimd", lambda E, ctx, po=po, slot=slot, wsrc=wsrc: E.dma_start(
                        out=ctx["hp"][:, slot * FCH * 128:(slot + 1) * FCH * 128],
                        in_=wsrc[po]), "gd", 16)
                    hgd_tot[po] = p.cnt["gd"]

                h_dma(0)
                for po in range(FCH):
                    slot = po % 2
                    if po + 1 < FCH:
                        h_dma(po + 1)
                    p.wait("tensor", "gd", hgd_tot[po])
                    for nt in range(NT3):
                        par = (po * NT3 + nt) % 2
                        pz = "PZ1" if par == 0 else "PZ2"
                        preg = 512 if par == 0 else 0
                        p.wait("tensor", "sc", gates2[par])
                        for kc in range(FCH):
                            p.inst("tensor", lambda E, ctx, po=po, nt=nt, kc=kc, slot=slot,
                                   pz=pz, preg=preg, srcb=srcb: E.matmul(
                                ctx[pz][:, preg:preg + 512],
                                ctx["hp"][:, slot * FCH * 128 + kc * 128: slot * FCH * 128 + (kc + 1) * 128],
                                ctx[srcb][:, kc * TFh + nt * 512: kc * TFh + (nt + 1) * 512],
                                start=(kc == 0), stop=(kc == FCH - 1),
                                skip_group_check=True), "tm" if kc == FCH - 1 else None)
                        p.wait("scalar", "tm")
                        p.inst("scalar", lambda E, ctx, po=po, nt=nt, pz=pz, preg=preg,
                               scoff=scoff, dstb=dstb: E.activation(
                            ctx[dstb][:, po * TFh + nt * 512: po * TFh + (nt + 1) * 512],
                            ctx[pz][:, preg:preg + 512], AF.Gelu,
                            scale=ctx["fsc"][:, scoff + po: scoff + po + 1]), "sc")
                        gates2[par] = p.cnt["sc"]
                    hgate[slot] = p.cnt["tm"]
                if srcb == "hA":
                    pass
            po7 = f"P7o{hf}"
            p.free("hB", "silu", "wp")
            p.phase(po7)
            sb("fout", (128, DCH * TFh))
            sb("lnr", (1, 2 * T), BF16)
            sb("lntmp", (1, 2048), F32)
            sb("sq3", (128, 4096))
            sb("b3r", (128, TFh)); sb("b3n", (128, TFh))
            sb("b2r", (128, TFh)); sb("b2n", (128, TFh))
            sb("t7", (128, 2048))
            sb("ostg", (128, 1024), F32)
            p.wait("tensor", "sc")
            p.wait("tensor", "ve")
            gates3 = {0: p.cnt["ve"], 1: p.cnt["ve"]}
            w3gate = {0: p.cnt["tm"], 1: p.cnt["tm"]}
            wgd_tot = {}

            def w3_dma(po):
                slot = po % 2
                p.wait("gpsimd", "tm", w3gate[slot])
                p.group("gpsimd", "gd")
                p.inst("gpsimd", lambda E, ctx, po=po, slot=slot: E.dma_start(
                    out=ctx["hp"][:, slot * FCH * 128:(slot + 1) * FCH * 128],
                    in_=w3p[po]), "gd", 16)
                wgd_tot[po] = p.cnt["gd"]

            w3_dma(0)
            for po in range(DCH):
                slot = po % 2
                if po + 1 < DCH:
                    w3_dma(po + 1)
                p.wait("tensor", "gd", wgd_tot[po])
                for nt in range(NT3):
                    par = (po * NT3 + nt) % 2
                    pz = "PZ1" if par == 0 else "PZ2"
                    preg = 512 if par == 0 else 0
                    p.wait("tensor", "ve", gates3[par])
                    for kc in range(FCH):
                        p.inst("tensor", lambda E, ctx, po=po, nt=nt, kc=kc, slot=slot,
                               pz=pz, preg=preg: E.matmul(
                            ctx[pz][:, preg:preg + 512],
                            ctx["hp"][:, slot * FCH * 128 + kc * 128: slot * FCH * 128 + (kc + 1) * 128],
                            ctx["hA"][:, kc * TFh + nt * 512: kc * TFh + (nt + 1) * 512],
                            start=(kc == 0), stop=(kc == FCH - 1),
                            skip_group_check=True), "tm" if kc == FCH - 1 else None)
                    p.wait("vector", "tm")
                    p.inst("vector", lambda E, ctx, po=po, nt=nt, pz=pz, preg=preg: E.tensor_scalar_mul(
                        ctx["fout"][:, po * TFh + nt * 512: po * TFh + (nt + 1) * 512],
                        ctx[pz][:, preg:preg + 512],
                        ctx["fsc"][:, 4 * FCH + po: 4 * FCH + po + 1]), "ve")
                    gates3[par] = p.cnt["ve"]
                w3gate[slot] = p.cnt["tm"]
            for nt in range(NT3):
                p.wait("scalar", "tm")
                p.wait("scalar", "ve")
                for c in range(DCH):
                    reg = c * 512
                    p.inst("scalar", lambda E, ctx, c=c, nt=nt, reg=reg: E.activation(
                        ctx["sq3"][:, reg:reg + 512],
                        ctx["fout"][:, c * TFh + nt * 512: c * TFh + (nt + 1) * 512],
                        AF.Square), "sc")
                    p.wait("tensor", "sc")
                    p.wait("tensor", "ve")
                    p.inst("tensor", lambda E, ctx, c=c, nt=nt: E.matmul(
                        ctx["PA"][0:1, :], ctx["c_onesc"][:, :],
                        ctx["fout"][:, c * TFh + nt * 512: c * TFh + (nt + 1) * 512],
                        start=(c == 0), stop=(c == DCH - 1), skip_group_check=True))
                    p.inst("tensor", lambda E, ctx, c=c, nt=nt, reg=reg: E.matmul(
                        ctx["PB"][0:1, :], ctx["c_onesc"][:, :],
                        ctx["sq3"][:, reg:reg + 512],
                        start=(c == 0), stop=(c == DCH - 1), skip_group_check=True),
                        "tm" if c == DCH - 1 else None)
                p.wait("vector", "tm")
                p.inst("vector", lambda E, ctx: E.tensor_copy(
                    ctx["lntmp"][0:1, 0:512], ctx["PA"][0:1, :]), "ve")
                p.inst("vector", lambda E, ctx: E.tensor_copy(
                    ctx["lntmp"][0:1, 512:1024], ctx["PB"][0:1, :]), "ve")
                p.wait("tensor", "ve")
                p.wait("scalar", "ve")
                ln_stats_math(nt)
                gnt = hf * NT3 + nt
                p.wait("tensor", "ve")
                p.wait("tensor", "sc")
                p.inst("tensor", lambda E, ctx, nt=nt: E.matmul(
                    ctx["PA"][:, :], ctx["c_ones1b"][:, :],
                    ctx["lnr"][0:1, nt * 512:(nt + 1) * 512], start=True, stop=True,
                    skip_group_check=True))
                p.inst("tensor", lambda E, ctx, nt=nt: E.matmul(
                    ctx["PB"][:, :], ctx["c_ones1b"][:, :],
                    ctx["lnr"][0:1, T + nt * 512: T + (nt + 1) * 512], start=True,
                    stop=True, skip_group_check=True))
                p.inst("tensor", lambda E, ctx, gnt=gnt: E.matmul(
                    ctx["PC"][:, :], ctx["c_ones1b"][:, :],
                    ctx["st2"][0:1, T + gnt * 512: T + (gnt + 1) * 512], start=True,
                    stop=True, skip_group_check=True))
                p.inst("tensor", lambda E, ctx, gnt=gnt: E.matmul(
                    ctx["PZ1"][:, 0:512], ctx["c_ones1b"][:, :],
                    ctx["st2"][0:1, gnt * 512:(gnt + 1) * 512], start=True,
                    stop=True, skip_group_check=True), "tm")
                p.wait("vector", "tm")
                p.inst("vector", lambda E, ctx, nt=nt: E.tensor_copy(
                    ctx["b3r"][:, nt * 512:(nt + 1) * 512], ctx["PA"][:, :]), "ve")
                p.inst("vector", lambda E, ctx, nt=nt: E.tensor_copy(
                    ctx["b3n"][:, nt * 512:(nt + 1) * 512], ctx["PB"][:, :]), "ve")
                p.inst("vector", lambda E, ctx, nt=nt: E.tensor_copy(
                    ctx["b2r"][:, nt * 512:(nt + 1) * 512], ctx["PC"][:, :]), "ve")
                p.inst("vector", lambda E, ctx, nt=nt: E.tensor_copy(
                    ctx["b2n"][:, nt * 512:(nt + 1) * 512], ctx["PZ1"][:, 0:512]), "ve")
                p.wait("tensor", "ve")
            sgate = {0: 0, 1: 0}
            for c in range(DCH):
                for nt in range(NT3):
                    par = (c * NT3 + nt) % 2
                    treg = par * 512
                    p.wait("vector", "sd", sgate[par])
                    ta, tb = par * 1024, par * 1024 + 512
                    p.inst("vector", lambda E, ctx, c=c, nt=nt, toff=toff, ta=ta: E.tensor_mul(
                        ctx["t7"][:, ta:ta + 512],
                        ctx["y2"][:, c * T + toff + nt * 512: c * T + toff + (nt + 1) * 512],
                        ctx["b2r"][:, nt * 512:(nt + 1) * 512]), "ve")
                    p.inst("vector", lambda E, ctx, nt=nt, ta=ta: E.tensor_add(
                        ctx["t7"][:, ta:ta + 512], ctx["t7"][:, ta:ta + 512],
                        ctx["b2n"][:, nt * 512:(nt + 1) * 512]), "ve")
                    p.inst("vector", lambda E, ctx, c=c, nt=nt, tb=tb: E.tensor_mul(
                        ctx["t7"][:, tb:tb + 512],
                        ctx["fout"][:, c * TFh + nt * 512: c * TFh + (nt + 1) * 512],
                        ctx["b3r"][:, nt * 512:(nt + 1) * 512]), "ve")
                    p.inst("vector", lambda E, ctx, nt=nt, tb=tb: E.tensor_add(
                        ctx["t7"][:, tb:tb + 512], ctx["t7"][:, tb:tb + 512],
                        ctx["b3n"][:, nt * 512:(nt + 1) * 512]), "ve")
                    p.inst("vector", lambda E, ctx, ta=ta, tb=tb, treg=treg: E.tensor_add(
                        ctx["ostg"][:, treg:treg + 512], ctx["t7"][:, ta:ta + 512],
                        ctx["t7"][:, tb:tb + 512]), "ve")
                    p.wait("sync", "ve")
                    p.group("sync", "sd")
                    p.inst("sync", lambda E, ctx, c=c, nt=nt, toff=toff, treg=treg: E.dma_start(
                        out=out.rearrange("(c p) t -> p c t", p=128)[:, c,
                            toff + nt * 512: toff + (nt + 1) * 512],
                        in_=ctx["ostg"][:, treg:treg + 512]), "sd", 16)
                    sgate[par] = p.cnt["sd"]
            p.free("fout", "sq3", "b3r", "b3n", "b2r", "b2n", "t7", "ostg",
                   "lnr", "lntmp")

    if cfg.stage < 5:
        phsrc = "y1" if cfg.stage < 2 else "qfm"
        for e in ("ve", "sc", "tm"):
            p.wait("gpsimd", e)
        p.group("gpsimd", "gd")
        p.inst("gpsimd", lambda E, ctx, phsrc=phsrc: E.dma_start(out=out_pct, in_=ctx[phsrc].ap().rearrange("p (c t) -> p c t", c=DCH)), "gd", 16)
    p.wait("sync", "sd", p.cnt["sd"])
    p.wait("sync", "gd", p.cnt["gd"])
    p.wait("gpsimd", "gd", p.cnt["gd"])

    p.close()
    import contextlib
    nph = len(p.phases)

    reqs = []
    for pi, ph in enumerate(p.phases):
        for (name, kind, shape, dt, side) in p.allocs[ph]:
            if kind != "sbuf":
                continue
            fidx = nph - 1
            for (fpi, fname) in p.frees:
                if fname == name and fpi >= pi:
                    fidx = fpi
                    break
            size = shape[1] * (4 if dt == F32 else 2)
            reqs.append([pi, ph, name, shape, dt, size, fidx])
    base, top = (nc.sbuf_base + 127) // 128 * 128, nc.sbuf_top
    placed = []
    offs = {}
    for r in sorted(reqs, key=lambda r: (0 if (r[0] == 0 and r[6] == nph - 1) else 1, -r[5])):
        pi, ph, name, shape, dt, size, fidx = r
        cands = [base]
        for (o2, s2, p2, f2) in placed:
            if not (f2 < pi or p2 > fidx):
                cands.append((o2 + s2 + 127) // 128 * 128)
        ok_off = None
        for cand in sorted(cands):
            if cand + size > top:
                continue
            conflict = False
            for (o2, s2, p2, f2) in placed:
                if f2 < pi or p2 > fidx:
                    continue
                if not (cand + size <= o2 or o2 + s2 <= cand):
                    conflict = True
                    break
            if not conflict:
                ok_off = cand
                break
        if ok_off is None:
            occ = [sum(r2[5] + 127 for r2 in reqs if r2[0] <= qi <= r2[6]) for qi in range(nph)]
            raise MemoryError(f"SBUF plan OOM: {name} size {size} phase {ph}; cap {top-base}; per-phase {occ}")
        placed.append((ok_off, size, pi, fidx))
        offs[(pi, name)] = ok_off

    with contextlib.ExitStack() as outer:
        ctx = {}
        for sname in p.sem_names:
            ctx[sname] = outer.enter_context(nc.semaphore(sname))
        for pi, ph in enumerate(p.phases):
            for (name, kind, shape, dt, side) in p.allocs[ph]:
                uname = f"{name}__{ph}"
                if kind == "sbuf":
                    ctx[name] = nc.alloc_sbuf_tensor_at(
                        uname, shape, dt, offset=offs[(pi, name)])
                else:
                    ctx[name] = outer.enter_context(nc.psum_tensor(uname, shape, dt))
            with nc.Block() as block:
                for eng in ENGS:
                    a, b = p.seg[ph][eng]
                    steps = p.steps[eng][a:b]
                    if steps:
                        getattr(block, eng)(lambda E, steps=steps: [f(E, ctx) for f in steps])
    return nc, dbg

def prep_in_maps(cfg: CFG, inp):
    T = cfg.tpc
    x = np.asarray(inp["L0_attn_x"], np.float32)[0]
    xT = np.ascontiguousarray(x.T)
    con = build_consts()
    shared = {
        "wq": _retile_w(np.asarray(inp["L0_attn_Wq"], np.float32)),
        "bq": _cvec(np.asarray(inp["L0_attn_bq"], np.float32)),
        "wk": _retile_w(np.asarray(inp["L0_attn_Wk"], np.float32)),
        "bk": np.asarray(inp["L0_attn_bk"], np.float32).reshape(1, -1),
        "wv": _retile_w(np.asarray(inp["L0_attn_Wv"], np.float32)),
        "bv": np.asarray(inp["L0_attn_bv"], np.float32).reshape(1, -1),
        "wb": _retile_w(np.asarray(inp["L0_attn_Wbeta"], np.float32)),
        "bb": np.asarray(inp["L0_attn_bbeta"], np.float32).reshape(1, -1),
        "wo": _retile_w(np.asarray(inp["L0_attn_Wo"], np.float32)),
        "bo": _cvec(np.asarray(inp["L0_attn_bo"], np.float32)),
        "w1p": _panels(np.ascontiguousarray(np.asarray(inp["L0_ffn_w1_w"], np.float32).T)),
        "w1s": _cvec(np.asarray(inp["L0_ffn_w1_s"], np.float32)),
        "w2p": _panels(np.ascontiguousarray(np.asarray(inp["L0_ffn_w2_w"], np.float32).T)),
        "w2s": _cvec(np.asarray(inp["L0_ffn_w2_s"], np.float32)),
        "h1p": _panels(np.ascontiguousarray(np.asarray(inp["L0_ffn_h1_w"], np.float32).T)),
        "h1s": _cvec(np.asarray(inp["L0_ffn_h1_s"], np.float32)),
        "h2p": _panels(np.ascontiguousarray(np.asarray(inp["L0_ffn_h2_w"], np.float32).T)),
        "h2s": _cvec(np.asarray(inp["L0_ffn_h2_s"], np.float32)),
        "w3p": _panels(np.ascontiguousarray(np.asarray(inp["L0_ffn_w3_w"], np.float32).T)),
        "w3s": _cvec(np.asarray(inp["L0_ffn_w3_s"], np.float32)),
    }
    shared.update({k: np.ascontiguousarray(v) for k, v in con.items()})
    in_maps = []
    for ci in range(cfg.n_cores):
        m = dict(shared)
        m["x"] = np.ascontiguousarray(xT[:, ci * T:(ci + 1) * T])
        pm = np.zeros((64, 8), np.float32)
        pm[:, :ci] = 1.0
        m["pmask"] = pm
        im = np.zeros((64, 8 * 64), np.float32)
        for j in range(8):
            if j >= ci:
                im[:, j * 64:(j + 1) * 64] = np.eye(64, dtype=np.float32)
        m["imask"] = im
        in_maps.append(m)
    return in_maps


def kernel(**inputs):
    cfg = CFG()
    in_maps = prep_in_maps(cfg, inputs)
    nc, _ = build_nc(cfg)
    res = run_bass_kernel_spmd(nc, in_maps, core_ids=list(range(cfg.n_cores)))
    outs = [r["out"] for r in res.results]
    full = np.concatenate(outs, axis=1)
    return np.ascontiguousarray(full.T)[None].astype(np.float32)


# version 09
# speedup vs baseline: 1.3235x; 1.0388x over previous
from dataclasses import dataclass

import numpy as np
import concourse.bass as bass
import concourse.mybir as mybir
from concourse.bass_utils import run_bass_kernel_spmd

PIPE_LVL = 3

F32 = mybir.dt.float32
BF16 = mybir.dt.bfloat16
AF = mybir.ActivationFunctionType
ALU = mybir.AluOpType
AXX = mybir.AxisListType.X

NCORES = 8
DM = 1024
DCH = DM // 128
H = 16
D = 64
C = 128
FH = 4096
FCH = FH // 128
ENGS = ("tensor", "vector", "scalar", "gpsimd", "sync")


@dataclass
class CFG:
    tpc: int = 2048
    n_cores: int = NCORES
    stage: int = 9
    debug: bool = False
    p2c: int = 0
    p2s: int = 99


class Prog:
    def __init__(self):
        self.steps = {e: [] for e in ENGS}
        self.cnt = {}
        self.sem_names = []
        self._dmafloor = {}
        self.phases = []
        self.seg = {}
        self._seg_start = {e: 0 for e in ENGS}
        self.allocs = {}
        self.frees = []

    def phase(self, name):
        if self.phases:
            prev = self.phases[-1]
            self.seg[prev] = {e: (self._seg_start[e], len(self.steps[e])) for e in ENGS}
        self._seg_start = {e: len(self.steps[e]) for e in ENGS}
        self.phases.append(name)
        self.allocs[name] = []

    def close(self):
        prev = self.phases[-1]
        self.seg[prev] = {e: (self._seg_start[e], len(self.steps[e])) for e in ENGS}

    def alloc(self, name, kind, shape, dt, side=None):
        self.allocs[self.phases[-1]].append((name, kind, list(shape), dt, side))

    def free(self, *names):
        for n in names:
            self.frees.append((len(self.phases) - 1, n))

    def sem(self, name):
        self.sem_names.append(name)
        self.cnt[name] = 0

    def raw(self, eng, fn):
        self.steps[eng].append(fn)

    def wait(self, eng, sname, val=None):
        v = self.cnt[sname] if val is None else val
        assert v <= self.cnt[sname], (sname, v, self.cnt[sname])
        if v > 0:
            self.raw(eng, lambda E, ctx, s=sname, v=v: E.wait_ge(ctx[s], v))

    def inst(self, eng, fn, sname=None, n=1):
        if sname is None:
            self.raw(eng, fn)
        else:
            self.cnt[sname] += n
            self.raw(eng, lambda E, ctx, f=fn, s=sname, n=n: f(E, ctx).then_inc(ctx[s], n))

    def dma(self, eng, ch, fn):
        self.inst(eng, fn, ch, 16)

    def group(self, eng, ch):
        key = (eng, ch)
        if self.cnt[ch] > self._dmafloor.get(key, 0):
            self.wait(eng, ch, self.cnt[ch])
            self._dmafloor[key] = self.cnt[ch]


def _retile_w(w):
    I, O = w.shape
    return np.ascontiguousarray(
        w.reshape(I // 128, 128, O).transpose(1, 0, 2)).reshape(128, -1)


def _panels(wT, po=128):
    I, O = wT.shape
    npan = O // po
    a = wT.reshape(I // 128, 128, npan, po).transpose(2, 1, 0, 3)
    return np.ascontiguousarray(a).reshape(npan, 128, (I // 128) * po)


def _cvec(v):
    return np.ascontiguousarray(v.reshape(-1, 128).T)


def build_consts():
    ident = np.eye(128, dtype=np.float32)
    e64 = np.zeros((128, 64), np.float32); e64[:64, :64] = np.eye(64)
    maskl = np.tril(np.ones((128, 128), np.float32), -1)
    masku = np.triu(np.ones((128, 128), np.float32), 1)
    maskui = np.triu(np.ones((128, 128), np.float32), 0)
    ones1 = np.ones((1, 128), np.float32)
    onesc = np.ones((128, 1), np.float32)
    oneh16 = np.zeros((16, 16 * 128), np.float32)
    for h in range(16):
        oneh16[h, h * 128:(h + 1) * 128] = 1.0
    oneh4 = np.zeros((4, 4 * 128), np.float32)
    for i in range(4):
        oneh4[i, i * 128:(i + 1) * 128] = 1.0
    return dict(ident=ident, e64=e64, maskl=maskl, masku=masku, maskui=maskui,
                ones1=ones1, onesc=onesc, oneh16=oneh16, oneh4=oneh4)


def build_nc(cfg: CFG):
    T = cfg.tpc
    NCH = T // C
    NT = T // 512
    nc = bass.Bass("TRN2", target_bir_lowering=False, debug=False,
                   detect_race_conditions=False)

    def din(name, shape, dt=F32):
        return nc.dram_tensor(name, list(shape), dt, kind="ExternalInput").ap()

    x_in = din("x", (DM, T))
    wq = din("wq", (128, DCH * DM)); bq = din("bq", (128, DCH))
    wk = din("wk", (128, DCH * DM)); bk = din("bk", (1, DM))
    wv = din("wv", (128, DCH * DM)); bv = din("bv", (1, DM))
    wb = din("wb", (128, DCH * 16)); bb = din("bb", (1, 16))
    wo = din("wo", (128, DCH * DM)); bo = din("bo", (128, DCH))
    w1p = din("w1p", (FCH, 128, DCH * 128)); w1s = din("w1s", (128, FCH))
    w2p = din("w2p", (FCH, 128, DCH * 128)); w2s = din("w2s", (128, FCH))
    h1p = din("h1p", (FCH, 128, FCH * 128)); h1s = din("h1s", (128, FCH))
    h2p = din("h2p", (FCH, 128, FCH * 128)); h2s = din("h2s", (128, FCH))
    w3p = din("w3p", (DCH, 128, FCH * 128)); w3s = din("w3s", (128, DCH))
    ident = din("ident", (128, 128)); e64 = din("e64", (128, 64))
    maskl = din("maskl", (128, 128)); masku = din("masku", (128, 128))
    maskui = din("maskui", (128, 128))
    ones1 = din("ones1", (1, 128)); onesc = din("onesc", (128, 1))
    oneh16 = din("oneh16", (16, 16 * 128)); oneh4 = din("oneh4", (4, 4 * 128))
    pmask = din("pmask", (64, 8)); imask = din("imask", (64, 8 * 64))

    out = nc.dram_tensor("out", [DM, T], F32, kind="ExternalOutput").ap()
    out_pct = out.rearrange("(c p) t -> p c t", p=128)

    cUB = nc.dram_tensor("cUB", [NCH, 128, H * 64], BF16).ap()
    cUPT = nc.dram_tensor("cUPT", [NCH, 64, H * 128], BF16).ap()
    cZPT = nc.dram_tensor("cZPT", [NCH, 64, H * 64], BF16).ap()
    cB = nc.dram_tensor("cB", [NCH, 128, (H // 2) * 64], BF16).ap()
    agi = nc.dram_tensor("agi", [192, 1024], F32).ap()
    ago = nc.dram_tensor("ago", [cfg.n_cores * 192, 1024], F32).ap()

    dbg = {}
    if cfg.debug:
        def dout(name, shape, dt=F32):
            dbg[name] = nc.dram_tensor(name, list(shape), dt, kind="ExternalOutput").ap()
        dout("d_y1", (128, DCH * T))
        dout("d_ktm", (128, NCH * DM))
        dout("d_vtm", (128, NCH * DM))
        dout("d_qfm", (128, DCH * T))
        dout("d_kfm", (128, DCH * T))
        dout("d_beta", (128, NCH * 16))
        if cfg.stage >= 2:
            dout("d_UB", (NCH, 128, H * 64), BF16)
            dout("d_UPT", (NCH, 64, H * 128), BF16)
            dout("d_ZPT", (NCH, 64, H * 64), BF16)
            dout("d_B", (NCH, 128, (H // 2) * 64), BF16)
            dout("d_zfin", (128, (H // 2) * 128))
        if cfg.stage >= 3:
            dout("d_zcore", (64, H * 64), BF16)
        if cfg.stage >= 4:
            dout("d_ofm", (128, DCH * T))
        if cfg.stage >= 5:
            dout("d_y2", (128, DCH * T))
        if cfg.stage >= 9:
            dout("d_hffn", (128, FCH * (T // 2)), BF16)

    p = Prog()
    for s in ["gd", "sd", "tm", "ve", "sc", "cc"]:
        p.sem(s)

    def sb(name, shape, dt=BF16, side=None):
        p.alloc(name, "sbuf", list(shape), dt, side)

    def ps(name, shape, dt=F32):
        p.alloc(name, "psum", list(shape), dt, None)

    NPOW = 6
    TFh = T // 2

    p.phase("P0")
    ps("PA", (128, 512)); ps("PB", (128, 512)); ps("PC", (128, 512))
    ps("PZ1", (128, 1024)); ps("PZ2", (128, 1024))
    ps("PT", (128, 256), BF16)
    sb("c_ident", (128, 128)); sb("c_e64", (128, 64))
    sb("c_maskl", (128, 128)); sb("c_masku", (128, 128)); sb("c_maskui", (128, 128))
    sb("c_ones1", (1, 128), F32); sb("c_onesc", (128, 1), BF16)
    sb("czero", (128, 128))
    sb("c_ones1b", (1, 128), BF16)
    sb("y1", (128, DCH * T))
    sb("lntmp", (1, 2048), F32)
    sb("lnr", (1, 2 * T), BF16)
    sb("bc1", (128, T)); sb("bc2", (128, T))
    sb("sq", (128, 4096))

    def ln_stats_math(nt, save_rows=None):
        p.inst("scalar", lambda E, ctx: E.mul(ctx["lntmp"][0:1, 1024:1536], ctx["lntmp"][0:1, 0:512], np.float32(1.0 / DM)), "sc")
        p.inst("scalar", lambda E, ctx: E.mul(ctx["lntmp"][0:1, 1536:2048], ctx["lntmp"][0:1, 512:1024], np.float32(1.0 / DM)), "sc")
        p.wait("vector", "sc")
        p.inst("vector", lambda E, ctx: E.tensor_mul(
            ctx["lntmp"][0:1, 0:512], ctx["lntmp"][0:1, 1024:1536], ctx["lntmp"][0:1, 1024:1536]), "ve")
        p.inst("vector", lambda E, ctx: E.tensor_sub(
            ctx["lntmp"][0:1, 512:1024], ctx["lntmp"][0:1, 1536:2048], ctx["lntmp"][0:1, 0:512]), "ve")
        p.wait("scalar", "ve")
        p.inst("scalar", lambda E, ctx: E.activation(
            ctx["lntmp"][0:1, 0:512], ctx["lntmp"][0:1, 512:1024], AF.Ln, bias=np.float32(1e-5)), "sc")
        p.inst("scalar", lambda E, ctx, nt=nt: E.activation(
            ctx["lnr"][0:1, nt * 512:(nt + 1) * 512], ctx["lntmp"][0:1, 0:512],
            AF.Exp, scale=np.float32(-0.5)), "sc")
        p.inst("scalar", lambda E, ctx: E.activation(
            ctx["lntmp"][0:1, 512:1024], ctx["lntmp"][0:1, 0:512],
            AF.Exp, scale=np.float32(-0.5)), "sc")
        if save_rows is not None:
            p.inst("scalar", lambda E, ctx, sr=save_rows, nt=nt: E.activation(
                ctx[sr][0:1, T + nt * 512: T + (nt + 1) * 512], ctx["lntmp"][0:1, 0:512],
                AF.Exp, scale=np.float32(0.5)), "sc")
            p.inst("scalar", lambda E, ctx, sr=save_rows, nt=nt: E.copy(
                ctx[sr][0:1, nt * 512:(nt + 1) * 512], ctx["lntmp"][0:1, 1024:1536]), "sc")
        p.wait("vector", "sc")
        p.inst("vector", lambda E, ctx: E.tensor_mul(
            ctx["lntmp"][0:1, 0:512], ctx["lntmp"][0:1, 1024:1536], ctx["lntmp"][0:1, 512:1024]), "ve")
        p.inst("vector", lambda E, ctx, nt=nt: E.tensor_scalar_mul(
            ctx["lnr"][0:1, T + nt * 512: T + (nt + 1) * 512], ctx["lntmp"][0:1, 0:512], -1.0), "ve")

    def ln_bcast(nt):
        p.wait("tensor", "ve")
        p.wait("tensor", "sc")
        p.inst("tensor", lambda E, ctx, nt=nt: E.matmul(
            ctx["PA"][:, :], ctx["c_ones1b"][:, :],
            ctx["lnr"][0:1, nt * 512:(nt + 1) * 512], start=True, stop=True,
            skip_group_check=True))
        p.inst("tensor", lambda E, ctx, nt=nt: E.matmul(
            ctx["PB"][:, :], ctx["c_ones1b"][:, :],
            ctx["lnr"][0:1, T + nt * 512: T + (nt + 1) * 512], start=True, stop=True,
            skip_group_check=True), "tm")
        p.wait("vector", "tm")
        p.inst("vector", lambda E, ctx, nt=nt: E.tensor_copy(
            ctx["bc1"][:, nt * 512:(nt + 1) * 512], ctx["PA"][:, :]), "ve")
        p.inst("vector", lambda E, ctx, nt=nt: E.tensor_copy(
            ctx["bc2"][:, nt * 512:(nt + 1) * 512], ctx["PB"][:, :]), "ve")
        p.wait("tensor", "ve")

    def ln_feature_major(src, dst, save_rows=None):
        for nt in range(NT):
            p.wait("scalar", "tm")
            for c in range(DCH):
                reg = c * 512
                p.inst("scalar", lambda E, ctx, c=c, nt=nt, reg=reg, src=src: E.activation(
                    ctx["sq"][:, reg:reg + 512],
                    ctx[src][:, c * T + nt * 512: c * T + (nt + 1) * 512],
                    AF.Square), "sc")
                p.wait("tensor", "sc")
                p.inst("tensor", lambda E, ctx, c=c, nt=nt, src=src: E.matmul(
                    ctx["PA"][0:1, :], ctx["c_onesc"][:, :],
                    ctx[src][:, c * T + nt * 512: c * T + (nt + 1) * 512],
                    start=(c == 0), stop=(c == DCH - 1), skip_group_check=True))
                p.inst("tensor", lambda E, ctx, c=c, nt=nt, reg=reg: E.matmul(
                    ctx["PB"][0:1, :], ctx["c_onesc"][:, :],
                    ctx["sq"][:, reg:reg + 512],
                    start=(c == 0), stop=(c == DCH - 1), skip_group_check=True),
                    "tm" if c == DCH - 1 else None)
            p.wait("vector", "tm")
            p.inst("vector", lambda E, ctx, nt=nt: E.tensor_copy(
                ctx["lntmp"][0:1, 0:512], ctx["PA"][0:1, :]), "ve")
            p.inst("vector", lambda E, ctx, nt=nt: E.tensor_copy(
                ctx["lntmp"][0:1, 512:1024], ctx["PB"][0:1, :]), "ve")
            p.wait("tensor", "ve")
            p.wait("scalar", "ve")
            ln_stats_math(nt, save_rows)
            ln_bcast(nt)
        for c in range(DCH):
            p.inst("vector", lambda E, ctx, c=c, src=src, dst=dst: E.tensor_mul(
                ctx[dst][:, c * T:(c + 1) * T], ctx[src][:, c * T:(c + 1) * T],
                ctx["bc1"][:, :]), "ve")
            p.inst("vector", lambda E, ctx, c=c, dst=dst: E.tensor_add(
                ctx[dst][:, c * T:(c + 1) * T], ctx[dst][:, c * T:(c + 1) * T],
                ctx["bc2"][:, :]), "ve")

    consts = [(ident, "c_ident"), (e64, "c_e64"), (maskl, "c_maskl"),
              (masku, "c_masku"), (maskui, "c_maskui"), (ones1, "c_ones1"),
              (onesc, "c_onesc"), (ones1, "c_ones1b")]
    for iap, sname in consts:
        p.inst("gpsimd", lambda E, ctx, iap=iap, sname=sname: E.dma_start(
            out=ctx[sname].ap(), in_=iap), "gd", 16)
    p.inst("gpsimd", lambda E, ctx: E.dma_start(out=ctx["y1"].ap().rearrange("p (c t) -> p c t", c=DCH), in_=x_in.rearrange("(c p) t -> p c t", p=128)), "gd", 16)
    p.inst("vector", lambda E, ctx: E.memset(ctx["czero"][:, :], 0.0), "ve")
    ng0 = p.cnt["gd"]
    for e in ("scalar", "vector", "tensor", "sync"):
        p.wait(e, "gd", ng0)

    ln_feature_major("y1", "y1")

    p.free("bc1", "bc2", "sq")
    p.phase("P1a")
    sb("ktm", (128, NCH * DM))
    sb("vtm", (128, NCH * DM))
    sb("kfm", (128, DCH * T))
    sb("beta", (128, NCH * 16), F32)
    sb("nbeta", (128, NCH * 16), F32)
    sb("betab", (128, NCH * 16), BF16)
    sb("wt", (128, DCH * DM))
    sb("bconst", (128, DM), F32)
    sb("bcol", (128, DCH), F32)
    sb("knr", (128, NCH * 16), F32)
    sb("knrb", (128, NCH * 16), BF16)
    sb("sqk", (128, DM))

    def bias_broadcast(bias_ap, width):
        p.wait("sync", "ve")
        p.wait("sync", "tm")
        p.group("sync", "sd")
        p.inst("sync", lambda E, ctx, b=bias_ap, w=width: E.dma_start(
            out=ctx["lntmp"][0:1, 0:w], in_=b), "sd", 16)
        p.wait("tensor", "sd")
        ntw = (width + 511) // 512
        for nt in range(ntw):
            w0, w1 = nt * 512, min(width, (nt + 1) * 512)
            p.inst("tensor", lambda E, ctx, w0=w0, w1=w1: E.matmul(
                ctx["PA"][:, 0:w1 - w0], ctx["c_ones1"][:, :],
                ctx["lntmp"][0:1, w0:w1], start=True, stop=True, skip_group_check=True), "tm")
            p.wait("vector", "tm")
            p.inst("vector", lambda E, ctx, w0=w0, w1=w1: E.tensor_copy(
                ctx["bconst"][:, w0:w1], ctx["PA"][:, 0:w1 - w0]), "ve")
            p.wait("tensor", "ve")

    def load_w(src_ap, width=DCH * DM):
        p.wait("gpsimd", "tm")
        p.group("gpsimd", "gd")
        p.inst("gpsimd", lambda E, ctx, s=src_ap, w=width: E.dma_start(
            out=ctx["wt"][:, 0:w], in_=s), "gd", 16)

    def proj_tm(dst):
        for tc in range(NCH):
            for nt in range(2):
                pp = "PA" if (tc * 2 + nt) % 2 == 0 else "PB"
                for c in range(DCH):
                    p.inst("tensor", lambda E, ctx, tc=tc, nt=nt, c=c, pp=pp: E.matmul(
                        ctx[pp][:, :],
                        ctx["y1"][:, c * T + tc * 128: c * T + (tc + 1) * 128],
                        ctx["wt"][:, c * DM + nt * 512: c * DM + (nt + 1) * 512],
                        start=(c == 0), stop=(c == DCH - 1), skip_group_check=True),
                        "tm" if c == DCH - 1 else None)
                p.wait("vector", "tm")
                p.inst("vector", lambda E, ctx, dst=dst, tc=tc, nt=nt, pp=pp:
                       E.scalar_tensor_tensor(
                           ctx[dst][:, tc * DM + nt * 512: tc * DM + (nt + 1) * 512],
                           ctx[pp][:, :], 1.0,
                           ctx["bconst"][:, nt * 512:(nt + 1) * 512],
                           ALU.mult, ALU.add), "ve")
                if (tc * 2 + nt) % 2 == 1:
                    p.wait("tensor", "ve")

    def proj_fm(dst, src="y1", bias="bcol"):
        for oc in range(DCH):
            for nt in range(NT):
                pp = "PA" if (oc * NT + nt) % 2 == 0 else "PB"
                for c in range(DCH):
                    p.inst("tensor", lambda E, ctx, oc=oc, nt=nt, c=c, pp=pp, src=src: E.matmul(
                        ctx[pp][:, :],
                        ctx["wt"][:, c * DM + oc * 128: c * DM + (oc + 1) * 128],
                        ctx[src][:, c * T + nt * 512: c * T + (nt + 1) * 512],
                        start=(c == 0), stop=(c == DCH - 1), skip_group_check=True),
                        "tm" if c == DCH - 1 else None)
                p.wait("scalar", "tm")
                p.inst("scalar", lambda E, ctx, dst=dst, oc=oc, nt=nt, pp=pp, bias=bias:
                       E.activation(
                           ctx[dst][:, oc * T + nt * 512: oc * T + (nt + 1) * 512],
                           ctx[pp][:, :], AF.Identity,
                           bias=ctx[bias][:, oc:oc + 1]), "sc")
                if (oc * NT + nt) % 2 == 1:
                    p.wait("tensor", "sc")

    load_w(wk)
    bias_broadcast(bk, DM)
    p.wait("tensor", "gd")
    proj_tm("ktm")
    load_w(wv)
    bias_broadcast(bv, DM)
    p.wait("tensor", "gd")
    p.wait("vector", "gd")
    proj_tm("vtm")
    p.wait("gpsimd", "tm")
    p.group("gpsimd", "gd")
    p.inst("gpsimd", lambda E, ctx: E.dma_start(out=ctx["wt"][:, 0:DCH * 16], in_=wb), "gd", 16)
    bias_broadcast(bb, 16)
    p.wait("tensor", "gd")
    for tc in range(NCH):
        pp = "PA" if tc % 2 == 0 else "PB"
        for c in range(DCH):
            p.inst("tensor", lambda E, ctx, tc=tc, c=c, pp=pp: E.matmul(
                ctx[pp][:, 0:16],
                ctx["y1"][:, c * T + tc * 128: c * T + (tc + 1) * 128],
                ctx["wt"][:, c * 16:(c + 1) * 16],
                start=(c == 0), stop=(c == DCH - 1), skip_group_check=True),
                "tm" if c == DCH - 1 else None)
        p.wait("vector", "tm")
        p.inst("vector", lambda E, ctx, tc=tc, pp=pp: E.tensor_add(
            ctx["beta"][:, tc * 16:(tc + 1) * 16], ctx[pp][:, 0:16],
            ctx["bconst"][:, 0:16]), "ve")
        if tc % 2 == 1:
            p.wait("tensor", "ve")
    p.wait("scalar", "ve")
    p.inst("scalar", lambda E, ctx: E.activation(
        ctx["beta"][:, :], ctx["beta"][:, :], AF.Sigmoid), "sc")
    p.inst("scalar", lambda E, ctx: E.activation(
        ctx["nbeta"][:, :], ctx["beta"][:, :], AF.Identity, scale=np.float32(-1.0)), "sc")
    p.inst("scalar", lambda E, ctx: E.copy(ctx["betab"][:, :], ctx["beta"][:, :]), "sc")
    for tc in range(NCH):
        p.wait("scalar", "ve")
        p.wait("scalar", "sc")
        p.inst("scalar", lambda E, ctx, tc=tc: E.activation(
            ctx["sqk"][:, 0:DM], ctx["ktm"][:, tc * DM:(tc + 1) * DM], AF.Square), "sc")
        p.wait("vector", "sc")
        p.inst("vector", lambda E, ctx, tc=tc: E.reduce_sum(
            ctx["knr"][:, tc * 16:(tc + 1) * 16].rearrange("p (h o) -> p h o", o=1),
            ctx["sqk"][:, 0:DM].rearrange("p (h d) -> p h d", d=64), AXX), "ve")
    p.wait("scalar", "ve")
    p.inst("scalar", lambda E, ctx: E.activation(ctx["knr"][:, :], ctx["knr"][:, :], AF.Ln), "sc")
    p.inst("scalar", lambda E, ctx: E.activation(ctx["knr"][:, :], ctx["knr"][:, :], AF.Exp, scale=np.float32(0.5)), "sc")
    p.inst("scalar", lambda E, ctx: E.add(ctx["knr"][:, :], ctx["knr"][:, :], np.float32(1e-6)), "sc")
    p.wait("vector", "sc")
    p.inst("vector", lambda E, ctx: E.reciprocal(ctx["knr"][:, :], ctx["knr"][:, :]), "ve")
    p.inst("vector", lambda E, ctx: E.tensor_copy(ctx["knrb"][:, :], ctx["knr"][:, :]), "ve")
    for tc in range(NCH):
        p.inst("vector", lambda E, ctx, tc=tc: E.tensor_mul(
            ctx["ktm"][:, tc * DM:(tc + 1) * DM].rearrange("p (h d) -> p h d", d=64),
            ctx["ktm"][:, tc * DM:(tc + 1) * DM].rearrange("p (h d) -> p h d", d=64),
            ctx["knrb"][:, tc * 16:(tc + 1) * 16].rearrange("p (h o) -> p h o", o=1)
            .broadcast_to([128, 16, 64])), "ve")
    p.wait("tensor", "ve")
    for tc in range(NCH):
        for oc in range(DCH):
            half = ((tc * DCH + oc) % 2) * 128
            p.inst("tensor", lambda E, ctx, tc=tc, oc=oc, half=half: E.transpose(
                ctx["PT"][:, half:half + 128],
                ctx["ktm"][:, tc * DM + oc * 128: tc * DM + (oc + 1) * 128],
                ctx["c_ident"][:, :]), "tm")
            p.wait("vector", "tm")
            p.inst("vector", lambda E, ctx, tc=tc, oc=oc, half=half: E.tensor_copy(
                ctx["kfm"][:, oc * T + tc * 128: oc * T + (tc + 1) * 128],
                ctx["PT"][:, half:half + 128]), "ve")
            if (tc * DCH + oc) % 2 == 1:
                p.wait("tensor", "ve")

    p.free("bconst", "knr", "knrb", "sqk")
    p.phase("P1b")
    sb("qfm", (128, DCH * T))
    load_w(wq)
    p.wait("tensor", "gd")
    p.wait("scalar", "gd")
    p.inst("gpsimd", lambda E, ctx: E.dma_start(out=ctx["bcol"][:, 0:DCH], in_=bq), "gd", 16)
    p.wait("scalar", "gd")
    proj_fm("qfm")

    if cfg.debug:
        def dump(src, dname, reg=None):
            for e in ("ve", "sc", "tm"):
                p.wait("gpsimd", e)
            p.group("gpsimd", "gd")
            if reg is None:
                p.inst("gpsimd", lambda E, ctx, src=src, dname=dname: E.dma_start(
                    out=dbg[dname], in_=ctx[src].ap()), "gd", 16)
            else:
                p.inst("gpsimd", lambda E, ctx, src=src, dname=dname, reg=reg: E.dma_start(
                    out=dbg[dname], in_=ctx[src][reg]), "gd", 16)
        dump("y1", "d_y1"); dump("ktm", "d_ktm"); dump("vtm", "d_vtm")
        dump("qfm", "d_qfm"); dump("kfm", "d_kfm"); dump("beta", "d_beta")

    if cfg.stage >= 2:
        p.free("y1", "wt")
        p.phase("P2")
        sb("c_oneh16", (16, 16 * 128))
        sb("Tt", (128, 256))
        sb("pmw", (128, 1024))
        sb("Nlow", (128, 256)); sb("Nup", (128, 256))
        sb("chR", (128, 512))
        sb("mbB", (128, 256))
        sb("zbd", (128, (H // 2) * 256))
        sb("vg", (128, 128))
        sb("rh", (128, 256))
        sb("Ush", (128, 256))
        sb("stUB", (128, H * 64))
        sb("stUPT", (64, H * 128))
        sb("stZPT", (64, H * 64))
        sb("brow", (16, 128))
        sb("zfin", (128, 1024), F32)
        sb("ptfin", (64, 1024), F32)
        p.group("gpsimd", "gd")
        p.inst("gpsimd", lambda E, ctx: E.dma_start(
            out=ctx["c_oneh16"].ap(), in_=oneh16), "gd", 16)
        for e in ("tensor", "vector", "scalar"):
            p.wait(e, "gd")
        p.wait("tensor", "ve")
        for pr in range(8):
            p.inst("tensor", lambda E, ctx, pr=pr: E.matmul(
                ctx["PZ1"][0:64, pr * 128: pr * 128 + 64], ctx["c_e64"][:, :],
                ctx["c_e64"][:, :], start=(pr % 4 == 0), stop=False,
                tile_position=(0, 0), skip_group_check=True))
            p.inst("tensor", lambda E, ctx, pr=pr: E.matmul(
                ctx["PZ1"][64:128, pr * 128: pr * 128 + 64], ctx["c_e64"][:, :],
                ctx["c_e64"][:, :], start=(pr % 4 == 0), stop=False,
                tile_position=(0, 64), skip_group_check=True))
            for j in range(2):
                p.inst("tensor", lambda E, ctx, pr=pr, j=j: E.matmul(
                    ctx["PZ2"][0:64, pr * 128 + j * 64: pr * 128 + (j + 1) * 64],
                    ctx["c_e64"][:, :], ctx["c_e64"][:, :],
                    start=(pr % 4 == 0 and j == 0), stop=False,
                    tile_position=(0, 0), skip_group_check=True))
            p.inst("tensor", lambda E, ctx, pr=pr: E.matmul(
                ctx["PZ2"][64:128, pr * 128:(pr + 1) * 128], ctx["c_e64"][:, :],
                ctx["czero"][:, :], start=(pr % 4 == 0), stop=False,
                tile_position=(0, 64), skip_group_check=True))
            for j in range(2):
                hb = j * 64
                p.inst("tensor", lambda E, ctx, pr=pr, hb=hb: E.matmul(
                    ctx["PZ1"][hb:hb + 64, pr * 128 + 64:(pr + 1) * 128],
                    ctx["c_e64"][:, :], ctx["czero"][:, 0:64], start=False,
                    stop=False, tile_position=(0, hb), skip_group_check=True),
                    "tm" if (pr == 7 and hb == 64) else None)
        p.wait("vector", "tm")
        p.inst("vector", lambda E, ctx: E.memset(ctx["zbd"][:, :], 0.0), "ve")
        for pr in range(8):
            p.inst("vector", lambda E, ctx, pr=pr: E.tensor_copy(
                ctx["zbd"][0:64, pr * 256: pr * 256 + 128],
                ctx["PZ1"][0:64, pr * 128: (pr + 1) * 128]), "ve")
            p.inst("vector", lambda E, ctx, pr=pr: E.tensor_copy(
                ctx["zbd"][64:128, pr * 256 + 128: (pr + 1) * 256],
                ctx["PZ1"][64:128, pr * 128: (pr + 1) * 128]), "ve")

        def nprep_pair(c, pr):
            p.wait("vector", "tm")
            for j in range(2):
                jj = j * 128
                h = 2 * pr + j
                p.inst("vector", lambda E, ctx, jj=jj: E.scalar_tensor_tensor(
                    ctx["mbB"][:, jj:jj + 128], ctx["PB"][:, jj:jj + 128], 1.0,
                    ctx["c_masku"][:, :], ALU.mult, ALU.mult), "ve")
                p.inst("vector", lambda E, ctx, jj=jj: E.scalar_tensor_tensor(
                    ctx["Nup"][:, jj:jj + 128], ctx["PA"][:, jj:jj + 128], 1.0,
                    ctx["mbB"][:, jj:jj + 128], ALU.mult, ALU.mult), "ve")
                p.inst("vector", lambda E, ctx, jj=jj, c=c, h=h: E.scalar_tensor_tensor(
                    ctx["Nlow"][:, jj:jj + 128], ctx["PA"][:, jj:jj + 128],
                    ctx["beta"][:, c * 16 + h: c * 16 + h + 1],
                    ctx["c_maskl"][:, :], ALU.mult, ALU.mult), "ve")
                p.inst("vector", lambda E, ctx, j=j: E.tensor_sub(
                    ctx["chR"][:, j * 256: j * 256 + 128], ctx["c_ident"][:, :],
                    ctx["Nup"][:, j * 128:(j + 1) * 128]), "ve")
            return p.cnt["ve"]

        def inv_merged(c, pr, evg0, lim=99):
            if lim < 21:
                return
            evg = [evg0, evg0]
            for it in range(1, 8):
                cur = (it - 1) % 2
                nxt = it % 2
                for j in range(2):
                    pz = "PC" if j == 0 else "PB"
                    if it == 1:
                        pN = ("Nlow", slice(j * 128, j * 128 + 128))
                        pT = ("Nup", slice(j * 128, j * 128 + 128))
                    else:
                        off = j * 512 + cur * 256
                        pN = ("pmw", slice(off, off + 128))
                        pT = ("pmw", slice(off + 128, off + 256))
                    p.wait("tensor", "ve", evg[j])
                    last_sq = (it >= 7)
                    if it <= 6:
                        p.inst("tensor", lambda E, ctx, pN=pN, pT=pT, pz=pz: E.matmul(
                            ctx[pz][:, 0:128], ctx[pT[0]][:, pT[1]],
                            ctx[pN[0]][:, pN[1]], start=True, stop=True,
                            skip_group_check=True))
                        if it < 6:
                            p.inst("tensor", lambda E, ctx, pN=pN, pT=pT, pz=pz: E.matmul(
                                ctx[pz][:, 128:256], ctx[pN[0]][:, pN[1]],
                                ctx[pT[0]][:, pT[1]], start=True, stop=True,
                                skip_group_check=True), None if it >= 2 else "tm")
                        elif it < 2:
                            pass
                    if it >= 2:
                        src = j * 256 + ((it - 2) % 2) * 128
                        p.inst("tensor", lambda E, ctx, pN=pN, src=src, pz=pz: E.matmul(
                            ctx[pz][:, 256:384], ctx[pN[0]][:, pN[1]],
                            ctx["chR"][:, src:src + 128], start=True, stop=True,
                            skip_group_check=True), "tm")
                    p.wait("vector", "tm")
                    if it <= 6:
                        wid = 256 if it < 6 else 128
                        off2 = j * 512 + nxt * 256
                        p.inst("vector", lambda E, ctx, pz=pz, off2=off2, wid=wid: E.tensor_copy(
                            ctx["pmw"][:, off2:off2 + wid], ctx[pz][:, 0:wid]), "ve")
                    if it >= 2:
                        src = j * 256 + ((it - 2) % 2) * 128
                        dst = j * 256 + ((it - 1) % 2) * 128
                        if it < 7:
                            p.inst("vector", lambda E, ctx, src=src, dst=dst, pz=pz: E.scalar_tensor_tensor(
                                ctx["chR"][:, dst:dst + 128], ctx[pz][:, 256:384],
                                1.0, ctx["chR"][:, src:src + 128], ALU.mult, ALU.add), "ve")
                        else:
                            p.inst("vector", lambda E, ctx, src=src, j=j, pz=pz: E.scalar_tensor_tensor(
                                ctx["Tt"][:, j * 128:(j + 1) * 128],
                                ctx[pz][:, 256:384],
                                1.0, ctx["chR"][:, src:src + 128], ALU.mult, ALU.add), "ve")
                    evg[j] = p.cnt["ve"]


        for c in range(NCH if cfg.p2c == 0 else min(NCH, cfg.p2c)):
            if cfg.p2s < 1:
                continue
            p.wait("tensor", "ve")
            p.inst("tensor", lambda E, ctx, c=c: E.transpose(
                ctx["PT"][0:16, 0:128],
                ctx["betab"][:, c * 16:(c + 1) * 16], ctx["c_ident"][:, :]), "tm")
            p.wait("vector", "tm")
            p.inst("vector", lambda E, ctx: E.tensor_copy(
                ctx["brow"][:, :], ctx["PT"][0:16, 0:128]), "ve")
            p.wait("vector", "sd")
            p.wait("scalar", "sd")
            p.wait("vector", "tm")
            p.inst("vector", lambda E, ctx: E.tensor_copy(
                ctx["stZPT"][0:64, :], ctx["PZ2"][0:64, :]), "ve")
            p.wait("sync", "sc")
            p.wait("sync", "ve")
            p.group("sync", "sd")
            p.inst("sync", lambda E, ctx, c=c: E.dma_start(
                out=cZPT[c], in_=ctx["stZPT"].ap()), "sd", 16)
            p.inst("sync", lambda E, ctx, c=c: E.dma_start(
                out=cB[c][0:64, :],
                in_=ctx["zbd"].ap().rearrange("p (r x) -> p r x", r=8)[0:64, :, 64:128]), "sd", 16)
            p.inst("sync", lambda E, ctx, c=c: E.dma_start(
                out=cB[c][64:128, :],
                in_=ctx["zbd"].ap().rearrange("p (r x) -> p r x", r=8)[64:128, :, 192:256]), "sd", 16)
            if cfg.debug:
                p.group("sync", "sd")
                p.inst("sync", lambda E, ctx, c=c: E.dma_start(
                    out=dbg["d_ZPT"][c], in_=ctx["stZPT"].ap()), "sd", 16)
                p.inst("sync", lambda E, ctx, c=c: E.dma_start(
                    out=dbg["d_B"][c][0:64, :],
                    in_=ctx["zbd"].ap().rearrange("p (r x) -> p r x", r=8)[0:64, :, 64:128]), "sd", 16)
                p.inst("sync", lambda E, ctx, c=c: E.dma_start(
                    out=dbg["d_B"][c][64:128, :],
                    in_=ctx["zbd"].ap().rearrange("p (r x) -> p r x", r=8)[64:128, :, 192:256]), "sd", 16)
            zbd_cached = p.cnt["sd"]
            def front_pair(c, pr):
                for j in range(2):
                    hb = j * 64
                    p.inst("tensor", lambda E, ctx, pr=pr, c=c, hb=hb, j=j: E.matmul(
                        ctx["PA"][:, j * 128: j * 128 + 128],
                        ctx["kfm"][:, pr * T + c * 128: pr * T + (c + 1) * 128][hb:hb + 64, :],
                        ctx["kfm"][:, pr * T + c * 128: pr * T + (c + 1) * 128][hb:hb + 64, :],
                        start=True, stop=True, tile_position=(hb, 0),
                        skip_group_check=True))
                    p.inst("tensor", lambda E, ctx, pr=pr, j=j: E.matmul(
                        ctx["PB"][:, j * 128:(j + 1) * 128],
                        ctx["c_oneh16"][:, (2 * pr + j) * 128:(2 * pr + j + 1) * 128],
                        ctx["brow"][:, :], start=True, stop=True,
                        skip_group_check=True), "tm" if j == 1 else None)

            nprep_gate = 0
            for pr in range(8):
                if pr == 0:
                    p.wait("vector", "sd", zbd_cached)
                if pr == 0 or PIPE_LVL < 3:
                    p.wait("tensor", "ve")
                    p.wait("tensor", "sc")
                    front_pair(c, pr)
                    nprep_gate = nprep_pair(c, pr)
                if cfg.p2s < 2:
                    continue
                inv_merged(c, pr, nprep_gate, lim=cfg.p2s)
                if cfg.p2s < 3:
                    continue
                p.wait("tensor", "ve")
                p.inst("tensor", lambda E, ctx, pr=pr, c=c: E.matmul(
                    ctx["PB"][:, 256:512],
                    ctx["kfm"][:, pr * T + c * 128: pr * T + (c + 1) * 128],
                    ctx["zbd"][:, pr * 256:(pr + 1) * 256],
                    start=True, stop=True, skip_group_check=True), "tm")

                if cfg.p2s < 4:
                    continue
                p.wait("vector", "tm")
                for j in range(2):
                    h = 2 * pr + j
                    p.inst("vector", lambda E, ctx, j=j, c=c, h=h: E.tensor_scalar(
                        ctx["rh"][:, j * 128: j * 128 + 64],
                        ctx["PB"][:, 256 + j * 128: 256 + j * 128 + 64],
                        ctx["beta"][:, c * 16 + h: c * 16 + h + 1], -1.0,
                        ALU.mult, ALU.mult), "ve")
                    p.inst("vector", lambda E, ctx, j=j, c=c, h=h: E.tensor_sub(
                        ctx["vg"][:, j * 64:(j + 1) * 64],
                        ctx["vtm"][:, c * DM + h * 64: c * DM + (h + 1) * 64],
                        ctx["PB"][:, 256 + j * 128 + 64: 256 + (j + 1) * 128]), "ve")
                    p.inst("vector", lambda E, ctx, j=j, c=c, h=h: E.tensor_scalar(
                        ctx["rh"][:, j * 128 + 64:(j + 1) * 128],
                        ctx["vg"][:, j * 64:(j + 1) * 64],
                        ctx["beta"][:, c * 16 + h: c * 16 + h + 1], None,
                        ALU.mult), "ve")
                rhat_end = p.cnt["ve"]
                if cfg.p2s < 5:
                    continue
                if pr < 7 and PIPE_LVL >= 3:
                    p.wait("tensor", "ve", rhat_end)
                    front_pair(c, pr + 1)
                p.wait("tensor", "sc")
                p.wait("tensor", "ve", rhat_end)
                for j in range(2):
                    p.inst("tensor", lambda E, ctx, j=j: E.matmul(
                        ctx["PC"][:, j * 128:(j + 1) * 128],
                        ctx["Tt"][:, j * 128:(j + 1) * 128],
                        ctx["rh"][:, j * 128:(j + 1) * 128],
                        start=True, stop=True, skip_group_check=True))
                    p.inst("tensor", lambda E, ctx, j=j: E.matmul(
                        ctx["PC"][0:64, 256 + j * 128: 256 + (j + 1) * 128],
                        ctx["rh"][:, j * 128: j * 128 + 64],
                        ctx["Tt"][:, j * 128:(j + 1) * 128],
                        start=True, stop=True, skip_group_check=True),
                        "tm" if j == 1 else None)
                p.wait("vector", "tm")
                p.wait("scalar", "tm")
                for j in range(2):
                    h = 2 * pr + j
                    p.inst("vector", lambda E, ctx, j=j: E.tensor_copy(
                        ctx["Ush"][:, j * 128:(j + 1) * 128],
                        ctx["PC"][:, j * 128:(j + 1) * 128]), "ve")
                    p.inst("vector", lambda E, ctx, j=j, h=h: E.tensor_copy(
                        ctx["stUB"][:, h * 64:(h + 1) * 64],
                        ctx["PC"][:, j * 128 + 64:(j + 1) * 128]), "ve")
                    p.inst("vector", lambda E, ctx, j=j, h=h: E.tensor_copy(
                        ctx["stUPT"][0:64, h * 128:(h + 1) * 128],
                        ctx["PC"][0:64, 256 + j * 128: 256 + (j + 1) * 128]), "ve")
                evac_end = p.cnt["ve"]
                if pr < 7 and PIPE_LVL >= 3:
                    nprep_gate = nprep_pair(c, pr + 1)
                if cfg.p2s < 6:
                    continue
                p.wait("tensor", "ve", evac_end)
                for j in range(2):
                    h = 2 * pr + j
                    hb = j * 64
                    p.inst("tensor", lambda E, ctx, j=j, h=h, hb=hb, pr=pr, c=c: E.matmul(
                        ctx["PZ1"][hb:hb + 64, pr * 128:(pr + 1) * 128],
                        ctx["ktm"][:, c * DM + h * 64: c * DM + (h + 1) * 64],
                        ctx["Ush"][:, j * 128:(j + 1) * 128],
                        start=False, stop=False, tile_position=(0, hb),
                        skip_group_check=True))
                    p.inst("tensor", lambda E, ctx, j=j, h=h, pr=pr, c=c: E.matmul(
                        ctx["PZ2"][:, pr * 128 + j * 64: pr * 128 + (j + 1) * 64],
                        ctx["Ush"][:, j * 128:(j + 1) * 128],
                        ctx["ktm"][:, c * DM + h * 64: c * DM + (h + 1) * 64],
                        start=False, stop=False, tile_position=(0, 0),
                        skip_group_check=True), "tm" if j == 1 else None)
                p.wait("vector", "tm")
                p.inst("vector", lambda E, ctx, pr=pr: E.tensor_copy(
                    ctx["zbd"][0:64, pr * 256: pr * 256 + 128],
                    ctx["PZ1"][0:64, pr * 128:(pr + 1) * 128]), "ve")
                p.inst("vector", lambda E, ctx, pr=pr: E.tensor_copy(
                    ctx["zbd"][64:128, pr * 256 + 128:(pr + 1) * 256],
                    ctx["PZ1"][64:128, pr * 128:(pr + 1) * 128]), "ve")
            p.wait("sync", "ve")
            p.wait("sync", "sc")
            p.group("sync", "sd")
            p.inst("sync", lambda E, ctx, c=c: E.dma_start(
                out=cUB[c], in_=ctx["stUB"].ap()), "sd", 16)
            p.inst("sync", lambda E, ctx, c=c: E.dma_start(
                out=cUPT[c], in_=ctx["stUPT"].ap()), "sd", 16)
            if cfg.debug:
                p.group("sync", "sd")
                p.inst("sync", lambda E, ctx, c=c: E.dma_start(
                    out=dbg["d_UB"][c], in_=ctx["stUB"].ap()), "sd", 16)
                p.inst("sync", lambda E, ctx, c=c: E.dma_start(
                    out=dbg["d_UPT"][c], in_=ctx["stUPT"].ap()), "sd", 16)
        if cfg.debug:
            p.wait("vector", "tm")
            p.inst("vector", lambda E, ctx: E.tensor_copy(
                ctx["zfin"][:, :], ctx["PZ1"][:, :]), "ve")
            p.wait("gpsimd", "ve")
            p.group("gpsimd", "gd")
            p.inst("gpsimd", lambda E, ctx: E.dma_start(
                out=dbg["d_zfin"], in_=ctx["zfin"].ap()), "gd", 16)

    if cfg.stage >= 3:
        p.free("ktm", "vtm", "beta", "nbeta", "betab", "Tt", "pmw",
               "Nlow", "Nup", "chR", "mbB", "zbd", "vg", "rh", "Ush",
               "stUB", "stUPT", "stZPT", "brow", "c_oneh16")
        p.phase("P3")
        sb("gA", (64, 8 * DM), F32)
        sb("gB", (64, 8 * DM), F32)
        sb("zcore", (64, H * 64))
        sb("pj", (64, DM)); sb("bj", (64, DM), F32)
        sb("c_pmask", (64, 8), F32)
        sb("c_imask", (64, 8 * 64), F32)
        NC_ = cfg.n_cores
        p.group("gpsimd", "gd")
        p.inst("gpsimd", lambda E, ctx: E.dma_start(out=ctx["c_pmask"].ap(), in_=pmask), "gd", 16)
        p.inst("gpsimd", lambda E, ctx: E.dma_start(out=ctx["c_imask"].ap(), in_=imask), "gd", 16)
        p.wait("vector", "gd")
        p.wait("vector", "tm")
        p.inst("vector", lambda E, ctx: E.tensor_copy(
            ctx["zfin"][:, :], ctx["PZ1"][:, :]), "ve")
        p.wait("scalar", "tm")
        p.inst("scalar", lambda E, ctx: E.copy(
            ctx["ptfin"][0:64, :], ctx["PZ2"][0:64, :]), "sc")
        p.wait("sync", "ve")
        p.wait("sync", "sc")
        p.group("sync", "sd")
        p.inst("sync", lambda E, ctx: E.dma_start(
            out=agi[0:128, :], in_=ctx["zfin"].ap()), "sd", 16)
        p.inst("sync", lambda E, ctx: E.dma_start(
            out=agi[128:192, :], in_=ctx["ptfin"].ap()), "sd", 16)
        p.wait("gpsimd", "sd")
        p.inst("gpsimd", lambda E, ctx: E.collective_compute(
            "AllGather", ALU.bypass, replica_groups=[list(range(NC_))],
            ins=[agi], outs=[ago]), "cc")
        p.wait("gpsimd", "cc")
        p.group("gpsimd", "gd")
        for j in range(NC_):
            p.inst("gpsimd", lambda E, ctx, j=j: E.dma_start(
                out=ctx["gB"][0:64, j * 1024:(j + 1) * 1024],
                in_=ago[j * 192 + 128: j * 192 + 192, :]), "gd", 16)
            p.inst("gpsimd", lambda E, ctx, j=j: E.dma_start(
                out=ctx["gA"][0:64, j * 1024:(j + 1) * 1024]
                .rearrange("p (r x) -> p r x", x=128)[:, :, 0:64],
                in_=ago[j * 192: j * 192 + 64, :]
                .rearrange("p (r x) -> p r x", r=8)[:, :, 64:128]), "gd", 16)
            p.inst("gpsimd", lambda E, ctx, j=j: E.dma_start(
                out=ctx["gA"][0:64, j * 1024:(j + 1) * 1024]
                .rearrange("p (r x) -> p r x", x=128)[:, :, 64:128],
                in_=ago[j * 192 + 64: j * 192 + 128, :]
                .rearrange("p (r x) -> p r x", r=8)[:, :, 64:128]), "gd", 16)
        p.wait("vector", "gd")
        p.inst("vector", lambda E, ctx: E.memset(ctx["zcore"][:, :], 0.0), "ve")
        for j in range(NC_):
            p.inst("vector", lambda E, ctx, j=j: E.scalar_tensor_tensor(
                ctx["pj"][0:64, :].rearrange("p (h d) -> p h d", d=64),
                ctx["gB"][0:64, j * 1024:(j + 1) * 1024].rearrange("p (h d) -> p h d", d=64),
                ctx["c_pmask"][0:64, j:j + 1],
                ctx["c_imask"][0:64, j * 64:(j + 1) * 64]
                .rearrange("p (o d) -> p o d", o=1).broadcast_to([64, 16, 64]),
                ALU.mult, ALU.add), "ve")
            p.inst("vector", lambda E, ctx, j=j: E.tensor_scalar_mul(
                ctx["bj"][0:64, :], ctx["gA"][0:64, j * 1024:(j + 1) * 1024],
                ctx["c_pmask"][0:64, j:j + 1]), "ve")
            p.wait("tensor", "ve")
            for h in range(H):
                p.inst("tensor", lambda E, ctx, h=h: E.matmul(
                    ctx["PZ1"][0:64, h * 64:(h + 1) * 64],
                    ctx["pj"][0:64, h * 64:(h + 1) * 64],
                    ctx["zcore"][0:64, h * 64:(h + 1) * 64],
                    start=True, stop=True, skip_group_check=True),
                    "tm" if h == H - 1 else None)
            p.wait("vector", "tm")
            p.inst("vector", lambda E, ctx: E.scalar_tensor_tensor(
                ctx["zcore"][0:64, :], ctx["PZ1"][0:64, 0:1024], 1.0,
                ctx["bj"][0:64, :], ALU.mult, ALU.add), "ve")
        if cfg.debug:
            p.wait("gpsimd", "ve")
            p.group("gpsimd", "gd")
            p.inst("gpsimd", lambda E, ctx: E.dma_start(
                out=dbg["d_zcore"], in_=ctx["zcore"].ap()), "gd", 16)

    if cfg.stage >= 4:
        p.free("zfin", "ptfin", "gA", "gB", "pj", "bj", "c_pmask", "c_imask")
        p.phase("P4")
        sb("ofm", (128, DCH * T))
        sb("ldUB", (128, 2 * H * 64))
        sb("ldUPT", (64, 2 * H * 128))
        sb("ldZPT", (64, 2 * H * 64))
        sb("ldB", (128, 2 * (H // 2) * 64))
        sb("z0pad", (128, H * 64))
        sb("urel", (128, 128))
        sb("ltp", (128, 256))
        p.wait("vector", "cc")
        p.inst("vector", lambda E, ctx: E.memset(ctx["z0pad"][:, :], 0.0), "ve")
        for c in range(NCH):
            buf = c % 2
            p.wait("sync", "ve")
            p.group("sync", "sd")
            p.inst("sync", lambda E, ctx, c=c, buf=buf: E.dma_start(
                out=ctx["ldUB"][:, buf * H * 64:(buf + 1) * H * 64], in_=cUB[c]), "sd", 16)
            p.inst("sync", lambda E, ctx, c=c, buf=buf: E.dma_start(
                out=ctx["ldUPT"][0:64, buf * H * 128:(buf + 1) * H * 128], in_=cUPT[c]), "sd", 16)
            p.inst("sync", lambda E, ctx, c=c, buf=buf: E.dma_start(
                out=ctx["ldZPT"][0:64, buf * H * 64:(buf + 1) * H * 64], in_=cZPT[c]), "sd", 16)
            p.inst("sync", lambda E, ctx, c=c, buf=buf: E.dma_start(
                out=ctx["ldB"][:, buf * 512:(buf + 1) * 512], in_=cB[c]), "sd", 16)
            p.wait("tensor", "sd")
            oUB = buf * H * 64
            oUPT = buf * H * 128
            oZPT = buf * H * 64
            oB = buf * 512
            for pr in range(8):
                ccols = slice(pr * T + c * 128, pr * T + (c + 1) * 128)
                p.wait("tensor", "ve")
                for j in range(2):
                    h = 2 * pr + j
                    hb = j * 64
                    p.inst("tensor", lambda E, ctx, j=j, h=h, hb=hb, pr=pr, c=c: E.matmul(
                        ctx["PA"][:, j * 128:(j + 1) * 128],
                        ctx["kfm"][:, pr * T + c * 128: pr * T + (c + 1) * 128][hb:hb + 64, :],
                        ctx["qfm"][:, pr * T + c * 128: pr * T + (c + 1) * 128][hb:hb + 64, :],
                        start=True, stop=True, tile_position=(hb, 0),
                        skip_group_check=True))
                    p.inst("tensor", lambda E, ctx, j=j, h=h, oUPT=oUPT: E.matmul(
                        ctx["PB"][:, j * 64:(j + 1) * 64],
                        ctx["ldUPT"][0:64, oUPT + h * 128: oUPT + (h + 1) * 128],
                        ctx["zcore"][0:64, h * 64:(h + 1) * 64],
                        start=True, stop=True, skip_group_check=True))
                    p.inst("tensor", lambda E, ctx, j=j, h=h, hb=hb, oZPT=oZPT: E.matmul(
                        ctx["PB"][hb:hb + 64, 128 + j * 64: 128 + (j + 1) * 64],
                        ctx["ldZPT"][0:64, oZPT + h * 64: oZPT + (h + 1) * 64],
                        ctx["zcore"][0:64, h * 64:(h + 1) * 64],
                        start=True, stop=True, tile_position=(0, hb),
                        skip_group_check=True), "tm" if j == 1 else None)
                p.wait("vector", "tm")
                for j in range(2):
                    h = 2 * pr + j
                    hb = j * 64
                    p.inst("vector", lambda E, ctx, j=j: E.scalar_tensor_tensor(
                        ctx["ltp"][:, j * 128:(j + 1) * 128],
                        ctx["PA"][:, j * 128:(j + 1) * 128], 1.0,
                        ctx["c_maskui"][:, :], ALU.mult, ALU.mult), "ve")
                    p.inst("vector", lambda E, ctx, j=j, h=h, oUB=oUB: E.scalar_tensor_tensor(
                        ctx["urel"][:, j * 64:(j + 1) * 64],
                        ctx["PB"][:, j * 64:(j + 1) * 64], 1.0,
                        ctx["ldUB"][:, oUB + h * 64: oUB + (h + 1) * 64],
                        ALU.mult, ALU.add), "ve")
                    p.inst("vector", lambda E, ctx, j=j, h=h, hb=hb, pr=pr, oB=oB: E.scalar_tensor_tensor(
                        ctx["z0pad"][hb:hb + 64, h * 64:(h + 1) * 64],
                        ctx["PB"][hb:hb + 64, 128 + j * 64: 128 + (j + 1) * 64], 1.0,
                        ctx["ldB"][hb:hb + 64, oB + pr * 64: oB + (pr + 1) * 64],
                        ALU.mult, ALU.add), "ve")
                p.wait("tensor", "ve")
                for j in range(2):
                    h = 2 * pr + j
                    hb = j * 64
                    p.inst("tensor", lambda E, ctx, j=j, h=h, hb=hb, pr=pr, c=c: E.matmul(
                        ctx["PC"][hb:hb + 64, 0:128],
                        ctx["z0pad"][:, h * 64:(h + 1) * 64],
                        ctx["qfm"][:, pr * T + c * 128: pr * T + (c + 1) * 128],
                        start=True, stop=False, tile_position=(0, hb),
                        skip_group_check=True))
                    p.inst("tensor", lambda E, ctx, j=j, hb=hb: E.matmul(
                        ctx["PC"][hb:hb + 64, 0:128],
                        ctx["urel"][:, j * 64:(j + 1) * 64],
                        ctx["ltp"][:, j * 128:(j + 1) * 128],
                        start=False, stop=(j == 1), tile_position=(0, hb),
                        skip_group_check=True), "tm" if j == 1 else None)
                p.wait("vector", "tm")
                p.inst("vector", lambda E, ctx, pr=pr, c=c: E.tensor_copy(
                    ctx["ofm"][:, pr * T + c * 128: pr * T + (c + 1) * 128],
                    ctx["PC"][:, 0:128]), "ve")
        if cfg.debug:
            p.wait("gpsimd", "ve")
            p.group("gpsimd", "gd")
            p.inst("gpsimd", lambda E, ctx: E.dma_start(
                out=dbg["d_ofm"], in_=ctx["ofm"].ap()), "gd", 16)

    if cfg.stage >= 5:
        p.free("kfm", "qfm", "zcore", "ldUB", "ldUPT", "ldZPT", "ldB",
               "z0pad", "urel", "ltp")
        p.phase("P5")
        sb("wt", (128, DCH * DM))
        sb("anorm", (128, DCH * T))
        sb("bcol", (128, DCH), F32)
        p.wait("gpsimd", "tm")
        p.group("gpsimd", "gd")
        p.inst("gpsimd", lambda E, ctx: E.dma_start(out=ctx["wt"][:, :], in_=wo), "gd", 16)
        p.inst("gpsimd", lambda E, ctx: E.dma_start(out=ctx["bcol"][:, 0:DCH], in_=bo), "gd", 16)
        p.wait("tensor", "gd")
        p.wait("scalar", "gd")
        proj_fm("anorm", src="ofm", bias="bcol")

    if cfg.stage >= 5:
        p.free("ofm", "wt")
        p.phase("P6")
        sb("xf", (128, DCH * T), F32)
        sb("y2", (128, DCH * T))
        sb("bc1", (128, T)); sb("bc2", (128, T))
        sb("sq", (128, 4096))
        sb("x2c", (128, 4096))
        sb("st2", (1, 2 * T), BF16)
        p.group("gpsimd", "gd")
        p.inst("gpsimd", lambda E, ctx: E.dma_start(
            out=ctx["xf"].ap().rearrange("p (c t) -> p c t", c=DCH),
            in_=x_in.rearrange("(c p) t -> p c t", p=128)), "gd", 16)
        for e in ("tensor", "vector", "scalar"):
            p.wait(e, "gd")
        ln_feature_major("anorm", "anorm")
        for nt in range(NT):
            p.wait("scalar", "tm")
            p.wait("vector", "tm")
            for c in range(DCH):
                reg = c * 512
                p.inst("vector", lambda E, ctx, c=c, nt=nt, reg=reg: E.tensor_add(
                    ctx["x2c"][:, reg:reg + 512],
                    ctx["xf"][:, c * T + nt * 512: c * T + (nt + 1) * 512],
                    ctx["anorm"][:, c * T + nt * 512: c * T + (nt + 1) * 512]), "ve")
                p.wait("scalar", "ve")
                p.inst("scalar", lambda E, ctx, c=c, reg=reg: E.activation(
                    ctx["sq"][:, reg:reg + 512], ctx["x2c"][:, reg:reg + 512],
                    AF.Square), "sc")
                p.wait("tensor", "sc")
                p.inst("tensor", lambda E, ctx, c=c, reg=reg: E.matmul(
                    ctx["PA"][0:1, :], ctx["c_onesc"][:, :],
                    ctx["x2c"][:, reg:reg + 512],
                    start=(c == 0), stop=(c == DCH - 1), skip_group_check=True))
                p.inst("tensor", lambda E, ctx, c=c, reg=reg: E.matmul(
                    ctx["PB"][0:1, :], ctx["c_onesc"][:, :],
                    ctx["sq"][:, reg:reg + 512],
                    start=(c == 0), stop=(c == DCH - 1), skip_group_check=True),
                    "tm" if c == DCH - 1 else None)
            p.wait("vector", "tm")
            p.inst("vector", lambda E, ctx: E.tensor_copy(
                ctx["lntmp"][0:1, 0:512], ctx["PA"][0:1, :]), "ve")
            p.inst("vector", lambda E, ctx: E.tensor_copy(
                ctx["lntmp"][0:1, 512:1024], ctx["PB"][0:1, :]), "ve")
            p.wait("tensor", "ve")
            p.wait("scalar", "ve")
            ln_stats_math(nt, save_rows="st2")
            ln_bcast(nt)
        for c in range(DCH):
            p.inst("vector", lambda E, ctx, c=c: E.tensor_add(
                ctx["x2c"][:, 0:T], ctx["xf"][:, c * T:(c + 1) * T],
                ctx["anorm"][:, c * T:(c + 1) * T]), "ve")
            p.inst("vector", lambda E, ctx, c=c: E.tensor_mul(
                ctx["y2"][:, c * T:(c + 1) * T], ctx["x2c"][:, 0:T],
                ctx["bc1"][:, :]), "ve")
            p.inst("vector", lambda E, ctx, c=c: E.tensor_add(
                ctx["y2"][:, c * T:(c + 1) * T], ctx["y2"][:, c * T:(c + 1) * T],
                ctx["bc2"][:, :]), "ve")
        if cfg.debug and cfg.stage >= 5:
            p.wait("gpsimd", "ve")
            p.group("gpsimd", "gd")
            p.inst("gpsimd", lambda E, ctx: E.dma_start(
                out=dbg["d_y2"], in_=ctx["y2"].ap()), "gd", 16)

    if cfg.stage >= 9:
        p.free("anorm", "xf", "bc1", "bc2", "sq", "x2c", "bcol", "lnr", "lntmp",
               "c_maskl", "c_masku", "c_maskui", "c_ident", "c_e64", "czero")
        p.phase("P7")
        NT3 = TFh // 512
        sb("hA", (128, FCH * TFh))
        sb("hp", (128, 2 * FCH * 128))
        sb("fsc", (128, 4 * FCH + DCH), F32)
        p.group("gpsimd", "gd")
        for i_, src_ in enumerate([w1s, w2s, h1s, h2s]):
            p.inst("gpsimd", lambda E, ctx, i_=i_, src_=src_: E.dma_start(
                out=ctx["fsc"][:, i_ * FCH:(i_ + 1) * FCH], in_=src_), "gd", 16)
        p.inst("gpsimd", lambda E, ctx: E.dma_start(
            out=ctx["fsc"][:, 4 * FCH:4 * FCH + DCH], in_=w3s), "gd", 16)
        for e in ("tensor", "vector", "scalar"):
            p.wait(e, "gd")

        for hf in range(2):
            toff = hf * TFh
            pw = f"P7w{hf}"
            p.phase(pw)
            sb("hB", (128, FCH * TFh))
            sb("silu", (128, 1024))
            sb("wp", (128, 4 * DCH * 128))
            gates = {0: (p.cnt["ve"], p.cnt["sc"]), 1: (p.cnt["ve"], p.cnt["sc"])}
            wgate = {0: p.cnt["tm"], 1: p.cnt["tm"]}
            gd_tot = {}

            def w12_dma(po):
                slot = po % 2
                p.wait("gpsimd", "tm", wgate[slot])
                p.group("gpsimd", "gd")
                p.inst("gpsimd", lambda E, ctx, po=po, slot=slot: E.dma_start(
                    out=ctx["wp"][:, (2 * slot) * DCH * 128:(2 * slot + 1) * DCH * 128],
                    in_=w1p[po]), "gd", 16)
                p.inst("gpsimd", lambda E, ctx, po=po, slot=slot: E.dma_start(
                    out=ctx["wp"][:, (2 * slot + 1) * DCH * 128:(2 * slot + 2) * DCH * 128],
                    in_=w2p[po]), "gd", 16)
                gd_tot[po] = p.cnt["gd"]

            w12_dma(0)
            for po in range(FCH):
                slot = po % 2
                if po + 1 < FCH:
                    w12_dma(po + 1)
                p.wait("tensor", "gd", gd_tot[po])
                for nt in range(NT3):
                    par = (po * NT3 + nt) % 2
                    pzA = "PA" if par == 0 else "PC"
                    pzB = "PB" if par == 0 else "PZ1"
                    p.wait("tensor", "ve", gates[par][0])
                    p.wait("tensor", "sc", gates[par][1])
                    for w_i, pz in ((0, pzA), (1, pzB)):
                        for kc in range(DCH):
                            p.inst("tensor", lambda E, ctx, po=po, nt=nt, kc=kc, slot=slot,
                                   w_i=w_i, pz=pz, toff=toff: E.matmul(
                                ctx[pz][:, 0:512],
                                ctx["wp"][:, (2 * slot + w_i) * DCH * 128 + kc * 128:
                                          (2 * slot + w_i) * DCH * 128 + (kc + 1) * 128],
                                ctx["y2"][:, kc * T + toff + nt * 512: kc * T + toff + (nt + 1) * 512],
                                start=(kc == 0), stop=(kc == DCH - 1),
                                skip_group_check=True), "tm" if kc == DCH - 1 else None)
                    p.wait("scalar", "tm")
                    p.wait("scalar", "ve", gates[par][0])
                    p.inst("scalar", lambda E, ctx, po=po, nt=nt, par=par, pzA=pzA: E.activation(
                        ctx["silu"][:, par * 512:(par + 1) * 512], ctx[pzA][:, 0:512],
                        AF.Silu, scale=ctx["fsc"][:, po:po + 1]), "sc")
                    p.wait("vector", "sc")
                    p.wait("vector", "tm")
                    p.inst("vector", lambda E, ctx, po=po, nt=nt, par=par, pzB=pzB: E.scalar_tensor_tensor(
                        ctx["hA"][:, po * TFh + nt * 512: po * TFh + (nt + 1) * 512],
                        ctx[pzB][:, 0:512], ctx["fsc"][:, FCH + po: FCH + po + 1],
                        ctx["silu"][:, par * 512:(par + 1) * 512],
                        ALU.mult, ALU.mult), "ve")
                    gates[par] = (p.cnt["ve"], p.cnt["sc"])
                wgate[slot] = p.cnt["tm"]
            if cfg.debug and hf == 1:
                p.wait("gpsimd", "ve")
                p.group("gpsimd", "gd")
                p.inst("gpsimd", lambda E, ctx: E.dma_start(
                    out=dbg["d_hffn"], in_=ctx["hA"].ap()), "gd", 16)
            for (wsrc, scoff, srcb, dstb) in ((h1p, 2 * FCH, "hA", "hB"),
                                              (h2p, 3 * FCH, "hB", "hA")):
                p.wait("tensor", "ve")
                p.wait("tensor", "sc")
                gates2 = {0: p.cnt["sc"], 1: p.cnt["sc"]}
                hgate = {0: p.cnt["tm"], 1: p.cnt["tm"]}
                hgd_tot = {}

                def h_dma(po, wsrc=wsrc):
                    slot = po % 2
                    p.wait("gpsimd", "tm", hgate[slot])
                    p.group("gpsimd", "gd")
                    p.inst("gpsimd", lambda E, ctx, po=po, slot=slot, wsrc=wsrc: E.dma_start(
                        out=ctx["hp"][:, slot * FCH * 128:(slot + 1) * FCH * 128],
                        in_=wsrc[po]), "gd", 16)
                    hgd_tot[po] = p.cnt["gd"]

                h_dma(0)
                for po in range(FCH):
                    slot = po % 2
                    if po + 1 < FCH:
                        h_dma(po + 1)
                    p.wait("tensor", "gd", hgd_tot[po])
                    for nt in range(NT3):
                        par = (po * NT3 + nt) % 2
                        pz = "PZ1" if par == 0 else "PZ2"
                        preg = 512 if par == 0 else 0
                        p.wait("tensor", "sc", gates2[par])
                        for kc in range(FCH):
                            p.inst("tensor", lambda E, ctx, po=po, nt=nt, kc=kc, slot=slot,
                                   pz=pz, preg=preg, srcb=srcb: E.matmul(
                                ctx[pz][:, preg:preg + 512],
                                ctx["hp"][:, slot * FCH * 128 + kc * 128: slot * FCH * 128 + (kc + 1) * 128],
                                ctx[srcb][:, kc * TFh + nt * 512: kc * TFh + (nt + 1) * 512],
                                start=(kc == 0), stop=(kc == FCH - 1),
                                skip_group_check=True), "tm" if kc == FCH - 1 else None)
                        p.wait("scalar", "tm")
                        p.inst("scalar", lambda E, ctx, po=po, nt=nt, pz=pz, preg=preg,
                               scoff=scoff, dstb=dstb: E.activation(
                            ctx[dstb][:, po * TFh + nt * 512: po * TFh + (nt + 1) * 512],
                            ctx[pz][:, preg:preg + 512], AF.Gelu,
                            scale=ctx["fsc"][:, scoff + po: scoff + po + 1]), "sc")
                        gates2[par] = p.cnt["sc"]
                    hgate[slot] = p.cnt["tm"]
                if srcb == "hA":
                    pass
            po7 = f"P7o{hf}"
            p.free("hB", "silu", "wp")
            p.phase(po7)
            sb("fout", (128, DCH * TFh))
            sb("lnr", (1, 2 * T), BF16)
            sb("lntmp", (1, 2048), F32)
            sb("sq3", (128, 4096))
            sb("b3r", (128, TFh)); sb("b3n", (128, TFh))
            sb("b2r", (128, TFh)); sb("b2n", (128, TFh))
            sb("t7", (128, 2048))
            sb("ostg", (128, 1024), F32)
            p.wait("tensor", "sc")
            p.wait("tensor", "ve")
            gates3 = {0: p.cnt["ve"], 1: p.cnt["ve"]}
            w3gate = {0: p.cnt["tm"], 1: p.cnt["tm"]}
            wgd_tot = {}

            def w3_dma(po):
                slot = po % 2
                p.wait("gpsimd", "tm", w3gate[slot])
                p.group("gpsimd", "gd")
                p.inst("gpsimd", lambda E, ctx, po=po, slot=slot: E.dma_start(
                    out=ctx["hp"][:, slot * FCH * 128:(slot + 1) * FCH * 128],
                    in_=w3p[po]), "gd", 16)
                wgd_tot[po] = p.cnt["gd"]

            w3_dma(0)
            for po in range(DCH):
                slot = po % 2
                if po + 1 < DCH:
                    w3_dma(po + 1)
                p.wait("tensor", "gd", wgd_tot[po])
                for nt in range(NT3):
                    par = (po * NT3 + nt) % 2
                    pz = "PZ1" if par == 0 else "PZ2"
                    preg = 512 if par == 0 else 0
                    p.wait("tensor", "ve", gates3[par])
                    for kc in range(FCH):
                        p.inst("tensor", lambda E, ctx, po=po, nt=nt, kc=kc, slot=slot,
                               pz=pz, preg=preg: E.matmul(
                            ctx[pz][:, preg:preg + 512],
                            ctx["hp"][:, slot * FCH * 128 + kc * 128: slot * FCH * 128 + (kc + 1) * 128],
                            ctx["hA"][:, kc * TFh + nt * 512: kc * TFh + (nt + 1) * 512],
                            start=(kc == 0), stop=(kc == FCH - 1),
                            skip_group_check=True), "tm" if kc == FCH - 1 else None)
                    p.wait("vector", "tm")
                    p.inst("vector", lambda E, ctx, po=po, nt=nt, pz=pz, preg=preg: E.tensor_scalar_mul(
                        ctx["fout"][:, po * TFh + nt * 512: po * TFh + (nt + 1) * 512],
                        ctx[pz][:, preg:preg + 512],
                        ctx["fsc"][:, 4 * FCH + po: 4 * FCH + po + 1]), "ve")
                    gates3[par] = p.cnt["ve"]
                w3gate[slot] = p.cnt["tm"]
            for nt in range(NT3):
                p.wait("scalar", "tm")
                p.wait("scalar", "ve")
                for c in range(DCH):
                    reg = c * 512
                    p.inst("scalar", lambda E, ctx, c=c, nt=nt, reg=reg: E.activation(
                        ctx["sq3"][:, reg:reg + 512],
                        ctx["fout"][:, c * TFh + nt * 512: c * TFh + (nt + 1) * 512],
                        AF.Square), "sc")
                    p.wait("tensor", "sc")
                    p.wait("tensor", "ve")
                    p.inst("tensor", lambda E, ctx, c=c, nt=nt: E.matmul(
                        ctx["PA"][0:1, :], ctx["c_onesc"][:, :],
                        ctx["fout"][:, c * TFh + nt * 512: c * TFh + (nt + 1) * 512],
                        start=(c == 0), stop=(c == DCH - 1), skip_group_check=True))
                    p.inst("tensor", lambda E, ctx, c=c, nt=nt, reg=reg: E.matmul(
                        ctx["PB"][0:1, :], ctx["c_onesc"][:, :],
                        ctx["sq3"][:, reg:reg + 512],
                        start=(c == 0), stop=(c == DCH - 1), skip_group_check=True),
                        "tm" if c == DCH - 1 else None)
                p.wait("vector", "tm")
                p.inst("vector", lambda E, ctx: E.tensor_copy(
                    ctx["lntmp"][0:1, 0:512], ctx["PA"][0:1, :]), "ve")
                p.inst("vector", lambda E, ctx: E.tensor_copy(
                    ctx["lntmp"][0:1, 512:1024], ctx["PB"][0:1, :]), "ve")
                p.wait("tensor", "ve")
                p.wait("scalar", "ve")
                ln_stats_math(nt)
                gnt = hf * NT3 + nt
                p.wait("tensor", "ve")
                p.wait("tensor", "sc")
                p.inst("tensor", lambda E, ctx, nt=nt: E.matmul(
                    ctx["PA"][:, :], ctx["c_ones1b"][:, :],
                    ctx["lnr"][0:1, nt * 512:(nt + 1) * 512], start=True, stop=True,
                    skip_group_check=True))
                p.inst("tensor", lambda E, ctx, nt=nt: E.matmul(
                    ctx["PB"][:, :], ctx["c_ones1b"][:, :],
                    ctx["lnr"][0:1, T + nt * 512: T + (nt + 1) * 512], start=True,
                    stop=True, skip_group_check=True))
                p.inst("tensor", lambda E, ctx, gnt=gnt: E.matmul(
                    ctx["PC"][:, :], ctx["c_ones1b"][:, :],
                    ctx["st2"][0:1, T + gnt * 512: T + (gnt + 1) * 512], start=True,
                    stop=True, skip_group_check=True))
                p.inst("tensor", lambda E, ctx, gnt=gnt: E.matmul(
                    ctx["PZ1"][:, 0:512], ctx["c_ones1b"][:, :],
                    ctx["st2"][0:1, gnt * 512:(gnt + 1) * 512], start=True,
                    stop=True, skip_group_check=True), "tm")
                p.wait("vector", "tm")
                p.inst("vector", lambda E, ctx, nt=nt: E.tensor_copy(
                    ctx["b3r"][:, nt * 512:(nt + 1) * 512], ctx["PA"][:, :]), "ve")
                p.inst("vector", lambda E, ctx, nt=nt: E.tensor_copy(
                    ctx["b3n"][:, nt * 512:(nt + 1) * 512], ctx["PB"][:, :]), "ve")
                p.inst("vector", lambda E, ctx, nt=nt: E.tensor_copy(
                    ctx["b2r"][:, nt * 512:(nt + 1) * 512], ctx["PC"][:, :]), "ve")
                p.inst("vector", lambda E, ctx, nt=nt: E.tensor_copy(
                    ctx["b2n"][:, nt * 512:(nt + 1) * 512], ctx["PZ1"][:, 0:512]), "ve")
                p.wait("tensor", "ve")
            sgate = {0: 0, 1: 0}
            for c in range(DCH):
                for nt in range(NT3):
                    par = (c * NT3 + nt) % 2
                    treg = par * 512
                    p.wait("vector", "sd", sgate[par])
                    ta, tb = par * 1024, par * 1024 + 512
                    p.inst("vector", lambda E, ctx, c=c, nt=nt, toff=toff, ta=ta: E.tensor_mul(
                        ctx["t7"][:, ta:ta + 512],
                        ctx["y2"][:, c * T + toff + nt * 512: c * T + toff + (nt + 1) * 512],
                        ctx["b2r"][:, nt * 512:(nt + 1) * 512]), "ve")
                    p.inst("vector", lambda E, ctx, nt=nt, ta=ta: E.tensor_add(
                        ctx["t7"][:, ta:ta + 512], ctx["t7"][:, ta:ta + 512],
                        ctx["b2n"][:, nt * 512:(nt + 1) * 512]), "ve")
                    p.inst("vector", lambda E, ctx, c=c, nt=nt, tb=tb: E.tensor_mul(
                        ctx["t7"][:, tb:tb + 512],
                        ctx["fout"][:, c * TFh + nt * 512: c * TFh + (nt + 1) * 512],
                        ctx["b3r"][:, nt * 512:(nt + 1) * 512]), "ve")
                    p.inst("vector", lambda E, ctx, nt=nt, tb=tb: E.tensor_add(
                        ctx["t7"][:, tb:tb + 512], ctx["t7"][:, tb:tb + 512],
                        ctx["b3n"][:, nt * 512:(nt + 1) * 512]), "ve")
                    p.inst("vector", lambda E, ctx, ta=ta, tb=tb, treg=treg: E.tensor_add(
                        ctx["ostg"][:, treg:treg + 512], ctx["t7"][:, ta:ta + 512],
                        ctx["t7"][:, tb:tb + 512]), "ve")
                    p.wait("sync", "ve")
                    p.group("sync", "sd")
                    p.inst("sync", lambda E, ctx, c=c, nt=nt, toff=toff, treg=treg: E.dma_start(
                        out=out.rearrange("(c p) t -> p c t", p=128)[:, c,
                            toff + nt * 512: toff + (nt + 1) * 512],
                        in_=ctx["ostg"][:, treg:treg + 512]), "sd", 16)
                    sgate[par] = p.cnt["sd"]
            p.free("fout", "sq3", "b3r", "b3n", "b2r", "b2n", "t7", "ostg",
                   "lnr", "lntmp")

    if cfg.stage < 5:
        phsrc = "y1" if cfg.stage < 2 else "qfm"
        for e in ("ve", "sc", "tm"):
            p.wait("gpsimd", e)
        p.group("gpsimd", "gd")
        p.inst("gpsimd", lambda E, ctx, phsrc=phsrc: E.dma_start(out=out_pct, in_=ctx[phsrc].ap().rearrange("p (c t) -> p c t", c=DCH)), "gd", 16)
    p.wait("sync", "sd", p.cnt["sd"])
    p.wait("sync", "gd", p.cnt["gd"])
    p.wait("gpsimd", "gd", p.cnt["gd"])

    p.close()
    import contextlib
    nph = len(p.phases)

    reqs = []
    for pi, ph in enumerate(p.phases):
        for (name, kind, shape, dt, side) in p.allocs[ph]:
            if kind != "sbuf":
                continue
            fidx = nph - 1
            for (fpi, fname) in p.frees:
                if fname == name and fpi >= pi:
                    fidx = fpi
                    break
            size = shape[1] * (4 if dt == F32 else 2)
            reqs.append([pi, ph, name, shape, dt, size, fidx])
    base, top = (nc.sbuf_base + 127) // 128 * 128, nc.sbuf_top
    placed = []
    offs = {}
    for r in sorted(reqs, key=lambda r: (0 if (r[0] == 0 and r[6] == nph - 1) else 1, -r[5])):
        pi, ph, name, shape, dt, size, fidx = r
        cands = [base]
        for (o2, s2, p2, f2) in placed:
            if not (f2 < pi or p2 > fidx):
                cands.append((o2 + s2 + 127) // 128 * 128)
        ok_off = None
        for cand in sorted(cands):
            if cand + size > top:
                continue
            conflict = False
            for (o2, s2, p2, f2) in placed:
                if f2 < pi or p2 > fidx:
                    continue
                if not (cand + size <= o2 or o2 + s2 <= cand):
                    conflict = True
                    break
            if not conflict:
                ok_off = cand
                break
        if ok_off is None:
            occ = [sum(r2[5] + 127 for r2 in reqs if r2[0] <= qi <= r2[6]) for qi in range(nph)]
            raise MemoryError(f"SBUF plan OOM: {name} size {size} phase {ph}; cap {top-base}; per-phase {occ}")
        placed.append((ok_off, size, pi, fidx))
        offs[(pi, name)] = ok_off

    with contextlib.ExitStack() as outer:
        ctx = {}
        for sname in p.sem_names:
            ctx[sname] = outer.enter_context(nc.semaphore(sname))
        for pi, ph in enumerate(p.phases):
            for (name, kind, shape, dt, side) in p.allocs[ph]:
                uname = f"{name}__{ph}"
                if kind == "sbuf":
                    ctx[name] = nc.alloc_sbuf_tensor_at(
                        uname, shape, dt, offset=offs[(pi, name)])
                else:
                    ctx[name] = outer.enter_context(nc.psum_tensor(uname, shape, dt))
            with nc.Block() as block:
                for eng in ENGS:
                    a, b = p.seg[ph][eng]
                    steps = p.steps[eng][a:b]
                    if steps:
                        getattr(block, eng)(lambda E, steps=steps: [f(E, ctx) for f in steps])
    return nc, dbg

def prep_in_maps(cfg: CFG, inp):
    T = cfg.tpc
    x = np.asarray(inp["L0_attn_x"], np.float32)[0]
    xT = np.ascontiguousarray(x.T)
    con = build_consts()
    shared = {
        "wq": _retile_w(np.asarray(inp["L0_attn_Wq"], np.float32)),
        "bq": _cvec(np.asarray(inp["L0_attn_bq"], np.float32)),
        "wk": _retile_w(np.asarray(inp["L0_attn_Wk"], np.float32)),
        "bk": np.asarray(inp["L0_attn_bk"], np.float32).reshape(1, -1),
        "wv": _retile_w(np.asarray(inp["L0_attn_Wv"], np.float32)),
        "bv": np.asarray(inp["L0_attn_bv"], np.float32).reshape(1, -1),
        "wb": _retile_w(np.asarray(inp["L0_attn_Wbeta"], np.float32)),
        "bb": np.asarray(inp["L0_attn_bbeta"], np.float32).reshape(1, -1),
        "wo": _retile_w(np.asarray(inp["L0_attn_Wo"], np.float32)),
        "bo": _cvec(np.asarray(inp["L0_attn_bo"], np.float32)),
        "w1p": _panels(np.ascontiguousarray(np.asarray(inp["L0_ffn_w1_w"], np.float32).T)),
        "w1s": _cvec(np.asarray(inp["L0_ffn_w1_s"], np.float32)),
        "w2p": _panels(np.ascontiguousarray(np.asarray(inp["L0_ffn_w2_w"], np.float32).T)),
        "w2s": _cvec(np.asarray(inp["L0_ffn_w2_s"], np.float32)),
        "h1p": _panels(np.ascontiguousarray(np.asarray(inp["L0_ffn_h1_w"], np.float32).T)),
        "h1s": _cvec(np.asarray(inp["L0_ffn_h1_s"], np.float32)),
        "h2p": _panels(np.ascontiguousarray(np.asarray(inp["L0_ffn_h2_w"], np.float32).T)),
        "h2s": _cvec(np.asarray(inp["L0_ffn_h2_s"], np.float32)),
        "w3p": _panels(np.ascontiguousarray(np.asarray(inp["L0_ffn_w3_w"], np.float32).T)),
        "w3s": _cvec(np.asarray(inp["L0_ffn_w3_s"], np.float32)),
    }
    shared.update({k: np.ascontiguousarray(v) for k, v in con.items()})
    in_maps = []
    for ci in range(cfg.n_cores):
        m = dict(shared)
        m["x"] = np.ascontiguousarray(xT[:, ci * T:(ci + 1) * T])
        pm = np.zeros((64, 8), np.float32)
        pm[:, :ci] = 1.0
        m["pmask"] = pm
        im = np.zeros((64, 8 * 64), np.float32)
        for j in range(8):
            if j >= ci:
                im[:, j * 64:(j + 1) * 64] = np.eye(64, dtype=np.float32)
        m["imask"] = im
        in_maps.append(m)
    return in_maps


def kernel(**inputs):
    cfg = CFG()
    in_maps = prep_in_maps(cfg, inputs)
    nc, _ = build_nc(cfg)
    res = run_bass_kernel_spmd(nc, in_maps, core_ids=list(range(cfg.n_cores)))
    outs = [r["out"] for r in res.results]
    full = np.concatenate(outs, axis=1)
    return np.ascontiguousarray(full.T)[None].astype(np.float32)


# version 10
# speedup vs baseline: 1.3235x; 1.0001x over previous
from dataclasses import dataclass

import numpy as np
import concourse.bass as bass
import concourse.mybir as mybir
from concourse.bass_utils import run_bass_kernel_spmd

PIPE_LVL = 3

F32 = mybir.dt.float32
BF16 = mybir.dt.bfloat16
AF = mybir.ActivationFunctionType
ALU = mybir.AluOpType
AXX = mybir.AxisListType.X

NCORES = 8
DM = 1024
DCH = DM // 128
H = 16
D = 64
C = 128
FH = 4096
FCH = FH // 128
ENGS = ("tensor", "vector", "scalar", "gpsimd", "sync")


@dataclass
class CFG:
    tpc: int = 2048
    n_cores: int = NCORES
    stage: int = 9
    debug: bool = False
    p2c: int = 0
    p2s: int = 99


class Prog:
    def __init__(self):
        self.steps = {e: [] for e in ENGS}
        self.cnt = {}
        self.sem_names = []
        self._dmafloor = {}
        self.phases = []
        self.seg = {}
        self._seg_start = {e: 0 for e in ENGS}
        self.allocs = {}
        self.frees = []

    def phase(self, name):
        if self.phases:
            prev = self.phases[-1]
            self.seg[prev] = {e: (self._seg_start[e], len(self.steps[e])) for e in ENGS}
        self._seg_start = {e: len(self.steps[e]) for e in ENGS}
        self.phases.append(name)
        self.allocs[name] = []

    def close(self):
        prev = self.phases[-1]
        self.seg[prev] = {e: (self._seg_start[e], len(self.steps[e])) for e in ENGS}

    def alloc(self, name, kind, shape, dt, side=None):
        self.allocs[self.phases[-1]].append((name, kind, list(shape), dt, side))

    def free(self, *names):
        for n in names:
            self.frees.append((len(self.phases) - 1, n))

    def sem(self, name):
        self.sem_names.append(name)
        self.cnt[name] = 0

    def raw(self, eng, fn):
        self.steps[eng].append(fn)

    def wait(self, eng, sname, val=None):
        v = self.cnt[sname] if val is None else val
        assert v <= self.cnt[sname], (sname, v, self.cnt[sname])
        if v > 0:
            self.raw(eng, lambda E, ctx, s=sname, v=v: E.wait_ge(ctx[s], v))

    def inst(self, eng, fn, sname=None, n=1):
        if sname is None:
            self.raw(eng, fn)
        else:
            self.cnt[sname] += n
            self.raw(eng, lambda E, ctx, f=fn, s=sname, n=n: f(E, ctx).then_inc(ctx[s], n))

    def dma(self, eng, ch, fn):
        self.inst(eng, fn, ch, 16)

    def group(self, eng, ch):
        key = (eng, ch)
        if self.cnt[ch] > self._dmafloor.get(key, 0):
            self.wait(eng, ch, self.cnt[ch])
            self._dmafloor[key] = self.cnt[ch]


def _retile_w(w):
    I, O = w.shape
    return np.ascontiguousarray(
        w.reshape(I // 128, 128, O).transpose(1, 0, 2)).reshape(128, -1)


def _panels(wT, po=128):
    I, O = wT.shape
    npan = O // po
    a = wT.reshape(I // 128, 128, npan, po).transpose(2, 1, 0, 3)
    return np.ascontiguousarray(a).reshape(npan, 128, (I // 128) * po)


def _cvec(v):
    return np.ascontiguousarray(v.reshape(-1, 128).T)


def build_consts():
    ident = np.eye(128, dtype=np.float32)
    e64 = np.zeros((128, 64), np.float32); e64[:64, :64] = np.eye(64)
    maskl = np.tril(np.ones((128, 128), np.float32), -1)
    masku = np.triu(np.ones((128, 128), np.float32), 1)
    maskui = np.triu(np.ones((128, 128), np.float32), 0)
    ones1 = np.ones((1, 128), np.float32)
    onesc = np.ones((128, 1), np.float32)
    oneh16 = np.zeros((16, 16 * 128), np.float32)
    for h in range(16):
        oneh16[h, h * 128:(h + 1) * 128] = 1.0
    oneh4 = np.zeros((4, 4 * 128), np.float32)
    for i in range(4):
        oneh4[i, i * 128:(i + 1) * 128] = 1.0
    return dict(ident=ident, e64=e64, maskl=maskl, masku=masku, maskui=maskui,
                ones1=ones1, onesc=onesc, oneh16=oneh16, oneh4=oneh4)


def build_nc(cfg: CFG):
    T = cfg.tpc
    NCH = T // C
    NT = T // 512
    nc = bass.Bass("TRN2", target_bir_lowering=False, debug=False,
                   detect_race_conditions=False)

    def din(name, shape, dt=F32):
        return nc.dram_tensor(name, list(shape), dt, kind="ExternalInput").ap()

    x_in = din("x", (DM, T))
    wq = din("wq", (128, DCH * DM)); bq = din("bq", (128, DCH))
    wk = din("wk", (128, DCH * DM)); bk = din("bk", (1, DM))
    wv = din("wv", (128, DCH * DM)); bv = din("bv", (1, DM))
    wb = din("wb", (128, DCH * 16)); bb = din("bb", (1, 16))
    wo = din("wo", (128, DCH * DM)); bo = din("bo", (128, DCH))
    w1p = din("w1p", (FCH, 128, DCH * 128)); w1s = din("w1s", (128, FCH))
    w2p = din("w2p", (FCH, 128, DCH * 128)); w2s = din("w2s", (128, FCH))
    h1p = din("h1p", (FCH, 128, FCH * 128)); h1s = din("h1s", (128, FCH))
    h2p = din("h2p", (FCH, 128, FCH * 128)); h2s = din("h2s", (128, FCH))
    w3p = din("w3p", (DCH, 128, FCH * 128)); w3s = din("w3s", (128, DCH))
    ident = din("ident", (128, 128)); e64 = din("e64", (128, 64))
    maskl = din("maskl", (128, 128)); masku = din("masku", (128, 128))
    maskui = din("maskui", (128, 128))
    ones1 = din("ones1", (1, 128)); onesc = din("onesc", (128, 1))
    oneh16 = din("oneh16", (16, 16 * 128)); oneh4 = din("oneh4", (4, 4 * 128))
    pmask = din("pmask", (64, 8)); imask = din("imask", (64, 8 * 64))

    out = nc.dram_tensor("out", [DM, T], F32, kind="ExternalOutput").ap()
    out_pct = out.rearrange("(c p) t -> p c t", p=128)

    cUB = nc.dram_tensor("cUB", [NCH, 128, H * 64], BF16).ap()
    cUPT = nc.dram_tensor("cUPT", [NCH, 64, H * 128], BF16).ap()
    cZPT = nc.dram_tensor("cZPT", [NCH, 64, H * 64], BF16).ap()
    cB = nc.dram_tensor("cB", [NCH, 128, (H // 2) * 64], BF16).ap()
    agi = nc.dram_tensor("agi", [192, 1024], F32).ap()
    ago = nc.dram_tensor("ago", [cfg.n_cores * 192, 1024], F32).ap()

    dbg = {}
    if cfg.debug:
        def dout(name, shape, dt=F32):
            dbg[name] = nc.dram_tensor(name, list(shape), dt, kind="ExternalOutput").ap()
        dout("d_y1", (128, DCH * T))
        dout("d_ktm", (128, NCH * DM))
        dout("d_vtm", (128, NCH * DM))
        dout("d_qfm", (128, DCH * T))
        dout("d_kfm", (128, DCH * T))
        dout("d_beta", (128, NCH * 16))
        if cfg.stage >= 2:
            dout("d_UB", (NCH, 128, H * 64), BF16)
            dout("d_UPT", (NCH, 64, H * 128), BF16)
            dout("d_ZPT", (NCH, 64, H * 64), BF16)
            dout("d_B", (NCH, 128, (H // 2) * 64), BF16)
            dout("d_zfin", (128, (H // 2) * 128))
        if cfg.stage >= 3:
            dout("d_zcore", (64, H * 64), BF16)
        if cfg.stage >= 4:
            dout("d_ofm", (128, DCH * T))
        if cfg.stage >= 5:
            dout("d_y2", (128, DCH * T))
        if cfg.stage >= 9:
            dout("d_hffn", (128, FCH * (T // 2)), BF16)

    p = Prog()
    for s in ["gd", "sd", "tm", "ve", "sc", "cc"]:
        p.sem(s)

    def sb(name, shape, dt=BF16, side=None):
        p.alloc(name, "sbuf", list(shape), dt, side)

    def ps(name, shape, dt=F32):
        p.alloc(name, "psum", list(shape), dt, None)

    NPOW = 6
    TFh = T // 2

    p.phase("P0")
    ps("PA", (128, 512)); ps("PB", (128, 512)); ps("PC", (128, 512))
    ps("PZ1", (128, 1024)); ps("PZ2", (128, 1024))
    ps("PT", (128, 256), BF16)
    sb("c_ident", (128, 128)); sb("c_e64", (128, 64))
    sb("c_maskl", (128, 128)); sb("c_masku", (128, 128)); sb("c_maskui", (128, 128))
    sb("c_ones1", (1, 128), F32); sb("c_onesc", (128, 1), BF16)
    sb("czero", (128, 128))
    sb("c_ones1b", (1, 128), BF16)
    sb("y1", (128, DCH * T))
    sb("lntmp", (1, 2048), F32)
    sb("lnr", (1, 2 * T), BF16)
    sb("bc1", (128, T)); sb("bc2", (128, T))
    sb("sq", (128, 4096))

    def ln_stats_math(nt, save_rows=None):
        p.inst("scalar", lambda E, ctx: E.mul(ctx["lntmp"][0:1, 1024:1536], ctx["lntmp"][0:1, 0:512], np.float32(1.0 / DM)), "sc")
        p.inst("scalar", lambda E, ctx: E.mul(ctx["lntmp"][0:1, 1536:2048], ctx["lntmp"][0:1, 512:1024], np.float32(1.0 / DM)), "sc")
        p.wait("vector", "sc")
        p.inst("vector", lambda E, ctx: E.tensor_mul(
            ctx["lntmp"][0:1, 0:512], ctx["lntmp"][0:1, 1024:1536], ctx["lntmp"][0:1, 1024:1536]), "ve")
        p.inst("vector", lambda E, ctx: E.tensor_sub(
            ctx["lntmp"][0:1, 512:1024], ctx["lntmp"][0:1, 1536:2048], ctx["lntmp"][0:1, 0:512]), "ve")
        p.wait("scalar", "ve")
        p.inst("scalar", lambda E, ctx: E.activation(
            ctx["lntmp"][0:1, 0:512], ctx["lntmp"][0:1, 512:1024], AF.Ln, bias=np.float32(1e-5)), "sc")
        p.inst("scalar", lambda E, ctx, nt=nt: E.activation(
            ctx["lnr"][0:1, nt * 512:(nt + 1) * 512], ctx["lntmp"][0:1, 0:512],
            AF.Exp, scale=np.float32(-0.5)), "sc")
        p.inst("scalar", lambda E, ctx: E.activation(
            ctx["lntmp"][0:1, 512:1024], ctx["lntmp"][0:1, 0:512],
            AF.Exp, scale=np.float32(-0.5)), "sc")
        if save_rows is not None:
            p.inst("scalar", lambda E, ctx, sr=save_rows, nt=nt: E.activation(
                ctx[sr][0:1, T + nt * 512: T + (nt + 1) * 512], ctx["lntmp"][0:1, 0:512],
                AF.Exp, scale=np.float32(0.5)), "sc")
            p.inst("scalar", lambda E, ctx, sr=save_rows, nt=nt: E.copy(
                ctx[sr][0:1, nt * 512:(nt + 1) * 512], ctx["lntmp"][0:1, 1024:1536]), "sc")
        p.wait("vector", "sc")
        p.inst("vector", lambda E, ctx: E.tensor_mul(
            ctx["lntmp"][0:1, 0:512], ctx["lntmp"][0:1, 1024:1536], ctx["lntmp"][0:1, 512:1024]), "ve")
        p.inst("vector", lambda E, ctx, nt=nt: E.tensor_scalar_mul(
            ctx["lnr"][0:1, T + nt * 512: T + (nt + 1) * 512], ctx["lntmp"][0:1, 0:512], -1.0), "ve")

    def ln_bcast(nt):
        p.wait("tensor", "ve")
        p.wait("tensor", "sc")
        p.inst("tensor", lambda E, ctx, nt=nt: E.matmul(
            ctx["PA"][:, :], ctx["c_ones1b"][:, :],
            ctx["lnr"][0:1, nt * 512:(nt + 1) * 512], start=True, stop=True,
            skip_group_check=True))
        p.inst("tensor", lambda E, ctx, nt=nt: E.matmul(
            ctx["PB"][:, :], ctx["c_ones1b"][:, :],
            ctx["lnr"][0:1, T + nt * 512: T + (nt + 1) * 512], start=True, stop=True,
            skip_group_check=True), "tm")
        p.wait("vector", "tm")
        p.inst("vector", lambda E, ctx, nt=nt: E.tensor_copy(
            ctx["bc1"][:, nt * 512:(nt + 1) * 512], ctx["PA"][:, :]), "ve")
        p.inst("vector", lambda E, ctx, nt=nt: E.tensor_copy(
            ctx["bc2"][:, nt * 512:(nt + 1) * 512], ctx["PB"][:, :]), "ve")
        p.wait("tensor", "ve")

    def ln_feature_major(src, dst, save_rows=None):
        for nt in range(NT):
            p.wait("scalar", "tm")
            for c in range(DCH):
                reg = c * 512
                p.inst("scalar", lambda E, ctx, c=c, nt=nt, reg=reg, src=src: E.activation(
                    ctx["sq"][:, reg:reg + 512],
                    ctx[src][:, c * T + nt * 512: c * T + (nt + 1) * 512],
                    AF.Square), "sc")
                p.wait("tensor", "sc")
                p.inst("tensor", lambda E, ctx, c=c, nt=nt, src=src: E.matmul(
                    ctx["PA"][0:1, :], ctx["c_onesc"][:, :],
                    ctx[src][:, c * T + nt * 512: c * T + (nt + 1) * 512],
                    start=(c == 0), stop=(c == DCH - 1), skip_group_check=True))
                p.inst("tensor", lambda E, ctx, c=c, nt=nt, reg=reg: E.matmul(
                    ctx["PB"][0:1, :], ctx["c_onesc"][:, :],
                    ctx["sq"][:, reg:reg + 512],
                    start=(c == 0), stop=(c == DCH - 1), skip_group_check=True),
                    "tm" if c == DCH - 1 else None)
            p.wait("vector", "tm")
            p.inst("vector", lambda E, ctx, nt=nt: E.tensor_copy(
                ctx["lntmp"][0:1, 0:512], ctx["PA"][0:1, :]), "ve")
            p.inst("vector", lambda E, ctx, nt=nt: E.tensor_copy(
                ctx["lntmp"][0:1, 512:1024], ctx["PB"][0:1, :]), "ve")
            p.wait("tensor", "ve")
            p.wait("scalar", "ve")
            ln_stats_math(nt, save_rows)
            ln_bcast(nt)
        for c in range(DCH):
            p.inst("vector", lambda E, ctx, c=c, src=src, dst=dst: E.tensor_mul(
                ctx[dst][:, c * T:(c + 1) * T], ctx[src][:, c * T:(c + 1) * T],
                ctx["bc1"][:, :]), "ve")
            p.inst("vector", lambda E, ctx, c=c, dst=dst: E.tensor_add(
                ctx[dst][:, c * T:(c + 1) * T], ctx[dst][:, c * T:(c + 1) * T],
                ctx["bc2"][:, :]), "ve")

    consts = [(ident, "c_ident"), (e64, "c_e64"), (maskl, "c_maskl"),
              (masku, "c_masku"), (maskui, "c_maskui"), (ones1, "c_ones1"),
              (onesc, "c_onesc"), (ones1, "c_ones1b")]
    for iap, sname in consts:
        p.inst("gpsimd", lambda E, ctx, iap=iap, sname=sname: E.dma_start(
            out=ctx[sname].ap(), in_=iap), "gd", 16)
    p.inst("gpsimd", lambda E, ctx: E.dma_start(out=ctx["y1"].ap().rearrange("p (c t) -> p c t", c=DCH), in_=x_in.rearrange("(c p) t -> p c t", p=128)), "gd", 16)
    p.inst("vector", lambda E, ctx: E.memset(ctx["czero"][:, :], 0.0), "ve")
    ng0 = p.cnt["gd"]
    for e in ("scalar", "vector", "tensor", "sync"):
        p.wait(e, "gd", ng0)

    ln_feature_major("y1", "y1")

    p.free("bc1", "bc2", "sq")
    p.phase("P1a")
    sb("ktm", (128, NCH * DM))
    sb("vtm", (128, NCH * DM))
    sb("kfm", (128, DCH * T))
    sb("beta", (128, NCH * 16), F32)
    sb("nbeta", (128, NCH * 16), F32)
    sb("betab", (128, NCH * 16), BF16)
    sb("wt", (128, DCH * DM))
    sb("bconst", (128, DM), F32)
    sb("bcol", (128, DCH), F32)
    sb("knr", (128, NCH * 16), F32)
    sb("knrb", (128, NCH * 16), BF16)
    sb("sqk", (128, DM))

    def bias_broadcast(bias_ap, width):
        p.wait("sync", "ve")
        p.wait("sync", "tm")
        p.group("sync", "sd")
        p.inst("sync", lambda E, ctx, b=bias_ap, w=width: E.dma_start(
            out=ctx["lntmp"][0:1, 0:w], in_=b), "sd", 16)
        p.wait("tensor", "sd")
        ntw = (width + 511) // 512
        for nt in range(ntw):
            w0, w1 = nt * 512, min(width, (nt + 1) * 512)
            p.inst("tensor", lambda E, ctx, w0=w0, w1=w1: E.matmul(
                ctx["PA"][:, 0:w1 - w0], ctx["c_ones1"][:, :],
                ctx["lntmp"][0:1, w0:w1], start=True, stop=True, skip_group_check=True), "tm")
            p.wait("vector", "tm")
            p.inst("vector", lambda E, ctx, w0=w0, w1=w1: E.tensor_copy(
                ctx["bconst"][:, w0:w1], ctx["PA"][:, 0:w1 - w0]), "ve")
            p.wait("tensor", "ve")

    def load_w(src_ap, width=DCH * DM):
        p.wait("gpsimd", "tm")
        p.group("gpsimd", "gd")
        p.inst("gpsimd", lambda E, ctx, s=src_ap, w=width: E.dma_start(
            out=ctx["wt"][:, 0:w], in_=s), "gd", 16)

    def proj_tm(dst):
        for tc in range(NCH):
            for nt in range(2):
                pp = "PA" if (tc * 2 + nt) % 2 == 0 else "PB"
                for c in range(DCH):
                    p.inst("tensor", lambda E, ctx, tc=tc, nt=nt, c=c, pp=pp: E.matmul(
                        ctx[pp][:, :],
                        ctx["y1"][:, c * T + tc * 128: c * T + (tc + 1) * 128],
                        ctx["wt"][:, c * DM + nt * 512: c * DM + (nt + 1) * 512],
                        start=(c == 0), stop=(c == DCH - 1), skip_group_check=True),
                        "tm" if c == DCH - 1 else None)
                p.wait("vector", "tm")
                p.inst("vector", lambda E, ctx, dst=dst, tc=tc, nt=nt, pp=pp:
                       E.scalar_tensor_tensor(
                           ctx[dst][:, tc * DM + nt * 512: tc * DM + (nt + 1) * 512],
                           ctx[pp][:, :], 1.0,
                           ctx["bconst"][:, nt * 512:(nt + 1) * 512],
                           ALU.mult, ALU.add), "ve")
                if (tc * 2 + nt) % 2 == 1:
                    p.wait("tensor", "ve")

    def proj_fm(dst, src="y1", bias="bcol"):
        for oc in range(DCH):
            for nt in range(NT):
                pp = "PA" if (oc * NT + nt) % 2 == 0 else "PB"
                for c in range(DCH):
                    p.inst("tensor", lambda E, ctx, oc=oc, nt=nt, c=c, pp=pp, src=src: E.matmul(
                        ctx[pp][:, :],
                        ctx["wt"][:, c * DM + oc * 128: c * DM + (oc + 1) * 128],
                        ctx[src][:, c * T + nt * 512: c * T + (nt + 1) * 512],
                        start=(c == 0), stop=(c == DCH - 1), skip_group_check=True),
                        "tm" if c == DCH - 1 else None)
                p.wait("scalar", "tm")
                p.inst("scalar", lambda E, ctx, dst=dst, oc=oc, nt=nt, pp=pp, bias=bias:
                       E.activation(
                           ctx[dst][:, oc * T + nt * 512: oc * T + (nt + 1) * 512],
                           ctx[pp][:, :], AF.Identity,
                           bias=ctx[bias][:, oc:oc + 1]), "sc")
                if (oc * NT + nt) % 2 == 1:
                    p.wait("tensor", "sc")

    load_w(wk)
    bias_broadcast(bk, DM)
    p.wait("tensor", "gd")
    proj_tm("ktm")
    load_w(wv)
    bias_broadcast(bv, DM)
    p.wait("tensor", "gd")
    p.wait("vector", "gd")
    proj_tm("vtm")
    p.wait("gpsimd", "tm")
    p.group("gpsimd", "gd")
    p.inst("gpsimd", lambda E, ctx: E.dma_start(out=ctx["wt"][:, 0:DCH * 16], in_=wb), "gd", 16)
    bias_broadcast(bb, 16)
    p.wait("tensor", "gd")
    for tc in range(NCH):
        pp = "PA" if tc % 2 == 0 else "PB"
        for c in range(DCH):
            p.inst("tensor", lambda E, ctx, tc=tc, c=c, pp=pp: E.matmul(
                ctx[pp][:, 0:16],
                ctx["y1"][:, c * T + tc * 128: c * T + (tc + 1) * 128],
                ctx["wt"][:, c * 16:(c + 1) * 16],
                start=(c == 0), stop=(c == DCH - 1), skip_group_check=True),
                "tm" if c == DCH - 1 else None)
        p.wait("vector", "tm")
        p.inst("vector", lambda E, ctx, tc=tc, pp=pp: E.tensor_add(
            ctx["beta"][:, tc * 16:(tc + 1) * 16], ctx[pp][:, 0:16],
            ctx["bconst"][:, 0:16]), "ve")
        if tc % 2 == 1:
            p.wait("tensor", "ve")
    p.wait("scalar", "ve")
    p.inst("scalar", lambda E, ctx: E.activation(
        ctx["beta"][:, :], ctx["beta"][:, :], AF.Sigmoid), "sc")
    p.inst("scalar", lambda E, ctx: E.activation(
        ctx["nbeta"][:, :], ctx["beta"][:, :], AF.Identity, scale=np.float32(-1.0)), "sc")
    p.inst("scalar", lambda E, ctx: E.copy(ctx["betab"][:, :], ctx["beta"][:, :]), "sc")
    for tc in range(NCH):
        p.wait("scalar", "ve")
        p.wait("scalar", "sc")
        p.inst("scalar", lambda E, ctx, tc=tc: E.activation(
            ctx["sqk"][:, 0:DM], ctx["ktm"][:, tc * DM:(tc + 1) * DM], AF.Square), "sc")
        p.wait("vector", "sc")
        p.inst("vector", lambda E, ctx, tc=tc: E.reduce_sum(
            ctx["knr"][:, tc * 16:(tc + 1) * 16].rearrange("p (h o) -> p h o", o=1),
            ctx["sqk"][:, 0:DM].rearrange("p (h d) -> p h d", d=64), AXX), "ve")
    p.wait("scalar", "ve")
    p.inst("scalar", lambda E, ctx: E.activation(ctx["knr"][:, :], ctx["knr"][:, :], AF.Ln), "sc")
    p.inst("scalar", lambda E, ctx: E.activation(ctx["knr"][:, :], ctx["knr"][:, :], AF.Exp, scale=np.float32(0.5)), "sc")
    p.inst("scalar", lambda E, ctx: E.add(ctx["knr"][:, :], ctx["knr"][:, :], np.float32(1e-6)), "sc")
    p.wait("vector", "sc")
    p.inst("vector", lambda E, ctx: E.reciprocal(ctx["knr"][:, :], ctx["knr"][:, :]), "ve")
    p.inst("vector", lambda E, ctx: E.tensor_copy(ctx["knrb"][:, :], ctx["knr"][:, :]), "ve")
    for tc in range(NCH):
        p.inst("vector", lambda E, ctx, tc=tc: E.tensor_mul(
            ctx["ktm"][:, tc * DM:(tc + 1) * DM].rearrange("p (h d) -> p h d", d=64),
            ctx["ktm"][:, tc * DM:(tc + 1) * DM].rearrange("p (h d) -> p h d", d=64),
            ctx["knrb"][:, tc * 16:(tc + 1) * 16].rearrange("p (h o) -> p h o", o=1)
            .broadcast_to([128, 16, 64])), "ve")
    p.wait("tensor", "ve")
    for tc in range(NCH):
        for oc in range(DCH):
            half = ((tc * DCH + oc) % 2) * 128
            p.inst("tensor", lambda E, ctx, tc=tc, oc=oc, half=half: E.transpose(
                ctx["PT"][:, half:half + 128],
                ctx["ktm"][:, tc * DM + oc * 128: tc * DM + (oc + 1) * 128],
                ctx["c_ident"][:, :]), "tm")
            p.wait("vector", "tm")
            p.inst("vector", lambda E, ctx, tc=tc, oc=oc, half=half: E.tensor_copy(
                ctx["kfm"][:, oc * T + tc * 128: oc * T + (tc + 1) * 128],
                ctx["PT"][:, half:half + 128]), "ve")
            if (tc * DCH + oc) % 2 == 1:
                p.wait("tensor", "ve")

    p.free("bconst", "knr", "knrb", "sqk")
    p.phase("P1b")
    sb("qfm", (128, DCH * T))
    load_w(wq)
    p.wait("tensor", "gd")
    p.wait("scalar", "gd")
    p.inst("gpsimd", lambda E, ctx: E.dma_start(out=ctx["bcol"][:, 0:DCH], in_=bq), "gd", 16)
    p.wait("scalar", "gd")
    proj_fm("qfm")

    if cfg.debug:
        def dump(src, dname, reg=None):
            for e in ("ve", "sc", "tm"):
                p.wait("gpsimd", e)
            p.group("gpsimd", "gd")
            if reg is None:
                p.inst("gpsimd", lambda E, ctx, src=src, dname=dname: E.dma_start(
                    out=dbg[dname], in_=ctx[src].ap()), "gd", 16)
            else:
                p.inst("gpsimd", lambda E, ctx, src=src, dname=dname, reg=reg: E.dma_start(
                    out=dbg[dname], in_=ctx[src][reg]), "gd", 16)
        dump("y1", "d_y1"); dump("ktm", "d_ktm"); dump("vtm", "d_vtm")
        dump("qfm", "d_qfm"); dump("kfm", "d_kfm"); dump("beta", "d_beta")

    if cfg.stage >= 2:
        p.free("y1", "wt")
        p.phase("P2")
        sb("c_oneh16", (16, 16 * 128))
        sb("Tt", (128, 256))
        sb("pmw", (128, 1024))
        sb("Nlow", (128, 256)); sb("Nup", (128, 256))
        sb("chR", (128, 512))
        sb("mbB", (128, 256))
        sb("zbd", (128, (H // 2) * 256))
        sb("vg", (128, 128))
        sb("rh", (128, 256))
        sb("Ush", (128, 256))
        sb("stUB", (128, H * 64))
        sb("stUPT", (64, H * 128))
        sb("stZPT", (64, H * 64))
        sb("brow", (16, 128))
        sb("zfin", (128, 1024), F32)
        sb("ptfin", (64, 1024), F32)
        p.group("gpsimd", "gd")
        p.inst("gpsimd", lambda E, ctx: E.dma_start(
            out=ctx["c_oneh16"].ap(), in_=oneh16), "gd", 16)
        for e in ("tensor", "vector", "scalar"):
            p.wait(e, "gd")
        p.wait("tensor", "ve")
        for pr in range(8):
            p.inst("tensor", lambda E, ctx, pr=pr: E.matmul(
                ctx["PZ1"][0:64, pr * 128: pr * 128 + 64], ctx["c_e64"][:, :],
                ctx["c_e64"][:, :], start=(pr % 4 == 0), stop=False,
                tile_position=(0, 0), skip_group_check=True))
            p.inst("tensor", lambda E, ctx, pr=pr: E.matmul(
                ctx["PZ1"][64:128, pr * 128: pr * 128 + 64], ctx["c_e64"][:, :],
                ctx["c_e64"][:, :], start=(pr % 4 == 0), stop=False,
                tile_position=(0, 64), skip_group_check=True))
            for j in range(2):
                p.inst("tensor", lambda E, ctx, pr=pr, j=j: E.matmul(
                    ctx["PZ2"][0:64, pr * 128 + j * 64: pr * 128 + (j + 1) * 64],
                    ctx["c_e64"][:, :], ctx["c_e64"][:, :],
                    start=(pr % 4 == 0 and j == 0), stop=False,
                    tile_position=(0, 0), skip_group_check=True))
            p.inst("tensor", lambda E, ctx, pr=pr: E.matmul(
                ctx["PZ2"][64:128, pr * 128:(pr + 1) * 128], ctx["c_e64"][:, :],
                ctx["czero"][:, :], start=(pr % 4 == 0), stop=False,
                tile_position=(0, 64), skip_group_check=True))
            for j in range(2):
                hb = j * 64
                p.inst("tensor", lambda E, ctx, pr=pr, hb=hb: E.matmul(
                    ctx["PZ1"][hb:hb + 64, pr * 128 + 64:(pr + 1) * 128],
                    ctx["c_e64"][:, :], ctx["czero"][:, 0:64], start=False,
                    stop=False, tile_position=(0, hb), skip_group_check=True),
                    "tm" if (pr == 7 and hb == 64) else None)
        p.wait("vector", "tm")
        p.inst("vector", lambda E, ctx: E.memset(ctx["zbd"][:, :], 0.0), "ve")
        for pr in range(8):
            p.inst("vector", lambda E, ctx, pr=pr: E.tensor_copy(
                ctx["zbd"][0:64, pr * 256: pr * 256 + 128],
                ctx["PZ1"][0:64, pr * 128: (pr + 1) * 128]), "ve")
            p.inst("vector", lambda E, ctx, pr=pr: E.tensor_copy(
                ctx["zbd"][64:128, pr * 256 + 128: (pr + 1) * 256],
                ctx["PZ1"][64:128, pr * 128: (pr + 1) * 128]), "ve")

        def nprep_pair(c, pr):
            p.wait("vector", "tm")
            for j in range(2):
                jj = j * 128
                h = 2 * pr + j
                p.inst("vector", lambda E, ctx, jj=jj: E.scalar_tensor_tensor(
                    ctx["mbB"][:, jj:jj + 128], ctx["PB"][:, jj:jj + 128], 1.0,
                    ctx["c_masku"][:, :], ALU.mult, ALU.mult), "ve")
                p.inst("vector", lambda E, ctx, jj=jj: E.scalar_tensor_tensor(
                    ctx["Nup"][:, jj:jj + 128], ctx["PA"][:, jj:jj + 128], 1.0,
                    ctx["mbB"][:, jj:jj + 128], ALU.mult, ALU.mult), "ve")
                p.inst("vector", lambda E, ctx, jj=jj, c=c, h=h: E.scalar_tensor_tensor(
                    ctx["Nlow"][:, jj:jj + 128], ctx["PA"][:, jj:jj + 128],
                    ctx["beta"][:, c * 16 + h: c * 16 + h + 1],
                    ctx["c_maskl"][:, :], ALU.mult, ALU.mult), "ve")
                p.inst("vector", lambda E, ctx, j=j: E.tensor_sub(
                    ctx["chR"][:, j * 256: j * 256 + 128], ctx["c_ident"][:, :],
                    ctx["Nup"][:, j * 128:(j + 1) * 128]), "ve")
            return p.cnt["ve"]

        def inv_merged(c, pr, evg0, lim=99):
            if lim < 21:
                return
            evg = [evg0, evg0]
            for it in range(1, 8):
                cur = (it - 1) % 2
                nxt = it % 2
                for j in range(2):
                    pz = "PC" if j == 0 else "PB"
                    if it == 1:
                        pN = ("Nlow", slice(j * 128, j * 128 + 128))
                        pT = ("Nup", slice(j * 128, j * 128 + 128))
                    else:
                        off = j * 512 + cur * 256
                        pN = ("pmw", slice(off, off + 128))
                        pT = ("pmw", slice(off + 128, off + 256))
                    p.wait("tensor", "ve", evg[j])
                    last_sq = (it >= 7)
                    if it <= 6:
                        p.inst("tensor", lambda E, ctx, pN=pN, pT=pT, pz=pz: E.matmul(
                            ctx[pz][:, 0:128], ctx[pT[0]][:, pT[1]],
                            ctx[pN[0]][:, pN[1]], start=True, stop=True,
                            skip_group_check=True))
                        if it < 6:
                            p.inst("tensor", lambda E, ctx, pN=pN, pT=pT, pz=pz: E.matmul(
                                ctx[pz][:, 128:256], ctx[pN[0]][:, pN[1]],
                                ctx[pT[0]][:, pT[1]], start=True, stop=True,
                                skip_group_check=True), None if it >= 2 else "tm")
                        elif it < 2:
                            pass
                    if it >= 2:
                        src = j * 256 + ((it - 2) % 2) * 128
                        p.inst("tensor", lambda E, ctx, pN=pN, src=src, pz=pz: E.matmul(
                            ctx[pz][:, 256:384], ctx[pN[0]][:, pN[1]],
                            ctx["chR"][:, src:src + 128], start=True, stop=True,
                            skip_group_check=True), "tm")
                    p.wait("vector", "tm")
                    if it <= 6:
                        wid = 256 if it < 6 else 128
                        off2 = j * 512 + nxt * 256
                        p.inst("vector", lambda E, ctx, pz=pz, off2=off2, wid=wid: E.tensor_copy(
                            ctx["pmw"][:, off2:off2 + wid], ctx[pz][:, 0:wid]), "ve")
                    if it >= 2:
                        src = j * 256 + ((it - 2) % 2) * 128
                        dst = j * 256 + ((it - 1) % 2) * 128
                        if it < 7:
                            p.inst("vector", lambda E, ctx, src=src, dst=dst, pz=pz: E.scalar_tensor_tensor(
                                ctx["chR"][:, dst:dst + 128], ctx[pz][:, 256:384],
                                1.0, ctx["chR"][:, src:src + 128], ALU.mult, ALU.add), "ve")
                        else:
                            p.inst("vector", lambda E, ctx, src=src, j=j, pz=pz: E.scalar_tensor_tensor(
                                ctx["Tt"][:, j * 128:(j + 1) * 128],
                                ctx[pz][:, 256:384],
                                1.0, ctx["chR"][:, src:src + 128], ALU.mult, ALU.add), "ve")
                    evg[j] = p.cnt["ve"]


        for c in range(NCH if cfg.p2c == 0 else min(NCH, cfg.p2c)):
            if cfg.p2s < 1:
                continue
            p.wait("tensor", "ve")
            p.inst("tensor", lambda E, ctx, c=c: E.transpose(
                ctx["PT"][0:16, 0:128],
                ctx["betab"][:, c * 16:(c + 1) * 16], ctx["c_ident"][:, :]), "tm")
            p.wait("vector", "tm")
            p.inst("vector", lambda E, ctx: E.tensor_copy(
                ctx["brow"][:, :], ctx["PT"][0:16, 0:128]), "ve")
            p.wait("vector", "sd")
            p.wait("scalar", "sd")
            p.wait("vector", "tm")
            p.inst("vector", lambda E, ctx: E.tensor_copy(
                ctx["stZPT"][0:64, :], ctx["PZ2"][0:64, :]), "ve")
            p.wait("sync", "sc")
            p.wait("sync", "ve")
            p.group("sync", "sd")
            p.inst("sync", lambda E, ctx, c=c: E.dma_start(
                out=cZPT[c], in_=ctx["stZPT"].ap()), "sd", 16)
            p.inst("sync", lambda E, ctx, c=c: E.dma_start(
                out=cB[c][0:64, :],
                in_=ctx["zbd"].ap().rearrange("p (r x) -> p r x", r=8)[0:64, :, 64:128]), "sd", 16)
            p.inst("sync", lambda E, ctx, c=c: E.dma_start(
                out=cB[c][64:128, :],
                in_=ctx["zbd"].ap().rearrange("p (r x) -> p r x", r=8)[64:128, :, 192:256]), "sd", 16)
            if cfg.debug:
                p.group("sync", "sd")
                p.inst("sync", lambda E, ctx, c=c: E.dma_start(
                    out=dbg["d_ZPT"][c], in_=ctx["stZPT"].ap()), "sd", 16)
                p.inst("sync", lambda E, ctx, c=c: E.dma_start(
                    out=dbg["d_B"][c][0:64, :],
                    in_=ctx["zbd"].ap().rearrange("p (r x) -> p r x", r=8)[0:64, :, 64:128]), "sd", 16)
                p.inst("sync", lambda E, ctx, c=c: E.dma_start(
                    out=dbg["d_B"][c][64:128, :],
                    in_=ctx["zbd"].ap().rearrange("p (r x) -> p r x", r=8)[64:128, :, 192:256]), "sd", 16)
            zbd_cached = p.cnt["sd"]
            def front_pair(c, pr):
                for j in range(2):
                    hb = j * 64
                    p.inst("tensor", lambda E, ctx, pr=pr, c=c, hb=hb, j=j: E.matmul(
                        ctx["PA"][:, j * 128: j * 128 + 128],
                        ctx["kfm"][:, pr * T + c * 128: pr * T + (c + 1) * 128][hb:hb + 64, :],
                        ctx["kfm"][:, pr * T + c * 128: pr * T + (c + 1) * 128][hb:hb + 64, :],
                        start=True, stop=True, tile_position=(hb, 0),
                        skip_group_check=True))
                    p.inst("tensor", lambda E, ctx, pr=pr, j=j: E.matmul(
                        ctx["PB"][:, j * 128:(j + 1) * 128],
                        ctx["c_oneh16"][:, (2 * pr + j) * 128:(2 * pr + j + 1) * 128],
                        ctx["brow"][:, :], start=True, stop=True,
                        skip_group_check=True), "tm" if j == 1 else None)

            nprep_gate = 0
            for pr in range(8):
                if pr == 0:
                    p.wait("vector", "sd", zbd_cached)
                if pr == 0 or PIPE_LVL < 3:
                    p.wait("tensor", "ve")
                    p.wait("tensor", "sc")
                    front_pair(c, pr)
                    nprep_gate = nprep_pair(c, pr)
                if cfg.p2s < 2:
                    continue
                inv_merged(c, pr, nprep_gate, lim=cfg.p2s)
                if cfg.p2s < 3:
                    continue
                p.wait("tensor", "ve")
                p.inst("tensor", lambda E, ctx, pr=pr, c=c: E.matmul(
                    ctx["PB"][:, 256:512],
                    ctx["kfm"][:, pr * T + c * 128: pr * T + (c + 1) * 128],
                    ctx["zbd"][:, pr * 256:(pr + 1) * 256],
                    start=True, stop=True, skip_group_check=True), "tm")

                if cfg.p2s < 4:
                    continue
                p.wait("vector", "tm")
                for j in range(2):
                    h = 2 * pr + j
                    p.inst("vector", lambda E, ctx, j=j, c=c, h=h: E.tensor_scalar(
                        ctx["rh"][:, j * 128: j * 128 + 64],
                        ctx["PB"][:, 256 + j * 128: 256 + j * 128 + 64],
                        ctx["beta"][:, c * 16 + h: c * 16 + h + 1], -1.0,
                        ALU.mult, ALU.mult), "ve")
                    p.inst("vector", lambda E, ctx, j=j, c=c, h=h: E.tensor_sub(
                        ctx["vg"][:, j * 64:(j + 1) * 64],
                        ctx["vtm"][:, c * DM + h * 64: c * DM + (h + 1) * 64],
                        ctx["PB"][:, 256 + j * 128 + 64: 256 + (j + 1) * 128]), "ve")
                    p.inst("vector", lambda E, ctx, j=j, c=c, h=h: E.tensor_scalar(
                        ctx["rh"][:, j * 128 + 64:(j + 1) * 128],
                        ctx["vg"][:, j * 64:(j + 1) * 64],
                        ctx["beta"][:, c * 16 + h: c * 16 + h + 1], None,
                        ALU.mult), "ve")
                rhat_end = p.cnt["ve"]
                if cfg.p2s < 5:
                    continue
                if pr < 7 and PIPE_LVL >= 3:
                    p.wait("tensor", "ve", rhat_end)
                    front_pair(c, pr + 1)
                p.wait("tensor", "sc")
                p.wait("tensor", "ve", rhat_end)
                for j in range(2):
                    p.inst("tensor", lambda E, ctx, j=j: E.matmul(
                        ctx["PC"][:, j * 128:(j + 1) * 128],
                        ctx["Tt"][:, j * 128:(j + 1) * 128],
                        ctx["rh"][:, j * 128:(j + 1) * 128],
                        start=True, stop=True, skip_group_check=True))
                    p.inst("tensor", lambda E, ctx, j=j: E.matmul(
                        ctx["PC"][0:64, 256 + j * 128: 256 + (j + 1) * 128],
                        ctx["rh"][:, j * 128: j * 128 + 64],
                        ctx["Tt"][:, j * 128:(j + 1) * 128],
                        start=True, stop=True, skip_group_check=True),
                        "tm" if j == 1 else None)
                p.wait("vector", "tm")
                p.wait("scalar", "tm")
                for j in range(2):
                    h = 2 * pr + j
                    p.inst("vector", lambda E, ctx, j=j: E.tensor_copy(
                        ctx["Ush"][:, j * 128:(j + 1) * 128],
                        ctx["PC"][:, j * 128:(j + 1) * 128]), "ve")
                    p.inst("vector", lambda E, ctx, j=j, h=h: E.tensor_copy(
                        ctx["stUB"][:, h * 64:(h + 1) * 64],
                        ctx["PC"][:, j * 128 + 64:(j + 1) * 128]), "ve")
                    p.inst("vector", lambda E, ctx, j=j, h=h: E.tensor_copy(
                        ctx["stUPT"][0:64, h * 128:(h + 1) * 128],
                        ctx["PC"][0:64, 256 + j * 128: 256 + (j + 1) * 128]), "ve")
                evac_end = p.cnt["ve"]
                if pr < 7 and PIPE_LVL >= 3:
                    nprep_gate = nprep_pair(c, pr + 1)
                if cfg.p2s < 6:
                    continue
                p.wait("tensor", "ve", evac_end)
                for j in range(2):
                    h = 2 * pr + j
                    hb = j * 64
                    p.inst("tensor", lambda E, ctx, j=j, h=h, hb=hb, pr=pr, c=c: E.matmul(
                        ctx["PZ1"][hb:hb + 64, pr * 128:(pr + 1) * 128],
                        ctx["ktm"][:, c * DM + h * 64: c * DM + (h + 1) * 64],
                        ctx["Ush"][:, j * 128:(j + 1) * 128],
                        start=False, stop=False, tile_position=(0, hb),
                        skip_group_check=True))
                    p.inst("tensor", lambda E, ctx, j=j, h=h, pr=pr, c=c: E.matmul(
                        ctx["PZ2"][:, pr * 128 + j * 64: pr * 128 + (j + 1) * 64],
                        ctx["Ush"][:, j * 128:(j + 1) * 128],
                        ctx["ktm"][:, c * DM + h * 64: c * DM + (h + 1) * 64],
                        start=False, stop=False, tile_position=(0, 0),
                        skip_group_check=True), "tm" if j == 1 else None)
                p.wait("vector", "tm")
                p.inst("vector", lambda E, ctx, pr=pr: E.tensor_copy(
                    ctx["zbd"][0:64, pr * 256: pr * 256 + 128],
                    ctx["PZ1"][0:64, pr * 128:(pr + 1) * 128]), "ve")
                p.inst("vector", lambda E, ctx, pr=pr: E.tensor_copy(
                    ctx["zbd"][64:128, pr * 256 + 128:(pr + 1) * 256],
                    ctx["PZ1"][64:128, pr * 128:(pr + 1) * 128]), "ve")
            p.wait("sync", "ve")
            p.wait("sync", "sc")
            p.group("sync", "sd")
            p.inst("sync", lambda E, ctx, c=c: E.dma_start(
                out=cUB[c], in_=ctx["stUB"].ap()), "sd", 16)
            p.inst("sync", lambda E, ctx, c=c: E.dma_start(
                out=cUPT[c], in_=ctx["stUPT"].ap()), "sd", 16)
            if cfg.debug:
                p.group("sync", "sd")
                p.inst("sync", lambda E, ctx, c=c: E.dma_start(
                    out=dbg["d_UB"][c], in_=ctx["stUB"].ap()), "sd", 16)
                p.inst("sync", lambda E, ctx, c=c: E.dma_start(
                    out=dbg["d_UPT"][c], in_=ctx["stUPT"].ap()), "sd", 16)
        if cfg.debug:
            p.wait("vector", "tm")
            p.inst("vector", lambda E, ctx: E.tensor_copy(
                ctx["zfin"][:, :], ctx["PZ1"][:, :]), "ve")
            p.wait("gpsimd", "ve")
            p.group("gpsimd", "gd")
            p.inst("gpsimd", lambda E, ctx: E.dma_start(
                out=dbg["d_zfin"], in_=ctx["zfin"].ap()), "gd", 16)

    if cfg.stage >= 3:
        p.free("ktm", "vtm", "beta", "nbeta", "betab", "Tt", "pmw",
               "Nlow", "Nup", "chR", "mbB", "zbd", "vg", "rh", "Ush",
               "stUB", "stUPT", "stZPT", "brow", "c_oneh16")
        p.phase("P3")
        sb("gA", (64, 8 * DM), F32)
        sb("gB", (64, 8 * DM), F32)
        sb("zcore", (64, H * 64))
        sb("pj", (64, DM)); sb("bj", (64, DM), F32)
        sb("c_pmask", (64, 8), F32)
        sb("c_imask", (64, 8 * 64), F32)
        NC_ = cfg.n_cores
        p.group("gpsimd", "gd")
        p.inst("gpsimd", lambda E, ctx: E.dma_start(out=ctx["c_pmask"].ap(), in_=pmask), "gd", 16)
        p.inst("gpsimd", lambda E, ctx: E.dma_start(out=ctx["c_imask"].ap(), in_=imask), "gd", 16)
        p.wait("vector", "gd")
        p.wait("vector", "tm")
        p.inst("vector", lambda E, ctx: E.tensor_copy(
            ctx["zfin"][:, :], ctx["PZ1"][:, :]), "ve")
        p.wait("scalar", "tm")
        p.inst("scalar", lambda E, ctx: E.copy(
            ctx["ptfin"][0:64, :], ctx["PZ2"][0:64, :]), "sc")
        p.wait("sync", "ve")
        p.wait("sync", "sc")
        p.group("sync", "sd")
        p.inst("sync", lambda E, ctx: E.dma_start(
            out=agi[0:128, :], in_=ctx["zfin"].ap()), "sd", 16)
        p.inst("sync", lambda E, ctx: E.dma_start(
            out=agi[128:192, :], in_=ctx["ptfin"].ap()), "sd", 16)
        p.wait("gpsimd", "sd")
        p.inst("gpsimd", lambda E, ctx: E.collective_compute(
            "AllGather", ALU.bypass, replica_groups=[list(range(NC_))],
            ins=[agi], outs=[ago]), "cc")
        p.wait("gpsimd", "cc")
        p.group("gpsimd", "gd")
        for j in range(NC_):
            p.inst("gpsimd", lambda E, ctx, j=j: E.dma_start(
                out=ctx["gB"][0:64, j * 1024:(j + 1) * 1024],
                in_=ago[j * 192 + 128: j * 192 + 192, :]), "gd", 16)
            p.inst("gpsimd", lambda E, ctx, j=j: E.dma_start(
                out=ctx["gA"][0:64, j * 1024:(j + 1) * 1024]
                .rearrange("p (r x) -> p r x", x=128)[:, :, 0:64],
                in_=ago[j * 192: j * 192 + 64, :]
                .rearrange("p (r x) -> p r x", r=8)[:, :, 64:128]), "gd", 16)
            p.inst("gpsimd", lambda E, ctx, j=j: E.dma_start(
                out=ctx["gA"][0:64, j * 1024:(j + 1) * 1024]
                .rearrange("p (r x) -> p r x", x=128)[:, :, 64:128],
                in_=ago[j * 192 + 64: j * 192 + 128, :]
                .rearrange("p (r x) -> p r x", r=8)[:, :, 64:128]), "gd", 16)
        p.wait("vector", "gd")
        p.inst("vector", lambda E, ctx: E.memset(ctx["zcore"][:, :], 0.0), "ve")
        for j in range(NC_):
            p.inst("vector", lambda E, ctx, j=j: E.scalar_tensor_tensor(
                ctx["pj"][0:64, :].rearrange("p (h d) -> p h d", d=64),
                ctx["gB"][0:64, j * 1024:(j + 1) * 1024].rearrange("p (h d) -> p h d", d=64),
                ctx["c_pmask"][0:64, j:j + 1],
                ctx["c_imask"][0:64, j * 64:(j + 1) * 64]
                .rearrange("p (o d) -> p o d", o=1).broadcast_to([64, 16, 64]),
                ALU.mult, ALU.add), "ve")
            p.inst("vector", lambda E, ctx, j=j: E.tensor_scalar_mul(
                ctx["bj"][0:64, :], ctx["gA"][0:64, j * 1024:(j + 1) * 1024],
                ctx["c_pmask"][0:64, j:j + 1]), "ve")
            p.wait("tensor", "ve")
            for h in range(H):
                p.inst("tensor", lambda E, ctx, h=h: E.matmul(
                    ctx["PZ1"][0:64, h * 64:(h + 1) * 64],
                    ctx["pj"][0:64, h * 64:(h + 1) * 64],
                    ctx["zcore"][0:64, h * 64:(h + 1) * 64],
                    start=True, stop=True, skip_group_check=True),
                    "tm" if h == H - 1 else None)
            p.wait("vector", "tm")
            p.inst("vector", lambda E, ctx: E.scalar_tensor_tensor(
                ctx["zcore"][0:64, :], ctx["PZ1"][0:64, 0:1024], 1.0,
                ctx["bj"][0:64, :], ALU.mult, ALU.add), "ve")
        if cfg.debug:
            p.wait("gpsimd", "ve")
            p.group("gpsimd", "gd")
            p.inst("gpsimd", lambda E, ctx: E.dma_start(
                out=dbg["d_zcore"], in_=ctx["zcore"].ap()), "gd", 16)

    if cfg.stage >= 4:
        p.free("zfin", "ptfin", "gA", "gB", "pj", "bj", "c_pmask", "c_imask")
        p.phase("P4")
        sb("ofm", (128, DCH * T))
        sb("ldUB", (128, 2 * H * 64))
        sb("ldUPT", (64, 2 * H * 128))
        sb("ldZPT", (64, 2 * H * 64))
        sb("ldB", (128, 2 * (H // 2) * 64))
        sb("z0pad", (128, H * 64))
        sb("urel", (128, 128))
        sb("ltp", (128, 256))
        p.wait("vector", "cc")
        p.inst("vector", lambda E, ctx: E.memset(ctx["z0pad"][:, :], 0.0), "ve")
        p4_gate = None
        for c in range(NCH):
            buf = c % 2
            p.wait("sync", "ve")
            p.group("sync", "sd")
            p.inst("sync", lambda E, ctx, c=c, buf=buf: E.dma_start(
                out=ctx["ldUB"][:, buf * H * 64:(buf + 1) * H * 64], in_=cUB[c]), "sd", 16)
            p.inst("sync", lambda E, ctx, c=c, buf=buf: E.dma_start(
                out=ctx["ldUPT"][0:64, buf * H * 128:(buf + 1) * H * 128], in_=cUPT[c]), "sd", 16)
            p.inst("sync", lambda E, ctx, c=c, buf=buf: E.dma_start(
                out=ctx["ldZPT"][0:64, buf * H * 64:(buf + 1) * H * 64], in_=cZPT[c]), "sd", 16)
            p.inst("sync", lambda E, ctx, c=c, buf=buf: E.dma_start(
                out=ctx["ldB"][:, buf * 512:(buf + 1) * 512], in_=cB[c]), "sd", 16)
            p.wait("tensor", "sd")
            oUB = buf * H * 64
            oUPT = buf * H * 128
            oZPT = buf * H * 64
            oB = buf * 512
            for pr in range(8):
                ccols = slice(pr * T + c * 128, pr * T + (c + 1) * 128)
                if p4_gate is None:
                    p.wait("tensor", "ve")
                else:
                    p.wait("tensor", "ve", p4_gate)
                for j in range(2):
                    h = 2 * pr + j
                    hb = j * 64
                    p.inst("tensor", lambda E, ctx, j=j, h=h, hb=hb, pr=pr, c=c: E.matmul(
                        ctx["PA"][:, j * 128:(j + 1) * 128],
                        ctx["kfm"][:, pr * T + c * 128: pr * T + (c + 1) * 128][hb:hb + 64, :],
                        ctx["qfm"][:, pr * T + c * 128: pr * T + (c + 1) * 128][hb:hb + 64, :],
                        start=True, stop=True, tile_position=(hb, 0),
                        skip_group_check=True))
                    p.inst("tensor", lambda E, ctx, j=j, h=h, oUPT=oUPT: E.matmul(
                        ctx["PB"][:, j * 64:(j + 1) * 64],
                        ctx["ldUPT"][0:64, oUPT + h * 128: oUPT + (h + 1) * 128],
                        ctx["zcore"][0:64, h * 64:(h + 1) * 64],
                        start=True, stop=True, skip_group_check=True))
                    p.inst("tensor", lambda E, ctx, j=j, h=h, hb=hb, oZPT=oZPT: E.matmul(
                        ctx["PB"][hb:hb + 64, 128 + j * 64: 128 + (j + 1) * 64],
                        ctx["ldZPT"][0:64, oZPT + h * 64: oZPT + (h + 1) * 64],
                        ctx["zcore"][0:64, h * 64:(h + 1) * 64],
                        start=True, stop=True, tile_position=(0, hb),
                        skip_group_check=True), "tm" if j == 1 else None)
                p.wait("vector", "tm")
                for j in range(2):
                    h = 2 * pr + j
                    hb = j * 64
                    p.inst("vector", lambda E, ctx, j=j: E.scalar_tensor_tensor(
                        ctx["ltp"][:, j * 128:(j + 1) * 128],
                        ctx["PA"][:, j * 128:(j + 1) * 128], 1.0,
                        ctx["c_maskui"][:, :], ALU.mult, ALU.mult), "ve")
                    p.inst("vector", lambda E, ctx, j=j, h=h, oUB=oUB: E.scalar_tensor_tensor(
                        ctx["urel"][:, j * 64:(j + 1) * 64],
                        ctx["PB"][:, j * 64:(j + 1) * 64], 1.0,
                        ctx["ldUB"][:, oUB + h * 64: oUB + (h + 1) * 64],
                        ALU.mult, ALU.add), "ve")
                    p.inst("vector", lambda E, ctx, j=j, h=h, hb=hb, pr=pr, oB=oB: E.scalar_tensor_tensor(
                        ctx["z0pad"][hb:hb + 64, h * 64:(h + 1) * 64],
                        ctx["PB"][hb:hb + 64, 128 + j * 64: 128 + (j + 1) * 64], 1.0,
                        ctx["ldB"][hb:hb + 64, oB + pr * 64: oB + (pr + 1) * 64],
                        ALU.mult, ALU.add), "ve")
                p4_gate = p.cnt["ve"]
                p.wait("tensor", "ve")
                for j in range(2):
                    h = 2 * pr + j
                    hb = j * 64
                    p.inst("tensor", lambda E, ctx, j=j, h=h, hb=hb, pr=pr, c=c: E.matmul(
                        ctx["PC"][hb:hb + 64, 0:128],
                        ctx["z0pad"][:, h * 64:(h + 1) * 64],
                        ctx["qfm"][:, pr * T + c * 128: pr * T + (c + 1) * 128],
                        start=True, stop=False, tile_position=(0, hb),
                        skip_group_check=True))
                    p.inst("tensor", lambda E, ctx, j=j, hb=hb: E.matmul(
                        ctx["PC"][hb:hb + 64, 0:128],
                        ctx["urel"][:, j * 64:(j + 1) * 64],
                        ctx["ltp"][:, j * 128:(j + 1) * 128],
                        start=False, stop=(j == 1), tile_position=(0, hb),
                        skip_group_check=True), "tm" if j == 1 else None)
                p.wait("vector", "tm")
                p.inst("vector", lambda E, ctx, pr=pr, c=c: E.tensor_copy(
                    ctx["ofm"][:, pr * T + c * 128: pr * T + (c + 1) * 128],
                    ctx["PC"][:, 0:128]), "ve")
        if cfg.debug:
            p.wait("gpsimd", "ve")
            p.group("gpsimd", "gd")
            p.inst("gpsimd", lambda E, ctx: E.dma_start(
                out=dbg["d_ofm"], in_=ctx["ofm"].ap()), "gd", 16)

    if cfg.stage >= 5:
        p.free("kfm", "qfm", "zcore", "ldUB", "ldUPT", "ldZPT", "ldB",
               "z0pad", "urel", "ltp")
        p.phase("P5")
        sb("wt", (128, DCH * DM))
        sb("anorm", (128, DCH * T))
        sb("bcol", (128, DCH), F32)
        p.wait("gpsimd", "tm")
        p.group("gpsimd", "gd")
        p.inst("gpsimd", lambda E, ctx: E.dma_start(out=ctx["wt"][:, :], in_=wo), "gd", 16)
        p.inst("gpsimd", lambda E, ctx: E.dma_start(out=ctx["bcol"][:, 0:DCH], in_=bo), "gd", 16)
        p.wait("tensor", "gd")
        p.wait("scalar", "gd")
        proj_fm("anorm", src="ofm", bias="bcol")

    if cfg.stage >= 5:
        p.free("ofm", "wt")
        p.phase("P6")
        sb("xf", (128, DCH * T), F32)
        sb("y2", (128, DCH * T))
        sb("bc1", (128, T)); sb("bc2", (128, T))
        sb("sq", (128, 4096))
        sb("x2c", (128, 4096))
        sb("st2", (1, 2 * T), BF16)
        p.group("gpsimd", "gd")
        p.inst("gpsimd", lambda E, ctx: E.dma_start(
            out=ctx["xf"].ap().rearrange("p (c t) -> p c t", c=DCH),
            in_=x_in.rearrange("(c p) t -> p c t", p=128)), "gd", 16)
        for e in ("tensor", "vector", "scalar"):
            p.wait(e, "gd")
        ln_feature_major("anorm", "anorm")
        for nt in range(NT):
            p.wait("scalar", "tm")
            p.wait("vector", "tm")
            for c in range(DCH):
                reg = c * 512
                p.inst("vector", lambda E, ctx, c=c, nt=nt, reg=reg: E.tensor_add(
                    ctx["x2c"][:, reg:reg + 512],
                    ctx["xf"][:, c * T + nt * 512: c * T + (nt + 1) * 512],
                    ctx["anorm"][:, c * T + nt * 512: c * T + (nt + 1) * 512]), "ve")
                p.wait("scalar", "ve")
                p.inst("scalar", lambda E, ctx, c=c, reg=reg: E.activation(
                    ctx["sq"][:, reg:reg + 512], ctx["x2c"][:, reg:reg + 512],
                    AF.Square), "sc")
                p.wait("tensor", "sc")
                p.inst("tensor", lambda E, ctx, c=c, reg=reg: E.matmul(
                    ctx["PA"][0:1, :], ctx["c_onesc"][:, :],
                    ctx["x2c"][:, reg:reg + 512],
                    start=(c == 0), stop=(c == DCH - 1), skip_group_check=True))
                p.inst("tensor", lambda E, ctx, c=c, reg=reg: E.matmul(
                    ctx["PB"][0:1, :], ctx["c_onesc"][:, :],
                    ctx["sq"][:, reg:reg + 512],
                    start=(c == 0), stop=(c == DCH - 1), skip_group_check=True),
                    "tm" if c == DCH - 1 else None)
            p.wait("vector", "tm")
            p.inst("vector", lambda E, ctx: E.tensor_copy(
                ctx["lntmp"][0:1, 0:512], ctx["PA"][0:1, :]), "ve")
            p.inst("vector", lambda E, ctx: E.tensor_copy(
                ctx["lntmp"][0:1, 512:1024], ctx["PB"][0:1, :]), "ve")
            p.wait("tensor", "ve")
            p.wait("scalar", "ve")
            ln_stats_math(nt, save_rows="st2")
            ln_bcast(nt)
        for c in range(DCH):
            p.inst("vector", lambda E, ctx, c=c: E.tensor_add(
                ctx["x2c"][:, 0:T], ctx["xf"][:, c * T:(c + 1) * T],
                ctx["anorm"][:, c * T:(c + 1) * T]), "ve")
            p.inst("vector", lambda E, ctx, c=c: E.tensor_mul(
                ctx["y2"][:, c * T:(c + 1) * T], ctx["x2c"][:, 0:T],
                ctx["bc1"][:, :]), "ve")
            p.inst("vector", lambda E, ctx, c=c: E.tensor_add(
                ctx["y2"][:, c * T:(c + 1) * T], ctx["y2"][:, c * T:(c + 1) * T],
                ctx["bc2"][:, :]), "ve")
        if cfg.debug and cfg.stage >= 5:
            p.wait("gpsimd", "ve")
            p.group("gpsimd", "gd")
            p.inst("gpsimd", lambda E, ctx: E.dma_start(
                out=dbg["d_y2"], in_=ctx["y2"].ap()), "gd", 16)

    if cfg.stage >= 9:
        p.free("anorm", "xf", "bc1", "bc2", "sq", "x2c", "bcol", "lnr", "lntmp",
               "c_maskl", "c_masku", "c_maskui", "c_ident", "c_e64", "czero")
        p.phase("P7")
        NT3 = TFh // 512
        sb("hA", (128, FCH * TFh))
        sb("hp", (128, 2 * FCH * 128))
        sb("fsc", (128, 4 * FCH + DCH), F32)
        p.group("gpsimd", "gd")
        for i_, src_ in enumerate([w1s, w2s, h1s, h2s]):
            p.inst("gpsimd", lambda E, ctx, i_=i_, src_=src_: E.dma_start(
                out=ctx["fsc"][:, i_ * FCH:(i_ + 1) * FCH], in_=src_), "gd", 16)
        p.inst("gpsimd", lambda E, ctx: E.dma_start(
            out=ctx["fsc"][:, 4 * FCH:4 * FCH + DCH], in_=w3s), "gd", 16)
        for e in ("tensor", "vector", "scalar"):
            p.wait(e, "gd")

        for hf in range(2):
            toff = hf * TFh
            pw = f"P7w{hf}"
            p.phase(pw)
            sb("hB", (128, FCH * TFh))
            sb("silu", (128, 1024))
            sb("wp", (128, 4 * DCH * 128))
            gates = {0: (p.cnt["ve"], p.cnt["sc"]), 1: (p.cnt["ve"], p.cnt["sc"])}
            wgate = {0: p.cnt["tm"], 1: p.cnt["tm"]}
            gd_tot = {}

            def w12_dma(po):
                slot = po % 2
                p.wait("gpsimd", "tm", wgate[slot])
                p.group("gpsimd", "gd")
                p.inst("gpsimd", lambda E, ctx, po=po, slot=slot: E.dma_start(
                    out=ctx["wp"][:, (2 * slot) * DCH * 128:(2 * slot + 1) * DCH * 128],
                    in_=w1p[po]), "gd", 16)
                p.inst("gpsimd", lambda E, ctx, po=po, slot=slot: E.dma_start(
                    out=ctx["wp"][:, (2 * slot + 1) * DCH * 128:(2 * slot + 2) * DCH * 128],
                    in_=w2p[po]), "gd", 16)
                gd_tot[po] = p.cnt["gd"]

            w12_dma(0)
            for po in range(FCH):
                slot = po % 2
                if po + 1 < FCH:
                    w12_dma(po + 1)
                p.wait("tensor", "gd", gd_tot[po])
                for nt in range(NT3):
                    par = (po * NT3 + nt) % 2
                    pzA = "PA" if par == 0 else "PC"
                    pzB = "PB" if par == 0 else "PZ1"
                    p.wait("tensor", "ve", gates[par][0])
                    p.wait("tensor", "sc", gates[par][1])
                    for w_i, pz in ((0, pzA), (1, pzB)):
                        for kc in range(DCH):
                            p.inst("tensor", lambda E, ctx, po=po, nt=nt, kc=kc, slot=slot,
                                   w_i=w_i, pz=pz, toff=toff: E.matmul(
                                ctx[pz][:, 0:512],
                                ctx["wp"][:, (2 * slot + w_i) * DCH * 128 + kc * 128:
                                          (2 * slot + w_i) * DCH * 128 + (kc + 1) * 128],
                                ctx["y2"][:, kc * T + toff + nt * 512: kc * T + toff + (nt + 1) * 512],
                                start=(kc == 0), stop=(kc == DCH - 1),
                                skip_group_check=True), "tm" if kc == DCH - 1 else None)
                    p.wait("scalar", "tm")
                    p.wait("scalar", "ve", gates[par][0])
                    p.inst("scalar", lambda E, ctx, po=po, nt=nt, par=par, pzA=pzA: E.activation(
                        ctx["silu"][:, par * 512:(par + 1) * 512], ctx[pzA][:, 0:512],
                        AF.Silu, scale=ctx["fsc"][:, po:po + 1]), "sc")
                    p.wait("vector", "sc")
                    p.wait("vector", "tm")
                    p.inst("vector", lambda E, ctx, po=po, nt=nt, par=par, pzB=pzB: E.scalar_tensor_tensor(
                        ctx["hA"][:, po * TFh + nt * 512: po * TFh + (nt + 1) * 512],
                        ctx[pzB][:, 0:512], ctx["fsc"][:, FCH + po: FCH + po + 1],
                        ctx["silu"][:, par * 512:(par + 1) * 512],
                        ALU.mult, ALU.mult), "ve")
                    gates[par] = (p.cnt["ve"], p.cnt["sc"])
                wgate[slot] = p.cnt["tm"]
            if cfg.debug and hf == 1:
                p.wait("gpsimd", "ve")
                p.group("gpsimd", "gd")
                p.inst("gpsimd", lambda E, ctx: E.dma_start(
                    out=dbg["d_hffn"], in_=ctx["hA"].ap()), "gd", 16)
            for (wsrc, scoff, srcb, dstb) in ((h1p, 2 * FCH, "hA", "hB"),
                                              (h2p, 3 * FCH, "hB", "hA")):
                p.wait("tensor", "ve")
                p.wait("tensor", "sc")
                gates2 = {0: p.cnt["sc"], 1: p.cnt["sc"]}
                hgate = {0: p.cnt["tm"], 1: p.cnt["tm"]}
                hgd_tot = {}

                def h_dma(po, wsrc=wsrc):
                    slot = po % 2
                    p.wait("gpsimd", "tm", hgate[slot])
                    p.group("gpsimd", "gd")
                    p.inst("gpsimd", lambda E, ctx, po=po, slot=slot, wsrc=wsrc: E.dma_start(
                        out=ctx["hp"][:, slot * FCH * 128:(slot + 1) * FCH * 128],
                        in_=wsrc[po]), "gd", 16)
                    hgd_tot[po] = p.cnt["gd"]

                h_dma(0)
                for po in range(FCH):
                    slot = po % 2
                    if po + 1 < FCH:
                        h_dma(po + 1)
                    p.wait("tensor", "gd", hgd_tot[po])
                    for nt in range(NT3):
                        par = (po * NT3 + nt) % 2
                        pz = "PZ1" if par == 0 else "PZ2"
                        preg = 512 if par == 0 else 0
                        p.wait("tensor", "sc", gates2[par])
                        for kc in range(FCH):
                            p.inst("tensor", lambda E, ctx, po=po, nt=nt, kc=kc, slot=slot,
                                   pz=pz, preg=preg, srcb=srcb: E.matmul(
                                ctx[pz][:, preg:preg + 512],
                                ctx["hp"][:, slot * FCH * 128 + kc * 128: slot * FCH * 128 + (kc + 1) * 128],
                                ctx[srcb][:, kc * TFh + nt * 512: kc * TFh + (nt + 1) * 512],
                                start=(kc == 0), stop=(kc == FCH - 1),
                                skip_group_check=True), "tm" if kc == FCH - 1 else None)
                        p.wait("scalar", "tm")
                        p.inst("scalar", lambda E, ctx, po=po, nt=nt, pz=pz, preg=preg,
                               scoff=scoff, dstb=dstb: E.activation(
                            ctx[dstb][:, po * TFh + nt * 512: po * TFh + (nt + 1) * 512],
                            ctx[pz][:, preg:preg + 512], AF.Gelu,
                            scale=ctx["fsc"][:, scoff + po: scoff + po + 1]), "sc")
                        gates2[par] = p.cnt["sc"]
                    hgate[slot] = p.cnt["tm"]
                if srcb == "hA":
                    pass
            po7 = f"P7o{hf}"
            p.free("hB", "silu", "wp")
            p.phase(po7)
            sb("fout", (128, DCH * TFh))
            sb("lnr", (1, 2 * T), BF16)
            sb("lntmp", (1, 2048), F32)
            sb("sq3", (128, 4096))
            sb("b3r", (128, TFh)); sb("b3n", (128, TFh))
            sb("b2r", (128, TFh)); sb("b2n", (128, TFh))
            sb("t7", (128, 2048))
            sb("ostg", (128, 1024), F32)
            p.wait("tensor", "sc")
            p.wait("tensor", "ve")
            gates3 = {0: p.cnt["ve"], 1: p.cnt["ve"]}
            w3gate = {0: p.cnt["tm"], 1: p.cnt["tm"]}
            wgd_tot = {}

            def w3_dma(po):
                slot = po % 2
                p.wait("gpsimd", "tm", w3gate[slot])
                p.group("gpsimd", "gd")
                p.inst("gpsimd", lambda E, ctx, po=po, slot=slot: E.dma_start(
                    out=ctx["hp"][:, slot * FCH * 128:(slot + 1) * FCH * 128],
                    in_=w3p[po]), "gd", 16)
                wgd_tot[po] = p.cnt["gd"]

            w3_dma(0)
            for po in range(DCH):
                slot = po % 2
                if po + 1 < DCH:
                    w3_dma(po + 1)
                p.wait("tensor", "gd", wgd_tot[po])
                for nt in range(NT3):
                    par = (po * NT3 + nt) % 2
                    pz = "PZ1" if par == 0 else "PZ2"
                    preg = 512 if par == 0 else 0
                    p.wait("tensor", "ve", gates3[par])
                    for kc in range(FCH):
                        p.inst("tensor", lambda E, ctx, po=po, nt=nt, kc=kc, slot=slot,
                               pz=pz, preg=preg: E.matmul(
                            ctx[pz][:, preg:preg + 512],
                            ctx["hp"][:, slot * FCH * 128 + kc * 128: slot * FCH * 128 + (kc + 1) * 128],
                            ctx["hA"][:, kc * TFh + nt * 512: kc * TFh + (nt + 1) * 512],
                            start=(kc == 0), stop=(kc == FCH - 1),
                            skip_group_check=True), "tm" if kc == FCH - 1 else None)
                    p.wait("vector", "tm")
                    p.inst("vector", lambda E, ctx, po=po, nt=nt, pz=pz, preg=preg: E.tensor_scalar_mul(
                        ctx["fout"][:, po * TFh + nt * 512: po * TFh + (nt + 1) * 512],
                        ctx[pz][:, preg:preg + 512],
                        ctx["fsc"][:, 4 * FCH + po: 4 * FCH + po + 1]), "ve")
                    gates3[par] = p.cnt["ve"]
                w3gate[slot] = p.cnt["tm"]
            for nt in range(NT3):
                p.wait("scalar", "tm")
                p.wait("scalar", "ve")
                for c in range(DCH):
                    reg = c * 512
                    p.inst("scalar", lambda E, ctx, c=c, nt=nt, reg=reg: E.activation(
                        ctx["sq3"][:, reg:reg + 512],
                        ctx["fout"][:, c * TFh + nt * 512: c * TFh + (nt + 1) * 512],
                        AF.Square), "sc")
                    p.wait("tensor", "sc")
                    p.wait("tensor", "ve")
                    p.inst("tensor", lambda E, ctx, c=c, nt=nt: E.matmul(
                        ctx["PA"][0:1, :], ctx["c_onesc"][:, :],
                        ctx["fout"][:, c * TFh + nt * 512: c * TFh + (nt + 1) * 512],
                        start=(c == 0), stop=(c == DCH - 1), skip_group_check=True))
                    p.inst("tensor", lambda E, ctx, c=c, nt=nt, reg=reg: E.matmul(
                        ctx["PB"][0:1, :], ctx["c_onesc"][:, :],
                        ctx["sq3"][:, reg:reg + 512],
                        start=(c == 0), stop=(c == DCH - 1), skip_group_check=True),
                        "tm" if c == DCH - 1 else None)
                p.wait("vector", "tm")
                p.inst("vector", lambda E, ctx: E.tensor_copy(
                    ctx["lntmp"][0:1, 0:512], ctx["PA"][0:1, :]), "ve")
                p.inst("vector", lambda E, ctx: E.tensor_copy(
                    ctx["lntmp"][0:1, 512:1024], ctx["PB"][0:1, :]), "ve")
                p.wait("tensor", "ve")
                p.wait("scalar", "ve")
                ln_stats_math(nt)
                gnt = hf * NT3 + nt
                p.wait("tensor", "ve")
                p.wait("tensor", "sc")
                p.inst("tensor", lambda E, ctx, nt=nt: E.matmul(
                    ctx["PA"][:, :], ctx["c_ones1b"][:, :],
                    ctx["lnr"][0:1, nt * 512:(nt + 1) * 512], start=True, stop=True,
                    skip_group_check=True))
                p.inst("tensor", lambda E, ctx, nt=nt: E.matmul(
                    ctx["PB"][:, :], ctx["c_ones1b"][:, :],
                    ctx["lnr"][0:1, T + nt * 512: T + (nt + 1) * 512], start=True,
                    stop=True, skip_group_check=True))
                p.inst("tensor", lambda E, ctx, gnt=gnt: E.matmul(
                    ctx["PC"][:, :], ctx["c_ones1b"][:, :],
                    ctx["st2"][0:1, T + gnt * 512: T + (gnt + 1) * 512], start=True,
                    stop=True, skip_group_check=True))
                p.inst("tensor", lambda E, ctx, gnt=gnt: E.matmul(
                    ctx["PZ1"][:, 0:512], ctx["c_ones1b"][:, :],
                    ctx["st2"][0:1, gnt * 512:(gnt + 1) * 512], start=True,
                    stop=True, skip_group_check=True), "tm")
                p.wait("vector", "tm")
                p.inst("vector", lambda E, ctx, nt=nt: E.tensor_copy(
                    ctx["b3r"][:, nt * 512:(nt + 1) * 512], ctx["PA"][:, :]), "ve")
                p.inst("vector", lambda E, ctx, nt=nt: E.tensor_copy(
                    ctx["b3n"][:, nt * 512:(nt + 1) * 512], ctx["PB"][:, :]), "ve")
                p.inst("vector", lambda E, ctx, nt=nt: E.tensor_copy(
                    ctx["b2r"][:, nt * 512:(nt + 1) * 512], ctx["PC"][:, :]), "ve")
                p.inst("vector", lambda E, ctx, nt=nt: E.tensor_copy(
                    ctx["b2n"][:, nt * 512:(nt + 1) * 512], ctx["PZ1"][:, 0:512]), "ve")
                p.wait("tensor", "ve")
            sgate = {0: 0, 1: 0}
            for c in range(DCH):
                for nt in range(NT3):
                    par = (c * NT3 + nt) % 2
                    treg = par * 512
                    p.wait("vector", "sd", sgate[par])
                    ta, tb = par * 1024, par * 1024 + 512
                    p.inst("vector", lambda E, ctx, c=c, nt=nt, toff=toff, ta=ta: E.tensor_mul(
                        ctx["t7"][:, ta:ta + 512],
                        ctx["y2"][:, c * T + toff + nt * 512: c * T + toff + (nt + 1) * 512],
                        ctx["b2r"][:, nt * 512:(nt + 1) * 512]), "ve")
                    p.inst("vector", lambda E, ctx, nt=nt, ta=ta: E.tensor_add(
                        ctx["t7"][:, ta:ta + 512], ctx["t7"][:, ta:ta + 512],
                        ctx["b2n"][:, nt * 512:(nt + 1) * 512]), "ve")
                    p.inst("vector", lambda E, ctx, c=c, nt=nt, tb=tb: E.tensor_mul(
                        ctx["t7"][:, tb:tb + 512],
                        ctx["fout"][:, c * TFh + nt * 512: c * TFh + (nt + 1) * 512],
                        ctx["b3r"][:, nt * 512:(nt + 1) * 512]), "ve")
                    p.inst("vector", lambda E, ctx, nt=nt, tb=tb: E.tensor_add(
                        ctx["t7"][:, tb:tb + 512], ctx["t7"][:, tb:tb + 512],
                        ctx["b3n"][:, nt * 512:(nt + 1) * 512]), "ve")
                    p.inst("vector", lambda E, ctx, ta=ta, tb=tb, treg=treg: E.tensor_add(
                        ctx["ostg"][:, treg:treg + 512], ctx["t7"][:, ta:ta + 512],
                        ctx["t7"][:, tb:tb + 512]), "ve")
                    p.wait("sync", "ve")
                    p.group("sync", "sd")
                    p.inst("sync", lambda E, ctx, c=c, nt=nt, toff=toff, treg=treg: E.dma_start(
                        out=out.rearrange("(c p) t -> p c t", p=128)[:, c,
                            toff + nt * 512: toff + (nt + 1) * 512],
                        in_=ctx["ostg"][:, treg:treg + 512]), "sd", 16)
                    sgate[par] = p.cnt["sd"]
            p.free("fout", "sq3", "b3r", "b3n", "b2r", "b2n", "t7", "ostg",
                   "lnr", "lntmp")

    if cfg.stage < 5:
        phsrc = "y1" if cfg.stage < 2 else "qfm"
        for e in ("ve", "sc", "tm"):
            p.wait("gpsimd", e)
        p.group("gpsimd", "gd")
        p.inst("gpsimd", lambda E, ctx, phsrc=phsrc: E.dma_start(out=out_pct, in_=ctx[phsrc].ap().rearrange("p (c t) -> p c t", c=DCH)), "gd", 16)
    p.wait("sync", "sd", p.cnt["sd"])
    p.wait("sync", "gd", p.cnt["gd"])
    p.wait("gpsimd", "gd", p.cnt["gd"])

    p.close()
    import contextlib
    nph = len(p.phases)

    reqs = []
    for pi, ph in enumerate(p.phases):
        for (name, kind, shape, dt, side) in p.allocs[ph]:
            if kind != "sbuf":
                continue
            fidx = nph - 1
            for (fpi, fname) in p.frees:
                if fname == name and fpi >= pi:
                    fidx = fpi
                    break
            size = shape[1] * (4 if dt == F32 else 2)
            reqs.append([pi, ph, name, shape, dt, size, fidx])
    base, top = (nc.sbuf_base + 127) // 128 * 128, nc.sbuf_top
    placed = []
    offs = {}
    for r in sorted(reqs, key=lambda r: (0 if (r[0] == 0 and r[6] == nph - 1) else 1, -r[5])):
        pi, ph, name, shape, dt, size, fidx = r
        cands = [base]
        for (o2, s2, p2, f2) in placed:
            if not (f2 < pi or p2 > fidx):
                cands.append((o2 + s2 + 127) // 128 * 128)
        ok_off = None
        for cand in sorted(cands):
            if cand + size > top:
                continue
            conflict = False
            for (o2, s2, p2, f2) in placed:
                if f2 < pi or p2 > fidx:
                    continue
                if not (cand + size <= o2 or o2 + s2 <= cand):
                    conflict = True
                    break
            if not conflict:
                ok_off = cand
                break
        if ok_off is None:
            occ = [sum(r2[5] + 127 for r2 in reqs if r2[0] <= qi <= r2[6]) for qi in range(nph)]
            raise MemoryError(f"SBUF plan OOM: {name} size {size} phase {ph}; cap {top-base}; per-phase {occ}")
        placed.append((ok_off, size, pi, fidx))
        offs[(pi, name)] = ok_off

    with contextlib.ExitStack() as outer:
        ctx = {}
        for sname in p.sem_names:
            ctx[sname] = outer.enter_context(nc.semaphore(sname))
        for pi, ph in enumerate(p.phases):
            for (name, kind, shape, dt, side) in p.allocs[ph]:
                uname = f"{name}__{ph}"
                if kind == "sbuf":
                    ctx[name] = nc.alloc_sbuf_tensor_at(
                        uname, shape, dt, offset=offs[(pi, name)])
                else:
                    ctx[name] = outer.enter_context(nc.psum_tensor(uname, shape, dt))
            with nc.Block() as block:
                for eng in ENGS:
                    a, b = p.seg[ph][eng]
                    steps = p.steps[eng][a:b]
                    if steps:
                        getattr(block, eng)(lambda E, steps=steps: [f(E, ctx) for f in steps])
    return nc, dbg

def prep_in_maps(cfg: CFG, inp):
    T = cfg.tpc
    x = np.asarray(inp["L0_attn_x"], np.float32)[0]
    xT = np.ascontiguousarray(x.T)
    con = build_consts()
    shared = {
        "wq": _retile_w(np.asarray(inp["L0_attn_Wq"], np.float32)),
        "bq": _cvec(np.asarray(inp["L0_attn_bq"], np.float32)),
        "wk": _retile_w(np.asarray(inp["L0_attn_Wk"], np.float32)),
        "bk": np.asarray(inp["L0_attn_bk"], np.float32).reshape(1, -1),
        "wv": _retile_w(np.asarray(inp["L0_attn_Wv"], np.float32)),
        "bv": np.asarray(inp["L0_attn_bv"], np.float32).reshape(1, -1),
        "wb": _retile_w(np.asarray(inp["L0_attn_Wbeta"], np.float32)),
        "bb": np.asarray(inp["L0_attn_bbeta"], np.float32).reshape(1, -1),
        "wo": _retile_w(np.asarray(inp["L0_attn_Wo"], np.float32)),
        "bo": _cvec(np.asarray(inp["L0_attn_bo"], np.float32)),
        "w1p": _panels(np.ascontiguousarray(np.asarray(inp["L0_ffn_w1_w"], np.float32).T)),
        "w1s": _cvec(np.asarray(inp["L0_ffn_w1_s"], np.float32)),
        "w2p": _panels(np.ascontiguousarray(np.asarray(inp["L0_ffn_w2_w"], np.float32).T)),
        "w2s": _cvec(np.asarray(inp["L0_ffn_w2_s"], np.float32)),
        "h1p": _panels(np.ascontiguousarray(np.asarray(inp["L0_ffn_h1_w"], np.float32).T)),
        "h1s": _cvec(np.asarray(inp["L0_ffn_h1_s"], np.float32)),
        "h2p": _panels(np.ascontiguousarray(np.asarray(inp["L0_ffn_h2_w"], np.float32).T)),
        "h2s": _cvec(np.asarray(inp["L0_ffn_h2_s"], np.float32)),
        "w3p": _panels(np.ascontiguousarray(np.asarray(inp["L0_ffn_w3_w"], np.float32).T)),
        "w3s": _cvec(np.asarray(inp["L0_ffn_w3_s"], np.float32)),
    }
    shared.update({k: np.ascontiguousarray(v) for k, v in con.items()})
    in_maps = []
    for ci in range(cfg.n_cores):
        m = dict(shared)
        m["x"] = np.ascontiguousarray(xT[:, ci * T:(ci + 1) * T])
        pm = np.zeros((64, 8), np.float32)
        pm[:, :ci] = 1.0
        m["pmask"] = pm
        im = np.zeros((64, 8 * 64), np.float32)
        for j in range(8):
            if j >= ci:
                im[:, j * 64:(j + 1) * 64] = np.eye(64, dtype=np.float32)
        m["imask"] = im
        in_maps.append(m)
    return in_maps


def kernel(**inputs):
    cfg = CFG()
    in_maps = prep_in_maps(cfg, inputs)
    nc, _ = build_nc(cfg)
    res = run_bass_kernel_spmd(nc, in_maps, core_ids=list(range(cfg.n_cores)))
    outs = [r["out"] for r in res.results]
    full = np.concatenate(outs, axis=1)
    return np.ascontiguousarray(full.T)[None].astype(np.float32)
